# version 09
# baseline (speedup 1.0000x reference)
import numpy as np
import concourse.bass as bass
import concourse.mybir as mybir
from concourse.bass_utils import run_bass_kernel_spmd

F32 = mybir.dt.float32
BF16 = mybir.dt.bfloat16
I32 = mybir.dt.int32
AF = mybir.ActivationFunctionType
ALU = mybir.AluOpType
AX = mybir.AxisListType

T = 2048
D = 1024
NT = T // 128
KC = D // 128
N_CORES = 8

SB_BASE = 16512
OFF_X = SB_BASE
OFF_YFM = SB_BASE + 65536
OFF_COM = SB_BASE + 98304
OFF_ARENA = SB_BASE + 107520
ARENA_SZ = 229376 - OFF_ARENA

ENGINES = ("tensor", "vector", "scalar", "gpsimd", "sync")


class Prog:

    def __init__(self, nc):
        self.nc = nc
        self.ops = {e: [] for e in ENGINES}
        self.sem = {e: nc.alloc_semaphore(f"sem_{e}") for e in ENGINES}
        self.sem.update({e + ".d": nc.alloc_semaphore(f"semd_{e}")
                         for e in ("sync", "gpsimd", "scalar", "vector")})
        self.count = {s: 0 for s in self.sem}
        self.last_wait = {}
        self.writer = {}
        self.readers = {}

    def group(self, eng, reads=(), writes=(), chain=None):
        return _Group(self, eng, list(reads), list(writes), chain)

    def barrier(self):
        snap = dict(self.count)
        for e in ENGINES:
            for tgt, v in snap.items():
                if v == 0:
                    continue
                k = (e, tgt)
                if self.last_wait.get(k, 0) >= v:
                    continue
                self.last_wait[k] = v
                sem = self.sem[tgt]
                self.ops[e].append(lambda en, s=sem, v=v: en.wait_ge(s, v))

    def finalize(self):
        with self.nc.Block() as blk:
            for e in ENGINES:
                ops = self.ops[e]
                getattr(blk, e)(lambda en, ops=ops: [f(en) for f in ops])


class _Group:

    def __init__(self, P, eng, reads, writes, chain=None):
        self.P, self.eng, self.reads, self.writes = P, eng, reads, writes
        self.chain = chain if chain is not None else (eng != "tensor")
        self.items = []

    def emit(self, f):
        self.items.append(f)

    def dma_(self, out, in_, **kw):
        self.items.append(("dma", lambda en, o=out, i=in_, kw=kw: en.dma_start(out=o, in_=i, **kw)))

    def __enter__(self):
        return self

    def __exit__(self, et, ev, tb):
        if et is not None:
            return False
        P, eng = self.P, self.eng
        n_dma = sum(1 for it in self.items if isinstance(it, tuple))
        assert n_dma == 0 or n_dma == len(self.items), "don't mix dma and compute"
        stream = eng + ".d" if n_dma else eng
        deps = []
        for key in self.reads:
            w = P.writer.get(key)
            if w:
                deps.append(w)
        for key in self.writes:
            w = P.writer.get(key)
            if w:
                deps.append(w)
            for s2, cnt in P.readers.get(key, {}).items():
                deps.append((s2, cnt))
        waits = {}
        for tgt, cnt in deps:
            if eng == "tensor" and tgt == eng:
                continue
            if P.last_wait.get((eng, tgt), 0) >= cnt:
                continue
            waits[tgt] = max(waits.get(tgt, 0), cnt)
        for tgt, cnt in waits.items():
            P.last_wait[(eng, tgt)] = cnt
            sem = P.sem[tgt]
            P.ops[eng].append(lambda en, s=sem, v=cnt: en.wait_ge(s, v))
        sem = P.sem[stream]
        if n_dma:
            for _, f in self.items:
                P.ops[eng].append(lambda en, f=f, s=sem: f(en).then_inc(s, 16))
            P.count[stream] += 16 * n_dma
            tot = P.count[stream]
            P.ops[eng].append(lambda en, s=sem, v=tot: en.wait_ge(s, v))
        else:
            assert self.items, "empty group"
            if eng == "tensor":
                for f in self.items[:-1]:
                    P.ops[eng].append(f)
                last = self.items[-1]
                P.ops[eng].append(lambda en, f=last, s=sem: f(en).then_inc(s, 1))
            else:
                for j, f in enumerate(self.items[:-1]):
                    if j and self.chain:
                        P.ops[eng].append(lambda en: en.drain(fusable=True))
                    P.ops[eng].append(f)
                if len(self.items) > 1 and self.chain:
                    P.ops[eng].append(lambda en: en.drain(fusable=True))
                last = self.items[-1]
                P.ops[eng].append(lambda en, f=last, s=sem: f(en).then_inc(s, 1))
            P.count[stream] += 1
        cnt = P.count[stream]
        for key in self.writes:
            P.writer[key] = (stream, cnt)
            P.readers[key] = {}
        for key in self.reads:
            P.readers.setdefault(key, {})[stream] = cnt
        if PARANOID:
            P.barrier()
        return False


PARANOID = False
L0B_HLIM = 16
L0B_TLIM = 16


class Ctx:
    def __init__(self):
        self.nc = bass.Bass("TRN2", target_bir_lowering=False)
        self.dram = {}

    def dram_in(self, name, shape, dtype=F32):
        if name not in self.dram:
            self.dram[name] = self.nc.dram_tensor(name, list(shape), dtype, kind="ExternalInput").ap()
        return self.dram[name]

    def dram_out(self, name, shape, dtype=F32):
        self.dram[name] = self.nc.dram_tensor(name, list(shape), dtype, kind="ExternalOutput").ap()
        return self.dram[name]

    def dram_tmp(self, name, shape, dtype=F32):
        if name not in self.dram:
            self.dram[name] = self.nc.dram_tensor(name, list(shape), dtype, kind="Internal").ap()
        return self.dram[name]

    def sb_at(self, name, shape, dtype, off):
        nbytes = int(np.prod(shape[1:])) * mybir.dt.size(dtype)
        assert off + nbytes <= 229376, (name, off, nbytes)
        return self.nc.alloc_sbuf_tensor_at(name, list(shape), dtype, offset=off)

    def ps(self, name, shape, dtype=F32):
        return self.nc.alloc_psum_tensor(name, list(shape), dtype)


def dram2d(ap, row0, nrows, ncols, col0=0, row_len=None, p=128):
    if row_len is None:
        row_len = ap.tensor.shape[-1]
    ng = nrows // p
    return bass.AP(ap.tensor, row0 * row_len + col0,
                   [[row_len, p], [p * row_len, ng], [1, ncols]])


def alloc_common(C):
    C.x_tm = C.sb_at("x_tm", (128, NT*D), F32, OFF_X)
    C.y_fm = C.sb_at("y_fm", (128, KC*T), BF16, OFF_YFM)
    o = OFF_COM
    C.ln_stats = C.sb_at("ln_stats", (128, 5*NT), F32, o); o += 5*NT*4
    C.sb_sq = C.sb_at("sb_sq", (128, D), F32, o); o += D*4
    C.ident = C.sb_at("ident", (128, 128), BF16, o); o += 256
    C.identf = C.sb_at("identf", (128, 128), F32, o); o += 512
    C.ones_row = C.sb_at("ones_row", (1, 512), F32, o); o += 2048
    C.ones_cb = C.sb_at("ones_cb", (128, 1), BF16, o); o += 32
    C.com_off = o
    assert o <= OFF_ARENA
    C.consts = C.sb_at("consts", (128, 8), F32, o); o += 32
    C.com_off = o
    CONST_VALS = [1e-5, 1.0, 1e-8]
    for j, v in enumerate(CONST_VALS):
        C.nc.const_aps.register(F32, v, C.consts[:, j:j+1])
    C.ps_tr = C.ps("ps_tr", (128, 1024), BF16)
    C.psum_f32 = [C.ps("psf%d" % i, (128, 512), F32) for i in range(6)]
    C.ps_aux = C.ps("ps_aux", (128, 512), F32)
    P = C.P
    with P.group("gpsimd", writes=["consts"]) as g:
        for j, v in enumerate(CONST_VALS):
            g.emit(lambda g_, j=j, v=v: g_.memset(C.consts[:, j:j+1], v))
    with P.group("gpsimd", writes=["ident", "ones"]) as g:
        g.emit(lambda g_: g_.iota(C.ident[:], [[1, 128]], base=0, channel_multiplier=-1,
                                  allow_small_or_imprecise_dtypes=True))
        g.emit(lambda g_: g_.tensor_single_scalar(C.ident[:], C.ident[:], 0.0, ALU.is_equal))
        g.emit(lambda g_: g_.iota(C.identf[:], [[1, 128]], base=0, channel_multiplier=-1,
                                  allow_small_or_imprecise_dtypes=True))
        g.emit(lambda g_: g_.tensor_single_scalar(C.identf[:], C.identf[:], 0.0, ALU.is_equal))
        g.emit(lambda g_: g_.memset(C.ones_row[:], 1.0))
        g.emit(lambda g_: g_.memset(C.ones_cb[:], 1.0))


def emit_layernorm(C, x_tm, y_tm, src_key="x", dst_key="y_tm", eps=1e-5, nt=NT,
                   x_row=D, y_row=D):
    P = C.P
    st = C.ln_stats
    sq = C.sb_sq
    s_sum = st[:, 0*NT:0*NT+nt]
    s_ss = st[:, 1*NT:1*NT+nt]
    s_var = st[:, 2*NT:2*NT+nt]
    s_rstd = st[:, 3*NT:3*NT+nt]
    s_nb = st[:, 4*NT:4*NT+nt]
    for t in range(nt):
        xt = x_tm[:, t*x_row:t*x_row+D]
        with P.group("vector", reads=[src_key], writes=["ln_sum"]) as g:
            g.emit(lambda v, xt=xt, o=st[:, 0*NT+t:0*NT+t+1]: v.reduce_sum(o, xt, AX.X))
        with P.group("scalar", reads=[src_key], writes=["ln_ss"]) as g:
            g.emit(lambda s, xt=xt, o=st[:, 1*NT+t:1*NT+t+1]:
                   s.activation(sq[:], xt, AF.Square, accum_out=o))
    with P.group("vector", reads=["ln_sum", "ln_ss"], writes=["ln_var", "ln_sum"]) as g:
        g.emit(lambda v: v.tensor_scalar_mul(s_sum, s_sum, 1.0/D))
        g.emit(lambda v: v.tensor_scalar_mul(s_ss, s_ss, 1.0/D))
        g.emit(lambda v: v.tensor_tensor(s_var, s_sum, s_sum, ALU.mult))
        g.emit(lambda v: v.tensor_tensor(s_var, s_ss, s_var, ALU.subtract))
    with P.group("scalar", reads=["ln_var", "consts"], writes=["ln_rstd"]) as g:
        g.emit(lambda s: s.activation(s_rstd, s_var, AF.Ln, bias=eps))
        g.emit(lambda s: s.activation(s_rstd, s_rstd, AF.Exp, scale=-0.5))
    with P.group("vector", reads=["ln_rstd", "ln_sum"], writes=["ln_nb"]) as g:
        g.emit(lambda v: v.scalar_tensor_tensor(s_nb, s_sum, -1.0, s_rstd, ALU.mult, ALU.mult))
    for t in range(nt):
        xt = x_tm[:, t*x_row:t*x_row+D]
        yt = y_tm[:, t*y_row:t*y_row+D]
        with P.group("scalar", reads=[src_key, "ln_rstd", "ln_nb"], writes=[dst_key]) as g:
            g.emit(lambda s, xt=xt, yt=yt, sc=st[:, 3*NT+t:3*NT+t+1], b=st[:, 4*NT+t:4*NT+t+1]:
                   s.activation(yt, xt, AF.Identity, bias=b, scale=sc))


def emit_tm_to_fm(C, src_tm, dst_fm, src_key, dst_key, nt=NT, kc=KC,
                  src_row=None, dst_row=None, dst_col0=0):
    P = C.P
    pst = C.ps_tr
    ident = C.ident
    if src_row is None:
        src_row = kc * 128
    if dst_row is None:
        dst_row = nt * 128
    for c in range(kc):
        for k0 in range(0, nt, 8):
            kn = min(8, nt - k0)
            with P.group("tensor", reads=[src_key], writes=["ps_tr"]) as g:
                for j in range(kn):
                    t = k0 + j
                    src = src_tm[:, t*src_row + c*128: t*src_row + (c+1)*128]
                    g.emit(lambda p, s=src, o=pst[:, j*128:(j+1)*128]:
                           p.transpose(o, s, ident[:]))
            dst = dst_fm[:, c*dst_row + dst_col0 + k0*128:
                         c*dst_row + dst_col0 + (k0+kn)*128]
            with P.group("vector", reads=["ps_tr"], writes=[dst_key]) as g:
                g.emit(lambda v, d=dst, s=pst[:, :kn*128]: v.tensor_copy(out=d, in_=s))


def emit_matmul_acc(C, psum, lhsT_slices, rhs_slices, reads, writes):
    P = C.P
    with P.group("tensor", reads=list(reads), writes=list(writes)) as g:
        n = len(lhsT_slices)
        for i, (lt, rh) in enumerate(zip(lhsT_slices, rhs_slices)):
            g.emit(lambda p, lt=lt, rh=rh, st=(i == 0), sp=(i == n-1):
                   p.matmul(psum, lt, rh, start=st, stop=sp))


def emit_spill_x(C):
    P = C.P
    xs = C.dram_tmp("xspill", (128, NT*D), F32)
    with P.group("sync", reads=["x"], writes=["xspill"]) as g:
        for t in range(0, NT, 8):
            g.dma_(xs[:, t*D:(t+8)*D], C.x_tm[:, t*D:(t+8)*D])


def emit_reload_x(C, claims=()):
    P = C.P
    xs = C.dram_tmp("xspill", (128, NT*D), F32)
    with P.group("sync", reads=["xspill"], writes=["x"] + list(claims)) as g:
        for t in range(0, NT, 8):
            g.dma_(C.x_tm[:, t*D:(t+8)*D], xs[:, t*D:(t+8)*D])


def emit_prologue(C):
    y_tm = C.sb_at("y_tm_%d" % C.blk_i, (128, NT*D), BF16, OFF_ARENA)
    emit_layernorm(C, C.x_tm, y_tm, "x", "y_tm")
    emit_tm_to_fm(C, y_tm, C.y_fm, "y_tm", "y_fm")


def build_l1_ffn(C, tag):
    P = C.P
    S = 1.0 / 127.0
    H = 3072
    MC = H // 128
    wg_d = C.dram_in("l1ffn_wg_sb", (128, KC*H), I32)
    wu_d = C.dram_in("l1ffn_wu_sb", (128, MC*1024), I32)
    wd_d = C.dram_in("l1ffn_wd_sb", (128, MC*1024), I32)
    i = C.blk_i
    wu = C.sb_at("l1f_wu%d" % i, (128, MC*1024), BF16, OFF_X)
    hbuf = C.sb_at("l1f_h%d" % i, (128, MC*256), BF16, OFF_X + 49152)
    sgbuf = C.sb_at("l1f_sg%d" % i, (128, 256), F32, OFF_X + 61440)
    wg = C.sb_at("l1f_wg%d" % i, (128, KC*H), BF16, OFF_ARENA)
    wd = C.sb_at("l1f_wd%d" % i, (128, MC*1024), BF16, OFF_ARENA + 49152)
    xstage = C.sb_at("l1f_xs%d" % i, (128, 512), F32, OFF_ARENA + 98304)
    xsp = C.dram_tmp("xspill", (128, NT*D), F32)
    emit_spill_x(C)
    NLC = 4
    for q in range(NLC):
        with P.group("gpsimd", writes=["wg%d" % q, "y_tm"]) as g:
            g.dma_(wg[:, q*2*H:(q+1)*2*H], wg_d[:, q*2*H:(q+1)*2*H])
        with P.group("gpsimd", writes=["wu%d" % q] + (["x"] if q == 0 else [])) as g:
            g.dma_(wu[:, q*6144:(q+1)*6144], wu_d[:, q*6144:(q+1)*6144])
    for q in range(NLC):
        with P.group("gpsimd", writes=["wd%d" % q]) as g:
            g.dma_(wd[:, q*6144:(q+1)*6144], wd_d[:, q*6144:(q+1)*6144])
    PS = C.psum_f32

    for tc in range(0, NT, 2):
        for m in range(MC):
            ps_g, ps_u = PS[4][:, 0:256], PS[5][:, 0:256]
            lh = [wg[:, c*H + m*128: c*H + (m+1)*128] for c in range(KC)]
            rh = [C.y_fm[:, c*T + tc*128: (c*T) + (tc+2)*128] for c in range(KC)]
            emit_matmul_acc(C, ps_g, lh, rh,
                            reads=["wg%d" % (c*H//6144) for c in range(KC)] + ["y_fm"],
                            writes=["ps4"])
            lh = [wu[:, m*1024 + c*128: m*1024 + (c+1)*128] for c in range(KC)]
            emit_matmul_acc(C, ps_u, lh, rh, reads=["wu%d" % (m*1024//6144), "y_fm"],
                            writes=["ps5"])
            first = (tc == 0 and m == 0)
            with P.group("scalar", reads=["ps4"],
                         writes=["sg"] + (["x"] if first else [])) as g:
                g.emit(lambda s, ip=ps_g: s.activation(sgbuf[:, :], ip, AF.Silu, scale=S))
            with P.group("vector", reads=["ps5", "sg"],
                         writes=["h"] + (["x"] if first else [])) as g:
                g.emit(lambda v, ip=ps_u, o=hbuf[:, m*256:(m+1)*256]:
                       v.scalar_tensor_tensor(o, ip, S, sgbuf[:, :], ALU.mult, ALU.mult))
        for m in range(MC):
            with P.group("tensor", reads=["h", "wd%d" % (m*1024//6144)],
                         writes=["ps0", "ps1", "ps2", "ps3"]) as g:
                for tt in range(2):
                    for dh in range(2):
                        g.emit(lambda p, ps=PS[tt*2+dh][:, 0:512],
                               lt=hbuf[:, m*256 + tt*128: m*256 + (tt+1)*128],
                               rh=wd[:, m*1024 + dh*512: m*1024 + (dh+1)*512],
                               st=(m == 0), sp=(m == MC-1):
                               p.matmul(ps, lt, rh, start=st, stop=sp))
        for tt in range(2):
            for dh in range(2):
                ps = PS[tt*2 + dh][:, 0:512]
                with P.group("sync", reads=["xspill", "l1f_xst"], writes=["l1f_xst"]) as g:
                    g.dma_(xstage[:], xsp[:, (tc+tt)*D + dh*512: (tc+tt)*D + (dh+1)*512])
                with P.group("vector", reads=["ps%d" % (tt*2+dh), "l1f_xst"],
                             writes=["l1f_xst"]) as g:
                    g.emit(lambda v, ip=ps:
                           v.scalar_tensor_tensor(xstage[:], ip, S, xstage[:],
                                                  ALU.mult, ALU.add))
                with P.group("sync", reads=["l1f_xst"], writes=["xspill"]) as g:
                    g.dma_(xsp[:, (tc+tt)*D + dh*512: (tc+tt)*D + (dh+1)*512], xstage[:])
    emit_reload_x(C, claims=["wu0", "wu1", "wu2", "wu3", "h", "sg"])


def build_l1_attn_a(C, tag):
    P = C.P
    H = 16
    i = C.blk_i
    A = OFF_ARENA
    k_tm = C.sb_at("l1a_k%d" % i, (128, NT*D), BF16, A)
    v_tm = C.sb_at("l1a_v%d" % i, (128, NT*D), BF16, A + 32768)
    wsl = C.sb_at("l1a_w%d" % i, (128, KC*D), BF16, A + 65536)
    bkbc = C.sb_at("l1a_bk%d" % i, (128, D), F32, A + 81920)
    bvbc = C.sb_at("l1a_bv%d" % i, (128, D), F32, A + 86016)
    bobc = C.sb_at("l1a_bo%d" % i, (128, D), F32, A + 90112)
    bqc = C.sb_at("l1a_bq%d" % i, (128, KC), F32, A + 94208)
    kv_sb = C.sb_at("l1a_kv%d" % i, (128, 8*65), BF16, A + 94272)
    onesc = C.sb_at("l1a_one%d" % i, (128, 1), BF16, A + 95360)
    s1 = C.sb_at("l1a_s1%d" % i, (128, 512), F32, A + 95424)
    s2 = C.sb_at("l1a_s2%d" % i, (128, 512), F32, A + 97472)
    rzc = C.sb_at("l1a_rz%d" % i, (128, 1), F32, A + 99520)
    brow = C.sb_at("l1a_br%d" % i, (1, 512), F32, A + 99584)
    wq_d = C.dram_in("l1aa_wq", (128, KC*D), F32)
    wk_d = C.dram_in("l1aa_wk", (128, KC*D), F32)
    wv_d = C.dram_in("l1aa_wv", (128, KC*D), F32)
    wo_d = C.dram_in("l1aa_wo", (128, KC*D), F32)
    bq_d = C.dram_in("l1aa_bq", (128, KC), F32)
    bk_d = C.dram_in("l1aa_bk", (1, D), F32)
    bv_d = C.dram_in("l1aa_bv", (1, D), F32)
    bo_d = C.dram_in("l1aa_bo", (1, D), F32)
    PS = C.psum_f32
    kvps = C.ps_aux

    with P.group("gpsimd", writes=["bqc"]) as g:
        g.dma_(bqc[:], bq_d)
    with P.group("gpsimd", writes=["onesc"]) as g:
        g.emit(lambda g_: g_.memset(onesc[:], 1.0))
    for j, (bc, bd) in enumerate([(bkbc, bk_d), (bvbc, bv_d), (bobc, bo_d)]):
        for hh in range(2):
            pj = 4 + (2*j + hh) % 2
            with P.group("gpsimd", writes=["brow"]) as g:
                g.dma_(brow[:], bd[0:1, hh*512:(hh+1)*512])
            emit_matmul_acc(C, PS[pj][:, 0:512],
                            [C.ones_row[0:1, 0:128]], [brow[0:1, :]],
                            reads=["brow", "ones"], writes=["ps%d" % pj])
            with P.group("vector", reads=["ps%d" % pj], writes=["biasbc"]) as g:
                g.emit(lambda v, ps=PS[pj][:, 0:512], o=bc[:, hh*512:(hh+1)*512]:
                       v.tensor_copy(out=o, in_=ps))

    def load_w(dram, extra_claims=()):
        with P.group("gpsimd", writes=["wsl"] + list(extra_claims)) as g:
            for c in range(0, KC, 4):
                g.dma_(wsl[:, c*D:(c+4)*D], dram[:, c*D:(c+4)*D])

    def fmap_evac(ps, pskey, dsl, bias_args, dst_keys):
        with P.group("vector", reads=[pskey, "biasbc", "bqc"], writes=["s1"]) as g:
            if bias_args[0] == "col":
                g.emit(lambda v, ps=ps, b=bias_args[1]:
                       v.tensor_scalar(s1[:, 0:ps.shape[1]], ps, b, None, ALU.add))
            else:
                g.emit(lambda v, ps=ps, b=bias_args[1]:
                       v.tensor_add(s1[:, 0:ps.shape[1]], ps, b))
            g.emit(lambda v, n=ps.shape[1]: v.tensor_scalar_min(s2[:, 0:n], s1[:, 0:n], 0.0))
        with P.group("scalar", reads=["s1"], writes=["s2e"]) as g:
            g.emit(lambda s, n=ps.shape[1]: s.activation(s2[:, 0:n], s2[:, 0:n], AF.Exp))
        with P.group("vector", reads=["s2e", "s1"], writes=list(dst_keys)) as g:
            g.emit(lambda v, o=dsl, n=ps.shape[1]: v.scalar_tensor_tensor(
                o, s1[:, 0:n], 0.0, s2[:, 0:n], ALU.max, ALU.add))

    def proj_tm(dst, dst_key, bias_bc, fmap, extra_claims=()):
        for t in range(NT):
            for mh in range(2):
                pj = (2*t + mh) % 2
                ps = PS[pj][:, 0:512]
                lh = [C.y_fm[:, c*T + t*128:(c*T) + (t+1)*128] for c in range(KC)]
                rh = [wsl[:, c*D + mh*512: c*D + (mh+1)*512] for c in range(KC)]
                emit_matmul_acc(C, ps, lh, rh, reads=["wsl", "y_fm"],
                                writes=["ps%d" % pj])
                dsl = dst[:, t*D + mh*512: t*D + (mh+1)*512]
                bsl = bias_bc[:, mh*512:(mh+1)*512]
                if not fmap:
                    with P.group("vector", reads=["ps%d" % pj, "biasbc"],
                                 writes=[dst_key] + list(extra_claims)) as g:
                        g.emit(lambda v, ps=ps, o=dsl, b=bsl: v.tensor_add(o, ps, b))
                else:
                    fmap_evac(ps, "ps%d" % pj, dsl, ("bc", bsl),
                              [dst_key] + list(extra_claims))

    load_w(wk_d, ["y_tm"])
    proj_tm(k_tm, "k_tm", bkbc, True)
    load_w(wv_d)
    proj_tm(v_tm, "v_tm", bvbc, False)
    for h in range(H):
        p0 = (h % 2) * 64
        tp = (0, p0)
        out_ap = kvps[p0:p0+64, 0:65]
        with P.group("tensor", reads=["k_tm", "v_tm", "onesc"], writes=["ps_aux"]) as g:
            for t in range(NT):
                g.emit(lambda p, lt=k_tm[:, t*D + h*64: t*D + (h+1)*64],
                       rh=v_tm[:, t*D + h*64: t*D + (h+1)*64], st=(t == 0),
                       sp=(t == NT-1), o=out_ap[:, 0:64], tp=tp:
                       p.matmul(o, lt, rh, start=st, stop=sp, tile_position=tp))
            for t in range(NT):
                g.emit(lambda p, lt=k_tm[:, t*D + h*64: t*D + (h+1)*64], st=(t == 0),
                       sp=(t == NT-1), o=out_ap[:, 64:65], tp=tp:
                       p.matmul(o, lt, onesc[:], start=st, stop=sp, tile_position=tp))
        with P.group("vector", reads=["ps_aux"], writes=["kv_sb"]) as g:
            g.emit(lambda v, o=kv_sb[p0:p0+64, (h//2)*65:(h//2+1)*65], s=out_ap:
                   v.tensor_copy(out=o, in_=s))
    q_fm = C.sb_at("l1a_q%d" % i, (128, KC*T), BF16, A)
    load_w(wq_d)
    for cc in range(KC):
        for tq in range(4):
            pj = (cc + tq) % 2
            ps = PS[pj][:, 0:512]
            lh = [wsl[:, c*D + cc*128: c*D + (cc+1)*128] for c in range(KC)]
            rh = [C.y_fm[:, c*T + tq*512:(c*T) + (tq+1)*512] for c in range(KC)]
            emit_matmul_acc(C, ps, lh, rh, reads=["wsl", "y_fm"],
                            writes=["ps%d" % pj])
            dsl = q_fm[:, cc*T + tq*512: cc*T + (tq+1)*512]
            fmap_evac(ps, "ps%d" % pj, dsl, ("col", bqc[:, cc:cc+1]),
                      ["q_fm", "k_tm"])
    out_tm = C.sb_at("l1a_o%d" % i, (128, NT*D), BF16, A + 32768)
    for t in range(NT):
        for h in range(H):
            p0 = (h % 2) * 64
            pj = h % 4
            ps = PS[pj][:, 0:65]
            with P.group("tensor", reads=["q_fm", "kv_sb"], writes=["ps%d" % pj]) as g:
                g.emit(lambda p, ps=ps,
                       lt=q_fm[p0:p0+64, (h//2)*T + t*128:(h//2)*T + (t+1)*128],
                       rh=kv_sb[p0:p0+64, (h//2)*65:(h//2+1)*65]:
                       p.matmul(ps, lt, rh))
            with P.group("vector", reads=["ps%d" % pj], writes=["out_tm", "v_tm"]) as g:
                g.emit(lambda v, ps=ps: v.tensor_scalar_max(rzc[:], ps[:, 64:65], 1e-6))
                g.emit(lambda v: v.reciprocal(rzc[:], rzc[:]))
                g.emit(lambda v, ps=ps, o=out_tm[:, t*D + h*64: t*D + (h+1)*64]:
                       v.tensor_scalar(o, ps[:, 0:64], rzc[:], None, ALU.mult))
    out_fm = C.sb_at("l1a_of%d" % i, (128, KC*T), BF16, A)
    emit_tm_to_fm(C, out_tm, out_fm, "out_tm", "out_fm")
    load_w(wo_d)
    for t in range(NT):
        for mh in range(2):
            pj = (2*t + mh) % 2
            ps = PS[pj][:, 0:512]
            lh = [out_fm[:, c*T + t*128:(c*T) + (t+1)*128] for c in range(KC)]
            rh = [wsl[:, c*D + mh*512: c*D + (mh+1)*512] for c in range(KC)]
            emit_matmul_acc(C, ps, lh, rh, reads=["wsl", "out_fm"],
                            writes=["ps%d" % pj])
            xs = C.x_tm[:, t*D + mh*512: t*D + (mh+1)*512]
            with P.group("vector", reads=["ps%d" % pj, "biasbc", "x"],
                         writes=["x"]) as g:
                g.emit(lambda v, ps=ps, b=bobc[:, mh*512:(mh+1)*512]:
                       v.tensor_add(s1[:], ps, b))
                g.emit(lambda v, o=xs: v.tensor_add(o, o, s1[:]))


def prep_l1_attn_a(inputs, h):
    for nm, key in [("Wq", "l1aa_wq"), ("Wk", "l1aa_wk"), ("Wv", "l1aa_wv"),
                    ("Wo", "l1aa_wo")]:
        w = np.asarray(inputs["L1_attn_a_" + nm], np.float32)
        h[key] = sb_image_kc(np.ascontiguousarray(w))
    h["l1aa_bq"] = np.ascontiguousarray(
        np.asarray(inputs["L1_attn_a_bq"], np.float32).reshape(KC, 128).T)
    h["l1aa_bk"] = np.asarray(inputs["L1_attn_a_bk"], np.float32).reshape(1, D)
    h["l1aa_bv"] = np.asarray(inputs["L1_attn_a_bv"], np.float32).reshape(1, D)
    h["l1aa_bo"] = np.asarray(inputs["L1_attn_a_bo"], np.float32).reshape(1, D)


def build_l0_attn_b(C, tag):
    P = C.P
    H = 16
    i = C.blk_i
    A = OFF_ARENA
    KROW = T + 256
    q_fm = C.sb_at("l0b_q%d" % i, (128, KC*T), BF16, OFF_X)
    out_fm = C.sb_at("l0b_of%d" % i, (128, KC*T), BF16, OFF_YFM)
    k_fm = C.sb_at("l0b_k%d" % i, (128, KC*KROW), BF16, A)
    v_tm = C.sb_at("l0b_v%d" % i, (128, 18*D), BF16, A + 36864)
    wsl = C.sb_at("l0b_w%d" % i, (128, KC*D), BF16, A + 73728)
    madd = C.sb_at("l0b_m%d" % i, (128, 384), F32, A + 90112)
    s1 = C.sb_at("l0b_s1%d" % i, (128, 384), F32, A + 91648)
    attn_sb = C.sb_at("l0b_a%d" % i, (128, 384), BF16, A + 93184)
    attnT_sb = C.sb_at("l0b_at%d" % i, (128, 384), BF16, A + 93952)
    bobc = C.sb_at("l0b_bo%d" % i, (128, D), F32, A + 94720)
    mcol = C.sb_at("l0b_mc%d" % i, (128, 4), F32, A + 98816)
    brow = C.sb_at("l0b_br%d" % i, (1, 512), F32, A + 98848)
    wq_d = C.dram_in("l0ab_wq", (128, KC*D), F32)
    wk_d = C.dram_in("l0ab_wk", (128, KC*D), F32)
    wv_d = C.dram_in("l0ab_wv", (128, KC*D), F32)
    wo_d = C.dram_in("l0ab_wo", (128, KC*D), F32)
    bo_d = C.dram_in("l0ab_bo", (1, D), F32)
    PS = C.psum_f32

    emit_spill_x(C)

    def load_w(dram, extra_claims=()):
        with P.group("gpsimd", writes=["wsl"] + list(extra_claims)) as g:
            for c in range(0, KC, 4):
                g.dma_(wsl[:, c*D:(c+4)*D], dram[:, c*D:(c+4)*D])

    for hh in range(2):
        with P.group("gpsimd", writes=["brow"]) as g:
            g.dma_(brow[:], bo_d[0:1, hh*512:(hh+1)*512])
        pj = 4 + hh
        emit_matmul_acc(C, PS[pj][:, 0:512], [C.ones_row[0:1, 0:128]], [brow[0:1, :]],
                        reads=["brow", "ones"], writes=["ps%d" % pj])
        with P.group("vector", reads=["ps%d" % pj], writes=["biasbc"]) as g:
            g.emit(lambda v, ps=PS[pj][:, 0:512], o=bobc[:, hh*512:(hh+1)*512]:
                   v.tensor_copy(out=o, in_=ps))
    with P.group("gpsimd", writes=["madd", "s1g"]) as g:
        g.emit(lambda g_: g_.iota(madd[:], [[1, 384]], base=0, channel_multiplier=-1,
                                  allow_small_or_imprecise_dtypes=True))
        g.emit(lambda g_: g_.tensor_single_scalar(s1[:], madd[:], 96.0, ALU.is_ge))
        g.emit(lambda g_: g_.tensor_single_scalar(madd[:], madd[:], 159.0, ALU.is_le))
        g.emit(lambda g_: g_.tensor_mul(madd[:], madd[:], s1[:]))
        g.emit(lambda g_: g_.tensor_scalar_add(madd[:], madd[:], -1.0))
        g.emit(lambda g_: g_.tensor_scalar_mul(madd[:], madd[:], 1e30))

    load_w(wq_d)
    for cc in range(KC):
        for tq in range(4):
            pj = (cc + tq) % 2
            ps = PS[pj][:, 0:512]
            lh = [wsl[:, c*D + cc*128: c*D + (cc+1)*128] for c in range(KC)]
            rh = [C.y_fm[:, c*T + tq*512:(c*T) + (tq+1)*512] for c in range(KC)]
            emit_matmul_acc(C, ps, lh, rh, reads=["wsl", "y_fm"], writes=["ps%d" % pj])
            with P.group("vector", reads=["ps%d" % pj], writes=["q_fm", "x"]) as g:
                g.emit(lambda v, ps=ps, o=q_fm[:, cc*T + tq*512: cc*T + (tq+1)*512]:
                       v.tensor_copy(out=o, in_=ps))
    load_w(wk_d, ["y_tm"])
    with P.group("gpsimd", writes=["k_fm"]) as g:
        for c in range(KC):
            g.emit(lambda g_, c=c: g_.memset(k_fm[:, c*KROW: c*KROW + 128], 0.0))
            g.emit(lambda g_, c=c: g_.memset(k_fm[:, c*KROW + 128 + T: (c+1)*KROW], 0.0))
    for cc in range(KC):
        for tq in range(4):
            pj = (cc + tq) % 2
            ps = PS[pj][:, 0:512]
            lh = [wsl[:, c*D + cc*128: c*D + (cc+1)*128] for c in range(KC)]
            rh = [C.y_fm[:, c*T + tq*512:(c*T) + (tq+1)*512] for c in range(KC)]
            emit_matmul_acc(C, ps, lh, rh, reads=["wsl", "y_fm"], writes=["ps%d" % pj])
            with P.group("vector", reads=["ps%d" % pj], writes=["k_fm"]) as g:
                g.emit(lambda v, ps=ps,
                       o=k_fm[:, cc*KROW + 128 + tq*512: cc*KROW + 128 + (tq+1)*512]:
                       v.tensor_copy(out=o, in_=ps))
    load_w(wv_d)
    with P.group("gpsimd", writes=["v_tm"]) as g:
        g.emit(lambda g_: g_.memset(v_tm[:, 0:D], 0.0))
        g.emit(lambda g_: g_.memset(v_tm[:, 17*D:18*D], 0.0))
    for t in range(NT):
        for mh in range(2):
            pj = (2*t + mh) % 2
            ps = PS[pj][:, 0:512]
            lh = [C.y_fm[:, c*T + t*128:(c*T) + (t+1)*128] for c in range(KC)]
            rh = [wsl[:, c*D + mh*512: c*D + (mh+1)*512] for c in range(KC)]
            emit_matmul_acc(C, ps, lh, rh, reads=["wsl", "y_fm"], writes=["ps%d" % pj])
            with P.group("vector", reads=["ps%d" % pj], writes=["v_tm"]) as g:
                g.emit(lambda v, ps=ps,
                       o=v_tm[:, (t+1)*D + mh*512: (t+1)*D + (mh+1)*512]:
                       v.tensor_copy(out=o, in_=ps))
    for h in range(L0B_HLIM):
        p0 = (h % 2) * 64
        cc = h // 2
        for t in range(L0B_TLIM):
            pj = (h + t) % 2
            ps = PS[pj][:, 0:384]
            with P.group("tensor", reads=["q_fm", "k_fm"], writes=["ps%d" % pj]) as g:
                g.emit(lambda p, ps=ps,
                       lt=q_fm[p0:p0+64, cc*T + t*128: cc*T + (t+1)*128],
                       rh=k_fm[p0:p0+64, cc*KROW + t*128: cc*KROW + t*128 + 384]:
                       p.matmul(ps, lt, rh))
            with P.group("vector", reads=["ps%d" % pj, "madd"], writes=["s1g"]) as g:
                g.emit(lambda v, ps=ps: v.scalar_tensor_tensor(
                    s1[:], ps, 0.125, madd[:], ALU.mult, ALU.add))
                g.emit(lambda v: v.reduce_max(mcol[:, 0:1], s1[:], AX.X))
                g.emit(lambda v: v.tensor_scalar_mul(mcol[:, 1:2], mcol[:, 0:1], -1.0))
            with P.group("scalar", reads=["s1g"], writes=["s1e"]) as g:
                g.emit(lambda s: s.activation(s1[:], s1[:], AF.Exp,
                                              bias=mcol[:, 1:2], accum_out=mcol[:, 2:3]))
            with P.group("vector", reads=["s1e"], writes=["attn_sb"]) as g:
                g.emit(lambda v: v.reciprocal(mcol[:, 3:4], mcol[:, 2:3]))
                g.emit(lambda v: v.tensor_scalar(attn_sb[:], s1[:], mcol[:, 3:4],
                                                 None, ALU.mult))
            with P.group("tensor", reads=["attn_sb"], writes=["ps_tr"]) as g:
                for jj in range(3):
                    g.emit(lambda p, jj=jj: p.transpose(
                        C.ps_tr[:, jj*128:(jj+1)*128],
                        attn_sb[:, jj*128:(jj+1)*128], C.ident[:]))
            with P.group("vector", reads=["ps_tr"], writes=["attnT"]) as g:
                g.emit(lambda v: v.tensor_copy(out=attnT_sb[:], in_=C.ps_tr[:, 0:384]))
            po = C.ps_aux[p0:p0+64, 0:128]
            with P.group("tensor", reads=["attnT", "v_tm"], writes=["ps_aux"]) as g:
                for jj in range(3):
                    g.emit(lambda p, po=po, tp=(0, p0),
                           lt=v_tm[:, (t+jj)*D + h*64: (t+jj)*D + (h+1)*64],
                           rh=attnT_sb[:, jj*128:(jj+1)*128],
                           st=(jj == 0), sp=(jj == 2):
                           p.matmul(po, lt, rh, start=st, stop=sp, tile_position=tp))
            with P.group("vector", reads=["ps_aux"], writes=["out_fm", "y_fm"]) as g:
                g.emit(lambda v, po=po,
                       o=out_fm[p0:p0+64, cc*T + t*128: cc*T + (t+1)*128]:
                       v.tensor_copy(out=o, in_=po))
            if L0B_HLIM == 1:
                with P.group("vector", reads=["ps_aux"], writes=["sb_sq"]) as g:
                    g.emit(lambda v, po=po: v.tensor_copy(out=C.sb_sq[p0:p0+64, 0:128], in_=po))
    emit_reload_x(C, claims=["q_fm"])
    load_w(wo_d)
    for t in range(NT):
        for mh in range(2):
            pj = (2*t + mh) % 2
            ps = PS[pj][:, 0:512]
            lh = [out_fm[:, c*T + t*128:(c*T) + (t+1)*128] for c in range(KC)]
            rh = [wsl[:, c*D + mh*512: c*D + (mh+1)*512] for c in range(KC)]
            emit_matmul_acc(C, ps, lh, rh, reads=["wsl", "out_fm"], writes=["ps%d" % pj])
            xs = C.x_tm[:, t*D + mh*512: t*D + (mh+1)*512]
            with P.group("vector", reads=["ps%d" % pj, "biasbc", "x"], writes=["x"]) as g:
                g.emit(lambda v, ps=ps, b=bobc[:, mh*512:(mh+1)*512]:
                       v.tensor_add(s1[:, 0:256] if False else C.sb_sq[:, 0:512], ps, b))
                g.emit(lambda v, o=xs: v.tensor_add(o, o, C.sb_sq[:, 0:512]))


def prep_l0_attn_b(inputs, h):
    for nm, key in [("Wq", "l0ab_wq"), ("Wk", "l0ab_wk"), ("Wv", "l0ab_wv"),
                    ("Wo", "l0ab_wo")]:
        w = np.asarray(inputs["L0_attn_b_" + nm], np.float32).T
        h[key] = sb_image_kc(np.ascontiguousarray(w))
    h["l0ab_bo"] = np.asarray(inputs["L0_attn_b_bo"], np.float32).reshape(1, D)


def build_l0_attn_a(C, tag):
    P = C.P
    M = 256
    i = C.blk_i
    A = OFF_ARENA
    CSC = 1.0 / (16.0 * float(np.sqrt(T)))
    k_tm = C.sb_at("l0a_k%d" % i, (128, NT*M), BF16, A)
    q_fm = C.sb_at("l0a_q%d" % i, (128, 2*T), BF16, A + 8192)
    v_fm = C.sb_at("l0a_v%d" % i, (128, 2*T), BF16, A + 16384)
    r_fm = C.sb_at("l0a_r%d" % i, (128, 2*T), BF16, A + 24576)
    cumbc = C.sb_at("l0a_cb%d" % i, (128, T), F32, A + 32768)
    cumT = C.sb_at("l0a_ct%d" % i, (128, NT), F32, A + 40960)
    masks = C.sb_at("l0a_mk%d" % i, (128, 4*512), F32, A + 41024)
    s1 = C.sb_at("l0a_s1%d" % i, (128, 512), F32, A + 49216)
    s2 = C.sb_at("l0a_s2%d" % i, (128, 512), F32, A + 51264)
    w16 = C.sb_at("l0a_w16%d" % i, (128, 512), BF16, A + 53312)
    wk = C.sb_at("l0a_wk%d" % i, (128, KC*M), BF16, A + 54336)
    wq = C.sb_at("l0a_wq%d" % i, (128, KC*M), BF16, A + 58432)
    wv = C.sb_at("l0a_wv%d" % i, (128, KC*M), BF16, A + 62528)
    wo = C.sb_at("l0a_wo%d" % i, (128, 2*D), BF16, A + 66624)
    wg = C.sb_at("l0a_wg%d" % i, (128, KC), BF16, A + 70720)
    grow = C.sb_at("l0a_g%d" % i, (1, T), F32, A + 70784)
    onerow = C.sb_at("l0a_or%d" % i, (1, T), F32, A + 78976)
    bgc = C.sb_at("l0a_bg%d" % i, (1, 1), F32, A + 87168)
    wk_d = C.dram_in("l0aa_wkT", (128, KC*M), F32)
    wq_d = C.dram_in("l0aa_wqT", (128, KC*M), F32)
    wv_d = C.dram_in("l0aa_wvT", (128, KC*M), F32)
    wo_d = C.dram_in("l0aa_woT", (128, 2*D), F32)
    wg_d = C.dram_in("l0aa_wgT", (128, KC), F32)
    bg_d = C.dram_in("l0aa_bg", (1, 1), F32)
    PS = C.psum_f32

    with P.group("gpsimd", writes=["l0a_w", "y_tm"]) as g:
        g.dma_(wk[:], wk_d)
        g.dma_(wq[:], wq_d)
        g.dma_(wv[:], wv_d)
        g.dma_(wo[:], wo_d)
        g.dma_(wg[:], wg_d)
        g.dma_(bgc[:], bg_d)
    with P.group("gpsimd", writes=["l0a_ones", "l0a_masks", "l0a_s1"]) as g:
        g.emit(lambda g_: g_.memset(onerow[:], 1.0))
        g.emit(lambda g_: g_.iota(s1[:], [[1, 512]], base=0, channel_multiplier=-1,
                                  allow_small_or_imprecise_dtypes=True))
        for r in range(4):
            g.emit(lambda g_, r=r: g_.tensor_single_scalar(
                masks[:, r*512:(r+1)*512], s1[:], float(r*128), ALU.is_ge))
        g.emit(lambda g_: g_.tensor_scalar_mul(masks[:], masks[:], CSC))
    for mc in range(2):
        for tq in range(4):
            for dst, w, key in [(q_fm, wq, "q_fm"), (v_fm, wv, "v_fm")]:
                pj = (mc*4 + tq) % 2
                ps = PS[pj][:, 0:512]
                lh = [w[:, c*M + mc*128: c*M + (mc+1)*128] for c in range(KC)]
                rh = [C.y_fm[:, c*T + tq*512:(c*T) + (tq+1)*512] for c in range(KC)]
                emit_matmul_acc(C, ps, lh, rh, reads=["l0a_w", "y_fm"],
                                writes=["ps%d" % pj])
                with P.group("vector", reads=["ps%d" % pj], writes=[key]) as g:
                    g.emit(lambda v, ps=ps, o=dst[:, mc*T + tq*512: mc*T + (tq+1)*512]:
                           v.tensor_copy(out=o, in_=ps))
    for t in range(NT):
        pj = t % 2
        ps = PS[pj][:, 0:M]
        lh = [C.y_fm[:, c*T + t*128:(c*T) + (t+1)*128] for c in range(KC)]
        rh = [wk[:, c*M:(c+1)*M] for c in range(KC)]
        emit_matmul_acc(C, ps, lh, rh, reads=["l0a_w", "y_fm"], writes=["ps%d" % pj])
        with P.group("vector", reads=["ps%d" % pj], writes=["k_tm"]) as g:
            g.emit(lambda v, ps=ps, o=k_tm[:, t*M:(t+1)*M]: v.tensor_copy(out=o, in_=ps))
    for tq in range(4):
        ps = PS[4][:, 0:512]
        lh = [wg[:, c:c+1] for c in range(KC)]
        rh = [C.y_fm[:, c*T + tq*512:(c*T) + (tq+1)*512] for c in range(KC)]
        emit_matmul_acc(C, ps[0:1, :], lh, rh, reads=["l0a_w", "y_fm"], writes=["ps4"])
        with P.group("scalar", reads=["ps4", "l0a_w", "consts"], writes=["grow"]) as g:
            g.emit(lambda s, ps=ps, o=grow[0:1, tq*512:(tq+1)*512]:
                   s.activation(o, ps[0:1, :], AF.Sigmoid, bias=bgc[0:1, 0:1]))
            g.emit(lambda s, o=grow[0:1, tq*512:(tq+1)*512]:
                   s.activation(o, o, AF.Ln, bias=1e-8))
    with P.group("vector", reads=["grow", "l0a_ones"], writes=["grow"]) as g:
        g.emit(lambda v: v.tensor_tensor_scan(grow[0:1, :], onerow[0:1, :],
                                              grow[0:1, :], 0.0, ALU.mult, ALU.add))
    for tq in range(4):
        pj = 4 + tq % 2
        emit_matmul_acc(C, PS[pj][:, 0:512], [C.ones_row[0:1, 0:128]],
                        [grow[0:1, tq*512:(tq+1)*512]],
                        reads=["grow", "ones"], writes=["ps%d" % pj])
        with P.group("vector", reads=["ps%d" % pj], writes=["cumbc"]) as g:
            g.emit(lambda v, ps=PS[pj][:, 0:512], o=cumbc[:, tq*512:(tq+1)*512]:
                   v.tensor_copy(out=o, in_=ps))
    with P.group("tensor", reads=["grow", "ident"], writes=["ps_aux"]) as g:
        for t in range(NT):
            g.emit(lambda p, t=t: p.transpose(C.ps_aux[:, t:t+1],
                                              grow[0:1, t*128:(t+1)*128],
                                              C.identf[0:1, 0:1]))
    with P.group("vector", reads=["ps_aux"], writes=["cumT"]) as g:
        g.emit(lambda v: v.tensor_copy(out=cumT[:], in_=C.ps_aux[:, 0:NT]))
    for jc in range(4):
        first = True
        for it in range(0, (jc+1)*4):
            pj = it % 2
            ps = PS[pj][:, 0:512]
            lh = [v_fm[:, mc2*T + it*128: mc2*T + (it+1)*128] for mc2 in range(2)]
            rh = [q_fm[:, mc2*T + jc*512: mc2*T + (jc+1)*512] for mc2 in range(2)]
            emit_matmul_acc(C, ps, lh, rh, reads=["q_fm", "v_fm"], writes=["ps%d" % pj])
            with P.group("vector", reads=["cumbc", "cumT"], writes=["l0a_s1"]) as g:
                g.emit(lambda v, j0=jc*512, it=it: v.tensor_scalar(
                    s1[:], cumbc[:, j0:j0+512], cumT[:, it:it+1], 0.0,
                    ALU.subtract, ALU.min))
            with P.group("scalar", reads=["l0a_s1"], writes=["l0a_s2"]) as g:
                g.emit(lambda s: s.activation(s2[:], s1[:], AF.Exp))
            diag = (it // 4 == jc)
            with P.group("vector", reads=["l0a_s2", "ps%d" % pj, "l0a_masks"],
                         writes=["l0a_w16"] + (["l0a_s1"] if diag else [])) as g:
                if diag:
                    g.emit(lambda v, ps=ps: v.tensor_mul(s1[:], ps, s2[:]))
                    g.emit(lambda v, r=it % 4: v.tensor_tensor(
                        w16[:], s1[:], masks[:, r*512:(r+1)*512], ALU.mult))
                else:
                    g.emit(lambda v, ps=ps: v.scalar_tensor_tensor(
                        w16[:], ps, CSC, s2[:], ALU.mult, ALU.mult))
            for mc in range(2):
                with P.group("tensor", reads=["l0a_w16", "k_tm"],
                             writes=["ps%d" % (2+mc)]) as g:
                    g.emit(lambda p, lt=k_tm[:, it*M + mc*128: it*M + (mc+1)*128],
                           ps=PS[2+mc][:, 0:512],
                           st=first, sp=(it == (jc+1)*4 - 1):
                           p.matmul(ps, lt, w16[:], start=st, stop=sp))
            first = False
        for mc in range(2):
            with P.group("vector", reads=["ps%d" % (2+mc)], writes=["r_fm"]) as g:
                g.emit(lambda v, ps=PS[2+mc][:, 0:512],
                       o=r_fm[:, mc*T + jc*512: mc*T + (jc+1)*512]:
                       v.tensor_copy(out=o, in_=ps))
    for t in range(NT):
        for dh in range(2):
            pj = (2*t + dh) % 2
            ps = PS[pj][:, 0:512]
            lh = [r_fm[:, mc*T + t*128: mc*T + (t+1)*128] for mc in range(2)]
            rh = [wo[:, mc*D + dh*512: mc*D + (dh+1)*512] for mc in range(2)]
            emit_matmul_acc(C, ps, lh, rh, reads=["r_fm", "l0a_w"], writes=["ps%d" % pj])
            xs = C.x_tm[:, t*D + dh*512: t*D + (dh+1)*512]
            with P.group("vector", reads=["ps%d" % pj, "x"], writes=["x"]) as g:
                g.emit(lambda v, ps=ps, o=xs: v.tensor_add(o, o, ps))


def prep_l0_attn_a(inputs, h):
    h["l0aa_wkT"] = sb_image_kc(np.ascontiguousarray(
        np.asarray(inputs["L0_attn_a_Wk"], np.float32).T))
    h["l0aa_wqT"] = sb_image_kc(np.ascontiguousarray(
        np.asarray(inputs["L0_attn_a_Wq"], np.float32).T))
    h["l0aa_wvT"] = sb_image_kc(np.ascontiguousarray(
        np.asarray(inputs["L0_attn_a_Wv"], np.float32).T))
    h["l0aa_woT"] = sb_image_kc(np.ascontiguousarray(
        np.asarray(inputs["L0_attn_a_Wo"], np.float32).T), kc=2)
    h["l0aa_wgT"] = sb_image_kc(np.ascontiguousarray(
        np.asarray(inputs["L0_attn_a_Wg"], np.float32).T))
    h["l0aa_bg"] = np.asarray(inputs["L0_attn_a_bg"], np.float32).reshape(1, 1)


def build_l1_attn_b(C, tag):
    P = C.P
    DI = 2048
    MCI = DI // 128
    DS = 16
    DR = 64
    i = C.blk_i
    A = OFF_ARENA
    xc = C.sb_at("l1b_xc%d" % i, (128, MCI*T), BF16, OFF_X)
    winr = [C.sb_at("l1b_wi%d_%d" % (i, r), (128, KC*128), BF16, A + r*2048)
            for r in range(4)]
    stage = C.sb_at("l1b_st%d" % i, (128, 516), F32, A + 8192)
    sres_st = C.sb_at("l1b_sr%d" % i, (128, T), BF16, A + 10304)
    binc = C.sb_at("l1b_bi%d" % i, (128, 32), F32, A + 14400)
    cwc = C.sb_at("l1b_cw%d" % i, (128, 64), F32, A + 14528)
    cbc = C.sb_at("l1b_cb%d" % i, (128, 16), F32, A + 14784)
    alog = C.sb_at("l1b_al%d" % i, (128, 256), F32, A + 14848)
    asb = C.sb_at("l1b_as%d" % i, (128, 256), F32, A + 15872)
    dpc = C.sb_at("l1b_dp%d" % i, (128, 16), F32, A + 16896)
    bdtc = C.sb_at("l1b_bd%d" % i, (128, 16), F32, A + 16960)
    wxc = C.sb_at("l1b_wx%d" % i, (128, MCI*96), BF16, A + 17024)
    wdt = C.sb_at("l1b_wd%d" % i, (64, DI), BF16, A + 20096)
    xdbl = C.sb_at("l1b_xd%d" % i, (128, NT*96), F32, A + 24192)
    dbct = C.sb_at("l1b_db%d" % i, (96, NT*128), F32, A + 30336)
    delta = C.sb_at("l1b_dl%d" % i, (128, T), F32, A + 38528)
    dtx = C.sb_at("l1b_dx%d" % i, (128, T), F32, A + 46720)
    dA = C.sb_at("l1b_da%d" % i, (128, T), F32, A + 54912)
    ubuf = C.sb_at("l1b_u%d" % i, (128, T), F32, A + 63104)
    hbuf = C.sb_at("l1b_h%d" % i, (128, T), F32, A + 71296)
    hcbuf = C.sb_at("l1b_hc%d" % i, (128, T), F32, A + 79488)
    ybuf = C.sb_at("l1b_y%d" % i, (128, T), F32, A + 87680)
    wour = [C.sb_at("l1b_wo%d_%d" % (i, r), (128, D), BF16, A + 95872 + r*2048)
            for r in range(4)]
    drt_sb = C.sb_at("l1b_drt%d" % i, (64, T), BF16, A + 95872)
    bbc = C.sb_at("l1b_bb%d" % i, (128, T), F32, OFF_YFM)
    cbc2 = C.sb_at("l1b_cc%d" % i, (128, T), F32, OFF_YFM + 8192)
    bobc = C.sb_at("l1b_bo%d" % i, (128, D), F32, OFF_YFM + 16384)
    xstage = C.sb_at("l1b_xs%d" % i, (128, D), F32, OFF_YFM + 20480)
    ostage = C.sb_at("l1b_os%d" % i, (128, D), F32, OFF_YFM + 24576)
    brow = C.sb_at("l1b_br%d" % i, (1, 512), F32, OFF_YFM + 28672)
    win_d = C.dram_in("l1ab_win", (128, 32*KC*128), F32)
    binc_d = C.dram_in("l1ab_bin", (128, 32), F32)
    cw_d = C.dram_in("l1ab_cw", (128, 64), F32)
    cb_d = C.dram_in("l1ab_cbc", (128, 16), F32)
    wx_d = C.dram_in("l1ab_wx", (128, MCI*96), F32)
    wdt_d = C.dram_in("l1ab_wdt", (64, DI), F32)
    bdt_d = C.dram_in("l1ab_bdt", (128, 16), F32)
    alog_d = C.dram_in("l1ab_alog", (128, 256), F32)
    dp_d = C.dram_in("l1ab_dp", (128, 16), F32)
    wout_d = C.dram_in("l1ab_wout", (128, MCI*D), F32)
    bo_d = C.dram_in("l1ab_bo", (1, D), F32)
    bct_dr = C.dram_tmp("l1b_bct", (32, T), F32)
    sres_dr = C.dram_tmp("l1b_sres", (128, MCI*T), BF16)
    xsp = C.dram_tmp("xspill", (128, NT*D), F32)
    PS = C.psum_f32

    emit_spill_x(C)
    with P.group("gpsimd", writes=["l1b_small", "y_tm"]) as g:
        g.dma_(binc[:], binc_d)
        g.dma_(cwc[:], cw_d)
        g.dma_(cbc[:], cb_d)
        g.dma_(alog[:], alog_d)
        g.dma_(dpc[:], dp_d)
        g.dma_(bdtc[:], bdt_d)
        g.dma_(wxc[:], wx_d)
        g.dma_(wdt[:], wdt_d)
    with P.group("scalar", reads=["l1b_small"], writes=["l1b_asb"]) as g:
        g.emit(lambda s: s.activation(asb[:], alog[:], AF.Exp))
    with P.group("vector", reads=["l1b_asb"], writes=["l1b_asb"]) as g:
        g.emit(lambda v: v.tensor_scalar_mul(asb[:], asb[:], -1.0))
    def load_win(mc):
        with P.group("gpsimd", writes=["winr%d" % (mc % 4)] + (["y_tm"] if mc < 4 else [])) as g:
            g.dma_(winr[mc % 4][:], win_d[:, mc*1024:(mc+1)*1024])

    for mc in range(4):
        load_win(mc)
    for mc in range(32):
        isx = mc < MCI
        c = mc if isx else mc - MCI
        for tq in range(4):
            pj = (mc + tq) % 2
            ps = PS[pj][:, 0:512]
            wsl = winr[mc % 4]
            lh = [wsl[:, cc*128:(cc+1)*128] for cc in range(KC)]
            rh = [C.y_fm[:, cc*T + tq*512:(cc*T) + (tq+1)*512] for cc in range(KC)]
            emit_matmul_acc(C, ps, lh, rh, reads=["winr%d" % (mc % 4), "y_fm"],
                            writes=["ps%d" % pj])
            if isx:
                with P.group("vector", reads=["ps%d" % pj, "l1b_small", "l1b_stage"],
                             writes=(["l1b_stage", "y_tm"] if mc == 0 and tq == 0
                                     else ["l1b_stage"])) as g:
                    if tq == 0:
                        g.emit(lambda v: v.memset(stage[:, 0:3], 0.0))
                    else:
                        g.emit(lambda v: v.tensor_copy(out=stage[:, 0:3],
                                                       in_=stage[:, 512:515]))
                    g.emit(lambda v, ps=ps, b=binc[:, c:c+1]:
                           v.tensor_scalar(stage[:, 3:515], ps, b, None, ALU.add))
                    g.emit(lambda v, w=cwc[:, c*4+3:c*4+4]:
                           v.tensor_scalar(stage[:, 0:0] if False else ubuf[:, 0:512],
                                           stage[:, 3:515], w, None, ALU.mult))
                    for k in range(3):
                        g.emit(lambda v, k=k, w=cwc[:, c*4+2-k:c*4+3-k]:
                               v.scalar_tensor_tensor(ubuf[:, 0:512], stage[:, 2-k:514-k],
                                                      w, ubuf[:, 0:512], ALU.mult, ALU.add))
                with P.group("scalar", reads=["l1b_stage", "l1b_small"],
                             writes=(["xc", "x"] if mc == 0 and tq == 0 else ["xc"])) as g:
                    g.emit(lambda s, o=xc[:, c*T + tq*512: c*T + (tq+1)*512],
                           b=cbc[:, c:c+1]:
                           s.activation(o, ubuf[:, 0:512], AF.Silu, bias=b))
            else:
                with P.group("scalar", reads=["ps%d" % pj, "l1b_small"],
                             writes=(["l1b_srst", "y_tm"] if mc == 16 and tq == 0
                                     else ["l1b_srst"])) as g:
                    g.emit(lambda s, ps=ps, o=sres_st[:, tq*512:(tq+1)*512],
                           b=binc[:, 16+c:17+c]:
                           s.activation(o, ps, AF.Silu, bias=b))
        if not isx:
            with P.group("sync", reads=["l1b_srst"], writes=["l1b_sres_dr"]) as g:
                g.dma_(sres_dr[:, c*T:(c+1)*T], sres_st[:])
        if mc + 4 < 32:
            load_win(mc + 4)
    for t in range(NT):
        pj = t % 2
        ps = PS[pj][:, 0:96]
        lh = [xc[:, c*T + t*128: c*T + (t+1)*128] for c in range(MCI)]
        rh = [wxc[:, c*96:(c+1)*96] for c in range(MCI)]
        emit_matmul_acc(C, ps, lh, rh, reads=["xc", "l1b_small"], writes=["ps%d" % pj])
        with P.group("vector", reads=["ps%d" % pj],
                     writes=(["l1b_xdbl", "y_tm"] if t == 0 else ["l1b_xdbl"])) as g:
            g.emit(lambda v, ps=ps, o=xdbl[:, t*96:(t+1)*96]: v.tensor_copy(out=o, in_=ps))
    for t in range(NT):
        with P.group("tensor", reads=["l1b_xdbl", "ident"], writes=["ps_aux"]) as g:
            g.emit(lambda p, t=t: p.transpose(C.ps_aux[0:96, 0:128],
                                              xdbl[:, t*96:(t+1)*96], C.identf[:]))
        with P.group("vector", reads=["ps_aux"],
                     writes=["l1b_dbct", "l1b_drt"] + (["y_tm"] if t == 0 else [])) as g:
            g.emit(lambda v, t=t: v.tensor_copy(out=dbct[:, t*128:(t+1)*128],
                                                in_=C.ps_aux[0:96, 0:128]))
            g.emit(lambda v, t=t: v.tensor_copy(out=drt_sb[:, t*128:(t+1)*128],
                                                in_=C.ps_aux[0:64, 0:128]))
    with P.group("sync", reads=["l1b_dbct"], writes=["l1b_bct_dr"]) as g:
        g.dma_(bct_dr[:], dbct[64:96, :])
    for c in range(MCI):
        for tq in range(4):
            pj = (c + tq) % 2
            ps = PS[pj][:, 0:512]
            emit_matmul_acc(C, ps, [wdt[:, c*128:(c+1)*128]],
                            [drt_sb[:, tq*512:(tq+1)*512]],
                            reads=["l1b_drt", "l1b_small"], writes=["ps%d" % pj])
            with P.group("vector", reads=["ps%d" % pj, "l1b_small"], writes=["l1b_s1"]) as g:
                g.emit(lambda v, ps=ps, b=bdtc[:, c:c+1], o=delta[:, tq*512:(tq+1)*512]:
                       v.tensor_scalar(o, ps, b, None, ALU.add))
        with P.group("scalar", reads=["l1b_s1"], writes=["l1b_sp1"]) as g:
            g.emit(lambda s: s.activation(dA[:], delta[:], AF.Abs))
            g.emit(lambda s: s.activation(dA[:], dA[:], AF.Exp, scale=-1.0))
            g.emit(lambda s: s.activation(dA[:], dA[:], AF.Ln, bias=1.0))
        with P.group("vector", reads=["l1b_sp1", "xc"], writes=["l1b_delta"]) as g:
            g.emit(lambda v: v.scalar_tensor_tensor(delta[:], delta[:], 0.0, dA[:],
                                                    ALU.max, ALU.add))
            g.emit(lambda v, xs=xc[:, c*T:(c+1)*T]:
                   v.tensor_tensor(dtx[:], delta[:], xs, ALU.mult))
        for n in range(DS):
            with P.group("sync", writes=(["l1b_bbc", "y_fm"] if c == 0 and n == 0
                                           else ["l1b_bbc"])) as g:
                g.dma_(bbc[:], bass.AP(bct_dr.tensor, n*T, [[0, 128], [1, T]]))
            with P.group("scalar", writes=(["l1b_cbc2", "y_fm"] if c == 0 and n == 0
                                             else ["l1b_cbc2"])) as g:
                g.dma_(cbc2[:], bass.AP(bct_dr.tensor, (16+n)*T, [[0, 128], [1, T]]))
            with P.group("scalar", reads=["l1b_delta", "l1b_asb"],
                         writes=["l1b_dA", "l1b_sp1"]) as g:
                g.emit(lambda s, a=asb[:, c*16+n:c*16+n+1]:
                       s.activation(dA[:], delta[:], AF.Exp, scale=a))
            with P.group("vector", reads=["l1b_bbc", "l1b_delta"],
                         writes=["l1b_u", "l1b_stage"]) as g:
                g.emit(lambda v: v.tensor_tensor(ubuf[:], dtx[:], bbc[:], ALU.mult))
            with P.group("vector", reads=["l1b_u", "l1b_dA"], writes=["l1b_h"]) as g:
                g.emit(lambda v: v.tensor_tensor_scan(hbuf[:], dA[:], ubuf[:], 0.0,
                                                      ALU.mult, ALU.add))
            with P.group("gpsimd", reads=["l1b_h", "l1b_cbc2"], writes=["l1b_hc"]) as g:
                g.emit(lambda g_: g_.tensor_tensor(hcbuf[:], hbuf[:], cbc2[:], ALU.mult))
            with P.group("vector", reads=["l1b_hc"], writes=["l1b_y"]) as g:
                if n == 0:
                    g.emit(lambda v: v.tensor_copy(out=ybuf[:], in_=hcbuf[:]))
                else:
                    g.emit(lambda v: v.tensor_add(ybuf[:], ybuf[:], hcbuf[:]))
        with P.group("gpsimd", reads=["l1b_sres_dr"], writes=["l1b_srst"]) as g:
            g.dma_(sres_st[:], sres_dr[:, c*T:(c+1)*T])
        with P.group("vector", reads=["l1b_y", "xc", "l1b_small", "l1b_srst"],
                     writes=["xc", "l1b_y"]) as g:
            g.emit(lambda v, xs=xc[:, c*T:(c+1)*T], dcol=dpc[:, c:c+1]:
                   v.scalar_tensor_tensor(ybuf[:], xs, dcol, ybuf[:], ALU.mult, ALU.add))
            g.emit(lambda v, xs=xc[:, c*T:(c+1)*T]:
                   v.tensor_tensor(xs, ybuf[:], sres_st[:], ALU.mult))
    for hh in range(2):
        with P.group("gpsimd", writes=["l1b_brow", "y_fm"]) as g:
            g.dma_(brow[:], bo_d[0:1, hh*512:(hh+1)*512])
        pj = 4 + hh
        emit_matmul_acc(C, PS[pj][:, 0:512], [C.ones_row[0:1, 0:128]], [brow[0:1, :]],
                        reads=["l1b_brow", "ones"], writes=["ps%d" % pj])
        with P.group("vector", reads=["ps%d" % pj], writes=["l1b_bobc", "y_fm"]) as g:
            g.emit(lambda v, ps=PS[pj][:, 0:512], o=bobc[:, hh*512:(hh+1)*512]:
                   v.tensor_copy(out=o, in_=ps))

    for r in range(4):
        with P.group("gpsimd", writes=["wour%d" % r] + (["l1b_drt"] if r == 0 else [])) as g:
            g.dma_(wour[r][:], wout_d[:, r*D:(r+1)*D])
    for t in range(NT):
        with P.group("sync", reads=["xspill"],
                     writes=(["l1b_xst", "y_fm"] if t == 0 else ["l1b_xst"])) as g:
            g.dma_(xstage[:], xsp[:, t*D:(t+1)*D])
        for mc in range(MCI):
            with P.group("tensor", reads=["xc", "wour%d" % (mc % 4)],
                         writes=["ps0", "ps1"]) as g:
                for dh in range(2):
                    g.emit(lambda p, ps=PS[dh][:, 0:512],
                           lt=xc[:, mc*T + t*128: mc*T + (t+1)*128],
                           rh=wour[mc % 4][:, dh*512:(dh+1)*512],
                           st=(mc == 0), sp=(mc == MCI-1):
                           p.matmul(ps, lt, rh, start=st, stop=sp))
            nxt = mc + 4 if mc + 4 < MCI else (mc + 4 - MCI if t < NT-1 else -1)
            if nxt >= 0:
                with P.group("gpsimd", writes=["wour%d" % (nxt % 4)]) as g:
                    g.dma_(wour[nxt % 4][:], wout_d[:, nxt*D:(nxt+1)*D])
        for dh in range(2):
            with P.group("vector", reads=["ps%d" % dh, "l1b_xst", "l1b_bobc"],
                         writes=(["l1b_ost", "y_fm"] if t == 0 and dh == 0
                                 else ["l1b_ost"])) as g:
                g.emit(lambda v, ps=PS[dh][:, 0:512], o=ostage[:, dh*512:(dh+1)*512],
                       b=bobc[:, dh*512:(dh+1)*512]:
                       v.tensor_add(o, ps, b))
                g.emit(lambda v, o=ostage[:, dh*512:(dh+1)*512],
                       xs=xstage[:, dh*512:(dh+1)*512]:
                       v.tensor_add(o, o, xs))
        with P.group("sync", reads=["l1b_ost"], writes=["xspill"]) as g:
            g.dma_(xsp[:, t*D:(t+1)*D], ostage[:])
    emit_reload_x(C, claims=["xc"])


def prep_l1_attn_b(inputs, h):
    DI = 2048
    w_in = np.asarray(inputs["L1_attn_b_W_in"], np.float32)
    a = w_in.reshape(KC, 128, 32, 128).transpose(1, 2, 0, 3)
    h["l1ab_win"] = np.ascontiguousarray(a.reshape(128, 32*KC*128))
    h["l1ab_bin"] = np.ascontiguousarray(
        np.asarray(inputs["L1_attn_b_b_in"], np.float32).reshape(32, 128).T)
    h["l1ab_cw"] = np.ascontiguousarray(
        np.asarray(inputs["L1_attn_b_conv_w"], np.float32)[:, 0, :]
        .reshape(16, 128, 4).transpose(1, 0, 2).reshape(128, 64))
    h["l1ab_cbc"] = np.ascontiguousarray(
        np.asarray(inputs["L1_attn_b_conv_b"], np.float32).reshape(16, 128).T)
    h["l1ab_wx"] = sb_image_kc(np.ascontiguousarray(
        np.asarray(inputs["L1_attn_b_W_x"], np.float32)), kc=16)
    h["l1ab_wdt"] = np.ascontiguousarray(np.asarray(inputs["L1_attn_b_W_dt"], np.float32))
    h["l1ab_bdt"] = np.ascontiguousarray(
        np.asarray(inputs["L1_attn_b_b_dt"], np.float32).reshape(16, 128).T)
    h["l1ab_alog"] = np.ascontiguousarray(
        np.asarray(inputs["L1_attn_b_A_log"], np.float32).reshape(16, 128, 16)
        .transpose(1, 0, 2).reshape(128, 256))
    h["l1ab_dp"] = np.ascontiguousarray(
        np.asarray(inputs["L1_attn_b_D_par"], np.float32).reshape(16, 128).T)
    h["l1ab_wout"] = sb_image_kc(np.ascontiguousarray(
        np.asarray(inputs["L1_attn_b_W_out"], np.float32)), kc=16)
    h["l1ab_bo"] = np.asarray(inputs["L1_attn_b_b_out"], np.float32).reshape(1, D)


L0F_DEPTH = 6
L0F_GRP = [512, 512, 512, 512, 512, 256]
L0F_GOFF = [0, 512, 1024, 1536, 2048, 2560]
L0F_W1C = 16 * 2816
L0F_SKIP_ATTN = False
L0F_SKIP_FFN = False
L0F_WB_LIM = 8
L0F_SKIP_EPI = False
GP = 2816
NGC = GP // 128


def build_l0_ffn(C, tag):
    P = C.P
    i = C.blk_i
    A = OFF_ARENA
    ZROW = T + 1
    z = C.sb_at("l0f_z%d" % i, (128, KC*ZROW), F32, OFF_X)
    YR = OFF_X + 65568
    tabs = [C.sb_at("l0f_tab%d_%d" % (i, j), (128, NT*64), F32, YR + j*4096)
            for j in range(4)]
    qsbc = C.sb_at("l0f_qsb%d" % i, (128, 64), F32, YR + 16384)
    ksbc = C.sb_at("l0f_ksb%d" % i, (128, 64), F32, YR + 16640)
    madd = C.sb_at("l0f_mad%d" % i, (128, 4*512), F32, YR + 16896)
    rsc = C.sb_at("l0f_rsc%d" % i, (128, 1024), F32, YR + 25088)
    zstA = C.sb_at("l0f_zstA%d" % i, (128, 4), F32, YR + 29184)
    zstF = C.sb_at("l0f_zstF%d" % i, (128, 4), F32, YR + 29216)
    brow = C.sb_at("l0f_br%d" % i, (1, 512), F32, YR)
    rrowA = C.sb_at("l0f_rrA%d" % i, (1, 512), F32, YR + 2048)
    rrowB = C.sb_at("l0f_rrB%d" % i, (1, 512), F32, YR + 4096)
    rbc = C.sb_at("l0f_rbc%d" % i, (128, 512), F32, YR + 6144)
    nbbc = C.sb_at("l0f_nbc%d" % i, (128, 512), F32, YR + 8192)
    w1slB = C.sb_at("l0f_w1B%d" % i, (128, 16*512), BF16, YR + 10240)
    gsc = C.sb_at("l0f_gsc%d" % i, (128, 512), F32, YR + 26624)
    wqkv = C.sb_at("l0f_wqkv%d" % i, (128, KC*3072), BF16, A)
    wo = C.sb_at("l0f_wo%d" % i, (128, KC*1024), BF16, A + 49152)
    zb = C.sb_at("l0f_zb%d" % i, (128, KC*256), BF16, A + 65536)
    qkf = C.sb_at("l0f_qk%d" % i, (128, 1024), F32, A + 69632)
    qhat = C.sb_at("l0f_qh%d" % i, (128, 1024), F32, A + 73728)
    rstd = C.sb_at("l0f_rs%d" % i, (128, 16), F32, A + 77824)
    mcol = C.sb_at("l0f_mc%d" % i, (128, 16), F32, A + 77888)
    qrot = C.sb_at("l0f_qr%d" % i, (128, 2048), BF16, A + 77952)
    qpf = C.sb_at("l0f_qpf%d" % i, (128, 8*256), BF16, A + 82048)
    kpf2 = C.sb_at("l0f_kpf%d" % i, (128, 2*8*256), BF16, A + 86144)
    vtm = [C.sb_at("l0f_vtm%d_%d" % (i, j), (128, 2*1024), BF16, A + 94336 + j*4096)
           for j in range(2)]
    s1 = [C.sb_at("l0f_s1%d_%d" % (i, p), (128, 512), F32, A + 65536 + p*2048)
          for p in range(4)]
    attn_sb = [C.sb_at("l0f_at%d_%d" % (i, p), (128, 512), BF16, A + 77952 + p*1024)
               for p in range(2)]
    attnT_sb = [C.sb_at("l0f_aT%d_%d" % (i, p), (128, 512), BF16, A + 80000 + p*1024)
                for p in range(2)]
    aof = C.sb_at("l0f_aof%d" % i, (128, 8*256), BF16, A + 73728)
    w1slA = C.sb_at("l0f_w1A%d" % i, (128, 16*512), BF16, A)
    w1sl = [w1slA, w1slB]
    w2r = C.sb_at("l0f_w2%d" % i, (128, NGC*1024), BF16, A + 16384)
    hprod = C.sb_at("l0f_hp%d" % i, (128, NGC*512), BF16, A + 61440)
    xnb2 = [C.sb_at("l0f_xn%d_%d" % (i, p), (128, KC*512), BF16,
                    A + (83968 if p == 0 else 94336)) for p in range(2)]
    fs1 = C.sb_at("l0f_fs1%d" % i, (128, 512), F32, A + 92160)
    stat = C.sb_at("l0f_stc%d" % i, (128, 16), F32, A + 94208)
    xst = C.sb_at("l0f_xst%d" % i, (128, 1024), F32, A + 94336)
    ost = C.sb_at("l0f_ost%d" % i, (128, 1024), F32, A + 98432)
    b1ac = C.sb_at("l0f_b1a%d" % i, (128, NGC), F32, A + 102528)
    b1gc = C.sb_at("l0f_b1g%d" % i, (128, NGC), F32, A + 102656)
    b2c = C.sb_at("l0f_b2c%d" % i, (128, KC), F32, A + 102784)
    lngc = C.sb_at("l0f_lng%d" % i, (128, KC), F32, A + 102848)
    lnbc = C.sb_at("l0f_lnb%d" % i, (128, KC), F32, A + 102912)
    wqkv_d = C.dram_in("l0f_wqkv", (128, L0F_DEPTH*KC*3072), F32)
    wo_d = C.dram_in("l0f_wo", (128, L0F_DEPTH*KC*1024), F32)
    w1_d = C.dram_in("l0f_w1", (128, L0F_DEPTH*L0F_W1C), F32)
    w2_d = C.dram_in("l0f_w2", (128, L0F_DEPTH*NGC*1024), F32)
    b1a_d = C.dram_in("l0f_b1a", (128, L0F_DEPTH*NGC), F32)
    b1g_d = C.dram_in("l0f_b1g", (128, L0F_DEPTH*NGC), F32)
    b2_d = C.dram_in("l0f_b2", (128, L0F_DEPTH*KC), F32)
    lng_d = C.dram_in("l0f_lng", (128, L0F_DEPTH*KC), F32)
    lnb_d = C.dram_in("l0f_lnb", (128, L0F_DEPTH*KC), F32)
    qs_d = C.dram_in("l0f_qs", (L0F_DEPTH, 64), F32)
    ks_d = C.dram_in("l0f_ks", (L0F_DEPTH, 64), F32)
    tab_d = C.dram_in("l0f_tabs", (128, 4*NT*64), F32)
    madd_d = C.dram_in("l0f_madd", (128, 4*512), F32)
    xsp = C.dram_tmp("xspill", (128, NT*D), F32)
    PS = C.psum_f32

    emit_spill_x(C)
    P.barrier()
    with P.group("vector", reads=["y_fm"], writes=["z", "x"]) as g:
        for cc in range(KC):
            g.emit(lambda v, cc=cc: v.memset(z[:, cc*ZROW:cc*ZROW+1], 0.0))
            g.emit(lambda v, cc=cc: v.tensor_copy(
                out=z[:, cc*ZROW+1:(cc+1)*ZROW], in_=C.y_fm[:, cc*T:(cc+1)*T]))
    P.barrier()

    def attn_sublayer(l):
        with P.group("gpsimd", writes=["l0f_wqkv"]) as g:
            for cc in range(KC):
                g.dma_(wqkv[:, cc*3072:(cc+1)*3072],
                       wqkv_d[:, l*KC*3072 + cc*3072: l*KC*3072 + (cc+1)*3072])
        with P.group("gpsimd", writes=["l0f_wo"]) as g:
            g.dma_(wo[:], wo_d[:, l*KC*1024:(l+1)*KC*1024])
        with P.group("sync", writes=["l0f_tabs"]) as g:
            for j in range(4):
                g.dma_(tabs[j][:], tab_d[:, j*1024:(j+1)*1024])
        with P.group("gpsimd", writes=["l0f_qsrow"]) as g:
            g.dma_(qsbc[0:1, 0:64], qs_d[l:l+1, :])
            g.dma_(ksbc[0:1, 0:64], ks_d[l:l+1, :])
        for j in range(2):
            src = qsbc if j == 0 else ksbc
            emit_matmul_acc(C, PS[4+j][:, 0:64], [C.ones_row[0:1, 0:128]],
                            [src[0:1, 0:64]],
                            reads=["l0f_qsrow", "ones"], writes=["ps%d" % (4+j)])
            with P.group("vector", reads=["ps%d" % (4+j)],
                         writes=["l0f_qsbc", "l0f_qsrow"]) as g:
                g.emit(lambda v, ps=PS[4+j][:, 0:64], o=src:
                       v.tensor_copy(out=o[:], in_=ps))
        qsrep = bass.AP(qsbc, 0, [[64, 128], [0, NT], [1, 64]])
        qsrot = bass.AP(qsbc, 32, [[64, 128], [0, NT], [-32, 2], [1, 32]])
        ksrep = bass.AP(ksbc, 0, [[64, 128], [0, NT], [1, 64]])
        ksrot = bass.AP(ksbc, 32, [[64, 128], [0, NT], [-32, 2], [1, 32]])
        with P.group("vector", reads=["l0f_tabs", "l0f_qsbc"], writes=["l0f_tabs"], chain=False) as g:
            g.emit(lambda v: v.tensor_tensor(tabs[0][:], tabs[0][:], qsrep, ALU.mult))
            g.emit(lambda v: v.tensor_tensor(tabs[1][:], tabs[1][:], qsrot, ALU.mult))
            g.emit(lambda v: v.tensor_tensor(tabs[2][:], tabs[2][:], ksrep, ALU.mult))
            g.emit(lambda v: v.tensor_tensor(tabs[3][:], tabs[3][:], ksrot, ALU.mult))
        with P.group("gpsimd", writes=["l0f_kpf1", "l0f_vtm1"]) as g:
            g.emit(lambda g_: g_.memset(kpf2[:, 2048:4096], 0.0))
            g.emit(lambda g_: g_.memset(vtm[1][:], 0.0))
        with P.group("scalar", writes=["l0f_madd"]) as g:
            g.dma_(madd[:], madd_d[:])
        P.barrier()
        for wb in range(L0F_WB_LIM):
            cur = wb % 2
            prv = 1 - cur
            with P.group("vector", reads=["z", "l0f_zst"], writes=["l0f_zb"], chain=False) as g:
                for cc in range(KC):
                    off = (1 if cc < 4 else 0)
                    if cc < 4 or wb == 0:
                        g.emit(lambda v, o=zb[:, cc*256:(cc+1)*256],
                               s=z[:, cc*ZROW + off + wb*256: cc*ZROW + off + wb*256 + 256]:
                               v.tensor_copy(out=o, in_=s))
                    else:
                        g.emit(lambda v, o=zb[:, cc*256:cc*256+1],
                               s=zstA[:, cc-4:cc-3]: v.tensor_copy(out=o, in_=s))
                        g.emit(lambda v, o=zb[:, cc*256+1:(cc+1)*256],
                               s=z[:, cc*ZROW + 1 + wb*256: cc*ZROW + wb*256 + 256]:
                               v.tensor_copy(out=o, in_=s))
            with P.group("vector", reads=["z"], writes=["l0f_zst"], chain=False) as g:
                for cc in range(4, KC):
                    g.emit(lambda v, o=zstA[:, cc-4:cc-3],
                           s=z[:, cc*ZROW + 1 + wb*256 + 255: cc*ZROW + 1 + wb*256 + 256]:
                           v.tensor_copy(out=o, in_=s))
            for tt in range(2):
                for mg in range(6):
                    ps = PS[mg][:, 0:512]
                    lh = [zb[:, cc*256 + tt*128: cc*256 + (tt+1)*128] for cc in range(KC)]
                    rh = [wqkv[:, cc*3072 + mg*512: cc*3072 + (mg+1)*512] for cc in range(KC)]
                    emit_matmul_acc(C, ps, lh, rh, reads=["l0f_zb", "l0f_wqkv"],
                                    writes=["ps%d" % mg])
                with P.group("vector", reads=["ps4", "ps5"],
                             writes=["l0f_vtm%d" % cur], chain=False) as g:
                    g.emit(lambda v, o=vtm[cur][:, tt*1024:tt*1024+512]:
                           v.tensor_copy(out=o, in_=PS[4][:, 0:512]))
                    g.emit(lambda v, o=vtm[cur][:, tt*1024+512:(tt+1)*1024]:
                           v.tensor_copy(out=o, in_=PS[5][:, 0:512]))
                for qk in range(2):
                    pq = 0 if qk == 0 else 2
                    tA, tB = (tabs[0], tabs[1]) if qk == 0 else (tabs[2], tabs[3])
                    with P.group("vector", reads=["ps%d" % pq, "ps%d" % (pq+1)],
                                 writes=["l0f_qkf"], chain=False) as g:
                        g.emit(lambda v, pq=pq: v.tensor_copy(out=qkf[:, 0:512],
                                                              in_=PS[pq][:, 0:512]))
                        g.emit(lambda v, pq=pq: v.tensor_copy(out=qkf[:, 512:1024],
                                                              in_=PS[pq+1][:, 0:512]))
                    with P.group("scalar", reads=["l0f_qkf"], writes=["l0f_qh"]) as g:
                        g.emit(lambda s: s.activation(qhat[:], qkf[:], AF.Square))
                    with P.group("vector", reads=["l0f_qh"], writes=["l0f_rstd"]) as g:
                        g.emit(lambda v: v.tensor_reduce(
                            rstd[:], bass.AP(qhat, 0, [[1024, 128], [64, 16], [1, 64]]),
                            AX.X, ALU.add))
                        g.emit(lambda v: v.tensor_scalar_max(rstd[:], rstd[:], 1e-24))
                    with P.group("scalar", reads=["l0f_rstd"], writes=["l0f_rstd2"]) as g:
                        g.emit(lambda s: s.activation(rstd[:], rstd[:], AF.Ln))
                        g.emit(lambda s: s.activation(rstd[:], rstd[:], AF.Exp, scale=-0.5))
                    rrep = bass.AP(rstd, 0, [[16, 128], [1, 16], [0, 64]])
                    ttile = wb*2 + tt
                    tArep = bass.AP(tA, ttile*64, [[NT*64, 128], [0, 16], [1, 64]])
                    tBrep = bass.AP(tB, ttile*64, [[NT*64, 128], [0, 16], [1, 64]])
                    qhrot = bass.AP(qhat, 32, [[1024, 128], [64, 16], [-32, 2], [1, 32]])
                    with P.group("vector", reads=["l0f_rstd2", "l0f_qkf", "l0f_tabs"],
                                 writes=["l0f_qrot", "l0f_qkf", "l0f_qh"]) as g:
                        g.emit(lambda v, rr=rrep: v.tensor_tensor(qhat[:], qkf[:], rr, ALU.mult))
                        g.emit(lambda v, qr=qhrot, tr=tBrep:
                               v.tensor_tensor(rsc[:], qr, tr, ALU.mult))
                        g.emit(lambda v, tr=tArep: v.tensor_tensor(qkf[:], qhat[:], tr, ALU.mult))
                        g.emit(lambda v, o=qrot[:, qk*1024:(qk+1)*1024]:
                               v.tensor_add(o, qkf[:], rsc[:]))
                with P.group("tensor", reads=["l0f_qrot", "ident"], writes=["ps_tr"],
                             chain=False) as g:
                    for cc in range(KC):
                        g.emit(lambda p, cc=cc: p.transpose(
                            C.ps_tr[:, cc*128:(cc+1)*128],
                            qrot[:, cc*128:(cc+1)*128], C.ident[:]))
                with P.group("vector", reads=["ps_tr"], writes=["l0f_qpf"], chain=False) as g:
                    for cc in range(KC):
                        g.emit(lambda v, cc=cc, tt=tt: v.tensor_copy(
                            out=qpf[:, cc*256 + tt*128: cc*256 + (tt+1)*128],
                            in_=C.ps_tr[:, cc*128:(cc+1)*128]))
                with P.group("tensor", reads=["l0f_qrot", "ident"], writes=["ps_tr"],
                             chain=False) as g:
                    for cc in range(KC):
                        g.emit(lambda p, cc=cc: p.transpose(
                            C.ps_tr[:, cc*128:(cc+1)*128],
                            qrot[:, 1024 + cc*128: 1024 + (cc+1)*128], C.ident[:]))
                with P.group("vector", reads=["ps_tr"], writes=["l0f_kpf%d" % cur], chain=False) as g:
                    for cc in range(KC):
                        g.emit(lambda v, o=kpf2[:, cur*2048 + cc*256 + tt*128: cur*2048 + cc*256 + (tt+1)*128],
                               s=C.ps_tr[:, cc*128:(cc+1)*128]:
                               v.tensor_copy(out=o, in_=s))
            P.barrier()
            def st_qk(idx):
                h, it = idx // 2, idx % 2
                p0 = (h % 2) * 64
                cc = h // 2
                ps = PS[idx % 4][:, 0:512]
                rh = bass.AP(kpf2, p0*4096 + prv*2048 + cc*256,
                             [[4096, 64], [(cur - prv)*2048, 2], [1, 256]])
                with P.group("tensor", reads=["l0f_qpf", "l0f_kpf0", "l0f_kpf1"],
                             writes=["ps%d" % (idx % 4)], chain=False) as g:
                    g.emit(lambda p, ps=ps, rh=rh,
                           lt=qpf[p0:p0+64, cc*256 + it*128: cc*256 + (it+1)*128]:
                           p.matmul(ps, lt, rh))

            def st_s1(idx):
                j = idx % 4
                s1p = s1[j]
                mi = (idx % 2)*2 + (1 if wb == 0 else 0)
                ps = PS[j][:, 0:512]
                with P.group("vector", reads=["ps%d" % j, "l0f_madd"],
                             writes=["l0f_s1%d" % j, "l0f_s1e%d" % j]) as g:
                    g.emit(lambda v, ps=ps, s1p=s1p, m=madd[:, mi*512:(mi+1)*512]:
                           v.scalar_tensor_tensor(s1p[:], ps, -8.0, m, ALU.mult, ALU.add))

            def st_mx(idx):
                j = idx % 4
                with P.group("vector", reads=["l0f_s1%d" % j],
                             writes=["l0f_mc%d" % j]) as g:
                    g.emit(lambda v, s1p=s1[j], mc=mcol[:, j*4:j*4+4]:
                           v.tensor_reduce(mc[:, 0:1], s1p[:], AX.X, ALU.min))

            def st_exp(idx):
                j = idx % 4
                with P.group("scalar", reads=["l0f_s1%d" % j, "l0f_mc%d" % j],
                             writes=["l0f_s1e%d" % j, "l0f_mca%d" % j]) as g:
                    g.emit(lambda s, s1p=s1[j], mc=mcol[:, j*4:j*4+4]:
                           s.activation(s1p[:], s1p[:], AF.Exp, scale=-1.0,
                                        bias=mc[:, 0:1], accum_out=mc[:, 1:2]))

            def st_n1(idx):
                j = idx % 4
                with P.group("vector", reads=["l0f_mca%d" % j],
                             writes=["l0f_rcp%d" % j]) as g:
                    g.emit(lambda v, mc=mcol[:, j*4:j*4+4]:
                           v.reciprocal(mc[:, 2:3], mc[:, 1:2]))

            def st_n2(idx):
                j = idx % 4
                par = idx % 2
                with P.group("scalar", reads=["l0f_s1e%d" % j, "l0f_rcp%d" % j],
                             writes=["l0f_attn%d" % par]) as g:
                    g.emit(lambda s, s1p=s1[j], atp=attn_sb[par],
                           mc=mcol[:, j*4:j*4+4]:
                           s.activation(atp[:], s1p[:], AF.Identity,
                                        scale=mc[:, 2:3]))

            def st_tr(idx):
                par = idx % 2
                atp = attn_sb[par]
                pstr = C.ps_tr[:, par*512:(par+1)*512]
                with P.group("tensor", reads=["l0f_attn%d" % par, "ident"],
                             writes=["ps_tr%d" % par], chain=False) as g:
                    for jj in range(4):
                        g.emit(lambda p, jj=jj, atp=atp, pstr=pstr: p.transpose(
                            pstr[:, jj*128:(jj+1)*128],
                            atp[:, jj*128:(jj+1)*128], C.ident[:]))

            def st_tT(idx):
                par = idx % 2
                pstr = C.ps_tr[:, par*512:(par+1)*512]
                with P.group("vector", reads=["ps_tr%d" % par],
                             writes=["l0f_attnT%d" % par]) as g:
                    g.emit(lambda v, aTp=attnT_sb[par], pstr=pstr:
                           v.tensor_copy(out=aTp[:], in_=pstr))

            def st_pv(idx):
                h, it = idx // 2, idx % 2
                p0 = (h % 2) * 64
                par = idx % 2
                aTp = attnT_sb[par]
                po = PS[4 + par][p0:p0+64, 0:128]
                with P.group("tensor", reads=["l0f_attnT%d" % par,
                                              "l0f_vtm0", "l0f_vtm1"],
                             writes=["psv%d" % par], chain=False) as g:
                    for jj in range(4):
                        vsrc = vtm[prv] if jj < 2 else vtm[cur]
                        g.emit(lambda p, po=po, tp=(0, p0), aTp=aTp,
                               lt=vsrc[:, (jj % 2)*1024 + h*64: (jj % 2)*1024 + (h+1)*64],
                               rh=aTp[:, jj*128:(jj+1)*128],
                               st=(jj == 0), sp=(jj == 3):
                               p.matmul(po, lt, rh, start=st, stop=sp, tile_position=tp))

            def st_ao(idx):
                h, it = idx // 2, idx % 2
                p0 = (h % 2) * 64
                cc = h // 2
                par = idx % 2
                po = PS[4 + par][p0:p0+64, 0:128]
                with P.group("scalar", reads=["psv%d" % par],
                             writes=["l0f_aof"]) as g:
                    g.emit(lambda s, po=po,
                           o=aof[p0:p0+64, cc*256 + it*128: cc*256 + (it+1)*128]:
                           s.activation(o, po, AF.Identity))

            for idx in range(32):
                for fn in (st_qk, st_s1, st_mx, st_exp, st_n1, st_n2,
                           st_tr, st_tT, st_pv, st_ao):
                    fn(idx)
            for dp in range(4):
                pj = 2 + (dp % 2)
                ps = PS[pj]
                with P.group("tensor", reads=["l0f_aof", "l0f_wo"],
                             writes=["ps%d" % pj], chain=False) as g:
                    for half in range(2):
                        dc = dp*2 + half
                        for ic in range(KC):
                            g.emit(lambda p, ps=ps[:, half*256:(half+1)*256],
                                   lt=wo[:, ic*1024 + dc*128: ic*1024 + (dc+1)*128],
                                   rh=aof[:, ic*256:(ic+1)*256],
                                   st=(ic == 0), sp=(ic == KC-1):
                                   p.matmul(ps, lt, rh, start=st, stop=sp))
                with P.group("vector", reads=["ps%d" % pj, "z"], writes=["z"], chain=False) as g:
                    for half in range(2):
                        dc = dp*2 + half
                        zsl = z[:, dc*ZROW + 1 + wb*256: dc*ZROW + 1 + (wb+1)*256]
                        g.emit(lambda v, ps=ps[:, half*256:(half+1)*256], o=zsl:
                               v.tensor_add(o, o, ps))
            P.barrier()

    def ffn_sublayer(l):
        with P.group("gpsimd", writes=["l0f_fparm"]) as g:
            g.dma_(b1ac[:], b1a_d[:, l*NGC:(l+1)*NGC])
            g.dma_(b1gc[:], b1g_d[:, l*NGC:(l+1)*NGC])
            g.dma_(b2c[:], b2_d[:, l*KC:(l+1)*KC])
            g.dma_(lngc[:], lng_d[:, l*KC:(l+1)*KC])
            g.dma_(lnbc[:], lnb_d[:, l*KC:(l+1)*KC])
        with P.group("gpsimd", writes=["l0f_w2r"]) as g:
            for q4 in range(0, NGC, 6):
                qn = min(6, NGC - q4)
                g.dma_(w2r[:, q4*1024:(q4+qn)*1024],
                       w2_d[:, l*NGC*1024 + q4*1024: l*NGC*1024 + (q4+qn)*1024])
        P.barrier()

        def ffn_pre(tc):
            par = tc % 2
            xnb = xnb2[par]
            with P.group("vector", reads=["z", "l0f_zst"],
                         writes=["l0f_xnb%d" % par], chain=False) as g:
                for cc in range(KC):
                    off = (1 if cc < 4 else 0)
                    if cc < 4 or tc == 0:
                        g.emit(lambda v, o=xnb[:, cc*512:(cc+1)*512],
                               s=z[:, cc*ZROW + off + tc*512: cc*ZROW + off + tc*512 + 512]:
                               v.tensor_copy(out=o, in_=s))
                    else:
                        g.emit(lambda v, o=xnb[:, cc*512:cc*512+1],
                               s=zstF[:, cc-4:cc-3]: v.tensor_copy(out=o, in_=s))
                        g.emit(lambda v, o=xnb[:, cc*512+1:(cc+1)*512],
                               s=z[:, cc*ZROW + 1 + tc*512: cc*ZROW + tc*512 + 512]:
                               v.tensor_copy(out=o, in_=s))
            for tt in range(4):
                emit_matmul_acc(C, C.ps_aux[:, 128:129],
                                [xnb[:, cc*512 + tt*128: cc*512 + (tt+1)*128]
                                 for cc in range(KC)],
                                [C.ones_cb[:] for cc in range(KC)],
                                reads=["l0f_xnb%d" % par, "ones"], writes=["ps_aux"])
                emit_matmul_acc(C, C.ps_aux[:, 0:128],
                                [xnb[:, cc*512 + tt*128: cc*512 + (tt+1)*128]
                                 for cc in range(KC)],
                                [xnb[:, cc*512 + tt*128: cc*512 + (tt+1)*128]
                                 for cc in range(KC)],
                                reads=["l0f_xnb%d" % par], writes=["ps_aux"])
                with P.group("vector", reads=["ps_aux", "ident"],
                             writes=["l0f_stat"]) as g:
                    g.emit(lambda v, tt=tt: v.tensor_copy(out=stat[:, tt*4:tt*4+1],
                                                          in_=C.ps_aux[:, 128:129]))
                    g.emit(lambda v: v.tensor_tensor(
                        fs1[:, 0:128], C.ps_aux[:, 0:128], C.identf[:], ALU.mult))
                    g.emit(lambda v, o=stat[:, tt*4+1:tt*4+2]:
                           v.reduce_sum(o, fs1[:, 0:128], AX.X))
            sv = lambda c0: bass.AP(stat, c0, [[16, 128], [4, 4], [1, 1]])
            with P.group("vector", reads=["l0f_stat"], writes=["l0f_stat2"]) as g:
                g.emit(lambda v: v.tensor_scalar_mul(sv(0), sv(0), 1.0/D))
                g.emit(lambda v: v.tensor_scalar_mul(sv(1), sv(1), 1.0/D))
                g.emit(lambda v: v.tensor_tensor(sv(2), sv(0), sv(0), ALU.mult))
                g.emit(lambda v: v.tensor_tensor(sv(2), sv(1), sv(2), ALU.subtract))
                g.emit(lambda v: v.tensor_scalar_add(sv(2), sv(2), 1e-5))
            with P.group("scalar", reads=["l0f_stat2"], writes=["l0f_stat3"]) as g:
                g.emit(lambda s: s.activation(sv(3), sv(2), AF.Ln))
                g.emit(lambda s: s.activation(sv(3), sv(3), AF.Exp, scale=-0.5))
            with P.group("vector", reads=["l0f_stat3"], writes=["l0f_stat4"]) as g:
                g.emit(lambda v: v.tensor_tensor(sv(0), sv(0), sv(3), ALU.mult))
            for j in range(2):
                c0 = 3 if j == 0 else 0
                rrow = rrowA if j == 0 else rrowB
                with P.group("tensor", reads=["l0f_stat4", "ident"],
                             writes=["ps_aux"], chain=False) as g:
                    for tt in range(4):
                        g.emit(lambda p, tt=tt, c0=c0: p.transpose(
                            C.ps_aux[0:1, tt*128:(tt+1)*128],
                            stat[:, tt*4+c0:tt*4+c0+1], C.identf[:]))
                with P.group("vector", reads=["ps_aux"], writes=["l0f_rrow"]) as g:
                    g.emit(lambda v, rrow=rrow: v.tensor_copy(
                        out=rrow[0:1, :], in_=C.ps_aux[0:1, 0:512]))
                emit_matmul_acc(C, C.ps_aux[:, 0:512], [C.ones_row[0:1, 0:128]],
                                [rrow[0:1, :]],
                                reads=["l0f_rrow", "ones"], writes=["ps_aux"])
                with P.group("vector", reads=["ps_aux"], writes=["l0f_rbc"]) as g:
                    g.emit(lambda v, o=(rbc if j == 0 else nbbc):
                           v.tensor_copy(out=o[:], in_=C.ps_aux[:, 0:512]))
            for cc in range(KC):
                off = (1 if cc < 4 else 0)
                zsl = z[:, cc*ZROW + off + tc*512: cc*ZROW + off + tc*512 + 512]
                with P.group("vector", reads=["z", "l0f_rbc", "l0f_fparm", "l0f_zst"],
                             writes=["l0f_xnb%d" % par, "l0f_fs1"]) as g:
                    g.emit(lambda v, zsl=zsl: v.tensor_tensor(fs1[:], zsl, rbc[:], ALU.mult))
                    g.emit(lambda v: v.tensor_tensor(fs1[:], fs1[:], nbbc[:], ALU.subtract))
                    g.emit(lambda v, cc=cc, xnb=xnb: v.tensor_scalar(
                        xnb[:, cc*512:(cc+1)*512], fs1[:], lngc[:, cc:cc+1],
                        lnbc[:, cc:cc+1], ALU.mult, ALU.add))
                    if cc >= 4 and tc > 0:
                        g.emit(lambda v, s=zstF[:, cc-4:cc-3]: v.tensor_tensor(
                            fs1[:, 0:1], s, rbc[:, 0:1], ALU.mult))
                        g.emit(lambda v: v.tensor_tensor(
                            fs1[:, 0:1], fs1[:, 0:1], nbbc[:, 0:1], ALU.subtract))
                        g.emit(lambda v, cc=cc, xnb=xnb: v.tensor_scalar(
                            xnb[:, cc*512:cc*512+1], fs1[:, 0:1], lngc[:, cc:cc+1],
                            lnbc[:, cc:cc+1], ALU.mult, ALU.add))
            with P.group("vector", reads=["z"], writes=["l0f_zst"], chain=False) as g:
                for cc in range(4, KC):
                    g.emit(lambda v, o=zstF[:, cc-4:cc-3],
                           s=z[:, cc*ZROW + 1 + tc*512 + 511: cc*ZROW + 1 + tc*512 + 512]:
                           v.tensor_copy(out=o, in_=s))

        def ffn_w1(tc):
            par = tc % 2
            xnb = xnb2[par]
            for j in range(len(L0F_GRP)):
                G = L0F_GRP[j]
                w1s = w1sl[j % 2]
                with P.group("gpsimd", writes=["l0f_w1s%d" % (j % 2)]) as g:
                    base = l*L0F_W1C + 16*L0F_GOFF[j]
                    g.dma_(w1s[:, 0:16*G], w1_d[:, base: base + 16*G])
                for ms in range(G // 128):
                    gc = L0F_GOFF[j] // 128 + ms
                    bk = 4 if gc % 2 == 0 else 2
                    for ag in range(2):
                        pj = bk + ag
                        ps = PS[pj][:, 0:512]
                        lh = [w1s[:, cc*2*G + ag*G + ms*128: cc*2*G + ag*G + (ms+1)*128]
                              for cc in range(KC)]
                        rh = [xnb[:, cc*512:(cc+1)*512] for cc in range(KC)]
                        emit_matmul_acc(C, ps, lh, rh,
                                        reads=["l0f_w1s%d" % (j % 2), "l0f_xnb%d" % par],
                                        writes=["ps%d" % pj])
                    with P.group("scalar", reads=["ps%d" % (bk+1), "l0f_fparm", "l0f_gg2"],
                                 writes=["l0f_gg"]) as g:
                        g.emit(lambda s, b=b1gc[:, gc:gc+1], bk=bk:
                               s.activation(gsc[:], PS[bk+1][:, 0:512], AF.Gelu, bias=b))
                    with P.group("vector", reads=["ps%d" % bk, "l0f_gg", "l0f_fparm"],
                                 writes=["l0f_hp", "l0f_gg2"]) as g:
                        g.emit(lambda v, b=b1ac[:, gc:gc+1], bk=bk,
                               o=hprod[:, gc*512:(gc+1)*512]:
                               v.scalar_tensor_tensor(o, PS[bk][:, 0:512], b, gsc[:],
                                                      ALU.add, ALU.mult))

        def ffn_w2(tc):
            for th in range(2):
                for dp in range(4):
                    pj = dp % 4
                    ps = PS[pj]
                    with P.group("tensor", reads=["l0f_hp", "l0f_w2r"],
                                 writes=["ps%d" % pj], chain=False) as g:
                        for half in range(2):
                            dc = dp*2 + half
                            for gc in range(NGC):
                                g.emit(lambda p, ps=ps[:, half*256:(half+1)*256],
                                       lt=w2r[:, gc*1024 + dc*128: gc*1024 + (dc+1)*128],
                                       rh=hprod[:, gc*512 + th*256: gc*512 + (th+1)*256],
                                       st=(gc == 0), sp=(gc == NGC-1):
                                       p.matmul(ps, lt, rh, start=st, stop=sp))
                    with P.group("vector", reads=["ps%d" % pj, "z", "l0f_fparm"],
                                 writes=["z"], chain=False) as g:
                        for half in range(2):
                            dc = dp*2 + half
                            zsl = z[:, dc*ZROW + 1 + tc*512 + th*256:
                                    dc*ZROW + 1 + tc*512 + (th+1)*256]
                            g.emit(lambda v, ps=ps[:, half*256:(half+1)*256], o=zsl,
                                   b=b2c[:, dc:dc+1]:
                                   v.scalar_tensor_tensor(o, ps, b, o, ALU.add, ALU.add))

        ffn_pre(0)
        for tc in range(4):
            ffn_w1(tc)
            if tc < 3:
                ffn_pre(tc + 1)
            ffn_w2(tc)
        P.barrier()

    for l in range(L0F_DEPTH):
        if not L0F_SKIP_ATTN:
            attn_sublayer(l)
        P.barrier()
        if not L0F_SKIP_FFN:
            ffn_sublayer(l)
    P.barrier()
    for t in range(NT if not L0F_SKIP_EPI else 0):
        with P.group("sync", reads=["xspill"], writes=["l0f_xst"]) as g:
            g.dma_(xst[:], xsp[:, t*D:(t+1)*D])
        for hh in range(2):
            with P.group("tensor", reads=["z", "ident"], writes=["ps_aux"],
                         chain=False) as g:
                for q in range(4):
                    cc = hh*4 + q
                    g.emit(lambda p, cc=cc, q=q, t=t: p.transpose(
                        C.ps_aux[:, q*128:(q+1)*128],
                        z[:, cc*ZROW + 1 + t*128: cc*ZROW + 1 + (t+1)*128], C.identf[:]))
            with P.group("vector", reads=["ps_aux", "l0f_xst"], writes=["l0f_ost"]) as g:
                g.emit(lambda v, hh=hh: v.tensor_add(
                    ost[:, hh*512:(hh+1)*512], xst[:, hh*512:(hh+1)*512],
                    C.ps_aux[:, 0:512]))
        with P.group("sync", reads=["l0f_ost"], writes=["xspill"]) as g:
            g.dma_(xsp[:, t*D:(t+1)*D], ost[:])
    emit_reload_x(C, claims=["z"])


def prep_l0_ffn(inputs, h):
    if "l0f_wqkv" in h:
        return
    DEPTH = L0F_DEPTH
    q = np.arange(512, dtype=np.float64)[None, :] - np.arange(128, dtype=np.float64)[:, None]
    r = (np.arange(512)[None, :] >= 256)
    madd = np.zeros((128, 4*512), np.float32)
    for a in range(2):
        for w0 in range(2):
            m = (q >= a*128) & (q <= a*128 + 256)
            if w0:
                m = m & r
            madd[:, (a*2+w0)*512:(a*2+w0+1)*512] = \
                (m.astype(np.float32) - 1.0) * -1e30
    h["l0f_madd"] = madd
    wqkv = np.asarray(inputs["L0_ffn_Wqkv"], np.float32)
    h["l0f_wqkv"] = np.ascontiguousarray(np.concatenate(
        [sb_image_kc(np.ascontiguousarray(wqkv[l])) for l in range(DEPTH)], axis=1))
    wo = np.asarray(inputs["L0_ffn_Wo"], np.float32)
    h["l0f_wo"] = np.ascontiguousarray(np.concatenate(
        [sb_image_kc(np.ascontiguousarray(wo[l])) for l in range(DEPTH)], axis=1))
    W1 = np.asarray(inputs["L0_ffn_W1"], np.float32)
    b1 = np.asarray(inputs["L0_ffn_b1"], np.float32)
    W2 = np.asarray(inputs["L0_ffn_W2"], np.float32)
    w1_imgs, b1a_imgs, b1g_imgs, w2_imgs = [], [], [], []
    for l in range(DEPTH):
        W1a = np.zeros((D, GP), np.float32); W1a[:, :2730] = W1[l][:, :2730]
        W1g = np.zeros((D, GP), np.float32); W1g[:, :2730] = W1[l][:, 2730:]
        b1a = np.zeros(GP, np.float32); b1a[:2730] = b1[l][:2730]
        b1g = np.zeros(GP, np.float32); b1g[:2730] = b1[l][2730:]
        cols = []
        for j in range(len(L0F_GRP)):
            G, j0 = L0F_GRP[j], L0F_GOFF[j]
            grp = np.concatenate([W1a[:, j0:j0+G], W1g[:, j0:j0+G]], axis=1)
            cols.append(grp.reshape(KC, 128, 2*G).transpose(1, 0, 2).reshape(128, KC*2*G))
        w1_imgs.append(np.concatenate(cols, axis=1))
        b1a_imgs.append(b1a.reshape(NGC, 128).T)
        b1g_imgs.append(b1g.reshape(NGC, 128).T)
        W2p = np.zeros((GP, D), np.float32); W2p[:2730] = W2[l]
        w2_imgs.append(sb_image_kc(W2p, kc=NGC))
    h["l0f_w1"] = np.ascontiguousarray(np.concatenate(w1_imgs, axis=1))
    h["l0f_b1a"] = np.ascontiguousarray(np.concatenate(b1a_imgs, axis=1))
    h["l0f_b1g"] = np.ascontiguousarray(np.concatenate(b1g_imgs, axis=1))
    h["l0f_w2"] = np.ascontiguousarray(np.concatenate(w2_imgs, axis=1))
    h["l0f_b2"] = np.ascontiguousarray(
        np.asarray(inputs["L0_ffn_b2"], np.float32)[:DEPTH].reshape(DEPTH, KC, 128)
        .transpose(2, 0, 1).reshape(128, DEPTH*KC))
    h["l0f_lng"] = np.ascontiguousarray(
        np.asarray(inputs["L0_ffn_ln_g"], np.float32)[:DEPTH].reshape(DEPTH, KC, 128)
        .transpose(2, 0, 1).reshape(128, DEPTH*KC))
    h["l0f_lnb"] = np.ascontiguousarray(
        np.asarray(inputs["L0_ffn_ln_b"], np.float32)[:DEPTH].reshape(DEPTH, KC, 128)
        .transpose(2, 0, 1).reshape(128, DEPTH*KC))
    h["l0f_qs"] = np.ascontiguousarray(
        np.asarray(inputs["L0_ffn_q_scale"], np.float32)[:DEPTH])
    h["l0f_ks"] = np.ascontiguousarray(
        np.asarray(inputs["L0_ffn_k_scale"], np.float32)[:DEPTH])
    DIM_HEAD, WINDOW = 64, 256
    half = np.arange(0, DIM_HEAD, 2, dtype=np.float64)
    inv_freq = 1.0 / 10000.0 ** (half / DIM_HEAD)
    tt = np.arange(T, dtype=np.float64)
    freqs = np.concatenate([tt[:, None] * inv_freq[None, :]] * 2, axis=-1)
    base_scale = (half + 0.4 * DIM_HEAD) / (1.4 * DIM_HEAD)
    power = (tt - T // 2) / (WINDOW // 2)
    scale = np.concatenate([base_scale[None, :] ** power[:, None]] * 2, axis=-1)
    cos, sin = np.cos(freqs), np.sin(freqs)
    sgn = np.ones(64); sgn[:32] = -1.0
    def timg(tb):
        return tb.reshape(NT, 128, 64).transpose(1, 0, 2).reshape(128, NT*64)
    h["l0f_tabs"] = np.ascontiguousarray(np.concatenate(
        [timg((cos * scale).astype(np.float32)),
         timg((sin * scale * sgn[None, :]).astype(np.float32)),
         timg((cos / scale).astype(np.float32)),
         timg((sin / scale * sgn[None, :]).astype(np.float32))], axis=1))


BLOCK_BUILDERS = {"l1_ffn": build_l1_ffn, "l1_attn_a": build_l1_attn_a, "l0_attn_b": build_l0_attn_b, "l0_attn_a": build_l0_attn_a, "l1_attn_b": build_l1_attn_b, "l0_ffn": build_l0_ffn}


def sb_image_kc(wT, kc=8):
    F = wT.shape[1]
    return np.ascontiguousarray(wT.reshape(kc, 128, F).transpose(1, 0, 2).reshape(128, kc*F))


def sb_image_slabs(wT, mslab=128):
    K, M = wT.shape
    kc = K // 128
    nm = M // mslab
    a = wT.reshape(kc, 128, nm, mslab).transpose(1, 2, 0, 3)
    return np.ascontiguousarray(a.reshape(128, nm*kc*mslab))


def prep_l1_ffn(inputs, h):
    wgT = np.asarray(inputs["L1_ffn_w_gate"]).T
    wuT = np.asarray(inputs["L1_ffn_w_up"]).T
    wdT = np.asarray(inputs["L1_ffn_w_down"]).T
    h["l1ffn_wg_sb"] = sb_image_kc(np.ascontiguousarray(wgT))
    h["l1ffn_wu_sb"] = sb_image_slabs(np.ascontiguousarray(wuT), 128)
    h["l1ffn_wd_sb"] = sb_image_kc(np.ascontiguousarray(wdT), kc=24)


BLOCK_PREP = {"l1_ffn": prep_l1_ffn, "l1_attn_a": prep_l1_attn_a, "l0_attn_b": prep_l0_attn_b, "l0_attn_a": prep_l0_attn_a, "l1_attn_b": prep_l1_attn_b, "l0_ffn": prep_l0_ffn}

ALL_BLOCKS = ["l0_attn_a", "l0_ffn", "l0_attn_b", "l0_ffn", "l1_attn_a", "l1_ffn",
              "l1_attn_b", "l1_ffn"]


def build_program(blocks):
    C = Ctx()
    C.P = Prog(C.nc)
    _build_body(C, blocks)
    C.nc.all_engine_barrier()
    C.nc.reset()
    return C


def _build_body(C, blocks):
    P = C.P
    alloc_common(C)
    x_in = C.dram_in("x_in", (128, NT*D), F32)
    x_out = C.dram_out("x_out", (128, NT*D), F32)
    with P.group("sync", writes=["x"]) as g:
        for t in range(0, NT, 8):
            g.dma_(C.x_tm[:, t*D:(t+8)*D], x_in[:, t*D:(t+8)*D])
    for i, b in enumerate(blocks):
        C.blk_i = i
        P.barrier()
        emit_prologue(C)
        BLOCK_BUILDERS[b](C, "b%d" % i)
    P.barrier()
    for dn, (dap, dkeys) in getattr(C, "debug_tiles", {}).items():
        dd = C.dram_out("dbg_" + dn, list(dap.shape), dap.dtype)
        with P.group("sync", reads=list(dkeys)) as g:
            g.dma_(dd, dap)
    with P.group("sync", reads=["x"], writes=["xout"]) as g:
        for t in range(0, NT, 8):
            g.dma_(x_out[:, t*D:(t+8)*D], C.x_tm[:, t*D:(t+8)*D])
    sem, cnt = P.sem["sync.d"], P.count["sync.d"]
    P.ops["sync"].append(lambda en, s=sem, v=cnt: en.wait_ge(s, v))
    P.barrier()
    P.finalize()


def run_blocks(x, inputs, blocks):
    C = build_program(blocks)
    h = {}
    for b in set(blocks):
        BLOCK_PREP[b](inputs, h)
    in_maps = []
    for c in range(N_CORES):
        xc = np.ascontiguousarray(x[c].reshape(NT, 128, D).transpose(1, 0, 2).reshape(128, NT*D))
        m = {"x_in": xc}
        m.update(h)
        in_maps.append(m)
    res = run_bass_kernel_spmd(C.nc, in_maps, core_ids=list(range(N_CORES)))
    outs = []
    for r in res.results:
        xo = np.asarray(r["x_out"]).reshape(128, NT, D).transpose(1, 0, 2).reshape(T, D)
        outs.append(xo)
    return np.stack(outs)


def kernel(**inputs):
    x = np.asarray(inputs["L0_attn_a_x"])
    x = run_blocks(x, inputs, ALL_BLOCKS)
    return x


# version 11
# speedup vs baseline: 1.1505x; 1.1505x over previous
import numpy as np
import concourse.bass as bass
import concourse.mybir as mybir
from concourse.bass_utils import run_bass_kernel_spmd

F32 = mybir.dt.float32
BF16 = mybir.dt.bfloat16
I32 = mybir.dt.int32
AF = mybir.ActivationFunctionType
ALU = mybir.AluOpType
AX = mybir.AxisListType

T = 2048
D = 1024
NT = T // 128
KC = D // 128
N_CORES = 8

SB_BASE = 16512
OFF_X = SB_BASE
OFF_YFM = SB_BASE + 65536
OFF_COM = SB_BASE + 98304
OFF_ARENA = SB_BASE + 107520
ARENA_SZ = 229376 - OFF_ARENA

ENGINES = ("tensor", "vector", "scalar", "gpsimd", "sync")


class Prog:

    def __init__(self, nc):
        self.nc = nc
        self.ops = {e: [] for e in ENGINES}
        self.sem = {e: nc.alloc_semaphore(f"sem_{e}") for e in ENGINES}
        self.sem.update({e + ".d": nc.alloc_semaphore(f"semd_{e}")
                         for e in ("sync", "gpsimd", "scalar", "vector")})
        self.count = {s: 0 for s in self.sem}
        self.last_wait = {}
        self.writer = {}
        self.readers = {}

    def group(self, eng, reads=(), writes=(), chain=None):
        return _Group(self, eng, list(reads), list(writes), chain)

    def barrier(self):
        snap = dict(self.count)
        for e in ENGINES:
            for tgt, v in snap.items():
                if v == 0:
                    continue
                k = (e, tgt)
                if self.last_wait.get(k, 0) >= v:
                    continue
                self.last_wait[k] = v
                sem = self.sem[tgt]
                self.ops[e].append(lambda en, s=sem, v=v: en.wait_ge(s, v))

    def finalize(self):
        with self.nc.Block() as blk:
            for e in ENGINES:
                ops = self.ops[e]
                getattr(blk, e)(lambda en, ops=ops: [f(en) for f in ops])


class _Group:

    def __init__(self, P, eng, reads, writes, chain=None):
        self.P, self.eng, self.reads, self.writes = P, eng, reads, writes
        self.chain = chain if chain is not None else (eng != "tensor")
        self.items = []

    def emit(self, f):
        self.items.append(f)

    def dma_(self, out, in_, **kw):
        self.items.append(("dma", lambda en, o=out, i=in_, kw=kw: en.dma_start(out=o, in_=i, **kw)))

    def __enter__(self):
        return self

    def __exit__(self, et, ev, tb):
        if et is not None:
            return False
        P, eng = self.P, self.eng
        n_dma = sum(1 for it in self.items if isinstance(it, tuple))
        assert n_dma == 0 or n_dma == len(self.items), "don't mix dma and compute"
        stream = eng + ".d" if n_dma else eng
        deps = []
        for key in self.reads:
            w = P.writer.get(key)
            if w:
                deps.append(w)
        for key in self.writes:
            w = P.writer.get(key)
            if w:
                deps.append(w)
            for s2, cnt in P.readers.get(key, {}).items():
                deps.append((s2, cnt))
        waits = {}
        for tgt, cnt in deps:
            if eng == "tensor" and tgt == eng:
                continue
            if P.last_wait.get((eng, tgt), 0) >= cnt:
                continue
            waits[tgt] = max(waits.get(tgt, 0), cnt)
        for tgt, cnt in waits.items():
            P.last_wait[(eng, tgt)] = cnt
            sem = P.sem[tgt]
            P.ops[eng].append(lambda en, s=sem, v=cnt: en.wait_ge(s, v))
        sem = P.sem[stream]
        if n_dma:
            for _, f in self.items:
                P.ops[eng].append(lambda en, f=f, s=sem: f(en).then_inc(s, 16))
            P.count[stream] += 16 * n_dma
            tot = P.count[stream]
            P.ops[eng].append(lambda en, s=sem, v=tot: en.wait_ge(s, v))
        else:
            assert self.items, "empty group"
            if eng == "tensor":
                for f in self.items[:-1]:
                    P.ops[eng].append(f)
                last = self.items[-1]
                P.ops[eng].append(lambda en, f=last, s=sem: f(en).then_inc(s, 1))
            else:
                for j, f in enumerate(self.items[:-1]):
                    if j and self.chain:
                        P.ops[eng].append(lambda en: en.drain(fusable=True))
                    P.ops[eng].append(f)
                if len(self.items) > 1 and self.chain:
                    P.ops[eng].append(lambda en: en.drain(fusable=True))
                last = self.items[-1]
                P.ops[eng].append(lambda en, f=last, s=sem: f(en).then_inc(s, 1))
            P.count[stream] += 1
        cnt = P.count[stream]
        for key in self.writes:
            P.writer[key] = (stream, cnt)
            P.readers[key] = {}
        for key in self.reads:
            P.readers.setdefault(key, {})[stream] = cnt
        if PARANOID:
            P.barrier()
        return False


PARANOID = False
L0B_HLIM = 16
L0B_TLIM = 16


class Ctx:
    def __init__(self):
        self.nc = bass.Bass("TRN2", target_bir_lowering=False)
        self.dram = {}

    def dram_in(self, name, shape, dtype=F32):
        if name not in self.dram:
            self.dram[name] = self.nc.dram_tensor(name, list(shape), dtype, kind="ExternalInput").ap()
        return self.dram[name]

    def dram_out(self, name, shape, dtype=F32):
        self.dram[name] = self.nc.dram_tensor(name, list(shape), dtype, kind="ExternalOutput").ap()
        return self.dram[name]

    def dram_tmp(self, name, shape, dtype=F32):
        if name not in self.dram:
            self.dram[name] = self.nc.dram_tensor(name, list(shape), dtype, kind="Internal").ap()
        return self.dram[name]

    def sb_at(self, name, shape, dtype, off):
        nbytes = int(np.prod(shape[1:])) * mybir.dt.size(dtype)
        assert off + nbytes <= 229376, (name, off, nbytes)
        return self.nc.alloc_sbuf_tensor_at(name, list(shape), dtype, offset=off)

    def ps(self, name, shape, dtype=F32):
        return self.nc.alloc_psum_tensor(name, list(shape), dtype)


def dram2d(ap, row0, nrows, ncols, col0=0, row_len=None, p=128):
    if row_len is None:
        row_len = ap.tensor.shape[-1]
    ng = nrows // p
    return bass.AP(ap.tensor, row0 * row_len + col0,
                   [[row_len, p], [p * row_len, ng], [1, ncols]])


def alloc_common(C):
    C.x_tm = C.sb_at("x_tm", (128, NT*D), F32, OFF_X)
    C.y_fm = C.sb_at("y_fm", (128, KC*T), BF16, OFF_YFM)
    o = OFF_COM
    C.ln_stats = C.sb_at("ln_stats", (128, 5*NT), F32, o); o += 5*NT*4
    C.sb_sq = C.sb_at("sb_sq", (128, D), F32, o); o += D*4
    C.ident = C.sb_at("ident", (128, 128), BF16, o); o += 256
    C.identf = C.sb_at("identf", (128, 128), F32, o); o += 512
    C.ones_row = C.sb_at("ones_row", (1, 512), F32, o); o += 2048
    C.ones_cb = C.sb_at("ones_cb", (128, 1), BF16, o); o += 32
    C.com_off = o
    assert o <= OFF_ARENA
    C.consts = C.sb_at("consts", (128, 8), F32, o); o += 32
    C.com_off = o
    CONST_VALS = [1e-5, 1.0, 1e-8]
    for j, v in enumerate(CONST_VALS):
        C.nc.const_aps.register(F32, v, C.consts[:, j:j+1])
    C.ps_tr = C.ps("ps_tr", (128, 1024), BF16)
    C.psum_f32 = [C.ps("psf%d" % i, (128, 512), F32) for i in range(6)]
    C.ps_aux = C.ps("ps_aux", (128, 512), F32)
    P = C.P
    with P.group("gpsimd", writes=["consts"]) as g:
        for j, v in enumerate(CONST_VALS):
            g.emit(lambda g_, j=j, v=v: g_.memset(C.consts[:, j:j+1], v))
    with P.group("gpsimd", writes=["ident", "ones"]) as g:
        g.emit(lambda g_: g_.iota(C.ident[:], [[1, 128]], base=0, channel_multiplier=-1,
                                  allow_small_or_imprecise_dtypes=True))
        g.emit(lambda g_: g_.tensor_single_scalar(C.ident[:], C.ident[:], 0.0, ALU.is_equal))
        g.emit(lambda g_: g_.iota(C.identf[:], [[1, 128]], base=0, channel_multiplier=-1,
                                  allow_small_or_imprecise_dtypes=True))
        g.emit(lambda g_: g_.tensor_single_scalar(C.identf[:], C.identf[:], 0.0, ALU.is_equal))
        g.emit(lambda g_: g_.memset(C.ones_row[:], 1.0))
        g.emit(lambda g_: g_.memset(C.ones_cb[:], 1.0))


def emit_layernorm(C, x_tm, y_tm, src_key="x", dst_key="y_tm", eps=1e-5, nt=NT,
                   x_row=D, y_row=D):
    P = C.P
    st = C.ln_stats
    sq = C.sb_sq
    s_sum = st[:, 0*NT:0*NT+nt]
    s_ss = st[:, 1*NT:1*NT+nt]
    s_var = st[:, 2*NT:2*NT+nt]
    s_rstd = st[:, 3*NT:3*NT+nt]
    s_nb = st[:, 4*NT:4*NT+nt]
    for t in range(nt):
        xt = x_tm[:, t*x_row:t*x_row+D]
        with P.group("vector", reads=[src_key], writes=["ln_sum"]) as g:
            g.emit(lambda v, xt=xt, o=st[:, 0*NT+t:0*NT+t+1]: v.reduce_sum(o, xt, AX.X))
        with P.group("scalar", reads=[src_key], writes=["ln_ss"]) as g:
            g.emit(lambda s, xt=xt, o=st[:, 1*NT+t:1*NT+t+1]:
                   s.activation(sq[:], xt, AF.Square, accum_out=o))
    with P.group("vector", reads=["ln_sum", "ln_ss"], writes=["ln_var", "ln_sum"]) as g:
        g.emit(lambda v: v.tensor_scalar_mul(s_sum, s_sum, 1.0/D))
        g.emit(lambda v: v.tensor_scalar_mul(s_ss, s_ss, 1.0/D))
        g.emit(lambda v: v.tensor_tensor(s_var, s_sum, s_sum, ALU.mult))
        g.emit(lambda v: v.tensor_tensor(s_var, s_ss, s_var, ALU.subtract))
    with P.group("scalar", reads=["ln_var", "consts"], writes=["ln_rstd"]) as g:
        g.emit(lambda s: s.activation(s_rstd, s_var, AF.Ln, bias=eps))
        g.emit(lambda s: s.activation(s_rstd, s_rstd, AF.Exp, scale=-0.5))
    with P.group("vector", reads=["ln_rstd", "ln_sum"], writes=["ln_nb"]) as g:
        g.emit(lambda v: v.scalar_tensor_tensor(s_nb, s_sum, -1.0, s_rstd, ALU.mult, ALU.mult))
    for t in range(nt):
        xt = x_tm[:, t*x_row:t*x_row+D]
        yt = y_tm[:, t*y_row:t*y_row+D]
        with P.group("scalar", reads=[src_key, "ln_rstd", "ln_nb"], writes=[dst_key]) as g:
            g.emit(lambda s, xt=xt, yt=yt, sc=st[:, 3*NT+t:3*NT+t+1], b=st[:, 4*NT+t:4*NT+t+1]:
                   s.activation(yt, xt, AF.Identity, bias=b, scale=sc))


def emit_tm_to_fm(C, src_tm, dst_fm, src_key, dst_key, nt=NT, kc=KC,
                  src_row=None, dst_row=None, dst_col0=0):
    P = C.P
    pst = C.ps_tr
    ident = C.ident
    if src_row is None:
        src_row = kc * 128
    if dst_row is None:
        dst_row = nt * 128
    for c in range(kc):
        for k0 in range(0, nt, 8):
            kn = min(8, nt - k0)
            with P.group("tensor", reads=[src_key], writes=["ps_tr"]) as g:
                for j in range(kn):
                    t = k0 + j
                    src = src_tm[:, t*src_row + c*128: t*src_row + (c+1)*128]
                    g.emit(lambda p, s=src, o=pst[:, j*128:(j+1)*128]:
                           p.transpose(o, s, ident[:]))
            dst = dst_fm[:, c*dst_row + dst_col0 + k0*128:
                         c*dst_row + dst_col0 + (k0+kn)*128]
            with P.group("vector", reads=["ps_tr"], writes=[dst_key]) as g:
                g.emit(lambda v, d=dst, s=pst[:, :kn*128]: v.tensor_copy(out=d, in_=s))


def emit_matmul_acc(C, psum, lhsT_slices, rhs_slices, reads, writes):
    P = C.P
    with P.group("tensor", reads=list(reads), writes=list(writes)) as g:
        n = len(lhsT_slices)
        for i, (lt, rh) in enumerate(zip(lhsT_slices, rhs_slices)):
            g.emit(lambda p, lt=lt, rh=rh, st=(i == 0), sp=(i == n-1):
                   p.matmul(psum, lt, rh, start=st, stop=sp))


def emit_spill_x(C):
    P = C.P
    xs = C.dram_tmp("xspill", (128, NT*D), F32)
    with P.group("sync", reads=["x"], writes=["xspill"]) as g:
        for t in range(0, NT, 8):
            g.dma_(xs[:, t*D:(t+8)*D], C.x_tm[:, t*D:(t+8)*D])


def emit_reload_x(C, claims=()):
    P = C.P
    xs = C.dram_tmp("xspill", (128, NT*D), F32)
    with P.group("sync", reads=["xspill"], writes=["x"] + list(claims)) as g:
        for t in range(0, NT, 8):
            g.dma_(C.x_tm[:, t*D:(t+8)*D], xs[:, t*D:(t+8)*D])


def emit_prologue(C):
    y_tm = C.sb_at("y_tm_%d" % C.blk_i, (128, NT*D), BF16, OFF_ARENA)
    emit_layernorm(C, C.x_tm, y_tm, "x", "y_tm")
    emit_tm_to_fm(C, y_tm, C.y_fm, "y_tm", "y_fm")


def build_l1_ffn(C, tag):
    P = C.P
    S = 1.0 / 127.0
    H = 3072
    MC = H // 128
    wg_d = C.dram_in("l1ffn_wg_sb", (128, KC*H), I32)
    wu_d = C.dram_in("l1ffn_wu_sb", (128, MC*1024), I32)
    wd_d = C.dram_in("l1ffn_wd_sb", (128, MC*1024), I32)
    i = C.blk_i
    wu = C.sb_at("l1f_wu%d" % i, (128, MC*1024), BF16, OFF_X)
    hbuf = C.sb_at("l1f_h%d" % i, (128, MC*256), BF16, OFF_X + 49152)
    sgbuf = C.sb_at("l1f_sg%d" % i, (128, 256), F32, OFF_X + 61440)
    wg = C.sb_at("l1f_wg%d" % i, (128, KC*H), BF16, OFF_ARENA)
    wd = C.sb_at("l1f_wd%d" % i, (128, MC*1024), BF16, OFF_ARENA + 49152)
    xstage = C.sb_at("l1f_xs%d" % i, (128, 512), F32, OFF_ARENA + 98304)
    xsp = C.dram_tmp("xspill", (128, NT*D), F32)
    emit_spill_x(C)
    NLC = 4
    for q in range(NLC):
        with P.group("gpsimd", writes=["wg%d" % q, "y_tm"]) as g:
            g.dma_(wg[:, q*2*H:(q+1)*2*H], wg_d[:, q*2*H:(q+1)*2*H])
        with P.group("gpsimd", writes=["wu%d" % q] + (["x"] if q == 0 else [])) as g:
            g.dma_(wu[:, q*6144:(q+1)*6144], wu_d[:, q*6144:(q+1)*6144])
    for q in range(NLC):
        with P.group("gpsimd", writes=["wd%d" % q]) as g:
            g.dma_(wd[:, q*6144:(q+1)*6144], wd_d[:, q*6144:(q+1)*6144])
    PS = C.psum_f32

    for tc in range(0, NT, 2):
        for m in range(MC):
            ps_g, ps_u = PS[4][:, 0:256], PS[5][:, 0:256]
            lh = [wg[:, c*H + m*128: c*H + (m+1)*128] for c in range(KC)]
            rh = [C.y_fm[:, c*T + tc*128: (c*T) + (tc+2)*128] for c in range(KC)]
            emit_matmul_acc(C, ps_g, lh, rh,
                            reads=["wg%d" % (c*H//6144) for c in range(KC)] + ["y_fm"],
                            writes=["ps4"])
            lh = [wu[:, m*1024 + c*128: m*1024 + (c+1)*128] for c in range(KC)]
            emit_matmul_acc(C, ps_u, lh, rh, reads=["wu%d" % (m*1024//6144), "y_fm"],
                            writes=["ps5"])
            first = (tc == 0 and m == 0)
            with P.group("scalar", reads=["ps4"],
                         writes=["sg"] + (["x"] if first else [])) as g:
                g.emit(lambda s, ip=ps_g: s.activation(sgbuf[:, :], ip, AF.Silu, scale=S))
            with P.group("vector", reads=["ps5", "sg"],
                         writes=["h"] + (["x"] if first else [])) as g:
                g.emit(lambda v, ip=ps_u, o=hbuf[:, m*256:(m+1)*256]:
                       v.scalar_tensor_tensor(o, ip, S, sgbuf[:, :], ALU.mult, ALU.mult))
        for m in range(MC):
            with P.group("tensor", reads=["h", "wd%d" % (m*1024//6144)],
                         writes=["ps0", "ps1", "ps2", "ps3"]) as g:
                for tt in range(2):
                    for dh in range(2):
                        g.emit(lambda p, ps=PS[tt*2+dh][:, 0:512],
                               lt=hbuf[:, m*256 + tt*128: m*256 + (tt+1)*128],
                               rh=wd[:, m*1024 + dh*512: m*1024 + (dh+1)*512],
                               st=(m == 0), sp=(m == MC-1):
                               p.matmul(ps, lt, rh, start=st, stop=sp))
        for tt in range(2):
            for dh in range(2):
                ps = PS[tt*2 + dh][:, 0:512]
                with P.group("sync", reads=["xspill", "l1f_xst"], writes=["l1f_xst"]) as g:
                    g.dma_(xstage[:], xsp[:, (tc+tt)*D + dh*512: (tc+tt)*D + (dh+1)*512])
                with P.group("vector", reads=["ps%d" % (tt*2+dh), "l1f_xst"],
                             writes=["l1f_xst"]) as g:
                    g.emit(lambda v, ip=ps:
                           v.scalar_tensor_tensor(xstage[:], ip, S, xstage[:],
                                                  ALU.mult, ALU.add))
                with P.group("sync", reads=["l1f_xst"], writes=["xspill"]) as g:
                    g.dma_(xsp[:, (tc+tt)*D + dh*512: (tc+tt)*D + (dh+1)*512], xstage[:])
    emit_reload_x(C, claims=["wu0", "wu1", "wu2", "wu3", "h", "sg"])


def build_l1_attn_a(C, tag):
    P = C.P
    H = 16
    i = C.blk_i
    A = OFF_ARENA
    k_tm = C.sb_at("l1a_k%d" % i, (128, NT*D), BF16, A)
    v_tm = C.sb_at("l1a_v%d" % i, (128, NT*D), BF16, A + 32768)
    wsl = C.sb_at("l1a_w%d" % i, (128, KC*D), BF16, A + 65536)
    bkbc = C.sb_at("l1a_bk%d" % i, (128, D), F32, A + 81920)
    bvbc = C.sb_at("l1a_bv%d" % i, (128, D), F32, A + 86016)
    bobc = C.sb_at("l1a_bo%d" % i, (128, D), F32, A + 90112)
    bqc = C.sb_at("l1a_bq%d" % i, (128, KC), F32, A + 94208)
    kv_sb = C.sb_at("l1a_kv%d" % i, (128, 8*65), BF16, A + 94272)
    onesc = C.sb_at("l1a_one%d" % i, (128, 1), BF16, A + 95360)
    s1 = C.sb_at("l1a_s1%d" % i, (128, 512), F32, A + 95424)
    s2 = C.sb_at("l1a_s2%d" % i, (128, 512), F32, A + 97472)
    rzc = C.sb_at("l1a_rz%d" % i, (128, 1), F32, A + 99520)
    brow = C.sb_at("l1a_br%d" % i, (1, 512), F32, A + 99584)
    wq_d = C.dram_in("l1aa_wq", (128, KC*D), F32)
    wk_d = C.dram_in("l1aa_wk", (128, KC*D), F32)
    wv_d = C.dram_in("l1aa_wv", (128, KC*D), F32)
    wo_d = C.dram_in("l1aa_wo", (128, KC*D), F32)
    bq_d = C.dram_in("l1aa_bq", (128, KC), F32)
    bk_d = C.dram_in("l1aa_bk", (1, D), F32)
    bv_d = C.dram_in("l1aa_bv", (1, D), F32)
    bo_d = C.dram_in("l1aa_bo", (1, D), F32)
    PS = C.psum_f32
    kvps = C.ps_aux

    with P.group("gpsimd", writes=["bqc"]) as g:
        g.dma_(bqc[:], bq_d)
    with P.group("gpsimd", writes=["onesc"]) as g:
        g.emit(lambda g_: g_.memset(onesc[:], 1.0))
    for j, (bc, bd) in enumerate([(bkbc, bk_d), (bvbc, bv_d), (bobc, bo_d)]):
        for hh in range(2):
            pj = 4 + (2*j + hh) % 2
            with P.group("gpsimd", writes=["brow"]) as g:
                g.dma_(brow[:], bd[0:1, hh*512:(hh+1)*512])
            emit_matmul_acc(C, PS[pj][:, 0:512],
                            [C.ones_row[0:1, 0:128]], [brow[0:1, :]],
                            reads=["brow", "ones"], writes=["ps%d" % pj])
            with P.group("vector", reads=["ps%d" % pj], writes=["biasbc"]) as g:
                g.emit(lambda v, ps=PS[pj][:, 0:512], o=bc[:, hh*512:(hh+1)*512]:
                       v.tensor_copy(out=o, in_=ps))

    def load_w(dram, extra_claims=()):
        with P.group("gpsimd", writes=["wsl"] + list(extra_claims)) as g:
            for c in range(0, KC, 4):
                g.dma_(wsl[:, c*D:(c+4)*D], dram[:, c*D:(c+4)*D])

    def fmap_evac(ps, pskey, dsl, bias_args, dst_keys):
        with P.group("vector", reads=[pskey, "biasbc", "bqc"], writes=["s1"]) as g:
            if bias_args[0] == "col":
                g.emit(lambda v, ps=ps, b=bias_args[1]:
                       v.tensor_scalar(s1[:, 0:ps.shape[1]], ps, b, None, ALU.add))
            else:
                g.emit(lambda v, ps=ps, b=bias_args[1]:
                       v.tensor_add(s1[:, 0:ps.shape[1]], ps, b))
            g.emit(lambda v, n=ps.shape[1]: v.tensor_scalar_min(s2[:, 0:n], s1[:, 0:n], 0.0))
        with P.group("scalar", reads=["s1"], writes=["s2e"]) as g:
            g.emit(lambda s, n=ps.shape[1]: s.activation(s2[:, 0:n], s2[:, 0:n], AF.Exp))
        with P.group("vector", reads=["s2e", "s1"], writes=list(dst_keys)) as g:
            g.emit(lambda v, o=dsl, n=ps.shape[1]: v.scalar_tensor_tensor(
                o, s1[:, 0:n], 0.0, s2[:, 0:n], ALU.max, ALU.add))

    def proj_tm(dst, dst_key, bias_bc, fmap, extra_claims=()):
        for t in range(NT):
            for mh in range(2):
                pj = (2*t + mh) % 2
                ps = PS[pj][:, 0:512]
                lh = [C.y_fm[:, c*T + t*128:(c*T) + (t+1)*128] for c in range(KC)]
                rh = [wsl[:, c*D + mh*512: c*D + (mh+1)*512] for c in range(KC)]
                emit_matmul_acc(C, ps, lh, rh, reads=["wsl", "y_fm"],
                                writes=["ps%d" % pj])
                dsl = dst[:, t*D + mh*512: t*D + (mh+1)*512]
                bsl = bias_bc[:, mh*512:(mh+1)*512]
                if not fmap:
                    with P.group("vector", reads=["ps%d" % pj, "biasbc"],
                                 writes=[dst_key] + list(extra_claims)) as g:
                        g.emit(lambda v, ps=ps, o=dsl, b=bsl: v.tensor_add(o, ps, b))
                else:
                    fmap_evac(ps, "ps%d" % pj, dsl, ("bc", bsl),
                              [dst_key] + list(extra_claims))

    load_w(wk_d, ["y_tm"])
    proj_tm(k_tm, "k_tm", bkbc, True)
    load_w(wv_d)
    proj_tm(v_tm, "v_tm", bvbc, False)
    for h in range(H):
        p0 = (h % 2) * 64
        tp = (0, p0)
        out_ap = kvps[p0:p0+64, 0:65]
        with P.group("tensor", reads=["k_tm", "v_tm", "onesc"], writes=["ps_aux"]) as g:
            for t in range(NT):
                g.emit(lambda p, lt=k_tm[:, t*D + h*64: t*D + (h+1)*64],
                       rh=v_tm[:, t*D + h*64: t*D + (h+1)*64], st=(t == 0),
                       sp=(t == NT-1), o=out_ap[:, 0:64], tp=tp:
                       p.matmul(o, lt, rh, start=st, stop=sp, tile_position=tp))
            for t in range(NT):
                g.emit(lambda p, lt=k_tm[:, t*D + h*64: t*D + (h+1)*64], st=(t == 0),
                       sp=(t == NT-1), o=out_ap[:, 64:65], tp=tp:
                       p.matmul(o, lt, onesc[:], start=st, stop=sp, tile_position=tp))
        with P.group("vector", reads=["ps_aux"], writes=["kv_sb"]) as g:
            g.emit(lambda v, o=kv_sb[p0:p0+64, (h//2)*65:(h//2+1)*65], s=out_ap:
                   v.tensor_copy(out=o, in_=s))
    q_fm = C.sb_at("l1a_q%d" % i, (128, KC*T), BF16, A)
    load_w(wq_d)
    for cc in range(KC):
        for tq in range(4):
            pj = (cc + tq) % 2
            ps = PS[pj][:, 0:512]
            lh = [wsl[:, c*D + cc*128: c*D + (cc+1)*128] for c in range(KC)]
            rh = [C.y_fm[:, c*T + tq*512:(c*T) + (tq+1)*512] for c in range(KC)]
            emit_matmul_acc(C, ps, lh, rh, reads=["wsl", "y_fm"],
                            writes=["ps%d" % pj])
            dsl = q_fm[:, cc*T + tq*512: cc*T + (tq+1)*512]
            fmap_evac(ps, "ps%d" % pj, dsl, ("col", bqc[:, cc:cc+1]),
                      ["q_fm", "k_tm"])
    out_tm = C.sb_at("l1a_o%d" % i, (128, NT*D), BF16, A + 32768)
    for t in range(NT):
        for h in range(H):
            p0 = (h % 2) * 64
            pj = h % 4
            ps = PS[pj][:, 0:65]
            with P.group("tensor", reads=["q_fm", "kv_sb"], writes=["ps%d" % pj]) as g:
                g.emit(lambda p, ps=ps,
                       lt=q_fm[p0:p0+64, (h//2)*T + t*128:(h//2)*T + (t+1)*128],
                       rh=kv_sb[p0:p0+64, (h//2)*65:(h//2+1)*65]:
                       p.matmul(ps, lt, rh))
            with P.group("vector", reads=["ps%d" % pj], writes=["out_tm", "v_tm"]) as g:
                g.emit(lambda v, ps=ps: v.tensor_scalar_max(rzc[:], ps[:, 64:65], 1e-6))
                g.emit(lambda v: v.reciprocal(rzc[:], rzc[:]))
                g.emit(lambda v, ps=ps, o=out_tm[:, t*D + h*64: t*D + (h+1)*64]:
                       v.tensor_scalar(o, ps[:, 0:64], rzc[:], None, ALU.mult))
    out_fm = C.sb_at("l1a_of%d" % i, (128, KC*T), BF16, A)
    emit_tm_to_fm(C, out_tm, out_fm, "out_tm", "out_fm")
    load_w(wo_d)
    for t in range(NT):
        for mh in range(2):
            pj = (2*t + mh) % 2
            ps = PS[pj][:, 0:512]
            lh = [out_fm[:, c*T + t*128:(c*T) + (t+1)*128] for c in range(KC)]
            rh = [wsl[:, c*D + mh*512: c*D + (mh+1)*512] for c in range(KC)]
            emit_matmul_acc(C, ps, lh, rh, reads=["wsl", "out_fm"],
                            writes=["ps%d" % pj])
            xs = C.x_tm[:, t*D + mh*512: t*D + (mh+1)*512]
            with P.group("vector", reads=["ps%d" % pj, "biasbc", "x"],
                         writes=["x"]) as g:
                g.emit(lambda v, ps=ps, b=bobc[:, mh*512:(mh+1)*512]:
                       v.tensor_add(s1[:], ps, b))
                g.emit(lambda v, o=xs: v.tensor_add(o, o, s1[:]))


def prep_l1_attn_a(inputs, h):
    for nm, key in [("Wq", "l1aa_wq"), ("Wk", "l1aa_wk"), ("Wv", "l1aa_wv"),
                    ("Wo", "l1aa_wo")]:
        w = np.asarray(inputs["L1_attn_a_" + nm], np.float32)
        h[key] = sb_image_kc(np.ascontiguousarray(w))
    h["l1aa_bq"] = np.ascontiguousarray(
        np.asarray(inputs["L1_attn_a_bq"], np.float32).reshape(KC, 128).T)
    h["l1aa_bk"] = np.asarray(inputs["L1_attn_a_bk"], np.float32).reshape(1, D)
    h["l1aa_bv"] = np.asarray(inputs["L1_attn_a_bv"], np.float32).reshape(1, D)
    h["l1aa_bo"] = np.asarray(inputs["L1_attn_a_bo"], np.float32).reshape(1, D)


def build_l0_attn_b(C, tag):
    P = C.P
    H = 16
    i = C.blk_i
    A = OFF_ARENA
    KROW = T + 256
    q_fm = C.sb_at("l0b_q%d" % i, (128, KC*T), BF16, OFF_X)
    out_fm = C.sb_at("l0b_of%d" % i, (128, KC*T), BF16, OFF_YFM)
    k_fm = C.sb_at("l0b_k%d" % i, (128, KC*KROW), BF16, A)
    v_tm = C.sb_at("l0b_v%d" % i, (128, 18*D), BF16, A + 36864)
    wsl = C.sb_at("l0b_w%d" % i, (128, KC*D), BF16, A + 73728)
    madd = C.sb_at("l0b_m%d" % i, (128, 384), F32, A + 90112)
    s1 = C.sb_at("l0b_s1%d" % i, (128, 384), F32, A + 91648)
    attn_sb = C.sb_at("l0b_a%d" % i, (128, 384), BF16, A + 93184)
    attnT_sb = C.sb_at("l0b_at%d" % i, (128, 384), BF16, A + 93952)
    bobc = C.sb_at("l0b_bo%d" % i, (128, D), F32, A + 94720)
    mcol = C.sb_at("l0b_mc%d" % i, (128, 4), F32, A + 98816)
    brow = C.sb_at("l0b_br%d" % i, (1, 512), F32, A + 98848)
    wq_d = C.dram_in("l0ab_wq", (128, KC*D), F32)
    wk_d = C.dram_in("l0ab_wk", (128, KC*D), F32)
    wv_d = C.dram_in("l0ab_wv", (128, KC*D), F32)
    wo_d = C.dram_in("l0ab_wo", (128, KC*D), F32)
    bo_d = C.dram_in("l0ab_bo", (1, D), F32)
    PS = C.psum_f32

    emit_spill_x(C)

    def load_w(dram, extra_claims=()):
        with P.group("gpsimd", writes=["wsl"] + list(extra_claims)) as g:
            for c in range(0, KC, 4):
                g.dma_(wsl[:, c*D:(c+4)*D], dram[:, c*D:(c+4)*D])

    for hh in range(2):
        with P.group("gpsimd", writes=["brow"]) as g:
            g.dma_(brow[:], bo_d[0:1, hh*512:(hh+1)*512])
        pj = 4 + hh
        emit_matmul_acc(C, PS[pj][:, 0:512], [C.ones_row[0:1, 0:128]], [brow[0:1, :]],
                        reads=["brow", "ones"], writes=["ps%d" % pj])
        with P.group("vector", reads=["ps%d" % pj], writes=["biasbc"]) as g:
            g.emit(lambda v, ps=PS[pj][:, 0:512], o=bobc[:, hh*512:(hh+1)*512]:
                   v.tensor_copy(out=o, in_=ps))
    with P.group("gpsimd", writes=["madd", "s1g"]) as g:
        g.emit(lambda g_: g_.iota(madd[:], [[1, 384]], base=0, channel_multiplier=-1,
                                  allow_small_or_imprecise_dtypes=True))
        g.emit(lambda g_: g_.tensor_single_scalar(s1[:], madd[:], 96.0, ALU.is_ge))
        g.emit(lambda g_: g_.tensor_single_scalar(madd[:], madd[:], 159.0, ALU.is_le))
        g.emit(lambda g_: g_.tensor_mul(madd[:], madd[:], s1[:]))
        g.emit(lambda g_: g_.tensor_scalar_add(madd[:], madd[:], -1.0))
        g.emit(lambda g_: g_.tensor_scalar_mul(madd[:], madd[:], 1e30))

    load_w(wq_d)
    for cc in range(KC):
        for tq in range(4):
            pj = (cc + tq) % 2
            ps = PS[pj][:, 0:512]
            lh = [wsl[:, c*D + cc*128: c*D + (cc+1)*128] for c in range(KC)]
            rh = [C.y_fm[:, c*T + tq*512:(c*T) + (tq+1)*512] for c in range(KC)]
            emit_matmul_acc(C, ps, lh, rh, reads=["wsl", "y_fm"], writes=["ps%d" % pj])
            with P.group("vector", reads=["ps%d" % pj], writes=["q_fm", "x"]) as g:
                g.emit(lambda v, ps=ps, o=q_fm[:, cc*T + tq*512: cc*T + (tq+1)*512]:
                       v.tensor_copy(out=o, in_=ps))
    load_w(wk_d, ["y_tm"])
    with P.group("gpsimd", writes=["k_fm"]) as g:
        for c in range(KC):
            g.emit(lambda g_, c=c: g_.memset(k_fm[:, c*KROW: c*KROW + 128], 0.0))
            g.emit(lambda g_, c=c: g_.memset(k_fm[:, c*KROW + 128 + T: (c+1)*KROW], 0.0))
    for cc in range(KC):
        for tq in range(4):
            pj = (cc + tq) % 2
            ps = PS[pj][:, 0:512]
            lh = [wsl[:, c*D + cc*128: c*D + (cc+1)*128] for c in range(KC)]
            rh = [C.y_fm[:, c*T + tq*512:(c*T) + (tq+1)*512] for c in range(KC)]
            emit_matmul_acc(C, ps, lh, rh, reads=["wsl", "y_fm"], writes=["ps%d" % pj])
            with P.group("vector", reads=["ps%d" % pj], writes=["k_fm"]) as g:
                g.emit(lambda v, ps=ps,
                       o=k_fm[:, cc*KROW + 128 + tq*512: cc*KROW + 128 + (tq+1)*512]:
                       v.tensor_copy(out=o, in_=ps))
    load_w(wv_d)
    with P.group("gpsimd", writes=["v_tm"]) as g:
        g.emit(lambda g_: g_.memset(v_tm[:, 0:D], 0.0))
        g.emit(lambda g_: g_.memset(v_tm[:, 17*D:18*D], 0.0))
    for t in range(NT):
        for mh in range(2):
            pj = (2*t + mh) % 2
            ps = PS[pj][:, 0:512]
            lh = [C.y_fm[:, c*T + t*128:(c*T) + (t+1)*128] for c in range(KC)]
            rh = [wsl[:, c*D + mh*512: c*D + (mh+1)*512] for c in range(KC)]
            emit_matmul_acc(C, ps, lh, rh, reads=["wsl", "y_fm"], writes=["ps%d" % pj])
            with P.group("vector", reads=["ps%d" % pj], writes=["v_tm"]) as g:
                g.emit(lambda v, ps=ps,
                       o=v_tm[:, (t+1)*D + mh*512: (t+1)*D + (mh+1)*512]:
                       v.tensor_copy(out=o, in_=ps))
    for h in range(L0B_HLIM):
        p0 = (h % 2) * 64
        cc = h // 2
        for t in range(L0B_TLIM):
            pj = (h + t) % 2
            ps = PS[pj][:, 0:384]
            with P.group("tensor", reads=["q_fm", "k_fm"], writes=["ps%d" % pj]) as g:
                g.emit(lambda p, ps=ps,
                       lt=q_fm[p0:p0+64, cc*T + t*128: cc*T + (t+1)*128],
                       rh=k_fm[p0:p0+64, cc*KROW + t*128: cc*KROW + t*128 + 384]:
                       p.matmul(ps, lt, rh))
            with P.group("vector", reads=["ps%d" % pj, "madd"], writes=["s1g"]) as g:
                g.emit(lambda v, ps=ps: v.scalar_tensor_tensor(
                    s1[:], ps, 0.125, madd[:], ALU.mult, ALU.add))
                g.emit(lambda v: v.reduce_max(mcol[:, 0:1], s1[:], AX.X))
                g.emit(lambda v: v.tensor_scalar_mul(mcol[:, 1:2], mcol[:, 0:1], -1.0))
            with P.group("scalar", reads=["s1g"], writes=["s1e"]) as g:
                g.emit(lambda s: s.activation(s1[:], s1[:], AF.Exp,
                                              bias=mcol[:, 1:2], accum_out=mcol[:, 2:3]))
            with P.group("vector", reads=["s1e"], writes=["attn_sb"]) as g:
                g.emit(lambda v: v.reciprocal(mcol[:, 3:4], mcol[:, 2:3]))
                g.emit(lambda v: v.tensor_scalar(attn_sb[:], s1[:], mcol[:, 3:4],
                                                 None, ALU.mult))
            with P.group("tensor", reads=["attn_sb"], writes=["ps_tr"]) as g:
                for jj in range(3):
                    g.emit(lambda p, jj=jj: p.transpose(
                        C.ps_tr[:, jj*128:(jj+1)*128],
                        attn_sb[:, jj*128:(jj+1)*128], C.ident[:]))
            with P.group("vector", reads=["ps_tr"], writes=["attnT"]) as g:
                g.emit(lambda v: v.tensor_copy(out=attnT_sb[:], in_=C.ps_tr[:, 0:384]))
            po = C.ps_aux[p0:p0+64, 0:128]
            with P.group("tensor", reads=["attnT", "v_tm"], writes=["ps_aux"]) as g:
                for jj in range(3):
                    g.emit(lambda p, po=po, tp=(0, p0),
                           lt=v_tm[:, (t+jj)*D + h*64: (t+jj)*D + (h+1)*64],
                           rh=attnT_sb[:, jj*128:(jj+1)*128],
                           st=(jj == 0), sp=(jj == 2):
                           p.matmul(po, lt, rh, start=st, stop=sp, tile_position=tp))
            with P.group("vector", reads=["ps_aux"], writes=["out_fm", "y_fm"]) as g:
                g.emit(lambda v, po=po,
                       o=out_fm[p0:p0+64, cc*T + t*128: cc*T + (t+1)*128]:
                       v.tensor_copy(out=o, in_=po))
            if L0B_HLIM == 1:
                with P.group("vector", reads=["ps_aux"], writes=["sb_sq"]) as g:
                    g.emit(lambda v, po=po: v.tensor_copy(out=C.sb_sq[p0:p0+64, 0:128], in_=po))
    emit_reload_x(C, claims=["q_fm"])
    load_w(wo_d)
    for t in range(NT):
        for mh in range(2):
            pj = (2*t + mh) % 2
            ps = PS[pj][:, 0:512]
            lh = [out_fm[:, c*T + t*128:(c*T) + (t+1)*128] for c in range(KC)]
            rh = [wsl[:, c*D + mh*512: c*D + (mh+1)*512] for c in range(KC)]
            emit_matmul_acc(C, ps, lh, rh, reads=["wsl", "out_fm"], writes=["ps%d" % pj])
            xs = C.x_tm[:, t*D + mh*512: t*D + (mh+1)*512]
            with P.group("vector", reads=["ps%d" % pj, "biasbc", "x"], writes=["x"]) as g:
                g.emit(lambda v, ps=ps, b=bobc[:, mh*512:(mh+1)*512]:
                       v.tensor_add(s1[:, 0:256] if False else C.sb_sq[:, 0:512], ps, b))
                g.emit(lambda v, o=xs: v.tensor_add(o, o, C.sb_sq[:, 0:512]))


def prep_l0_attn_b(inputs, h):
    for nm, key in [("Wq", "l0ab_wq"), ("Wk", "l0ab_wk"), ("Wv", "l0ab_wv"),
                    ("Wo", "l0ab_wo")]:
        w = np.asarray(inputs["L0_attn_b_" + nm], np.float32).T
        h[key] = sb_image_kc(np.ascontiguousarray(w))
    h["l0ab_bo"] = np.asarray(inputs["L0_attn_b_bo"], np.float32).reshape(1, D)


def build_l0_attn_a(C, tag):
    P = C.P
    M = 256
    i = C.blk_i
    A = OFF_ARENA
    CSC = 1.0 / (16.0 * float(np.sqrt(T)))
    k_tm = C.sb_at("l0a_k%d" % i, (128, NT*M), BF16, A)
    q_fm = C.sb_at("l0a_q%d" % i, (128, 2*T), BF16, A + 8192)
    v_fm = C.sb_at("l0a_v%d" % i, (128, 2*T), BF16, A + 16384)
    r_fm = C.sb_at("l0a_r%d" % i, (128, 2*T), BF16, A + 24576)
    cumbc = C.sb_at("l0a_cb%d" % i, (128, T), F32, A + 32768)
    cumT = C.sb_at("l0a_ct%d" % i, (128, NT), F32, A + 40960)
    masks = C.sb_at("l0a_mk%d" % i, (128, 4*512), F32, A + 41024)
    s1 = C.sb_at("l0a_s1%d" % i, (128, 512), F32, A + 49216)
    s2 = C.sb_at("l0a_s2%d" % i, (128, 512), F32, A + 51264)
    w16 = C.sb_at("l0a_w16%d" % i, (128, 512), BF16, A + 53312)
    wk = C.sb_at("l0a_wk%d" % i, (128, KC*M), BF16, A + 54336)
    wq = C.sb_at("l0a_wq%d" % i, (128, KC*M), BF16, A + 58432)
    wv = C.sb_at("l0a_wv%d" % i, (128, KC*M), BF16, A + 62528)
    wo = C.sb_at("l0a_wo%d" % i, (128, 2*D), BF16, A + 66624)
    wg = C.sb_at("l0a_wg%d" % i, (128, KC), BF16, A + 70720)
    grow = C.sb_at("l0a_g%d" % i, (1, T), F32, A + 70784)
    onerow = C.sb_at("l0a_or%d" % i, (1, T), F32, A + 78976)
    bgc = C.sb_at("l0a_bg%d" % i, (1, 1), F32, A + 87168)
    wk_d = C.dram_in("l0aa_wkT", (128, KC*M), F32)
    wq_d = C.dram_in("l0aa_wqT", (128, KC*M), F32)
    wv_d = C.dram_in("l0aa_wvT", (128, KC*M), F32)
    wo_d = C.dram_in("l0aa_woT", (128, 2*D), F32)
    wg_d = C.dram_in("l0aa_wgT", (128, KC), F32)
    bg_d = C.dram_in("l0aa_bg", (1, 1), F32)
    PS = C.psum_f32

    with P.group("gpsimd", writes=["l0a_w", "y_tm"]) as g:
        g.dma_(wk[:], wk_d)
        g.dma_(wq[:], wq_d)
        g.dma_(wv[:], wv_d)
        g.dma_(wo[:], wo_d)
        g.dma_(wg[:], wg_d)
        g.dma_(bgc[:], bg_d)
    with P.group("gpsimd", writes=["l0a_ones", "l0a_masks", "l0a_s1"]) as g:
        g.emit(lambda g_: g_.memset(onerow[:], 1.0))
        g.emit(lambda g_: g_.iota(s1[:], [[1, 512]], base=0, channel_multiplier=-1,
                                  allow_small_or_imprecise_dtypes=True))
        for r in range(4):
            g.emit(lambda g_, r=r: g_.tensor_single_scalar(
                masks[:, r*512:(r+1)*512], s1[:], float(r*128), ALU.is_ge))
        g.emit(lambda g_: g_.tensor_scalar_mul(masks[:], masks[:], CSC))
    for mc in range(2):
        for tq in range(4):
            for dst, w, key in [(q_fm, wq, "q_fm"), (v_fm, wv, "v_fm")]:
                pj = (mc*4 + tq) % 2
                ps = PS[pj][:, 0:512]
                lh = [w[:, c*M + mc*128: c*M + (mc+1)*128] for c in range(KC)]
                rh = [C.y_fm[:, c*T + tq*512:(c*T) + (tq+1)*512] for c in range(KC)]
                emit_matmul_acc(C, ps, lh, rh, reads=["l0a_w", "y_fm"],
                                writes=["ps%d" % pj])
                with P.group("vector", reads=["ps%d" % pj], writes=[key]) as g:
                    g.emit(lambda v, ps=ps, o=dst[:, mc*T + tq*512: mc*T + (tq+1)*512]:
                           v.tensor_copy(out=o, in_=ps))
    for t in range(NT):
        pj = t % 2
        ps = PS[pj][:, 0:M]
        lh = [C.y_fm[:, c*T + t*128:(c*T) + (t+1)*128] for c in range(KC)]
        rh = [wk[:, c*M:(c+1)*M] for c in range(KC)]
        emit_matmul_acc(C, ps, lh, rh, reads=["l0a_w", "y_fm"], writes=["ps%d" % pj])
        with P.group("vector", reads=["ps%d" % pj], writes=["k_tm"]) as g:
            g.emit(lambda v, ps=ps, o=k_tm[:, t*M:(t+1)*M]: v.tensor_copy(out=o, in_=ps))
    for tq in range(4):
        ps = PS[4][:, 0:512]
        lh = [wg[:, c:c+1] for c in range(KC)]
        rh = [C.y_fm[:, c*T + tq*512:(c*T) + (tq+1)*512] for c in range(KC)]
        emit_matmul_acc(C, ps[0:1, :], lh, rh, reads=["l0a_w", "y_fm"], writes=["ps4"])
        with P.group("scalar", reads=["ps4", "l0a_w", "consts"], writes=["grow"]) as g:
            g.emit(lambda s, ps=ps, o=grow[0:1, tq*512:(tq+1)*512]:
                   s.activation(o, ps[0:1, :], AF.Sigmoid, bias=bgc[0:1, 0:1]))
            g.emit(lambda s, o=grow[0:1, tq*512:(tq+1)*512]:
                   s.activation(o, o, AF.Ln, bias=1e-8))
    with P.group("vector", reads=["grow", "l0a_ones"], writes=["grow"]) as g:
        g.emit(lambda v: v.tensor_tensor_scan(grow[0:1, :], onerow[0:1, :],
                                              grow[0:1, :], 0.0, ALU.mult, ALU.add))
    for tq in range(4):
        pj = 4 + tq % 2
        emit_matmul_acc(C, PS[pj][:, 0:512], [C.ones_row[0:1, 0:128]],
                        [grow[0:1, tq*512:(tq+1)*512]],
                        reads=["grow", "ones"], writes=["ps%d" % pj])
        with P.group("vector", reads=["ps%d" % pj], writes=["cumbc"]) as g:
            g.emit(lambda v, ps=PS[pj][:, 0:512], o=cumbc[:, tq*512:(tq+1)*512]:
                   v.tensor_copy(out=o, in_=ps))
    with P.group("tensor", reads=["grow", "ident"], writes=["ps_aux"]) as g:
        for t in range(NT):
            g.emit(lambda p, t=t: p.transpose(C.ps_aux[:, t:t+1],
                                              grow[0:1, t*128:(t+1)*128],
                                              C.identf[0:1, 0:1]))
    with P.group("vector", reads=["ps_aux"], writes=["cumT"]) as g:
        g.emit(lambda v: v.tensor_copy(out=cumT[:], in_=C.ps_aux[:, 0:NT]))
    for jc in range(4):
        first = True
        for it in range(0, (jc+1)*4):
            pj = it % 2
            ps = PS[pj][:, 0:512]
            lh = [v_fm[:, mc2*T + it*128: mc2*T + (it+1)*128] for mc2 in range(2)]
            rh = [q_fm[:, mc2*T + jc*512: mc2*T + (jc+1)*512] for mc2 in range(2)]
            emit_matmul_acc(C, ps, lh, rh, reads=["q_fm", "v_fm"], writes=["ps%d" % pj])
            with P.group("vector", reads=["cumbc", "cumT"], writes=["l0a_s1"]) as g:
                g.emit(lambda v, j0=jc*512, it=it: v.tensor_scalar(
                    s1[:], cumbc[:, j0:j0+512], cumT[:, it:it+1], 0.0,
                    ALU.subtract, ALU.min))
            with P.group("scalar", reads=["l0a_s1"], writes=["l0a_s2"]) as g:
                g.emit(lambda s: s.activation(s2[:], s1[:], AF.Exp))
            diag = (it // 4 == jc)
            with P.group("vector", reads=["l0a_s2", "ps%d" % pj, "l0a_masks"],
                         writes=["l0a_w16"] + (["l0a_s1"] if diag else [])) as g:
                if diag:
                    g.emit(lambda v, ps=ps: v.tensor_mul(s1[:], ps, s2[:]))
                    g.emit(lambda v, r=it % 4: v.tensor_tensor(
                        w16[:], s1[:], masks[:, r*512:(r+1)*512], ALU.mult))
                else:
                    g.emit(lambda v, ps=ps: v.scalar_tensor_tensor(
                        w16[:], ps, CSC, s2[:], ALU.mult, ALU.mult))
            for mc in range(2):
                with P.group("tensor", reads=["l0a_w16", "k_tm"],
                             writes=["ps%d" % (2+mc)]) as g:
                    g.emit(lambda p, lt=k_tm[:, it*M + mc*128: it*M + (mc+1)*128],
                           ps=PS[2+mc][:, 0:512],
                           st=first, sp=(it == (jc+1)*4 - 1):
                           p.matmul(ps, lt, w16[:], start=st, stop=sp))
            first = False
        for mc in range(2):
            with P.group("vector", reads=["ps%d" % (2+mc)], writes=["r_fm"]) as g:
                g.emit(lambda v, ps=PS[2+mc][:, 0:512],
                       o=r_fm[:, mc*T + jc*512: mc*T + (jc+1)*512]:
                       v.tensor_copy(out=o, in_=ps))
    for t in range(NT):
        for dh in range(2):
            pj = (2*t + dh) % 2
            ps = PS[pj][:, 0:512]
            lh = [r_fm[:, mc*T + t*128: mc*T + (t+1)*128] for mc in range(2)]
            rh = [wo[:, mc*D + dh*512: mc*D + (dh+1)*512] for mc in range(2)]
            emit_matmul_acc(C, ps, lh, rh, reads=["r_fm", "l0a_w"], writes=["ps%d" % pj])
            xs = C.x_tm[:, t*D + dh*512: t*D + (dh+1)*512]
            with P.group("vector", reads=["ps%d" % pj, "x"], writes=["x"]) as g:
                g.emit(lambda v, ps=ps, o=xs: v.tensor_add(o, o, ps))


def prep_l0_attn_a(inputs, h):
    h["l0aa_wkT"] = sb_image_kc(np.ascontiguousarray(
        np.asarray(inputs["L0_attn_a_Wk"], np.float32).T))
    h["l0aa_wqT"] = sb_image_kc(np.ascontiguousarray(
        np.asarray(inputs["L0_attn_a_Wq"], np.float32).T))
    h["l0aa_wvT"] = sb_image_kc(np.ascontiguousarray(
        np.asarray(inputs["L0_attn_a_Wv"], np.float32).T))
    h["l0aa_woT"] = sb_image_kc(np.ascontiguousarray(
        np.asarray(inputs["L0_attn_a_Wo"], np.float32).T), kc=2)
    h["l0aa_wgT"] = sb_image_kc(np.ascontiguousarray(
        np.asarray(inputs["L0_attn_a_Wg"], np.float32).T))
    h["l0aa_bg"] = np.asarray(inputs["L0_attn_a_bg"], np.float32).reshape(1, 1)


def build_l1_attn_b(C, tag):
    P = C.P
    DI = 2048
    MCI = DI // 128
    DS = 16
    DR = 64
    i = C.blk_i
    A = OFF_ARENA
    xc = C.sb_at("l1b_xc%d" % i, (128, MCI*T), BF16, OFF_X)
    winr = [C.sb_at("l1b_wi%d_%d" % (i, r), (128, KC*128), BF16, A + r*2048)
            for r in range(4)]
    stage = C.sb_at("l1b_st%d" % i, (128, 516), F32, A + 8192)
    sres_st = C.sb_at("l1b_sr%d" % i, (128, T), BF16, A + 10304)
    binc = C.sb_at("l1b_bi%d" % i, (128, 32), F32, A + 14400)
    cwc = C.sb_at("l1b_cw%d" % i, (128, 64), F32, A + 14528)
    cbc = C.sb_at("l1b_cb%d" % i, (128, 16), F32, A + 14784)
    alog = C.sb_at("l1b_al%d" % i, (128, 256), F32, A + 14848)
    asb = C.sb_at("l1b_as%d" % i, (128, 256), F32, A + 15872)
    dpc = C.sb_at("l1b_dp%d" % i, (128, 16), F32, A + 16896)
    bdtc = C.sb_at("l1b_bd%d" % i, (128, 16), F32, A + 16960)
    wxc = C.sb_at("l1b_wx%d" % i, (128, MCI*96), BF16, A + 17024)
    wdt = C.sb_at("l1b_wd%d" % i, (64, DI), BF16, A + 20096)
    xdbl = C.sb_at("l1b_xd%d" % i, (128, NT*96), F32, A + 24192)
    dbct = C.sb_at("l1b_db%d" % i, (96, NT*128), F32, A + 30336)
    delta = C.sb_at("l1b_dl%d" % i, (128, T), F32, A + 38528)
    dtx = C.sb_at("l1b_dx%d" % i, (128, T), F32, A + 46720)
    dA = C.sb_at("l1b_da%d" % i, (128, T), F32, A + 54912)
    ubuf = C.sb_at("l1b_u%d" % i, (128, T), F32, A + 63104)
    hbuf = C.sb_at("l1b_h%d" % i, (128, T), F32, A + 71296)
    hcbuf = C.sb_at("l1b_hc%d" % i, (128, T), F32, A + 79488)
    ybuf = C.sb_at("l1b_y%d" % i, (128, T), F32, A + 87680)
    wour = [C.sb_at("l1b_wo%d_%d" % (i, r), (128, D), BF16, A + 95872 + r*2048)
            for r in range(4)]
    drt_sb = C.sb_at("l1b_drt%d" % i, (64, T), BF16, A + 95872)
    bbc = C.sb_at("l1b_bb%d" % i, (128, T), F32, OFF_YFM)
    cbc2 = C.sb_at("l1b_cc%d" % i, (128, T), F32, OFF_YFM + 8192)
    bobc = C.sb_at("l1b_bo%d" % i, (128, D), F32, OFF_YFM + 16384)
    xstage = C.sb_at("l1b_xs%d" % i, (128, D), F32, OFF_YFM + 20480)
    ostage = C.sb_at("l1b_os%d" % i, (128, D), F32, OFF_YFM + 24576)
    brow = C.sb_at("l1b_br%d" % i, (1, 512), F32, OFF_YFM + 28672)
    win_d = C.dram_in("l1ab_win", (128, 32*KC*128), F32)
    binc_d = C.dram_in("l1ab_bin", (128, 32), F32)
    cw_d = C.dram_in("l1ab_cw", (128, 64), F32)
    cb_d = C.dram_in("l1ab_cbc", (128, 16), F32)
    wx_d = C.dram_in("l1ab_wx", (128, MCI*96), F32)
    wdt_d = C.dram_in("l1ab_wdt", (64, DI), F32)
    bdt_d = C.dram_in("l1ab_bdt", (128, 16), F32)
    alog_d = C.dram_in("l1ab_alog", (128, 256), F32)
    dp_d = C.dram_in("l1ab_dp", (128, 16), F32)
    wout_d = C.dram_in("l1ab_wout", (128, MCI*D), F32)
    bo_d = C.dram_in("l1ab_bo", (1, D), F32)
    bct_dr = C.dram_tmp("l1b_bct", (32, T), F32)
    sres_dr = C.dram_tmp("l1b_sres", (128, MCI*T), BF16)
    xsp = C.dram_tmp("xspill", (128, NT*D), F32)
    PS = C.psum_f32

    emit_spill_x(C)
    with P.group("gpsimd", writes=["l1b_small", "y_tm"]) as g:
        g.dma_(binc[:], binc_d)
        g.dma_(cwc[:], cw_d)
        g.dma_(cbc[:], cb_d)
        g.dma_(alog[:], alog_d)
        g.dma_(dpc[:], dp_d)
        g.dma_(bdtc[:], bdt_d)
        g.dma_(wxc[:], wx_d)
        g.dma_(wdt[:], wdt_d)
    with P.group("scalar", reads=["l1b_small"], writes=["l1b_asb"]) as g:
        g.emit(lambda s: s.activation(asb[:], alog[:], AF.Exp))
    with P.group("vector", reads=["l1b_asb"], writes=["l1b_asb"]) as g:
        g.emit(lambda v: v.tensor_scalar_mul(asb[:], asb[:], -1.0))
    def load_win(mc):
        with P.group("gpsimd", writes=["winr%d" % (mc % 4)] + (["y_tm"] if mc < 4 else [])) as g:
            g.dma_(winr[mc % 4][:], win_d[:, mc*1024:(mc+1)*1024])

    for mc in range(4):
        load_win(mc)
    for mc in range(32):
        isx = mc < MCI
        c = mc if isx else mc - MCI
        for tq in range(4):
            pj = (mc + tq) % 2
            ps = PS[pj][:, 0:512]
            wsl = winr[mc % 4]
            lh = [wsl[:, cc*128:(cc+1)*128] for cc in range(KC)]
            rh = [C.y_fm[:, cc*T + tq*512:(cc*T) + (tq+1)*512] for cc in range(KC)]
            emit_matmul_acc(C, ps, lh, rh, reads=["winr%d" % (mc % 4), "y_fm"],
                            writes=["ps%d" % pj])
            if isx:
                with P.group("vector", reads=["ps%d" % pj, "l1b_small", "l1b_stage"],
                             writes=(["l1b_stage", "y_tm"] if mc == 0 and tq == 0
                                     else ["l1b_stage"])) as g:
                    if tq == 0:
                        g.emit(lambda v: v.memset(stage[:, 0:3], 0.0))
                    else:
                        g.emit(lambda v: v.tensor_copy(out=stage[:, 0:3],
                                                       in_=stage[:, 512:515]))
                    g.emit(lambda v, ps=ps, b=binc[:, c:c+1]:
                           v.tensor_scalar(stage[:, 3:515], ps, b, None, ALU.add))
                    g.emit(lambda v, w=cwc[:, c*4+3:c*4+4]:
                           v.tensor_scalar(stage[:, 0:0] if False else ubuf[:, 0:512],
                                           stage[:, 3:515], w, None, ALU.mult))
                    for k in range(3):
                        g.emit(lambda v, k=k, w=cwc[:, c*4+2-k:c*4+3-k]:
                               v.scalar_tensor_tensor(ubuf[:, 0:512], stage[:, 2-k:514-k],
                                                      w, ubuf[:, 0:512], ALU.mult, ALU.add))
                with P.group("scalar", reads=["l1b_stage", "l1b_small"],
                             writes=(["xc", "x"] if mc == 0 and tq == 0 else ["xc"])) as g:
                    g.emit(lambda s, o=xc[:, c*T + tq*512: c*T + (tq+1)*512],
                           b=cbc[:, c:c+1]:
                           s.activation(o, ubuf[:, 0:512], AF.Silu, bias=b))
            else:
                with P.group("scalar", reads=["ps%d" % pj, "l1b_small"],
                             writes=(["l1b_srst", "y_tm"] if mc == 16 and tq == 0
                                     else ["l1b_srst"])) as g:
                    g.emit(lambda s, ps=ps, o=sres_st[:, tq*512:(tq+1)*512],
                           b=binc[:, 16+c:17+c]:
                           s.activation(o, ps, AF.Silu, bias=b))
        if not isx:
            with P.group("sync", reads=["l1b_srst"], writes=["l1b_sres_dr"]) as g:
                g.dma_(sres_dr[:, c*T:(c+1)*T], sres_st[:])
        if mc + 4 < 32:
            load_win(mc + 4)
    for t in range(NT):
        pj = t % 2
        ps = PS[pj][:, 0:96]
        lh = [xc[:, c*T + t*128: c*T + (t+1)*128] for c in range(MCI)]
        rh = [wxc[:, c*96:(c+1)*96] for c in range(MCI)]
        emit_matmul_acc(C, ps, lh, rh, reads=["xc", "l1b_small"], writes=["ps%d" % pj])
        with P.group("vector", reads=["ps%d" % pj],
                     writes=(["l1b_xdbl", "y_tm"] if t == 0 else ["l1b_xdbl"])) as g:
            g.emit(lambda v, ps=ps, o=xdbl[:, t*96:(t+1)*96]: v.tensor_copy(out=o, in_=ps))
    for t in range(NT):
        with P.group("tensor", reads=["l1b_xdbl", "ident"], writes=["ps_aux"]) as g:
            g.emit(lambda p, t=t: p.transpose(C.ps_aux[0:96, 0:128],
                                              xdbl[:, t*96:(t+1)*96], C.identf[:]))
        with P.group("vector", reads=["ps_aux"],
                     writes=["l1b_dbct", "l1b_drt"] + (["y_tm"] if t == 0 else [])) as g:
            g.emit(lambda v, t=t: v.tensor_copy(out=dbct[:, t*128:(t+1)*128],
                                                in_=C.ps_aux[0:96, 0:128]))
            g.emit(lambda v, t=t: v.tensor_copy(out=drt_sb[:, t*128:(t+1)*128],
                                                in_=C.ps_aux[0:64, 0:128]))
    with P.group("sync", reads=["l1b_dbct"], writes=["l1b_bct_dr"]) as g:
        g.dma_(bct_dr[:], dbct[64:96, :])
    for c in range(MCI):
        for tq in range(4):
            pj = (c + tq) % 2
            ps = PS[pj][:, 0:512]
            emit_matmul_acc(C, ps, [wdt[:, c*128:(c+1)*128]],
                            [drt_sb[:, tq*512:(tq+1)*512]],
                            reads=["l1b_drt", "l1b_small"], writes=["ps%d" % pj])
            with P.group("vector", reads=["ps%d" % pj, "l1b_small"], writes=["l1b_s1"]) as g:
                g.emit(lambda v, ps=ps, b=bdtc[:, c:c+1], o=delta[:, tq*512:(tq+1)*512]:
                       v.tensor_scalar(o, ps, b, None, ALU.add))
        with P.group("scalar", reads=["l1b_s1"], writes=["l1b_sp1"]) as g:
            g.emit(lambda s: s.activation(dA[:], delta[:], AF.Abs))
            g.emit(lambda s: s.activation(dA[:], dA[:], AF.Exp, scale=-1.0))
            g.emit(lambda s: s.activation(dA[:], dA[:], AF.Ln, bias=1.0))
        with P.group("vector", reads=["l1b_sp1", "xc"], writes=["l1b_delta"]) as g:
            g.emit(lambda v: v.scalar_tensor_tensor(delta[:], delta[:], 0.0, dA[:],
                                                    ALU.max, ALU.add))
            g.emit(lambda v, xs=xc[:, c*T:(c+1)*T]:
                   v.tensor_tensor(dtx[:], delta[:], xs, ALU.mult))
        for n in range(DS):
            with P.group("sync", writes=(["l1b_bbc", "y_fm"] if c == 0 and n == 0
                                           else ["l1b_bbc"])) as g:
                g.dma_(bbc[:], bass.AP(bct_dr.tensor, n*T, [[0, 128], [1, T]]))
            with P.group("scalar", writes=(["l1b_cbc2", "y_fm"] if c == 0 and n == 0
                                             else ["l1b_cbc2"])) as g:
                g.dma_(cbc2[:], bass.AP(bct_dr.tensor, (16+n)*T, [[0, 128], [1, T]]))
            with P.group("scalar", reads=["l1b_delta", "l1b_asb"],
                         writes=["l1b_dA", "l1b_sp1"]) as g:
                g.emit(lambda s, a=asb[:, c*16+n:c*16+n+1]:
                       s.activation(dA[:], delta[:], AF.Exp, scale=a))
            with P.group("vector", reads=["l1b_bbc", "l1b_delta"],
                         writes=["l1b_u", "l1b_stage"]) as g:
                g.emit(lambda v: v.tensor_tensor(ubuf[:], dtx[:], bbc[:], ALU.mult))
            with P.group("vector", reads=["l1b_u", "l1b_dA"], writes=["l1b_h"]) as g:
                g.emit(lambda v: v.tensor_tensor_scan(hbuf[:], dA[:], ubuf[:], 0.0,
                                                      ALU.mult, ALU.add))
            with P.group("gpsimd", reads=["l1b_h", "l1b_cbc2"], writes=["l1b_hc"]) as g:
                g.emit(lambda g_: g_.tensor_tensor(hcbuf[:], hbuf[:], cbc2[:], ALU.mult))
            with P.group("vector", reads=["l1b_hc"], writes=["l1b_y"]) as g:
                if n == 0:
                    g.emit(lambda v: v.tensor_copy(out=ybuf[:], in_=hcbuf[:]))
                else:
                    g.emit(lambda v: v.tensor_add(ybuf[:], ybuf[:], hcbuf[:]))
        with P.group("gpsimd", reads=["l1b_sres_dr"], writes=["l1b_srst"]) as g:
            g.dma_(sres_st[:], sres_dr[:, c*T:(c+1)*T])
        with P.group("vector", reads=["l1b_y", "xc", "l1b_small", "l1b_srst"],
                     writes=["xc", "l1b_y"]) as g:
            g.emit(lambda v, xs=xc[:, c*T:(c+1)*T], dcol=dpc[:, c:c+1]:
                   v.scalar_tensor_tensor(ybuf[:], xs, dcol, ybuf[:], ALU.mult, ALU.add))
            g.emit(lambda v, xs=xc[:, c*T:(c+1)*T]:
                   v.tensor_tensor(xs, ybuf[:], sres_st[:], ALU.mult))
    for hh in range(2):
        with P.group("gpsimd", writes=["l1b_brow", "y_fm"]) as g:
            g.dma_(brow[:], bo_d[0:1, hh*512:(hh+1)*512])
        pj = 4 + hh
        emit_matmul_acc(C, PS[pj][:, 0:512], [C.ones_row[0:1, 0:128]], [brow[0:1, :]],
                        reads=["l1b_brow", "ones"], writes=["ps%d" % pj])
        with P.group("vector", reads=["ps%d" % pj], writes=["l1b_bobc", "y_fm"]) as g:
            g.emit(lambda v, ps=PS[pj][:, 0:512], o=bobc[:, hh*512:(hh+1)*512]:
                   v.tensor_copy(out=o, in_=ps))

    for r in range(4):
        with P.group("gpsimd", writes=["wour%d" % r] + (["l1b_drt"] if r == 0 else [])) as g:
            g.dma_(wour[r][:], wout_d[:, r*D:(r+1)*D])
    for t in range(NT):
        with P.group("sync", reads=["xspill"],
                     writes=(["l1b_xst", "y_fm"] if t == 0 else ["l1b_xst"])) as g:
            g.dma_(xstage[:], xsp[:, t*D:(t+1)*D])
        for mc in range(MCI):
            with P.group("tensor", reads=["xc", "wour%d" % (mc % 4)],
                         writes=["ps0", "ps1"]) as g:
                for dh in range(2):
                    g.emit(lambda p, ps=PS[dh][:, 0:512],
                           lt=xc[:, mc*T + t*128: mc*T + (t+1)*128],
                           rh=wour[mc % 4][:, dh*512:(dh+1)*512],
                           st=(mc == 0), sp=(mc == MCI-1):
                           p.matmul(ps, lt, rh, start=st, stop=sp))
            nxt = mc + 4 if mc + 4 < MCI else (mc + 4 - MCI if t < NT-1 else -1)
            if nxt >= 0:
                with P.group("gpsimd", writes=["wour%d" % (nxt % 4)]) as g:
                    g.dma_(wour[nxt % 4][:], wout_d[:, nxt*D:(nxt+1)*D])
        for dh in range(2):
            with P.group("vector", reads=["ps%d" % dh, "l1b_xst", "l1b_bobc"],
                         writes=(["l1b_ost", "y_fm"] if t == 0 and dh == 0
                                 else ["l1b_ost"])) as g:
                g.emit(lambda v, ps=PS[dh][:, 0:512], o=ostage[:, dh*512:(dh+1)*512],
                       b=bobc[:, dh*512:(dh+1)*512]:
                       v.tensor_add(o, ps, b))
                g.emit(lambda v, o=ostage[:, dh*512:(dh+1)*512],
                       xs=xstage[:, dh*512:(dh+1)*512]:
                       v.tensor_add(o, o, xs))
        with P.group("sync", reads=["l1b_ost"], writes=["xspill"]) as g:
            g.dma_(xsp[:, t*D:(t+1)*D], ostage[:])
    emit_reload_x(C, claims=["xc"])


def prep_l1_attn_b(inputs, h):
    DI = 2048
    w_in = np.asarray(inputs["L1_attn_b_W_in"], np.float32)
    a = w_in.reshape(KC, 128, 32, 128).transpose(1, 2, 0, 3)
    h["l1ab_win"] = np.ascontiguousarray(a.reshape(128, 32*KC*128))
    h["l1ab_bin"] = np.ascontiguousarray(
        np.asarray(inputs["L1_attn_b_b_in"], np.float32).reshape(32, 128).T)
    h["l1ab_cw"] = np.ascontiguousarray(
        np.asarray(inputs["L1_attn_b_conv_w"], np.float32)[:, 0, :]
        .reshape(16, 128, 4).transpose(1, 0, 2).reshape(128, 64))
    h["l1ab_cbc"] = np.ascontiguousarray(
        np.asarray(inputs["L1_attn_b_conv_b"], np.float32).reshape(16, 128).T)
    h["l1ab_wx"] = sb_image_kc(np.ascontiguousarray(
        np.asarray(inputs["L1_attn_b_W_x"], np.float32)), kc=16)
    h["l1ab_wdt"] = np.ascontiguousarray(np.asarray(inputs["L1_attn_b_W_dt"], np.float32))
    h["l1ab_bdt"] = np.ascontiguousarray(
        np.asarray(inputs["L1_attn_b_b_dt"], np.float32).reshape(16, 128).T)
    h["l1ab_alog"] = np.ascontiguousarray(
        np.asarray(inputs["L1_attn_b_A_log"], np.float32).reshape(16, 128, 16)
        .transpose(1, 0, 2).reshape(128, 256))
    h["l1ab_dp"] = np.ascontiguousarray(
        np.asarray(inputs["L1_attn_b_D_par"], np.float32).reshape(16, 128).T)
    h["l1ab_wout"] = sb_image_kc(np.ascontiguousarray(
        np.asarray(inputs["L1_attn_b_W_out"], np.float32)), kc=16)
    h["l1ab_bo"] = np.asarray(inputs["L1_attn_b_b_out"], np.float32).reshape(1, D)


L0F_DEPTH = 6
L0F_GRP = [512, 512, 512, 512, 512, 256]
L0F_GOFF = [0, 512, 1024, 1536, 2048, 2560]
L0F_W1C = 16 * 2816
L0F_SKIP_ATTN = False
L0F_SKIP_FFN = False
L0F_WB_LIM = 8
L0F_SKIP_EPI = False
GP = 2816
NGC = GP // 128


def build_l0_ffn(C, tag):
    P = C.P
    i = C.blk_i
    A = OFF_ARENA
    ZROW = T + 1
    z = C.sb_at("l0f_z%d" % i, (128, KC*ZROW), F32, OFF_X)
    YR = OFF_X + 65568
    tabs = [C.sb_at("l0f_tab%d_%d" % (i, j), (128, NT*64), F32, YR + j*4096)
            for j in range(4)]
    qsbc = C.sb_at("l0f_qsb%d" % i, (128, 64), F32, YR + 16384)
    ksbc = C.sb_at("l0f_ksb%d" % i, (128, 64), F32, YR + 16640)
    madd = C.sb_at("l0f_mad%d" % i, (128, 4*512), F32, YR + 16896)
    rsc = C.sb_at("l0f_rsc%d" % i, (128, 1024), F32, YR + 25088)
    zstA = C.sb_at("l0f_zstA%d" % i, (128, 4), F32, YR + 29184)
    zstF = C.sb_at("l0f_zstF%d" % i, (128, 4), F32, YR + 29216)
    brow = C.sb_at("l0f_br%d" % i, (1, 512), F32, YR)
    rrowA = C.sb_at("l0f_rrA%d" % i, (1, 512), F32, YR + 2048)
    rrowB = C.sb_at("l0f_rrB%d" % i, (1, 512), F32, YR + 4096)
    rbc = C.sb_at("l0f_rbc%d" % i, (128, 512), F32, YR + 6144)
    nbbc = C.sb_at("l0f_nbc%d" % i, (128, 512), F32, YR + 8192)
    w1slB = C.sb_at("l0f_w1B%d" % i, (128, 16*512), BF16, YR + 10240)
    gsc = C.sb_at("l0f_gsc%d" % i, (128, 512), F32, YR + 26624)
    wqkv = C.sb_at("l0f_wqkv%d" % i, (128, KC*3072), BF16, A)
    wo = C.sb_at("l0f_wo%d" % i, (128, KC*1024), BF16, A + 49152)
    zb = C.sb_at("l0f_zb%d" % i, (128, KC*256), BF16, A + 65536)
    qkf = C.sb_at("l0f_qk%d" % i, (128, 1024), F32, A + 69632)
    qhat = C.sb_at("l0f_qh%d" % i, (128, 1024), F32, A + 73728)
    rstd = C.sb_at("l0f_rs%d" % i, (128, 16), F32, A + 77824)
    mcol = C.sb_at("l0f_mc%d" % i, (128, 16), F32, A + 77888)
    qrot = C.sb_at("l0f_qr%d" % i, (128, 2048), BF16, A + 77952)
    qpf = C.sb_at("l0f_qpf%d" % i, (128, 8*256), BF16, A + 82048)
    kpf2 = C.sb_at("l0f_kpf%d" % i, (128, 2*8*256), BF16, A + 86144)
    vtm = [C.sb_at("l0f_vtm%d_%d" % (i, j), (128, 2*1024), BF16, A + 94336 + j*4096)
           for j in range(2)]
    s1 = [C.sb_at("l0f_s1%d_%d" % (i, p), (128, 512), F32, A + 65536 + p*2048)
          for p in range(4)]
    attn_sb = [C.sb_at("l0f_at%d_%d" % (i, p), (128, 512), BF16, A + 77952 + p*1024)
               for p in range(2)]
    attnT_sb = [C.sb_at("l0f_aT%d_%d" % (i, p), (128, 512), BF16, A + 80000 + p*1024)
                for p in range(2)]
    aof = C.sb_at("l0f_aof%d" % i, (128, 8*256), BF16, A + 73728)
    w1slA = C.sb_at("l0f_w1A%d" % i, (128, 16*512), BF16, A)
    w1sl = [w1slA, w1slB]
    w2r = C.sb_at("l0f_w2%d" % i, (128, NGC*1024), BF16, A + 16384)
    hprod = C.sb_at("l0f_hp%d" % i, (128, NGC*512), BF16, A + 61440)
    xnb2 = [C.sb_at("l0f_xn%d_%d" % (i, p), (128, KC*512), BF16,
                    A + (83968 if p == 0 else 94336)) for p in range(2)]
    fs1 = C.sb_at("l0f_fs1%d" % i, (128, 512), F32, A + 92160)
    stat = C.sb_at("l0f_stc%d" % i, (128, 16), F32, A + 94208)
    xst = C.sb_at("l0f_xst%d" % i, (128, 1024), F32, A + 94336)
    ost = C.sb_at("l0f_ost%d" % i, (128, 1024), F32, A + 98432)
    b1ac = C.sb_at("l0f_b1a%d" % i, (128, NGC), F32, A + 102528)
    b1gc = C.sb_at("l0f_b1g%d" % i, (128, NGC), F32, A + 102656)
    b2c = C.sb_at("l0f_b2c%d" % i, (128, KC), F32, A + 102784)
    lngc = C.sb_at("l0f_lng%d" % i, (128, KC), F32, A + 102848)
    lnbc = C.sb_at("l0f_lnb%d" % i, (128, KC), F32, A + 102912)
    wqkv_d = C.dram_in("l0f_wqkv", (128, L0F_DEPTH*KC*3072), F32)
    wo_d = C.dram_in("l0f_wo", (128, L0F_DEPTH*KC*1024), F32)
    w1_d = C.dram_in("l0f_w1", (128, L0F_DEPTH*L0F_W1C), F32)
    w2_d = C.dram_in("l0f_w2", (128, L0F_DEPTH*NGC*1024), F32)
    b1a_d = C.dram_in("l0f_b1a", (128, L0F_DEPTH*NGC), F32)
    b1g_d = C.dram_in("l0f_b1g", (128, L0F_DEPTH*NGC), F32)
    b2_d = C.dram_in("l0f_b2", (128, L0F_DEPTH*KC), F32)
    lng_d = C.dram_in("l0f_lng", (128, L0F_DEPTH*KC), F32)
    lnb_d = C.dram_in("l0f_lnb", (128, L0F_DEPTH*KC), F32)
    qs_d = C.dram_in("l0f_qs", (L0F_DEPTH, 64), F32)
    ks_d = C.dram_in("l0f_ks", (L0F_DEPTH, 64), F32)
    tab_d = C.dram_in("l0f_tabs", (128, 4*NT*64), F32)
    madd_d = C.dram_in("l0f_madd", (128, 4*512), F32)
    xsp = C.dram_tmp("xspill", (128, NT*D), F32)
    PS = C.psum_f32

    emit_spill_x(C)
    P.barrier()
    with P.group("vector", reads=["y_fm"], writes=["z", "x"]) as g:
        for cc in range(KC):
            g.emit(lambda v, cc=cc: v.memset(z[:, cc*ZROW:cc*ZROW+1], 0.0))
            g.emit(lambda v, cc=cc: v.tensor_copy(
                out=z[:, cc*ZROW+1:(cc+1)*ZROW], in_=C.y_fm[:, cc*T:(cc+1)*T]))
    P.barrier()

    def attn_sublayer(l):
        with P.group("gpsimd", writes=["l0f_wqkv"]) as g:
            for cc in range(KC):
                g.dma_(wqkv[:, cc*3072:(cc+1)*3072],
                       wqkv_d[:, l*KC*3072 + cc*3072: l*KC*3072 + (cc+1)*3072])
        with P.group("gpsimd", writes=["l0f_wo"]) as g:
            g.dma_(wo[:], wo_d[:, l*KC*1024:(l+1)*KC*1024])
        with P.group("sync", writes=["l0f_tabs"]) as g:
            for j in range(4):
                g.dma_(tabs[j][:], tab_d[:, j*1024:(j+1)*1024])
        with P.group("gpsimd", writes=["l0f_qsrow"]) as g:
            g.dma_(qsbc[0:1, 0:64], qs_d[l:l+1, :])
            g.dma_(ksbc[0:1, 0:64], ks_d[l:l+1, :])
        for j in range(2):
            src = qsbc if j == 0 else ksbc
            emit_matmul_acc(C, PS[4+j][:, 0:64], [C.ones_row[0:1, 0:128]],
                            [src[0:1, 0:64]],
                            reads=["l0f_qsrow", "ones"], writes=["ps%d" % (4+j)])
            with P.group("vector", reads=["ps%d" % (4+j)],
                         writes=["l0f_qsbc", "l0f_qsrow"]) as g:
                g.emit(lambda v, ps=PS[4+j][:, 0:64], o=src:
                       v.tensor_copy(out=o[:], in_=ps))
        qsrep = bass.AP(qsbc, 0, [[64, 128], [0, NT], [1, 64]])
        qsrot = bass.AP(qsbc, 32, [[64, 128], [0, NT], [-32, 2], [1, 32]])
        ksrep = bass.AP(ksbc, 0, [[64, 128], [0, NT], [1, 64]])
        ksrot = bass.AP(ksbc, 32, [[64, 128], [0, NT], [-32, 2], [1, 32]])
        with P.group("vector", reads=["l0f_tabs", "l0f_qsbc"], writes=["l0f_tabs"], chain=False) as g:
            g.emit(lambda v: v.tensor_tensor(tabs[0][:], tabs[0][:], qsrep, ALU.mult))
            g.emit(lambda v: v.tensor_tensor(tabs[1][:], tabs[1][:], qsrot, ALU.mult))
            g.emit(lambda v: v.tensor_tensor(tabs[2][:], tabs[2][:], ksrep, ALU.mult))
            g.emit(lambda v: v.tensor_tensor(tabs[3][:], tabs[3][:], ksrot, ALU.mult))
        with P.group("gpsimd", writes=["l0f_kpf1", "l0f_vtm1"]) as g:
            g.emit(lambda g_: g_.memset(kpf2[:, 2048:4096], 0.0))
            g.emit(lambda g_: g_.memset(vtm[1][:], 0.0))
        with P.group("scalar", writes=["l0f_madd"]) as g:
            g.dma_(madd[:], madd_d[:])
        P.barrier()
        for wb in range(L0F_WB_LIM):
            cur = wb % 2
            prv = 1 - cur
            with P.group("vector", reads=["z", "l0f_zst"], writes=["l0f_zb"], chain=False) as g:
                for cc in range(KC):
                    off = (1 if cc < 4 else 0)
                    if cc < 4 or wb == 0:
                        g.emit(lambda v, o=zb[:, cc*256:(cc+1)*256],
                               s=z[:, cc*ZROW + off + wb*256: cc*ZROW + off + wb*256 + 256]:
                               v.tensor_copy(out=o, in_=s))
                    else:
                        g.emit(lambda v, o=zb[:, cc*256:cc*256+1],
                               s=zstA[:, cc-4:cc-3]: v.tensor_copy(out=o, in_=s))
                        g.emit(lambda v, o=zb[:, cc*256+1:(cc+1)*256],
                               s=z[:, cc*ZROW + 1 + wb*256: cc*ZROW + wb*256 + 256]:
                               v.tensor_copy(out=o, in_=s))
            with P.group("vector", reads=["z"], writes=["l0f_zst"], chain=False) as g:
                for cc in range(4, KC):
                    g.emit(lambda v, o=zstA[:, cc-4:cc-3],
                           s=z[:, cc*ZROW + 1 + wb*256 + 255: cc*ZROW + 1 + wb*256 + 256]:
                           v.tensor_copy(out=o, in_=s))
            for tt in range(2):
                for mg in range(6):
                    ps = PS[mg][:, 0:512]
                    lh = [zb[:, cc*256 + tt*128: cc*256 + (tt+1)*128] for cc in range(KC)]
                    rh = [wqkv[:, cc*3072 + mg*512: cc*3072 + (mg+1)*512] for cc in range(KC)]
                    emit_matmul_acc(C, ps, lh, rh, reads=["l0f_zb", "l0f_wqkv"],
                                    writes=["ps%d" % mg])
                with P.group("vector", reads=["ps4", "ps5"],
                             writes=["l0f_vtm%d" % cur], chain=False) as g:
                    g.emit(lambda v, o=vtm[cur][:, tt*1024:tt*1024+512]:
                           v.tensor_copy(out=o, in_=PS[4][:, 0:512]))
                    g.emit(lambda v, o=vtm[cur][:, tt*1024+512:(tt+1)*1024]:
                           v.tensor_copy(out=o, in_=PS[5][:, 0:512]))
                for qk in range(2):
                    pq = 0 if qk == 0 else 2
                    tA, tB = (tabs[0], tabs[1]) if qk == 0 else (tabs[2], tabs[3])
                    with P.group("vector", reads=["ps%d" % pq, "ps%d" % (pq+1)],
                                 writes=["l0f_qkf"], chain=False) as g:
                        g.emit(lambda v, pq=pq: v.tensor_copy(out=qkf[:, 0:512],
                                                              in_=PS[pq][:, 0:512]))
                        g.emit(lambda v, pq=pq: v.tensor_copy(out=qkf[:, 512:1024],
                                                              in_=PS[pq+1][:, 0:512]))
                    with P.group("scalar", reads=["l0f_qkf"], writes=["l0f_qh"]) as g:
                        g.emit(lambda s: s.activation(qhat[:], qkf[:], AF.Square))
                    with P.group("vector", reads=["l0f_qh"], writes=["l0f_rstd"]) as g:
                        g.emit(lambda v: v.tensor_reduce(
                            rstd[:], bass.AP(qhat, 0, [[1024, 128], [64, 16], [1, 64]]),
                            AX.X, ALU.add))
                        g.emit(lambda v: v.tensor_scalar_max(rstd[:], rstd[:], 1e-24))
                    with P.group("scalar", reads=["l0f_rstd"], writes=["l0f_rstd2"]) as g:
                        g.emit(lambda s: s.activation(rstd[:], rstd[:], AF.Ln))
                        g.emit(lambda s: s.activation(rstd[:], rstd[:], AF.Exp, scale=-0.5))
                    rrep = bass.AP(rstd, 0, [[16, 128], [1, 16], [0, 64]])
                    ttile = wb*2 + tt
                    tArep = bass.AP(tA, ttile*64, [[NT*64, 128], [0, 16], [1, 64]])
                    tBrep = bass.AP(tB, ttile*64, [[NT*64, 128], [0, 16], [1, 64]])
                    qhrot = bass.AP(qhat, 32, [[1024, 128], [64, 16], [-32, 2], [1, 32]])
                    with P.group("vector", reads=["l0f_rstd2", "l0f_qkf", "l0f_tabs"],
                                 writes=["l0f_qrot", "l0f_qkf", "l0f_qh"]) as g:
                        g.emit(lambda v, rr=rrep: v.tensor_tensor(qhat[:], qkf[:], rr, ALU.mult))
                        g.emit(lambda v, qr=qhrot, tr=tBrep:
                               v.tensor_tensor(rsc[:], qr, tr, ALU.mult))
                        g.emit(lambda v, tr=tArep: v.tensor_tensor(qkf[:], qhat[:], tr, ALU.mult))
                        g.emit(lambda v, o=qrot[:, qk*1024:(qk+1)*1024]:
                               v.tensor_add(o, qkf[:], rsc[:]))
                with P.group("tensor", reads=["l0f_qrot", "ident"], writes=["ps_tr"],
                             chain=False) as g:
                    for cc in range(KC):
                        g.emit(lambda p, cc=cc: p.transpose(
                            C.ps_tr[:, cc*128:(cc+1)*128],
                            qrot[:, cc*128:(cc+1)*128], C.ident[:]))
                with P.group("vector", reads=["ps_tr"], writes=["l0f_qpf"], chain=False) as g:
                    for cc in range(KC):
                        g.emit(lambda v, cc=cc, tt=tt: v.tensor_copy(
                            out=qpf[:, cc*256 + tt*128: cc*256 + (tt+1)*128],
                            in_=C.ps_tr[:, cc*128:(cc+1)*128]))
                with P.group("tensor", reads=["l0f_qrot", "ident"], writes=["ps_tr"],
                             chain=False) as g:
                    for cc in range(KC):
                        g.emit(lambda p, cc=cc: p.transpose(
                            C.ps_tr[:, cc*128:(cc+1)*128],
                            qrot[:, 1024 + cc*128: 1024 + (cc+1)*128], C.ident[:]))
                with P.group("vector", reads=["ps_tr"], writes=["l0f_kpf%d" % cur], chain=False) as g:
                    for cc in range(KC):
                        g.emit(lambda v, o=kpf2[:, cur*2048 + cc*256 + tt*128: cur*2048 + cc*256 + (tt+1)*128],
                               s=C.ps_tr[:, cc*128:(cc+1)*128]:
                               v.tensor_copy(out=o, in_=s))
            P.barrier()
            def st_qk(idx):
                h, it = idx // 2, idx % 2
                p0 = (h % 2) * 64
                cc = h // 2
                ps = PS[idx % 4][:, 0:512]
                rh = bass.AP(kpf2, p0*4096 + prv*2048 + cc*256,
                             [[4096, 64], [(cur - prv)*2048, 2], [1, 256]])
                with P.group("tensor", reads=["l0f_qpf", "l0f_kpf0", "l0f_kpf1"],
                             writes=["ps%d" % (idx % 4)], chain=False) as g:
                    g.emit(lambda p, ps=ps, rh=rh,
                           lt=qpf[p0:p0+64, cc*256 + it*128: cc*256 + (it+1)*128]:
                           p.matmul(ps, lt, rh))

            def st_s1(idx):
                j = idx % 4
                s1p = s1[j]
                mi = (idx % 2)*2 + (1 if wb == 0 else 0)
                ps = PS[j][:, 0:512]
                with P.group("vector", reads=["ps%d" % j, "l0f_madd"],
                             writes=["l0f_s1%d" % j, "l0f_s1e%d" % j]) as g:
                    g.emit(lambda v, ps=ps, s1p=s1p, m=madd[:, mi*512:(mi+1)*512]:
                           v.scalar_tensor_tensor(s1p[:], ps, -8.0, m, ALU.mult, ALU.add))

            def st_mx(idx):
                j = idx % 4
                with P.group("vector", reads=["l0f_s1%d" % j],
                             writes=["l0f_mc%d" % j]) as g:
                    g.emit(lambda v, s1p=s1[j], mc=mcol[:, j*4:j*4+4]:
                           v.tensor_reduce(mc[:, 0:1], s1p[:], AX.X, ALU.min))

            def st_exp(idx):
                j = idx % 4
                with P.group("scalar", reads=["l0f_s1%d" % j, "l0f_mc%d" % j],
                             writes=["l0f_s1e%d" % j, "l0f_mca%d" % j]) as g:
                    g.emit(lambda s, s1p=s1[j], mc=mcol[:, j*4:j*4+4]:
                           s.activation(s1p[:], s1p[:], AF.Exp, scale=-1.0,
                                        bias=mc[:, 0:1], accum_out=mc[:, 1:2]))

            def st_n1(idx):
                j = idx % 4
                with P.group("vector", reads=["l0f_mca%d" % j],
                             writes=["l0f_rcp%d" % j]) as g:
                    g.emit(lambda v, mc=mcol[:, j*4:j*4+4]:
                           v.reciprocal(mc[:, 2:3], mc[:, 1:2]))

            def st_n2(idx):
                j = idx % 4
                par = idx % 2
                with P.group("scalar", reads=["l0f_s1e%d" % j, "l0f_rcp%d" % j],
                             writes=["l0f_attn%d" % par]) as g:
                    g.emit(lambda s, s1p=s1[j], atp=attn_sb[par],
                           mc=mcol[:, j*4:j*4+4]:
                           s.activation(atp[:], s1p[:], AF.Identity,
                                        scale=mc[:, 2:3]))

            def st_tr(idx):
                par = idx % 2
                atp = attn_sb[par]
                pstr = C.ps_tr[:, par*512:(par+1)*512]
                with P.group("tensor", reads=["l0f_attn%d" % par, "ident"],
                             writes=["ps_tr"], chain=False) as g:
                    for jj in range(4):
                        g.emit(lambda p, jj=jj, atp=atp, pstr=pstr: p.transpose(
                            pstr[:, jj*128:(jj+1)*128],
                            atp[:, jj*128:(jj+1)*128], C.ident[:]))

            def st_tT(idx):
                par = idx % 2
                pstr = C.ps_tr[:, par*512:(par+1)*512]
                with P.group("vector", reads=["ps_tr"],
                             writes=["l0f_attnT%d" % par]) as g:
                    g.emit(lambda v, aTp=attnT_sb[par], pstr=pstr:
                           v.tensor_copy(out=aTp[:], in_=pstr))

            def st_pv(idx):
                h, it = idx // 2, idx % 2
                p0 = (h % 2) * 64
                par = idx % 2
                aTp = attnT_sb[par]
                po = PS[4 + par][p0:p0+64, 0:128]
                with P.group("tensor", reads=["l0f_attnT%d" % par,
                                              "l0f_vtm0", "l0f_vtm1"],
                             writes=["psv%d" % par], chain=False) as g:
                    for jj in range(4):
                        vsrc = vtm[prv] if jj < 2 else vtm[cur]
                        g.emit(lambda p, po=po, tp=(0, p0), aTp=aTp,
                               lt=vsrc[:, (jj % 2)*1024 + h*64: (jj % 2)*1024 + (h+1)*64],
                               rh=aTp[:, jj*128:(jj+1)*128],
                               st=(jj == 0), sp=(jj == 3):
                               p.matmul(po, lt, rh, start=st, stop=sp, tile_position=tp))

            def st_ao(idx):
                h, it = idx // 2, idx % 2
                p0 = (h % 2) * 64
                cc = h // 2
                par = idx % 2
                po = PS[4 + par][p0:p0+64, 0:128]
                with P.group("scalar", reads=["psv%d" % par],
                             writes=["l0f_aof"]) as g:
                    g.emit(lambda s, po=po,
                           o=aof[p0:p0+64, cc*256 + it*128: cc*256 + (it+1)*128]:
                           s.activation(o, po, AF.Identity))

            def st_tail(idx):
                st_n2(idx); st_tr(idx); st_tT(idx); st_pv(idx); st_ao(idx)
            stages = [(5, st_tail), (4, st_n1), (3, st_exp), (2, st_mx),
                      (1, st_s1), (0, st_qk)]
            for step in range(32 + 5):
                for off, fn in stages:
                    if off <= step < 32 + off:
                        fn(step - off)
            for dp in range(4):
                pj = 2 + (dp % 2)
                ps = PS[pj]
                with P.group("tensor", reads=["l0f_aof", "l0f_wo"],
                             writes=["ps%d" % pj], chain=False) as g:
                    for half in range(2):
                        dc = dp*2 + half
                        for ic in range(KC):
                            g.emit(lambda p, ps=ps[:, half*256:(half+1)*256],
                                   lt=wo[:, ic*1024 + dc*128: ic*1024 + (dc+1)*128],
                                   rh=aof[:, ic*256:(ic+1)*256],
                                   st=(ic == 0), sp=(ic == KC-1):
                                   p.matmul(ps, lt, rh, start=st, stop=sp))
                with P.group("vector", reads=["ps%d" % pj, "z"], writes=["z"], chain=False) as g:
                    for half in range(2):
                        dc = dp*2 + half
                        zsl = z[:, dc*ZROW + 1 + wb*256: dc*ZROW + 1 + (wb+1)*256]
                        g.emit(lambda v, ps=ps[:, half*256:(half+1)*256], o=zsl:
                               v.tensor_add(o, o, ps))
            P.barrier()

    def ffn_sublayer(l):
        with P.group("gpsimd", writes=["l0f_fparm"]) as g:
            g.dma_(b1ac[:], b1a_d[:, l*NGC:(l+1)*NGC])
            g.dma_(b1gc[:], b1g_d[:, l*NGC:(l+1)*NGC])
            g.dma_(b2c[:], b2_d[:, l*KC:(l+1)*KC])
            g.dma_(lngc[:], lng_d[:, l*KC:(l+1)*KC])
            g.dma_(lnbc[:], lnb_d[:, l*KC:(l+1)*KC])
        with P.group("gpsimd", writes=["l0f_w2r"]) as g:
            for q4 in range(0, NGC, 6):
                qn = min(6, NGC - q4)
                g.dma_(w2r[:, q4*1024:(q4+qn)*1024],
                       w2_d[:, l*NGC*1024 + q4*1024: l*NGC*1024 + (q4+qn)*1024])
        P.barrier()

        def ffn_pre(tc):
            par = tc % 2
            xnb = xnb2[par]
            with P.group("vector", reads=["z", "l0f_zst"],
                         writes=["l0f_xnb%d" % par], chain=False) as g:
                for cc in range(KC):
                    off = (1 if cc < 4 else 0)
                    if cc < 4 or tc == 0:
                        g.emit(lambda v, o=xnb[:, cc*512:(cc+1)*512],
                               s=z[:, cc*ZROW + off + tc*512: cc*ZROW + off + tc*512 + 512]:
                               v.tensor_copy(out=o, in_=s))
                    else:
                        g.emit(lambda v, o=xnb[:, cc*512:cc*512+1],
                               s=zstF[:, cc-4:cc-3]: v.tensor_copy(out=o, in_=s))
                        g.emit(lambda v, o=xnb[:, cc*512+1:(cc+1)*512],
                               s=z[:, cc*ZROW + 1 + tc*512: cc*ZROW + tc*512 + 512]:
                               v.tensor_copy(out=o, in_=s))
            for tt in range(4):
                emit_matmul_acc(C, C.ps_aux[:, 128:129],
                                [xnb[:, cc*512 + tt*128: cc*512 + (tt+1)*128]
                                 for cc in range(KC)],
                                [C.ones_cb[:] for cc in range(KC)],
                                reads=["l0f_xnb%d" % par, "ones"], writes=["ps_aux"])
                emit_matmul_acc(C, C.ps_aux[:, 0:128],
                                [xnb[:, cc*512 + tt*128: cc*512 + (tt+1)*128]
                                 for cc in range(KC)],
                                [xnb[:, cc*512 + tt*128: cc*512 + (tt+1)*128]
                                 for cc in range(KC)],
                                reads=["l0f_xnb%d" % par], writes=["ps_aux"])
                with P.group("vector", reads=["ps_aux", "ident"],
                             writes=["l0f_stat"]) as g:
                    g.emit(lambda v, tt=tt: v.tensor_copy(out=stat[:, tt*4:tt*4+1],
                                                          in_=C.ps_aux[:, 128:129]))
                    g.emit(lambda v: v.tensor_tensor(
                        fs1[:, 0:128], C.ps_aux[:, 0:128], C.identf[:], ALU.mult))
                    g.emit(lambda v, o=stat[:, tt*4+1:tt*4+2]:
                           v.reduce_sum(o, fs1[:, 0:128], AX.X))
            sv = lambda c0: bass.AP(stat, c0, [[16, 128], [4, 4], [1, 1]])
            with P.group("vector", reads=["l0f_stat"], writes=["l0f_stat2"]) as g:
                g.emit(lambda v: v.tensor_scalar_mul(sv(0), sv(0), 1.0/D))
                g.emit(lambda v: v.tensor_scalar_mul(sv(1), sv(1), 1.0/D))
                g.emit(lambda v: v.tensor_tensor(sv(2), sv(0), sv(0), ALU.mult))
                g.emit(lambda v: v.tensor_tensor(sv(2), sv(1), sv(2), ALU.subtract))
                g.emit(lambda v: v.tensor_scalar_add(sv(2), sv(2), 1e-5))
            with P.group("scalar", reads=["l0f_stat2"], writes=["l0f_stat3"]) as g:
                g.emit(lambda s: s.activation(sv(3), sv(2), AF.Ln))
                g.emit(lambda s: s.activation(sv(3), sv(3), AF.Exp, scale=-0.5))
            with P.group("vector", reads=["l0f_stat3"], writes=["l0f_stat4"]) as g:
                g.emit(lambda v: v.tensor_tensor(sv(0), sv(0), sv(3), ALU.mult))
            for j in range(2):
                c0 = 3 if j == 0 else 0
                rrow = rrowA if j == 0 else rrowB
                with P.group("tensor", reads=["l0f_stat4", "ident"],
                             writes=["ps_aux"], chain=False) as g:
                    for tt in range(4):
                        g.emit(lambda p, tt=tt, c0=c0: p.transpose(
                            C.ps_aux[0:1, tt*128:(tt+1)*128],
                            stat[:, tt*4+c0:tt*4+c0+1], C.identf[:]))
                with P.group("vector", reads=["ps_aux"], writes=["l0f_rrow"]) as g:
                    g.emit(lambda v, rrow=rrow: v.tensor_copy(
                        out=rrow[0:1, :], in_=C.ps_aux[0:1, 0:512]))
                emit_matmul_acc(C, C.ps_aux[:, 0:512], [C.ones_row[0:1, 0:128]],
                                [rrow[0:1, :]],
                                reads=["l0f_rrow", "ones"], writes=["ps_aux"])
                with P.group("vector", reads=["ps_aux"], writes=["l0f_rbc"]) as g:
                    g.emit(lambda v, o=(rbc if j == 0 else nbbc):
                           v.tensor_copy(out=o[:], in_=C.ps_aux[:, 0:512]))
            for cc in range(KC):
                off = (1 if cc < 4 else 0)
                zsl = z[:, cc*ZROW + off + tc*512: cc*ZROW + off + tc*512 + 512]
                with P.group("vector", reads=["z", "l0f_rbc", "l0f_fparm", "l0f_zst"],
                             writes=["l0f_xnb%d" % par, "l0f_fs1"]) as g:
                    g.emit(lambda v, zsl=zsl: v.tensor_tensor(fs1[:], zsl, rbc[:], ALU.mult))
                    g.emit(lambda v: v.tensor_tensor(fs1[:], fs1[:], nbbc[:], ALU.subtract))
                    g.emit(lambda v, cc=cc, xnb=xnb: v.tensor_scalar(
                        xnb[:, cc*512:(cc+1)*512], fs1[:], lngc[:, cc:cc+1],
                        lnbc[:, cc:cc+1], ALU.mult, ALU.add))
                    if cc >= 4 and tc > 0:
                        g.emit(lambda v, s=zstF[:, cc-4:cc-3]: v.tensor_tensor(
                            fs1[:, 0:1], s, rbc[:, 0:1], ALU.mult))
                        g.emit(lambda v: v.tensor_tensor(
                            fs1[:, 0:1], fs1[:, 0:1], nbbc[:, 0:1], ALU.subtract))
                        g.emit(lambda v, cc=cc, xnb=xnb: v.tensor_scalar(
                            xnb[:, cc*512:cc*512+1], fs1[:, 0:1], lngc[:, cc:cc+1],
                            lnbc[:, cc:cc+1], ALU.mult, ALU.add))
            with P.group("vector", reads=["z"], writes=["l0f_zst"], chain=False) as g:
                for cc in range(4, KC):
                    g.emit(lambda v, o=zstF[:, cc-4:cc-3],
                           s=z[:, cc*ZROW + 1 + tc*512 + 511: cc*ZROW + 1 + tc*512 + 512]:
                           v.tensor_copy(out=o, in_=s))

        def ffn_w1(tc):
            par = tc % 2
            xnb = xnb2[par]
            for j in range(len(L0F_GRP)):
                G = L0F_GRP[j]
                w1s = w1sl[j % 2]
                with P.group("gpsimd", writes=["l0f_w1s%d" % (j % 2)]) as g:
                    base = l*L0F_W1C + 16*L0F_GOFF[j]
                    g.dma_(w1s[:, 0:16*G], w1_d[:, base: base + 16*G])
                for ms in range(G // 128):
                    gc = L0F_GOFF[j] // 128 + ms
                    bk = 4 if gc % 2 == 0 else 2
                    for ag in range(2):
                        pj = bk + ag
                        ps = PS[pj][:, 0:512]
                        lh = [w1s[:, cc*2*G + ag*G + ms*128: cc*2*G + ag*G + (ms+1)*128]
                              for cc in range(KC)]
                        rh = [xnb[:, cc*512:(cc+1)*512] for cc in range(KC)]
                        emit_matmul_acc(C, ps, lh, rh,
                                        reads=["l0f_w1s%d" % (j % 2), "l0f_xnb%d" % par],
                                        writes=["ps%d" % pj])
                    with P.group("scalar", reads=["ps%d" % (bk+1), "l0f_fparm", "l0f_gg2"],
                                 writes=["l0f_gg"]) as g:
                        g.emit(lambda s, b=b1gc[:, gc:gc+1], bk=bk:
                               s.activation(gsc[:], PS[bk+1][:, 0:512], AF.Gelu, bias=b))
                    with P.group("vector", reads=["ps%d" % bk, "l0f_gg", "l0f_fparm"],
                                 writes=["l0f_hp", "l0f_gg2"]) as g:
                        g.emit(lambda v, b=b1ac[:, gc:gc+1], bk=bk,
                               o=hprod[:, gc*512:(gc+1)*512]:
                               v.scalar_tensor_tensor(o, PS[bk][:, 0:512], b, gsc[:],
                                                      ALU.add, ALU.mult))

        def ffn_w2(tc):
            for th in range(2):
                for dp in range(4):
                    pj = dp % 4
                    ps = PS[pj]
                    with P.group("tensor", reads=["l0f_hp", "l0f_w2r"],
                                 writes=["ps%d" % pj], chain=False) as g:
                        for half in range(2):
                            dc = dp*2 + half
                            for gc in range(NGC):
                                g.emit(lambda p, ps=ps[:, half*256:(half+1)*256],
                                       lt=w2r[:, gc*1024 + dc*128: gc*1024 + (dc+1)*128],
                                       rh=hprod[:, gc*512 + th*256: gc*512 + (th+1)*256],
                                       st=(gc == 0), sp=(gc == NGC-1):
                                       p.matmul(ps, lt, rh, start=st, stop=sp))
                    with P.group("vector", reads=["ps%d" % pj, "z", "l0f_fparm"],
                                 writes=["z"], chain=False) as g:
                        for half in range(2):
                            dc = dp*2 + half
                            zsl = z[:, dc*ZROW + 1 + tc*512 + th*256:
                                    dc*ZROW + 1 + tc*512 + (th+1)*256]
                            g.emit(lambda v, ps=ps[:, half*256:(half+1)*256], o=zsl,
                                   b=b2c[:, dc:dc+1]:
                                   v.scalar_tensor_tensor(o, ps, b, o, ALU.add, ALU.add))

        ffn_pre(0)
        for tc in range(4):
            ffn_w1(tc)
            if tc < 3:
                ffn_pre(tc + 1)
            ffn_w2(tc)
        P.barrier()

    for l in range(L0F_DEPTH):
        if not L0F_SKIP_ATTN:
            attn_sublayer(l)
        P.barrier()
        if not L0F_SKIP_FFN:
            ffn_sublayer(l)
    P.barrier()
    for t in range(NT if not L0F_SKIP_EPI else 0):
        with P.group("sync", reads=["xspill"], writes=["l0f_xst"]) as g:
            g.dma_(xst[:], xsp[:, t*D:(t+1)*D])
        for hh in range(2):
            with P.group("tensor", reads=["z", "ident"], writes=["ps_aux"],
                         chain=False) as g:
                for q in range(4):
                    cc = hh*4 + q
                    g.emit(lambda p, cc=cc, q=q, t=t: p.transpose(
                        C.ps_aux[:, q*128:(q+1)*128],
                        z[:, cc*ZROW + 1 + t*128: cc*ZROW + 1 + (t+1)*128], C.identf[:]))
            with P.group("vector", reads=["ps_aux", "l0f_xst"], writes=["l0f_ost"]) as g:
                g.emit(lambda v, hh=hh: v.tensor_add(
                    ost[:, hh*512:(hh+1)*512], xst[:, hh*512:(hh+1)*512],
                    C.ps_aux[:, 0:512]))
        with P.group("sync", reads=["l0f_ost"], writes=["xspill"]) as g:
            g.dma_(xsp[:, t*D:(t+1)*D], ost[:])
    emit_reload_x(C, claims=["z"])


def prep_l0_ffn(inputs, h):
    if "l0f_wqkv" in h:
        return
    DEPTH = L0F_DEPTH
    q = np.arange(512, dtype=np.float64)[None, :] - np.arange(128, dtype=np.float64)[:, None]
    r = (np.arange(512)[None, :] >= 256)
    madd = np.zeros((128, 4*512), np.float32)
    for a in range(2):
        for w0 in range(2):
            m = (q >= a*128) & (q <= a*128 + 256)
            if w0:
                m = m & r
            madd[:, (a*2+w0)*512:(a*2+w0+1)*512] = \
                (m.astype(np.float32) - 1.0) * -1e30
    h["l0f_madd"] = madd
    wqkv = np.asarray(inputs["L0_ffn_Wqkv"], np.float32)
    h["l0f_wqkv"] = np.ascontiguousarray(np.concatenate(
        [sb_image_kc(np.ascontiguousarray(wqkv[l])) for l in range(DEPTH)], axis=1))
    wo = np.asarray(inputs["L0_ffn_Wo"], np.float32)
    h["l0f_wo"] = np.ascontiguousarray(np.concatenate(
        [sb_image_kc(np.ascontiguousarray(wo[l])) for l in range(DEPTH)], axis=1))
    W1 = np.asarray(inputs["L0_ffn_W1"], np.float32)
    b1 = np.asarray(inputs["L0_ffn_b1"], np.float32)
    W2 = np.asarray(inputs["L0_ffn_W2"], np.float32)
    w1_imgs, b1a_imgs, b1g_imgs, w2_imgs = [], [], [], []
    for l in range(DEPTH):
        W1a = np.zeros((D, GP), np.float32); W1a[:, :2730] = W1[l][:, :2730]
        W1g = np.zeros((D, GP), np.float32); W1g[:, :2730] = W1[l][:, 2730:]
        b1a = np.zeros(GP, np.float32); b1a[:2730] = b1[l][:2730]
        b1g = np.zeros(GP, np.float32); b1g[:2730] = b1[l][2730:]
        cols = []
        for j in range(len(L0F_GRP)):
            G, j0 = L0F_GRP[j], L0F_GOFF[j]
            grp = np.concatenate([W1a[:, j0:j0+G], W1g[:, j0:j0+G]], axis=1)
            cols.append(grp.reshape(KC, 128, 2*G).transpose(1, 0, 2).reshape(128, KC*2*G))
        w1_imgs.append(np.concatenate(cols, axis=1))
        b1a_imgs.append(b1a.reshape(NGC, 128).T)
        b1g_imgs.append(b1g.reshape(NGC, 128).T)
        W2p = np.zeros((GP, D), np.float32); W2p[:2730] = W2[l]
        w2_imgs.append(sb_image_kc(W2p, kc=NGC))
    h["l0f_w1"] = np.ascontiguousarray(np.concatenate(w1_imgs, axis=1))
    h["l0f_b1a"] = np.ascontiguousarray(np.concatenate(b1a_imgs, axis=1))
    h["l0f_b1g"] = np.ascontiguousarray(np.concatenate(b1g_imgs, axis=1))
    h["l0f_w2"] = np.ascontiguousarray(np.concatenate(w2_imgs, axis=1))
    h["l0f_b2"] = np.ascontiguousarray(
        np.asarray(inputs["L0_ffn_b2"], np.float32)[:DEPTH].reshape(DEPTH, KC, 128)
        .transpose(2, 0, 1).reshape(128, DEPTH*KC))
    h["l0f_lng"] = np.ascontiguousarray(
        np.asarray(inputs["L0_ffn_ln_g"], np.float32)[:DEPTH].reshape(DEPTH, KC, 128)
        .transpose(2, 0, 1).reshape(128, DEPTH*KC))
    h["l0f_lnb"] = np.ascontiguousarray(
        np.asarray(inputs["L0_ffn_ln_b"], np.float32)[:DEPTH].reshape(DEPTH, KC, 128)
        .transpose(2, 0, 1).reshape(128, DEPTH*KC))
    h["l0f_qs"] = np.ascontiguousarray(
        np.asarray(inputs["L0_ffn_q_scale"], np.float32)[:DEPTH])
    h["l0f_ks"] = np.ascontiguousarray(
        np.asarray(inputs["L0_ffn_k_scale"], np.float32)[:DEPTH])
    DIM_HEAD, WINDOW = 64, 256
    half = np.arange(0, DIM_HEAD, 2, dtype=np.float64)
    inv_freq = 1.0 / 10000.0 ** (half / DIM_HEAD)
    tt = np.arange(T, dtype=np.float64)
    freqs = np.concatenate([tt[:, None] * inv_freq[None, :]] * 2, axis=-1)
    base_scale = (half + 0.4 * DIM_HEAD) / (1.4 * DIM_HEAD)
    power = (tt - T // 2) / (WINDOW // 2)
    scale = np.concatenate([base_scale[None, :] ** power[:, None]] * 2, axis=-1)
    cos, sin = np.cos(freqs), np.sin(freqs)
    sgn = np.ones(64); sgn[:32] = -1.0
    def timg(tb):
        return tb.reshape(NT, 128, 64).transpose(1, 0, 2).reshape(128, NT*64)
    h["l0f_tabs"] = np.ascontiguousarray(np.concatenate(
        [timg((cos * scale).astype(np.float32)),
         timg((sin * scale * sgn[None, :]).astype(np.float32)),
         timg((cos / scale).astype(np.float32)),
         timg((sin / scale * sgn[None, :]).astype(np.float32))], axis=1))


BLOCK_BUILDERS = {"l1_ffn": build_l1_ffn, "l1_attn_a": build_l1_attn_a, "l0_attn_b": build_l0_attn_b, "l0_attn_a": build_l0_attn_a, "l1_attn_b": build_l1_attn_b, "l0_ffn": build_l0_ffn}


def sb_image_kc(wT, kc=8):
    F = wT.shape[1]
    return np.ascontiguousarray(wT.reshape(kc, 128, F).transpose(1, 0, 2).reshape(128, kc*F))


def sb_image_slabs(wT, mslab=128):
    K, M = wT.shape
    kc = K // 128
    nm = M // mslab
    a = wT.reshape(kc, 128, nm, mslab).transpose(1, 2, 0, 3)
    return np.ascontiguousarray(a.reshape(128, nm*kc*mslab))


def prep_l1_ffn(inputs, h):
    wgT = np.asarray(inputs["L1_ffn_w_gate"]).T
    wuT = np.asarray(inputs["L1_ffn_w_up"]).T
    wdT = np.asarray(inputs["L1_ffn_w_down"]).T
    h["l1ffn_wg_sb"] = sb_image_kc(np.ascontiguousarray(wgT))
    h["l1ffn_wu_sb"] = sb_image_slabs(np.ascontiguousarray(wuT), 128)
    h["l1ffn_wd_sb"] = sb_image_kc(np.ascontiguousarray(wdT), kc=24)


BLOCK_PREP = {"l1_ffn": prep_l1_ffn, "l1_attn_a": prep_l1_attn_a, "l0_attn_b": prep_l0_attn_b, "l0_attn_a": prep_l0_attn_a, "l1_attn_b": prep_l1_attn_b, "l0_ffn": prep_l0_ffn}

ALL_BLOCKS = ["l0_attn_a", "l0_ffn", "l0_attn_b", "l0_ffn", "l1_attn_a", "l1_ffn",
              "l1_attn_b", "l1_ffn"]


def build_program(blocks):
    C = Ctx()
    C.P = Prog(C.nc)
    _build_body(C, blocks)
    C.nc.all_engine_barrier()
    C.nc.reset()
    return C


def _build_body(C, blocks):
    P = C.P
    alloc_common(C)
    x_in = C.dram_in("x_in", (128, NT*D), F32)
    x_out = C.dram_out("x_out", (128, NT*D), F32)
    with P.group("sync", writes=["x"]) as g:
        for t in range(0, NT, 8):
            g.dma_(C.x_tm[:, t*D:(t+8)*D], x_in[:, t*D:(t+8)*D])
    for i, b in enumerate(blocks):
        C.blk_i = i
        P.barrier()
        emit_prologue(C)
        BLOCK_BUILDERS[b](C, "b%d" % i)
    P.barrier()
    for dn, (dap, dkeys) in getattr(C, "debug_tiles", {}).items():
        dd = C.dram_out("dbg_" + dn, list(dap.shape), dap.dtype)
        with P.group("sync", reads=list(dkeys)) as g:
            g.dma_(dd, dap)
    with P.group("sync", reads=["x"], writes=["xout"]) as g:
        for t in range(0, NT, 8):
            g.dma_(x_out[:, t*D:(t+8)*D], C.x_tm[:, t*D:(t+8)*D])
    sem, cnt = P.sem["sync.d"], P.count["sync.d"]
    P.ops["sync"].append(lambda en, s=sem, v=cnt: en.wait_ge(s, v))
    P.barrier()
    P.finalize()


def run_blocks(x, inputs, blocks):
    C = build_program(blocks)
    h = {}
    for b in set(blocks):
        BLOCK_PREP[b](inputs, h)
    in_maps = []
    for c in range(N_CORES):
        xc = np.ascontiguousarray(x[c].reshape(NT, 128, D).transpose(1, 0, 2).reshape(128, NT*D))
        m = {"x_in": xc}
        m.update(h)
        in_maps.append(m)
    res = run_bass_kernel_spmd(C.nc, in_maps, core_ids=list(range(N_CORES)))
    outs = []
    for r in res.results:
        xo = np.asarray(r["x_out"]).reshape(128, NT, D).transpose(1, 0, 2).reshape(T, D)
        outs.append(xo)
    return np.stack(outs)


def kernel(**inputs):
    x = np.asarray(inputs["L0_attn_a_x"])
    x = run_blocks(x, inputs, ALL_BLOCKS)
    return x


# version 12
# speedup vs baseline: 1.3513x; 1.1746x over previous
import numpy as np
import concourse.bass as bass
import concourse.mybir as mybir
from concourse.bass_utils import run_bass_kernel_spmd

F32 = mybir.dt.float32
BF16 = mybir.dt.bfloat16
I32 = mybir.dt.int32
AF = mybir.ActivationFunctionType
ALU = mybir.AluOpType
AX = mybir.AxisListType

T = 2048
D = 1024
NT = T // 128
KC = D // 128
N_CORES = 8

SB_BASE = 16512
OFF_X = SB_BASE
OFF_YFM = SB_BASE + 65536
OFF_COM = SB_BASE + 98304
OFF_ARENA = SB_BASE + 107520
ARENA_SZ = 229376 - OFF_ARENA

ENGINES = ("tensor", "vector", "scalar", "gpsimd", "sync")


class Prog:

    def __init__(self, nc):
        self.nc = nc
        self.ops = {e: [] for e in ENGINES}
        self.sem = {e: nc.alloc_semaphore(f"sem_{e}") for e in ENGINES}
        self.sem.update({e + ".d": nc.alloc_semaphore(f"semd_{e}")
                         for e in ("sync", "gpsimd", "scalar", "vector")})
        self.count = {s: 0 for s in self.sem}
        self.last_wait = {}
        self.writer = {}
        self.readers = {}

    def group(self, eng, reads=(), writes=(), chain=None):
        return _Group(self, eng, list(reads), list(writes), chain)

    def barrier(self):
        snap = dict(self.count)
        for e in ENGINES:
            for tgt, v in snap.items():
                if v == 0:
                    continue
                k = (e, tgt)
                if self.last_wait.get(k, 0) >= v:
                    continue
                self.last_wait[k] = v
                sem = self.sem[tgt]
                self.ops[e].append(lambda en, s=sem, v=v: en.wait_ge(s, v))

    def finalize(self):
        with self.nc.Block() as blk:
            for e in ENGINES:
                ops = self.ops[e]
                getattr(blk, e)(lambda en, ops=ops: [f(en) for f in ops])


class _Group:

    def __init__(self, P, eng, reads, writes, chain=None):
        self.P, self.eng, self.reads, self.writes = P, eng, reads, writes
        self.chain = chain if chain is not None else (eng != "tensor")
        self.items = []

    def emit(self, f):
        self.items.append(f)

    def dma_(self, out, in_, **kw):
        self.items.append(("dma", lambda en, o=out, i=in_, kw=kw: en.dma_start(out=o, in_=i, **kw)))

    def __enter__(self):
        return self

    def __exit__(self, et, ev, tb):
        if et is not None:
            return False
        P, eng = self.P, self.eng
        n_dma = sum(1 for it in self.items if isinstance(it, tuple))
        assert n_dma == 0 or n_dma == len(self.items), "don't mix dma and compute"
        stream = eng + ".d" if n_dma else eng
        deps = []
        for key in self.reads:
            w = P.writer.get(key)
            if w:
                deps.append(w)
        for key in self.writes:
            w = P.writer.get(key)
            if w:
                deps.append(w)
            for s2, cnt in P.readers.get(key, {}).items():
                deps.append((s2, cnt))
        waits = {}
        for tgt, cnt in deps:
            if eng == "tensor" and tgt == eng:
                continue
            if P.last_wait.get((eng, tgt), 0) >= cnt:
                continue
            waits[tgt] = max(waits.get(tgt, 0), cnt)
        for tgt, cnt in waits.items():
            P.last_wait[(eng, tgt)] = cnt
            sem = P.sem[tgt]
            P.ops[eng].append(lambda en, s=sem, v=cnt: en.wait_ge(s, v))
        sem = P.sem[stream]
        if n_dma:
            for _, f in self.items:
                P.ops[eng].append(lambda en, f=f, s=sem: f(en).then_inc(s, 16))
            P.count[stream] += 16 * n_dma
            tot = P.count[stream]
            P.ops[eng].append(lambda en, s=sem, v=tot: en.wait_ge(s, v))
        else:
            assert self.items, "empty group"
            if eng == "tensor":
                for f in self.items[:-1]:
                    P.ops[eng].append(f)
                last = self.items[-1]
                P.ops[eng].append(lambda en, f=last, s=sem: f(en).then_inc(s, 1))
            else:
                for j, f in enumerate(self.items[:-1]):
                    if j and self.chain:
                        P.ops[eng].append(lambda en: en.drain(fusable=True))
                    P.ops[eng].append(f)
                if len(self.items) > 1 and self.chain:
                    P.ops[eng].append(lambda en: en.drain(fusable=True))
                last = self.items[-1]
                P.ops[eng].append(lambda en, f=last, s=sem: f(en).then_inc(s, 1))
            P.count[stream] += 1
        cnt = P.count[stream]
        for key in self.writes:
            P.writer[key] = (stream, cnt)
            P.readers[key] = {}
        for key in self.reads:
            P.readers.setdefault(key, {})[stream] = cnt
        if PARANOID:
            P.barrier()
        return False


PARANOID = False
L0B_HLIM = 16
L0B_TLIM = 16


class Ctx:
    def __init__(self):
        self.nc = bass.Bass("TRN2", target_bir_lowering=False)
        self.dram = {}

    def dram_in(self, name, shape, dtype=F32):
        if name not in self.dram:
            self.dram[name] = self.nc.dram_tensor(name, list(shape), dtype, kind="ExternalInput").ap()
        return self.dram[name]

    def dram_out(self, name, shape, dtype=F32):
        self.dram[name] = self.nc.dram_tensor(name, list(shape), dtype, kind="ExternalOutput").ap()
        return self.dram[name]

    def dram_tmp(self, name, shape, dtype=F32):
        if name not in self.dram:
            self.dram[name] = self.nc.dram_tensor(name, list(shape), dtype, kind="Internal").ap()
        return self.dram[name]

    def sb_at(self, name, shape, dtype, off):
        nbytes = int(np.prod(shape[1:])) * mybir.dt.size(dtype)
        assert off + nbytes <= 229376, (name, off, nbytes)
        return self.nc.alloc_sbuf_tensor_at(name, list(shape), dtype, offset=off)

    def ps(self, name, shape, dtype=F32):
        return self.nc.alloc_psum_tensor(name, list(shape), dtype)


def dram2d(ap, row0, nrows, ncols, col0=0, row_len=None, p=128):
    if row_len is None:
        row_len = ap.tensor.shape[-1]
    ng = nrows // p
    return bass.AP(ap.tensor, row0 * row_len + col0,
                   [[row_len, p], [p * row_len, ng], [1, ncols]])


def alloc_common(C):
    C.x_tm = C.sb_at("x_tm", (128, NT*D), F32, OFF_X)
    C.y_fm = C.sb_at("y_fm", (128, KC*T), BF16, OFF_YFM)
    o = OFF_COM
    C.ln_stats = C.sb_at("ln_stats", (128, 5*NT), F32, o); o += 5*NT*4
    C.sb_sq = C.sb_at("sb_sq", (128, D), F32, o); o += D*4
    C.ident = C.sb_at("ident", (128, 128), BF16, o); o += 256
    C.identf = C.sb_at("identf", (128, 128), F32, o); o += 512
    C.ones_row = C.sb_at("ones_row", (1, 512), F32, o); o += 2048
    C.ones_cb = C.sb_at("ones_cb", (128, 1), BF16, o); o += 32
    C.com_off = o
    assert o <= OFF_ARENA
    C.consts = C.sb_at("consts", (128, 8), F32, o); o += 32
    C.com_off = o
    CONST_VALS = [1e-5, 1.0, 1e-8]
    for j, v in enumerate(CONST_VALS):
        C.nc.const_aps.register(F32, v, C.consts[:, j:j+1])
    C.ps_tr = C.ps("ps_tr", (128, 1024), BF16)
    C.psum_f32 = [C.ps("psf%d" % i, (128, 512), F32) for i in range(6)]
    C.ps_aux = C.ps("ps_aux", (128, 512), F32)
    P = C.P
    with P.group("gpsimd", writes=["consts"]) as g:
        for j, v in enumerate(CONST_VALS):
            g.emit(lambda g_, j=j, v=v: g_.memset(C.consts[:, j:j+1], v))
    with P.group("gpsimd", writes=["ident", "ones"]) as g:
        g.emit(lambda g_: g_.iota(C.ident[:], [[1, 128]], base=0, channel_multiplier=-1,
                                  allow_small_or_imprecise_dtypes=True))
        g.emit(lambda g_: g_.tensor_single_scalar(C.ident[:], C.ident[:], 0.0, ALU.is_equal))
        g.emit(lambda g_: g_.iota(C.identf[:], [[1, 128]], base=0, channel_multiplier=-1,
                                  allow_small_or_imprecise_dtypes=True))
        g.emit(lambda g_: g_.tensor_single_scalar(C.identf[:], C.identf[:], 0.0, ALU.is_equal))
        g.emit(lambda g_: g_.memset(C.ones_row[:], 1.0))
        g.emit(lambda g_: g_.memset(C.ones_cb[:], 1.0))


def emit_layernorm(C, x_tm, y_tm, src_key="x", dst_key="y_tm", eps=1e-5, nt=NT,
                   x_row=D, y_row=D):
    P = C.P
    st = C.ln_stats
    sq = C.sb_sq
    s_sum = st[:, 0*NT:0*NT+nt]
    s_ss = st[:, 1*NT:1*NT+nt]
    s_var = st[:, 2*NT:2*NT+nt]
    s_rstd = st[:, 3*NT:3*NT+nt]
    s_nb = st[:, 4*NT:4*NT+nt]
    for t in range(nt):
        xt = x_tm[:, t*x_row:t*x_row+D]
        with P.group("vector", reads=[src_key], writes=["ln_sum"]) as g:
            g.emit(lambda v, xt=xt, o=st[:, 0*NT+t:0*NT+t+1]: v.reduce_sum(o, xt, AX.X))
        with P.group("scalar", reads=[src_key], writes=["ln_ss"]) as g:
            g.emit(lambda s, xt=xt, o=st[:, 1*NT+t:1*NT+t+1]:
                   s.activation(sq[:], xt, AF.Square, accum_out=o))
    with P.group("vector", reads=["ln_sum", "ln_ss"], writes=["ln_var", "ln_sum"]) as g:
        g.emit(lambda v: v.tensor_scalar_mul(s_sum, s_sum, 1.0/D))
        g.emit(lambda v: v.tensor_scalar_mul(s_ss, s_ss, 1.0/D))
        g.emit(lambda v: v.tensor_tensor(s_var, s_sum, s_sum, ALU.mult))
        g.emit(lambda v: v.tensor_tensor(s_var, s_ss, s_var, ALU.subtract))
    with P.group("scalar", reads=["ln_var", "consts"], writes=["ln_rstd"]) as g:
        g.emit(lambda s: s.activation(s_rstd, s_var, AF.Ln, bias=eps))
        g.emit(lambda s: s.activation(s_rstd, s_rstd, AF.Exp, scale=-0.5))
    with P.group("vector", reads=["ln_rstd", "ln_sum"], writes=["ln_nb"]) as g:
        g.emit(lambda v: v.scalar_tensor_tensor(s_nb, s_sum, -1.0, s_rstd, ALU.mult, ALU.mult))
    for t in range(nt):
        xt = x_tm[:, t*x_row:t*x_row+D]
        yt = y_tm[:, t*y_row:t*y_row+D]
        with P.group("scalar", reads=[src_key, "ln_rstd", "ln_nb"], writes=[dst_key]) as g:
            g.emit(lambda s, xt=xt, yt=yt, sc=st[:, 3*NT+t:3*NT+t+1], b=st[:, 4*NT+t:4*NT+t+1]:
                   s.activation(yt, xt, AF.Identity, bias=b, scale=sc))


def emit_tm_to_fm(C, src_tm, dst_fm, src_key, dst_key, nt=NT, kc=KC,
                  src_row=None, dst_row=None, dst_col0=0):
    P = C.P
    pst = C.ps_tr
    ident = C.ident
    if src_row is None:
        src_row = kc * 128
    if dst_row is None:
        dst_row = nt * 128
    for c in range(kc):
        for k0 in range(0, nt, 8):
            kn = min(8, nt - k0)
            with P.group("tensor", reads=[src_key], writes=["ps_tr"]) as g:
                for j in range(kn):
                    t = k0 + j
                    src = src_tm[:, t*src_row + c*128: t*src_row + (c+1)*128]
                    g.emit(lambda p, s=src, o=pst[:, j*128:(j+1)*128]:
                           p.transpose(o, s, ident[:]))
            dst = dst_fm[:, c*dst_row + dst_col0 + k0*128:
                         c*dst_row + dst_col0 + (k0+kn)*128]
            with P.group("vector", reads=["ps_tr"], writes=[dst_key]) as g:
                g.emit(lambda v, d=dst, s=pst[:, :kn*128]: v.tensor_copy(out=d, in_=s))


def emit_matmul_acc(C, psum, lhsT_slices, rhs_slices, reads, writes):
    P = C.P
    with P.group("tensor", reads=list(reads), writes=list(writes)) as g:
        n = len(lhsT_slices)
        for i, (lt, rh) in enumerate(zip(lhsT_slices, rhs_slices)):
            g.emit(lambda p, lt=lt, rh=rh, st=(i == 0), sp=(i == n-1):
                   p.matmul(psum, lt, rh, start=st, stop=sp))


def emit_spill_x(C):
    P = C.P
    xs = C.dram_tmp("xspill", (128, NT*D), F32)
    with P.group("sync", reads=["x"], writes=["xspill"]) as g:
        for t in range(0, NT, 8):
            g.dma_(xs[:, t*D:(t+8)*D], C.x_tm[:, t*D:(t+8)*D])


def emit_reload_x(C, claims=()):
    P = C.P
    xs = C.dram_tmp("xspill", (128, NT*D), F32)
    with P.group("sync", reads=["xspill"], writes=["x"] + list(claims)) as g:
        for t in range(0, NT, 8):
            g.dma_(C.x_tm[:, t*D:(t+8)*D], xs[:, t*D:(t+8)*D])


def emit_prologue(C):
    y_tm = C.sb_at("y_tm_%d" % C.blk_i, (128, NT*D), BF16, OFF_ARENA)
    emit_layernorm(C, C.x_tm, y_tm, "x", "y_tm")
    emit_tm_to_fm(C, y_tm, C.y_fm, "y_tm", "y_fm")


def build_l1_ffn(C, tag):
    P = C.P
    S = 1.0 / 127.0
    H = 3072
    MC = H // 128
    wg_d = C.dram_in("l1ffn_wg_sb", (128, KC*H), I32)
    wu_d = C.dram_in("l1ffn_wu_sb", (128, MC*1024), I32)
    wd_d = C.dram_in("l1ffn_wd_sb", (128, MC*1024), I32)
    i = C.blk_i
    wu = C.sb_at("l1f_wu%d" % i, (128, MC*1024), BF16, OFF_X)
    hbuf = C.sb_at("l1f_h%d" % i, (128, MC*256), BF16, OFF_X + 49152)
    sgbuf = C.sb_at("l1f_sg%d" % i, (128, 256), F32, OFF_X + 61440)
    wg = C.sb_at("l1f_wg%d" % i, (128, KC*H), BF16, OFF_ARENA)
    wd = C.sb_at("l1f_wd%d" % i, (128, MC*1024), BF16, OFF_ARENA + 49152)
    xstage = C.sb_at("l1f_xs%d" % i, (128, 512), F32, OFF_ARENA + 98304)
    xsp = C.dram_tmp("xspill", (128, NT*D), F32)
    emit_spill_x(C)
    NLC = 4
    for q in range(NLC):
        with P.group("gpsimd", writes=["wg%d" % q, "y_tm"]) as g:
            g.dma_(wg[:, q*2*H:(q+1)*2*H], wg_d[:, q*2*H:(q+1)*2*H])
        with P.group("gpsimd", writes=["wu%d" % q] + (["x"] if q == 0 else [])) as g:
            g.dma_(wu[:, q*6144:(q+1)*6144], wu_d[:, q*6144:(q+1)*6144])
    for q in range(NLC):
        with P.group("gpsimd", writes=["wd%d" % q]) as g:
            g.dma_(wd[:, q*6144:(q+1)*6144], wd_d[:, q*6144:(q+1)*6144])
    PS = C.psum_f32

    for tc in range(0, NT, 2):
        for m in range(MC):
            ps_g, ps_u = PS[4][:, 0:256], PS[5][:, 0:256]
            lh = [wg[:, c*H + m*128: c*H + (m+1)*128] for c in range(KC)]
            rh = [C.y_fm[:, c*T + tc*128: (c*T) + (tc+2)*128] for c in range(KC)]
            emit_matmul_acc(C, ps_g, lh, rh,
                            reads=["wg%d" % (c*H//6144) for c in range(KC)] + ["y_fm"],
                            writes=["ps4"])
            lh = [wu[:, m*1024 + c*128: m*1024 + (c+1)*128] for c in range(KC)]
            emit_matmul_acc(C, ps_u, lh, rh, reads=["wu%d" % (m*1024//6144), "y_fm"],
                            writes=["ps5"])
            first = (tc == 0 and m == 0)
            with P.group("scalar", reads=["ps4"],
                         writes=["sg"] + (["x"] if first else [])) as g:
                g.emit(lambda s, ip=ps_g: s.activation(sgbuf[:, :], ip, AF.Silu, scale=S))
            with P.group("vector", reads=["ps5", "sg"],
                         writes=["h"] + (["x"] if first else [])) as g:
                g.emit(lambda v, ip=ps_u, o=hbuf[:, m*256:(m+1)*256]:
                       v.scalar_tensor_tensor(o, ip, S, sgbuf[:, :], ALU.mult, ALU.mult))
        for m in range(MC):
            with P.group("tensor", reads=["h", "wd%d" % (m*1024//6144)],
                         writes=["ps0", "ps1", "ps2", "ps3"]) as g:
                for tt in range(2):
                    for dh in range(2):
                        g.emit(lambda p, ps=PS[tt*2+dh][:, 0:512],
                               lt=hbuf[:, m*256 + tt*128: m*256 + (tt+1)*128],
                               rh=wd[:, m*1024 + dh*512: m*1024 + (dh+1)*512],
                               st=(m == 0), sp=(m == MC-1):
                               p.matmul(ps, lt, rh, start=st, stop=sp))
        for tt in range(2):
            for dh in range(2):
                ps = PS[tt*2 + dh][:, 0:512]
                with P.group("sync", reads=["xspill", "l1f_xst"], writes=["l1f_xst"]) as g:
                    g.dma_(xstage[:], xsp[:, (tc+tt)*D + dh*512: (tc+tt)*D + (dh+1)*512])
                with P.group("vector", reads=["ps%d" % (tt*2+dh), "l1f_xst"],
                             writes=["l1f_xst"]) as g:
                    g.emit(lambda v, ip=ps:
                           v.scalar_tensor_tensor(xstage[:], ip, S, xstage[:],
                                                  ALU.mult, ALU.add))
                with P.group("sync", reads=["l1f_xst"], writes=["xspill"]) as g:
                    g.dma_(xsp[:, (tc+tt)*D + dh*512: (tc+tt)*D + (dh+1)*512], xstage[:])
    emit_reload_x(C, claims=["wu0", "wu1", "wu2", "wu3", "h", "sg"])


def build_l1_attn_a(C, tag):
    P = C.P
    H = 16
    i = C.blk_i
    A = OFF_ARENA
    k_tm = C.sb_at("l1a_k%d" % i, (128, NT*D), BF16, A)
    v_tm = C.sb_at("l1a_v%d" % i, (128, NT*D), BF16, A + 32768)
    wsl = C.sb_at("l1a_w%d" % i, (128, KC*D), BF16, A + 65536)
    bkbc = C.sb_at("l1a_bk%d" % i, (128, D), F32, A + 81920)
    bvbc = C.sb_at("l1a_bv%d" % i, (128, D), F32, A + 86016)
    bobc = C.sb_at("l1a_bo%d" % i, (128, D), F32, A + 90112)
    bqc = C.sb_at("l1a_bq%d" % i, (128, KC), F32, A + 94208)
    kv_sb = C.sb_at("l1a_kv%d" % i, (128, 8*65), BF16, A + 94272)
    onesc = C.sb_at("l1a_one%d" % i, (128, 1), BF16, A + 95360)
    s1 = C.sb_at("l1a_s1%d" % i, (128, 512), F32, A + 95424)
    s2 = C.sb_at("l1a_s2%d" % i, (128, 512), F32, A + 97472)
    rzc = C.sb_at("l1a_rz%d" % i, (128, 1), F32, A + 99520)
    brow = C.sb_at("l1a_br%d" % i, (1, 512), F32, A + 99584)
    wq_d = C.dram_in("l1aa_wq", (128, KC*D), F32)
    wk_d = C.dram_in("l1aa_wk", (128, KC*D), F32)
    wv_d = C.dram_in("l1aa_wv", (128, KC*D), F32)
    wo_d = C.dram_in("l1aa_wo", (128, KC*D), F32)
    bq_d = C.dram_in("l1aa_bq", (128, KC), F32)
    bk_d = C.dram_in("l1aa_bk", (1, D), F32)
    bv_d = C.dram_in("l1aa_bv", (1, D), F32)
    bo_d = C.dram_in("l1aa_bo", (1, D), F32)
    PS = C.psum_f32
    kvps = C.ps_aux

    with P.group("gpsimd", writes=["bqc"]) as g:
        g.dma_(bqc[:], bq_d)
    with P.group("gpsimd", writes=["onesc"]) as g:
        g.emit(lambda g_: g_.memset(onesc[:], 1.0))
    for j, (bc, bd) in enumerate([(bkbc, bk_d), (bvbc, bv_d), (bobc, bo_d)]):
        for hh in range(2):
            pj = 4 + (2*j + hh) % 2
            with P.group("gpsimd", writes=["brow"]) as g:
                g.dma_(brow[:], bd[0:1, hh*512:(hh+1)*512])
            emit_matmul_acc(C, PS[pj][:, 0:512],
                            [C.ones_row[0:1, 0:128]], [brow[0:1, :]],
                            reads=["brow", "ones"], writes=["ps%d" % pj])
            with P.group("vector", reads=["ps%d" % pj], writes=["biasbc"]) as g:
                g.emit(lambda v, ps=PS[pj][:, 0:512], o=bc[:, hh*512:(hh+1)*512]:
                       v.tensor_copy(out=o, in_=ps))

    def load_w(dram, extra_claims=()):
        with P.group("gpsimd", writes=["wsl"] + list(extra_claims)) as g:
            for c in range(0, KC, 4):
                g.dma_(wsl[:, c*D:(c+4)*D], dram[:, c*D:(c+4)*D])

    def fmap_evac(ps, pskey, dsl, bias_args, dst_keys):
        with P.group("vector", reads=[pskey, "biasbc", "bqc"], writes=["s1"]) as g:
            if bias_args[0] == "col":
                g.emit(lambda v, ps=ps, b=bias_args[1]:
                       v.tensor_scalar(s1[:, 0:ps.shape[1]], ps, b, None, ALU.add))
            else:
                g.emit(lambda v, ps=ps, b=bias_args[1]:
                       v.tensor_add(s1[:, 0:ps.shape[1]], ps, b))
            g.emit(lambda v, n=ps.shape[1]: v.tensor_scalar_min(s2[:, 0:n], s1[:, 0:n], 0.0))
        with P.group("scalar", reads=["s1"], writes=["s2e"]) as g:
            g.emit(lambda s, n=ps.shape[1]: s.activation(s2[:, 0:n], s2[:, 0:n], AF.Exp))
        with P.group("vector", reads=["s2e", "s1"], writes=list(dst_keys)) as g:
            g.emit(lambda v, o=dsl, n=ps.shape[1]: v.scalar_tensor_tensor(
                o, s1[:, 0:n], 0.0, s2[:, 0:n], ALU.max, ALU.add))

    def proj_tm(dst, dst_key, bias_bc, fmap, extra_claims=()):
        for t in range(NT):
            for mh in range(2):
                pj = (2*t + mh) % 2
                ps = PS[pj][:, 0:512]
                lh = [C.y_fm[:, c*T + t*128:(c*T) + (t+1)*128] for c in range(KC)]
                rh = [wsl[:, c*D + mh*512: c*D + (mh+1)*512] for c in range(KC)]
                emit_matmul_acc(C, ps, lh, rh, reads=["wsl", "y_fm"],
                                writes=["ps%d" % pj])
                dsl = dst[:, t*D + mh*512: t*D + (mh+1)*512]
                bsl = bias_bc[:, mh*512:(mh+1)*512]
                if not fmap:
                    with P.group("vector", reads=["ps%d" % pj, "biasbc"],
                                 writes=[dst_key] + list(extra_claims)) as g:
                        g.emit(lambda v, ps=ps, o=dsl, b=bsl: v.tensor_add(o, ps, b))
                else:
                    fmap_evac(ps, "ps%d" % pj, dsl, ("bc", bsl),
                              [dst_key] + list(extra_claims))

    load_w(wk_d, ["y_tm"])
    proj_tm(k_tm, "k_tm", bkbc, True)
    load_w(wv_d)
    proj_tm(v_tm, "v_tm", bvbc, False)
    for h in range(H):
        p0 = (h % 2) * 64
        tp = (0, p0)
        out_ap = kvps[p0:p0+64, 0:65]
        with P.group("tensor", reads=["k_tm", "v_tm", "onesc"], writes=["ps_aux"]) as g:
            for t in range(NT):
                g.emit(lambda p, lt=k_tm[:, t*D + h*64: t*D + (h+1)*64],
                       rh=v_tm[:, t*D + h*64: t*D + (h+1)*64], st=(t == 0),
                       sp=(t == NT-1), o=out_ap[:, 0:64], tp=tp:
                       p.matmul(o, lt, rh, start=st, stop=sp, tile_position=tp))
            for t in range(NT):
                g.emit(lambda p, lt=k_tm[:, t*D + h*64: t*D + (h+1)*64], st=(t == 0),
                       sp=(t == NT-1), o=out_ap[:, 64:65], tp=tp:
                       p.matmul(o, lt, onesc[:], start=st, stop=sp, tile_position=tp))
        with P.group("vector", reads=["ps_aux"], writes=["kv_sb"]) as g:
            g.emit(lambda v, o=kv_sb[p0:p0+64, (h//2)*65:(h//2+1)*65], s=out_ap:
                   v.tensor_copy(out=o, in_=s))
    q_fm = C.sb_at("l1a_q%d" % i, (128, KC*T), BF16, A)
    load_w(wq_d)
    for cc in range(KC):
        for tq in range(4):
            pj = (cc + tq) % 2
            ps = PS[pj][:, 0:512]
            lh = [wsl[:, c*D + cc*128: c*D + (cc+1)*128] for c in range(KC)]
            rh = [C.y_fm[:, c*T + tq*512:(c*T) + (tq+1)*512] for c in range(KC)]
            emit_matmul_acc(C, ps, lh, rh, reads=["wsl", "y_fm"],
                            writes=["ps%d" % pj])
            dsl = q_fm[:, cc*T + tq*512: cc*T + (tq+1)*512]
            fmap_evac(ps, "ps%d" % pj, dsl, ("col", bqc[:, cc:cc+1]),
                      ["q_fm", "k_tm"])
    out_tm = C.sb_at("l1a_o%d" % i, (128, NT*D), BF16, A + 32768)
    for t in range(NT):
        for h in range(H):
            p0 = (h % 2) * 64
            pj = h % 4
            ps = PS[pj][:, 0:65]
            with P.group("tensor", reads=["q_fm", "kv_sb"], writes=["ps%d" % pj]) as g:
                g.emit(lambda p, ps=ps,
                       lt=q_fm[p0:p0+64, (h//2)*T + t*128:(h//2)*T + (t+1)*128],
                       rh=kv_sb[p0:p0+64, (h//2)*65:(h//2+1)*65]:
                       p.matmul(ps, lt, rh))
            with P.group("vector", reads=["ps%d" % pj], writes=["out_tm", "v_tm"]) as g:
                g.emit(lambda v, ps=ps: v.tensor_scalar_max(rzc[:], ps[:, 64:65], 1e-6))
                g.emit(lambda v: v.reciprocal(rzc[:], rzc[:]))
                g.emit(lambda v, ps=ps, o=out_tm[:, t*D + h*64: t*D + (h+1)*64]:
                       v.tensor_scalar(o, ps[:, 0:64], rzc[:], None, ALU.mult))
    out_fm = C.sb_at("l1a_of%d" % i, (128, KC*T), BF16, A)
    emit_tm_to_fm(C, out_tm, out_fm, "out_tm", "out_fm")
    load_w(wo_d)
    for t in range(NT):
        for mh in range(2):
            pj = (2*t + mh) % 2
            ps = PS[pj][:, 0:512]
            lh = [out_fm[:, c*T + t*128:(c*T) + (t+1)*128] for c in range(KC)]
            rh = [wsl[:, c*D + mh*512: c*D + (mh+1)*512] for c in range(KC)]
            emit_matmul_acc(C, ps, lh, rh, reads=["wsl", "out_fm"],
                            writes=["ps%d" % pj])
            xs = C.x_tm[:, t*D + mh*512: t*D + (mh+1)*512]
            with P.group("vector", reads=["ps%d" % pj, "biasbc", "x"],
                         writes=["x"]) as g:
                g.emit(lambda v, ps=ps, b=bobc[:, mh*512:(mh+1)*512]:
                       v.tensor_add(s1[:], ps, b))
                g.emit(lambda v, o=xs: v.tensor_add(o, o, s1[:]))


def prep_l1_attn_a(inputs, h):
    for nm, key in [("Wq", "l1aa_wq"), ("Wk", "l1aa_wk"), ("Wv", "l1aa_wv"),
                    ("Wo", "l1aa_wo")]:
        w = np.asarray(inputs["L1_attn_a_" + nm], np.float32)
        h[key] = sb_image_kc(np.ascontiguousarray(w))
    h["l1aa_bq"] = np.ascontiguousarray(
        np.asarray(inputs["L1_attn_a_bq"], np.float32).reshape(KC, 128).T)
    h["l1aa_bk"] = np.asarray(inputs["L1_attn_a_bk"], np.float32).reshape(1, D)
    h["l1aa_bv"] = np.asarray(inputs["L1_attn_a_bv"], np.float32).reshape(1, D)
    h["l1aa_bo"] = np.asarray(inputs["L1_attn_a_bo"], np.float32).reshape(1, D)


def build_l0_attn_b(C, tag):
    P = C.P
    H = 16
    i = C.blk_i
    A = OFF_ARENA
    KROW = T + 256
    q_fm = C.sb_at("l0b_q%d" % i, (128, KC*T), BF16, OFF_X)
    out_fm = C.sb_at("l0b_of%d" % i, (128, KC*T), BF16, OFF_YFM)
    k_fm = C.sb_at("l0b_k%d" % i, (128, KC*KROW), BF16, A)
    v_tm = C.sb_at("l0b_v%d" % i, (128, 18*D), BF16, A + 36864)
    wsl = C.sb_at("l0b_w%d" % i, (128, KC*D), BF16, A + 73728)
    madd = C.sb_at("l0b_m%d" % i, (128, 384), F32, A + 90112)
    s1 = C.sb_at("l0b_s1%d" % i, (128, 384), F32, A + 91648)
    attn_sb = C.sb_at("l0b_a%d" % i, (128, 384), BF16, A + 93184)
    attnT_sb = C.sb_at("l0b_at%d" % i, (128, 384), BF16, A + 93952)
    bobc = C.sb_at("l0b_bo%d" % i, (128, D), F32, A + 94720)
    mcol = C.sb_at("l0b_mc%d" % i, (128, 4), F32, A + 98816)
    brow = C.sb_at("l0b_br%d" % i, (1, 512), F32, A + 98848)
    wq_d = C.dram_in("l0ab_wq", (128, KC*D), F32)
    wk_d = C.dram_in("l0ab_wk", (128, KC*D), F32)
    wv_d = C.dram_in("l0ab_wv", (128, KC*D), F32)
    wo_d = C.dram_in("l0ab_wo", (128, KC*D), F32)
    bo_d = C.dram_in("l0ab_bo", (1, D), F32)
    PS = C.psum_f32

    emit_spill_x(C)

    def load_w(dram, extra_claims=()):
        with P.group("gpsimd", writes=["wsl"] + list(extra_claims)) as g:
            for c in range(0, KC, 4):
                g.dma_(wsl[:, c*D:(c+4)*D], dram[:, c*D:(c+4)*D])

    for hh in range(2):
        with P.group("gpsimd", writes=["brow"]) as g:
            g.dma_(brow[:], bo_d[0:1, hh*512:(hh+1)*512])
        pj = 4 + hh
        emit_matmul_acc(C, PS[pj][:, 0:512], [C.ones_row[0:1, 0:128]], [brow[0:1, :]],
                        reads=["brow", "ones"], writes=["ps%d" % pj])
        with P.group("vector", reads=["ps%d" % pj], writes=["biasbc"]) as g:
            g.emit(lambda v, ps=PS[pj][:, 0:512], o=bobc[:, hh*512:(hh+1)*512]:
                   v.tensor_copy(out=o, in_=ps))
    with P.group("gpsimd", writes=["madd", "s1g"]) as g:
        g.emit(lambda g_: g_.iota(madd[:], [[1, 384]], base=0, channel_multiplier=-1,
                                  allow_small_or_imprecise_dtypes=True))
        g.emit(lambda g_: g_.tensor_single_scalar(s1[:], madd[:], 96.0, ALU.is_ge))
        g.emit(lambda g_: g_.tensor_single_scalar(madd[:], madd[:], 159.0, ALU.is_le))
        g.emit(lambda g_: g_.tensor_mul(madd[:], madd[:], s1[:]))
        g.emit(lambda g_: g_.tensor_scalar_add(madd[:], madd[:], -1.0))
        g.emit(lambda g_: g_.tensor_scalar_mul(madd[:], madd[:], 1e30))

    load_w(wq_d)
    for cc in range(KC):
        for tq in range(4):
            pj = (cc + tq) % 2
            ps = PS[pj][:, 0:512]
            lh = [wsl[:, c*D + cc*128: c*D + (cc+1)*128] for c in range(KC)]
            rh = [C.y_fm[:, c*T + tq*512:(c*T) + (tq+1)*512] for c in range(KC)]
            emit_matmul_acc(C, ps, lh, rh, reads=["wsl", "y_fm"], writes=["ps%d" % pj])
            with P.group("vector", reads=["ps%d" % pj], writes=["q_fm", "x"]) as g:
                g.emit(lambda v, ps=ps, o=q_fm[:, cc*T + tq*512: cc*T + (tq+1)*512]:
                       v.tensor_copy(out=o, in_=ps))
    load_w(wk_d, ["y_tm"])
    with P.group("gpsimd", writes=["k_fm"]) as g:
        for c in range(KC):
            g.emit(lambda g_, c=c: g_.memset(k_fm[:, c*KROW: c*KROW + 128], 0.0))
            g.emit(lambda g_, c=c: g_.memset(k_fm[:, c*KROW + 128 + T: (c+1)*KROW], 0.0))
    for cc in range(KC):
        for tq in range(4):
            pj = (cc + tq) % 2
            ps = PS[pj][:, 0:512]
            lh = [wsl[:, c*D + cc*128: c*D + (cc+1)*128] for c in range(KC)]
            rh = [C.y_fm[:, c*T + tq*512:(c*T) + (tq+1)*512] for c in range(KC)]
            emit_matmul_acc(C, ps, lh, rh, reads=["wsl", "y_fm"], writes=["ps%d" % pj])
            with P.group("vector", reads=["ps%d" % pj], writes=["k_fm"]) as g:
                g.emit(lambda v, ps=ps,
                       o=k_fm[:, cc*KROW + 128 + tq*512: cc*KROW + 128 + (tq+1)*512]:
                       v.tensor_copy(out=o, in_=ps))
    load_w(wv_d)
    with P.group("gpsimd", writes=["v_tm"]) as g:
        g.emit(lambda g_: g_.memset(v_tm[:, 0:D], 0.0))
        g.emit(lambda g_: g_.memset(v_tm[:, 17*D:18*D], 0.0))
    for t in range(NT):
        for mh in range(2):
            pj = (2*t + mh) % 2
            ps = PS[pj][:, 0:512]
            lh = [C.y_fm[:, c*T + t*128:(c*T) + (t+1)*128] for c in range(KC)]
            rh = [wsl[:, c*D + mh*512: c*D + (mh+1)*512] for c in range(KC)]
            emit_matmul_acc(C, ps, lh, rh, reads=["wsl", "y_fm"], writes=["ps%d" % pj])
            with P.group("vector", reads=["ps%d" % pj], writes=["v_tm"]) as g:
                g.emit(lambda v, ps=ps,
                       o=v_tm[:, (t+1)*D + mh*512: (t+1)*D + (mh+1)*512]:
                       v.tensor_copy(out=o, in_=ps))
    for h in range(L0B_HLIM):
        p0 = (h % 2) * 64
        cc = h // 2
        for t in range(L0B_TLIM):
            pj = (h + t) % 2
            ps = PS[pj][:, 0:384]
            with P.group("tensor", reads=["q_fm", "k_fm"], writes=["ps%d" % pj]) as g:
                g.emit(lambda p, ps=ps,
                       lt=q_fm[p0:p0+64, cc*T + t*128: cc*T + (t+1)*128],
                       rh=k_fm[p0:p0+64, cc*KROW + t*128: cc*KROW + t*128 + 384]:
                       p.matmul(ps, lt, rh))
            with P.group("vector", reads=["ps%d" % pj, "madd"], writes=["s1g"]) as g:
                g.emit(lambda v, ps=ps: v.scalar_tensor_tensor(
                    s1[:], ps, 0.125, madd[:], ALU.mult, ALU.add))
                g.emit(lambda v: v.reduce_max(mcol[:, 0:1], s1[:], AX.X))
                g.emit(lambda v: v.tensor_scalar_mul(mcol[:, 1:2], mcol[:, 0:1], -1.0))
            with P.group("scalar", reads=["s1g"], writes=["s1e"]) as g:
                g.emit(lambda s: s.activation(s1[:], s1[:], AF.Exp,
                                              bias=mcol[:, 1:2], accum_out=mcol[:, 2:3]))
            with P.group("vector", reads=["s1e"], writes=["attn_sb"]) as g:
                g.emit(lambda v: v.reciprocal(mcol[:, 3:4], mcol[:, 2:3]))
                g.emit(lambda v: v.tensor_scalar(attn_sb[:], s1[:], mcol[:, 3:4],
                                                 None, ALU.mult))
            with P.group("tensor", reads=["attn_sb"], writes=["ps_tr"]) as g:
                for jj in range(3):
                    g.emit(lambda p, jj=jj: p.transpose(
                        C.ps_tr[:, jj*128:(jj+1)*128],
                        attn_sb[:, jj*128:(jj+1)*128], C.ident[:]))
            with P.group("vector", reads=["ps_tr"], writes=["attnT"]) as g:
                g.emit(lambda v: v.tensor_copy(out=attnT_sb[:], in_=C.ps_tr[:, 0:384]))
            po = C.ps_aux[p0:p0+64, 0:128]
            with P.group("tensor", reads=["attnT", "v_tm"], writes=["ps_aux"]) as g:
                for jj in range(3):
                    g.emit(lambda p, po=po, tp=(0, p0),
                           lt=v_tm[:, (t+jj)*D + h*64: (t+jj)*D + (h+1)*64],
                           rh=attnT_sb[:, jj*128:(jj+1)*128],
                           st=(jj == 0), sp=(jj == 2):
                           p.matmul(po, lt, rh, start=st, stop=sp, tile_position=tp))
            with P.group("vector", reads=["ps_aux"], writes=["out_fm", "y_fm"]) as g:
                g.emit(lambda v, po=po,
                       o=out_fm[p0:p0+64, cc*T + t*128: cc*T + (t+1)*128]:
                       v.tensor_copy(out=o, in_=po))
            if L0B_HLIM == 1:
                with P.group("vector", reads=["ps_aux"], writes=["sb_sq"]) as g:
                    g.emit(lambda v, po=po: v.tensor_copy(out=C.sb_sq[p0:p0+64, 0:128], in_=po))
    emit_reload_x(C, claims=["q_fm"])
    load_w(wo_d)
    for t in range(NT):
        for mh in range(2):
            pj = (2*t + mh) % 2
            ps = PS[pj][:, 0:512]
            lh = [out_fm[:, c*T + t*128:(c*T) + (t+1)*128] for c in range(KC)]
            rh = [wsl[:, c*D + mh*512: c*D + (mh+1)*512] for c in range(KC)]
            emit_matmul_acc(C, ps, lh, rh, reads=["wsl", "out_fm"], writes=["ps%d" % pj])
            xs = C.x_tm[:, t*D + mh*512: t*D + (mh+1)*512]
            with P.group("vector", reads=["ps%d" % pj, "biasbc", "x"], writes=["x"]) as g:
                g.emit(lambda v, ps=ps, b=bobc[:, mh*512:(mh+1)*512]:
                       v.tensor_add(s1[:, 0:256] if False else C.sb_sq[:, 0:512], ps, b))
                g.emit(lambda v, o=xs: v.tensor_add(o, o, C.sb_sq[:, 0:512]))


def prep_l0_attn_b(inputs, h):
    for nm, key in [("Wq", "l0ab_wq"), ("Wk", "l0ab_wk"), ("Wv", "l0ab_wv"),
                    ("Wo", "l0ab_wo")]:
        w = np.asarray(inputs["L0_attn_b_" + nm], np.float32).T
        h[key] = sb_image_kc(np.ascontiguousarray(w))
    h["l0ab_bo"] = np.asarray(inputs["L0_attn_b_bo"], np.float32).reshape(1, D)


def build_l0_attn_a(C, tag):
    P = C.P
    M = 256
    i = C.blk_i
    A = OFF_ARENA
    CSC = 1.0 / (16.0 * float(np.sqrt(T)))
    k_tm = C.sb_at("l0a_k%d" % i, (128, NT*M), BF16, A)
    q_fm = C.sb_at("l0a_q%d" % i, (128, 2*T), BF16, A + 8192)
    v_fm = C.sb_at("l0a_v%d" % i, (128, 2*T), BF16, A + 16384)
    r_fm = C.sb_at("l0a_r%d" % i, (128, 2*T), BF16, A + 24576)
    cumbc = C.sb_at("l0a_cb%d" % i, (128, T), F32, A + 32768)
    cumT = C.sb_at("l0a_ct%d" % i, (128, NT), F32, A + 40960)
    masks = C.sb_at("l0a_mk%d" % i, (128, 4*512), F32, A + 41024)
    s1 = C.sb_at("l0a_s1%d" % i, (128, 512), F32, A + 49216)
    s2 = C.sb_at("l0a_s2%d" % i, (128, 512), F32, A + 51264)
    w16 = C.sb_at("l0a_w16%d" % i, (128, 512), BF16, A + 53312)
    wk = C.sb_at("l0a_wk%d" % i, (128, KC*M), BF16, A + 54336)
    wq = C.sb_at("l0a_wq%d" % i, (128, KC*M), BF16, A + 58432)
    wv = C.sb_at("l0a_wv%d" % i, (128, KC*M), BF16, A + 62528)
    wo = C.sb_at("l0a_wo%d" % i, (128, 2*D), BF16, A + 66624)
    wg = C.sb_at("l0a_wg%d" % i, (128, KC), BF16, A + 70720)
    grow = C.sb_at("l0a_g%d" % i, (1, T), F32, A + 70784)
    onerow = C.sb_at("l0a_or%d" % i, (1, T), F32, A + 78976)
    bgc = C.sb_at("l0a_bg%d" % i, (1, 1), F32, A + 87168)
    wk_d = C.dram_in("l0aa_wkT", (128, KC*M), F32)
    wq_d = C.dram_in("l0aa_wqT", (128, KC*M), F32)
    wv_d = C.dram_in("l0aa_wvT", (128, KC*M), F32)
    wo_d = C.dram_in("l0aa_woT", (128, 2*D), F32)
    wg_d = C.dram_in("l0aa_wgT", (128, KC), F32)
    bg_d = C.dram_in("l0aa_bg", (1, 1), F32)
    PS = C.psum_f32

    with P.group("gpsimd", writes=["l0a_w", "y_tm"]) as g:
        g.dma_(wk[:], wk_d)
        g.dma_(wq[:], wq_d)
        g.dma_(wv[:], wv_d)
        g.dma_(wo[:], wo_d)
        g.dma_(wg[:], wg_d)
        g.dma_(bgc[:], bg_d)
    with P.group("gpsimd", writes=["l0a_ones", "l0a_masks", "l0a_s1"]) as g:
        g.emit(lambda g_: g_.memset(onerow[:], 1.0))
        g.emit(lambda g_: g_.iota(s1[:], [[1, 512]], base=0, channel_multiplier=-1,
                                  allow_small_or_imprecise_dtypes=True))
        for r in range(4):
            g.emit(lambda g_, r=r: g_.tensor_single_scalar(
                masks[:, r*512:(r+1)*512], s1[:], float(r*128), ALU.is_ge))
        g.emit(lambda g_: g_.tensor_scalar_mul(masks[:], masks[:], CSC))
    for mc in range(2):
        for tq in range(4):
            for dst, w, key in [(q_fm, wq, "q_fm"), (v_fm, wv, "v_fm")]:
                pj = (mc*4 + tq) % 2
                ps = PS[pj][:, 0:512]
                lh = [w[:, c*M + mc*128: c*M + (mc+1)*128] for c in range(KC)]
                rh = [C.y_fm[:, c*T + tq*512:(c*T) + (tq+1)*512] for c in range(KC)]
                emit_matmul_acc(C, ps, lh, rh, reads=["l0a_w", "y_fm"],
                                writes=["ps%d" % pj])
                with P.group("vector", reads=["ps%d" % pj], writes=[key]) as g:
                    g.emit(lambda v, ps=ps, o=dst[:, mc*T + tq*512: mc*T + (tq+1)*512]:
                           v.tensor_copy(out=o, in_=ps))
    for t in range(NT):
        pj = t % 2
        ps = PS[pj][:, 0:M]
        lh = [C.y_fm[:, c*T + t*128:(c*T) + (t+1)*128] for c in range(KC)]
        rh = [wk[:, c*M:(c+1)*M] for c in range(KC)]
        emit_matmul_acc(C, ps, lh, rh, reads=["l0a_w", "y_fm"], writes=["ps%d" % pj])
        with P.group("vector", reads=["ps%d" % pj], writes=["k_tm"]) as g:
            g.emit(lambda v, ps=ps, o=k_tm[:, t*M:(t+1)*M]: v.tensor_copy(out=o, in_=ps))
    for tq in range(4):
        ps = PS[4][:, 0:512]
        lh = [wg[:, c:c+1] for c in range(KC)]
        rh = [C.y_fm[:, c*T + tq*512:(c*T) + (tq+1)*512] for c in range(KC)]
        emit_matmul_acc(C, ps[0:1, :], lh, rh, reads=["l0a_w", "y_fm"], writes=["ps4"])
        with P.group("scalar", reads=["ps4", "l0a_w", "consts"], writes=["grow"]) as g:
            g.emit(lambda s, ps=ps, o=grow[0:1, tq*512:(tq+1)*512]:
                   s.activation(o, ps[0:1, :], AF.Sigmoid, bias=bgc[0:1, 0:1]))
            g.emit(lambda s, o=grow[0:1, tq*512:(tq+1)*512]:
                   s.activation(o, o, AF.Ln, bias=1e-8))
    with P.group("vector", reads=["grow", "l0a_ones"], writes=["grow"]) as g:
        g.emit(lambda v: v.tensor_tensor_scan(grow[0:1, :], onerow[0:1, :],
                                              grow[0:1, :], 0.0, ALU.mult, ALU.add))
    for tq in range(4):
        pj = 4 + tq % 2
        emit_matmul_acc(C, PS[pj][:, 0:512], [C.ones_row[0:1, 0:128]],
                        [grow[0:1, tq*512:(tq+1)*512]],
                        reads=["grow", "ones"], writes=["ps%d" % pj])
        with P.group("vector", reads=["ps%d" % pj], writes=["cumbc"]) as g:
            g.emit(lambda v, ps=PS[pj][:, 0:512], o=cumbc[:, tq*512:(tq+1)*512]:
                   v.tensor_copy(out=o, in_=ps))
    with P.group("tensor", reads=["grow", "ident"], writes=["ps_aux"]) as g:
        for t in range(NT):
            g.emit(lambda p, t=t: p.transpose(C.ps_aux[:, t:t+1],
                                              grow[0:1, t*128:(t+1)*128],
                                              C.identf[0:1, 0:1]))
    with P.group("vector", reads=["ps_aux"], writes=["cumT"]) as g:
        g.emit(lambda v: v.tensor_copy(out=cumT[:], in_=C.ps_aux[:, 0:NT]))
    for jc in range(4):
        first = True
        for it in range(0, (jc+1)*4):
            pj = it % 2
            ps = PS[pj][:, 0:512]
            lh = [v_fm[:, mc2*T + it*128: mc2*T + (it+1)*128] for mc2 in range(2)]
            rh = [q_fm[:, mc2*T + jc*512: mc2*T + (jc+1)*512] for mc2 in range(2)]
            emit_matmul_acc(C, ps, lh, rh, reads=["q_fm", "v_fm"], writes=["ps%d" % pj])
            with P.group("vector", reads=["cumbc", "cumT"], writes=["l0a_s1"]) as g:
                g.emit(lambda v, j0=jc*512, it=it: v.tensor_scalar(
                    s1[:], cumbc[:, j0:j0+512], cumT[:, it:it+1], 0.0,
                    ALU.subtract, ALU.min))
            with P.group("scalar", reads=["l0a_s1"], writes=["l0a_s2"]) as g:
                g.emit(lambda s: s.activation(s2[:], s1[:], AF.Exp))
            diag = (it // 4 == jc)
            with P.group("vector", reads=["l0a_s2", "ps%d" % pj, "l0a_masks"],
                         writes=["l0a_w16"] + (["l0a_s1"] if diag else [])) as g:
                if diag:
                    g.emit(lambda v, ps=ps: v.tensor_mul(s1[:], ps, s2[:]))
                    g.emit(lambda v, r=it % 4: v.tensor_tensor(
                        w16[:], s1[:], masks[:, r*512:(r+1)*512], ALU.mult))
                else:
                    g.emit(lambda v, ps=ps: v.scalar_tensor_tensor(
                        w16[:], ps, CSC, s2[:], ALU.mult, ALU.mult))
            for mc in range(2):
                with P.group("tensor", reads=["l0a_w16", "k_tm"],
                             writes=["ps%d" % (2+mc)]) as g:
                    g.emit(lambda p, lt=k_tm[:, it*M + mc*128: it*M + (mc+1)*128],
                           ps=PS[2+mc][:, 0:512],
                           st=first, sp=(it == (jc+1)*4 - 1):
                           p.matmul(ps, lt, w16[:], start=st, stop=sp))
            first = False
        for mc in range(2):
            with P.group("vector", reads=["ps%d" % (2+mc)], writes=["r_fm"]) as g:
                g.emit(lambda v, ps=PS[2+mc][:, 0:512],
                       o=r_fm[:, mc*T + jc*512: mc*T + (jc+1)*512]:
                       v.tensor_copy(out=o, in_=ps))
    for t in range(NT):
        for dh in range(2):
            pj = (2*t + dh) % 2
            ps = PS[pj][:, 0:512]
            lh = [r_fm[:, mc*T + t*128: mc*T + (t+1)*128] for mc in range(2)]
            rh = [wo[:, mc*D + dh*512: mc*D + (dh+1)*512] for mc in range(2)]
            emit_matmul_acc(C, ps, lh, rh, reads=["r_fm", "l0a_w"], writes=["ps%d" % pj])
            xs = C.x_tm[:, t*D + dh*512: t*D + (dh+1)*512]
            with P.group("vector", reads=["ps%d" % pj, "x"], writes=["x"]) as g:
                g.emit(lambda v, ps=ps, o=xs: v.tensor_add(o, o, ps))


def prep_l0_attn_a(inputs, h):
    h["l0aa_wkT"] = sb_image_kc(np.ascontiguousarray(
        np.asarray(inputs["L0_attn_a_Wk"], np.float32).T))
    h["l0aa_wqT"] = sb_image_kc(np.ascontiguousarray(
        np.asarray(inputs["L0_attn_a_Wq"], np.float32).T))
    h["l0aa_wvT"] = sb_image_kc(np.ascontiguousarray(
        np.asarray(inputs["L0_attn_a_Wv"], np.float32).T))
    h["l0aa_woT"] = sb_image_kc(np.ascontiguousarray(
        np.asarray(inputs["L0_attn_a_Wo"], np.float32).T), kc=2)
    h["l0aa_wgT"] = sb_image_kc(np.ascontiguousarray(
        np.asarray(inputs["L0_attn_a_Wg"], np.float32).T))
    h["l0aa_bg"] = np.asarray(inputs["L0_attn_a_bg"], np.float32).reshape(1, 1)


def build_l1_attn_b(C, tag):
    P = C.P
    DI = 2048
    MCI = DI // 128
    DS = 16
    DR = 64
    i = C.blk_i
    A = OFF_ARENA
    xc = C.sb_at("l1b_xc%d" % i, (128, MCI*T), BF16, OFF_X)
    winr = [C.sb_at("l1b_wi%d_%d" % (i, r), (128, KC*128), BF16, A + r*2048)
            for r in range(4)]
    stage = C.sb_at("l1b_st%d" % i, (128, 516), F32, A + 8192)
    sres_st = C.sb_at("l1b_sr%d" % i, (128, T), BF16, A + 10304)
    binc = C.sb_at("l1b_bi%d" % i, (128, 32), F32, A + 14400)
    cwc = C.sb_at("l1b_cw%d" % i, (128, 64), F32, A + 14528)
    cbc = C.sb_at("l1b_cb%d" % i, (128, 16), F32, A + 14784)
    alog = C.sb_at("l1b_al%d" % i, (128, 256), F32, A + 14848)
    asb = C.sb_at("l1b_as%d" % i, (128, 256), F32, A + 15872)
    dpc = C.sb_at("l1b_dp%d" % i, (128, 16), F32, A + 16896)
    bdtc = C.sb_at("l1b_bd%d" % i, (128, 16), F32, A + 16960)
    wxc = C.sb_at("l1b_wx%d" % i, (128, MCI*96), BF16, A + 17024)
    wdt = C.sb_at("l1b_wd%d" % i, (64, DI), BF16, A + 20096)
    xdbl = C.sb_at("l1b_xd%d" % i, (128, NT*96), F32, A + 24192)
    dbct = C.sb_at("l1b_db%d" % i, (96, NT*128), F32, A + 30336)
    delta = C.sb_at("l1b_dl%d" % i, (128, T), F32, A + 38528)
    dtx = C.sb_at("l1b_dx%d" % i, (128, T), F32, A + 46720)
    dA = C.sb_at("l1b_da%d" % i, (128, T), F32, A + 54912)
    ubuf = C.sb_at("l1b_u%d" % i, (128, T), F32, A + 63104)
    hbuf = C.sb_at("l1b_h%d" % i, (128, T), F32, A + 71296)
    hcbuf = C.sb_at("l1b_hc%d" % i, (128, T), F32, A + 79488)
    ybuf = C.sb_at("l1b_y%d" % i, (128, T), F32, A + 87680)
    wour = [C.sb_at("l1b_wo%d_%d" % (i, r), (128, D), BF16, A + 95872 + r*2048)
            for r in range(4)]
    drt_sb = C.sb_at("l1b_drt%d" % i, (64, T), BF16, A + 95872)
    bbc = C.sb_at("l1b_bb%d" % i, (128, T), F32, OFF_YFM)
    cbc2 = C.sb_at("l1b_cc%d" % i, (128, T), F32, OFF_YFM + 8192)
    bobc = C.sb_at("l1b_bo%d" % i, (128, D), F32, OFF_YFM + 16384)
    xstage = C.sb_at("l1b_xs%d" % i, (128, D), F32, OFF_YFM + 20480)
    ostage = C.sb_at("l1b_os%d" % i, (128, D), F32, OFF_YFM + 24576)
    brow = C.sb_at("l1b_br%d" % i, (1, 512), F32, OFF_YFM + 28672)
    win_d = C.dram_in("l1ab_win", (128, 32*KC*128), F32)
    binc_d = C.dram_in("l1ab_bin", (128, 32), F32)
    cw_d = C.dram_in("l1ab_cw", (128, 64), F32)
    cb_d = C.dram_in("l1ab_cbc", (128, 16), F32)
    wx_d = C.dram_in("l1ab_wx", (128, MCI*96), F32)
    wdt_d = C.dram_in("l1ab_wdt", (64, DI), F32)
    bdt_d = C.dram_in("l1ab_bdt", (128, 16), F32)
    alog_d = C.dram_in("l1ab_alog", (128, 256), F32)
    dp_d = C.dram_in("l1ab_dp", (128, 16), F32)
    wout_d = C.dram_in("l1ab_wout", (128, MCI*D), F32)
    bo_d = C.dram_in("l1ab_bo", (1, D), F32)
    bct_dr = C.dram_tmp("l1b_bct", (32, T), F32)
    sres_dr = C.dram_tmp("l1b_sres", (128, MCI*T), BF16)
    xsp = C.dram_tmp("xspill", (128, NT*D), F32)
    PS = C.psum_f32

    emit_spill_x(C)
    with P.group("gpsimd", writes=["l1b_small", "y_tm"]) as g:
        g.dma_(binc[:], binc_d)
        g.dma_(cwc[:], cw_d)
        g.dma_(cbc[:], cb_d)
        g.dma_(alog[:], alog_d)
        g.dma_(dpc[:], dp_d)
        g.dma_(bdtc[:], bdt_d)
        g.dma_(wxc[:], wx_d)
        g.dma_(wdt[:], wdt_d)
    with P.group("scalar", reads=["l1b_small"], writes=["l1b_asb"]) as g:
        g.emit(lambda s: s.activation(asb[:], alog[:], AF.Exp))
    with P.group("vector", reads=["l1b_asb"], writes=["l1b_asb"]) as g:
        g.emit(lambda v: v.tensor_scalar_mul(asb[:], asb[:], -1.0))
    def load_win(mc):
        with P.group("gpsimd", writes=["winr%d" % (mc % 4)] + (["y_tm"] if mc < 4 else [])) as g:
            g.dma_(winr[mc % 4][:], win_d[:, mc*1024:(mc+1)*1024])

    for mc in range(4):
        load_win(mc)
    for mc in range(32):
        isx = mc < MCI
        c = mc if isx else mc - MCI
        for tq in range(4):
            pj = (mc + tq) % 2
            ps = PS[pj][:, 0:512]
            wsl = winr[mc % 4]
            lh = [wsl[:, cc*128:(cc+1)*128] for cc in range(KC)]
            rh = [C.y_fm[:, cc*T + tq*512:(cc*T) + (tq+1)*512] for cc in range(KC)]
            emit_matmul_acc(C, ps, lh, rh, reads=["winr%d" % (mc % 4), "y_fm"],
                            writes=["ps%d" % pj])
            if isx:
                with P.group("vector", reads=["ps%d" % pj, "l1b_small", "l1b_stage"],
                             writes=(["l1b_stage", "y_tm"] if mc == 0 and tq == 0
                                     else ["l1b_stage"])) as g:
                    if tq == 0:
                        g.emit(lambda v: v.memset(stage[:, 0:3], 0.0))
                    else:
                        g.emit(lambda v: v.tensor_copy(out=stage[:, 0:3],
                                                       in_=stage[:, 512:515]))
                    g.emit(lambda v, ps=ps, b=binc[:, c:c+1]:
                           v.tensor_scalar(stage[:, 3:515], ps, b, None, ALU.add))
                    g.emit(lambda v, w=cwc[:, c*4+3:c*4+4]:
                           v.tensor_scalar(stage[:, 0:0] if False else ubuf[:, 0:512],
                                           stage[:, 3:515], w, None, ALU.mult))
                    for k in range(3):
                        g.emit(lambda v, k=k, w=cwc[:, c*4+2-k:c*4+3-k]:
                               v.scalar_tensor_tensor(ubuf[:, 0:512], stage[:, 2-k:514-k],
                                                      w, ubuf[:, 0:512], ALU.mult, ALU.add))
                with P.group("scalar", reads=["l1b_stage", "l1b_small"],
                             writes=(["xc", "x"] if mc == 0 and tq == 0 else ["xc"])) as g:
                    g.emit(lambda s, o=xc[:, c*T + tq*512: c*T + (tq+1)*512],
                           b=cbc[:, c:c+1]:
                           s.activation(o, ubuf[:, 0:512], AF.Silu, bias=b))
            else:
                with P.group("scalar", reads=["ps%d" % pj, "l1b_small"],
                             writes=(["l1b_srst", "y_tm"] if mc == 16 and tq == 0
                                     else ["l1b_srst"])) as g:
                    g.emit(lambda s, ps=ps, o=sres_st[:, tq*512:(tq+1)*512],
                           b=binc[:, 16+c:17+c]:
                           s.activation(o, ps, AF.Silu, bias=b))
        if not isx:
            with P.group("sync", reads=["l1b_srst"], writes=["l1b_sres_dr"]) as g:
                g.dma_(sres_dr[:, c*T:(c+1)*T], sres_st[:])
        if mc + 4 < 32:
            load_win(mc + 4)
    for t in range(NT):
        pj = t % 2
        ps = PS[pj][:, 0:96]
        lh = [xc[:, c*T + t*128: c*T + (t+1)*128] for c in range(MCI)]
        rh = [wxc[:, c*96:(c+1)*96] for c in range(MCI)]
        emit_matmul_acc(C, ps, lh, rh, reads=["xc", "l1b_small"], writes=["ps%d" % pj])
        with P.group("vector", reads=["ps%d" % pj],
                     writes=(["l1b_xdbl", "y_tm"] if t == 0 else ["l1b_xdbl"])) as g:
            g.emit(lambda v, ps=ps, o=xdbl[:, t*96:(t+1)*96]: v.tensor_copy(out=o, in_=ps))
    for t in range(NT):
        with P.group("tensor", reads=["l1b_xdbl", "ident"], writes=["ps_aux"]) as g:
            g.emit(lambda p, t=t: p.transpose(C.ps_aux[0:96, 0:128],
                                              xdbl[:, t*96:(t+1)*96], C.identf[:]))
        with P.group("vector", reads=["ps_aux"],
                     writes=["l1b_dbct", "l1b_drt"] + (["y_tm"] if t == 0 else [])) as g:
            g.emit(lambda v, t=t: v.tensor_copy(out=dbct[:, t*128:(t+1)*128],
                                                in_=C.ps_aux[0:96, 0:128]))
            g.emit(lambda v, t=t: v.tensor_copy(out=drt_sb[:, t*128:(t+1)*128],
                                                in_=C.ps_aux[0:64, 0:128]))
    with P.group("sync", reads=["l1b_dbct"], writes=["l1b_bct_dr"]) as g:
        g.dma_(bct_dr[:], dbct[64:96, :])
    for c in range(MCI):
        for tq in range(4):
            pj = (c + tq) % 2
            ps = PS[pj][:, 0:512]
            emit_matmul_acc(C, ps, [wdt[:, c*128:(c+1)*128]],
                            [drt_sb[:, tq*512:(tq+1)*512]],
                            reads=["l1b_drt", "l1b_small"], writes=["ps%d" % pj])
            with P.group("vector", reads=["ps%d" % pj, "l1b_small"], writes=["l1b_s1"]) as g:
                g.emit(lambda v, ps=ps, b=bdtc[:, c:c+1], o=delta[:, tq*512:(tq+1)*512]:
                       v.tensor_scalar(o, ps, b, None, ALU.add))
        with P.group("scalar", reads=["l1b_s1"], writes=["l1b_sp1"]) as g:
            g.emit(lambda s: s.activation(dA[:], delta[:], AF.Abs))
            g.emit(lambda s: s.activation(dA[:], dA[:], AF.Exp, scale=-1.0))
            g.emit(lambda s: s.activation(dA[:], dA[:], AF.Ln, bias=1.0))
        with P.group("vector", reads=["l1b_sp1", "xc"], writes=["l1b_delta"]) as g:
            g.emit(lambda v: v.scalar_tensor_tensor(delta[:], delta[:], 0.0, dA[:],
                                                    ALU.max, ALU.add))
            g.emit(lambda v, xs=xc[:, c*T:(c+1)*T]:
                   v.tensor_tensor(dtx[:], delta[:], xs, ALU.mult))
        for n in range(DS):
            with P.group("sync", writes=(["l1b_bbc", "y_fm"] if c == 0 and n == 0
                                           else ["l1b_bbc"])) as g:
                g.dma_(bbc[:], bass.AP(bct_dr.tensor, n*T, [[0, 128], [1, T]]))
            with P.group("scalar", writes=(["l1b_cbc2", "y_fm"] if c == 0 and n == 0
                                             else ["l1b_cbc2"])) as g:
                g.dma_(cbc2[:], bass.AP(bct_dr.tensor, (16+n)*T, [[0, 128], [1, T]]))
            with P.group("scalar", reads=["l1b_delta", "l1b_asb"],
                         writes=["l1b_dA", "l1b_sp1"]) as g:
                g.emit(lambda s, a=asb[:, c*16+n:c*16+n+1]:
                       s.activation(dA[:], delta[:], AF.Exp, scale=a))
            with P.group("vector", reads=["l1b_bbc", "l1b_delta"],
                         writes=["l1b_u", "l1b_stage"]) as g:
                g.emit(lambda v: v.tensor_tensor(ubuf[:], dtx[:], bbc[:], ALU.mult))
            with P.group("vector", reads=["l1b_u", "l1b_dA"], writes=["l1b_h"]) as g:
                g.emit(lambda v: v.tensor_tensor_scan(hbuf[:], dA[:], ubuf[:], 0.0,
                                                      ALU.mult, ALU.add))
            with P.group("gpsimd", reads=["l1b_h", "l1b_cbc2"], writes=["l1b_hc"]) as g:
                g.emit(lambda g_: g_.tensor_tensor(hcbuf[:], hbuf[:], cbc2[:], ALU.mult))
            with P.group("vector", reads=["l1b_hc"], writes=["l1b_y"]) as g:
                if n == 0:
                    g.emit(lambda v: v.tensor_copy(out=ybuf[:], in_=hcbuf[:]))
                else:
                    g.emit(lambda v: v.tensor_add(ybuf[:], ybuf[:], hcbuf[:]))
        with P.group("gpsimd", reads=["l1b_sres_dr"], writes=["l1b_srst"]) as g:
            g.dma_(sres_st[:], sres_dr[:, c*T:(c+1)*T])
        with P.group("vector", reads=["l1b_y", "xc", "l1b_small", "l1b_srst"],
                     writes=["xc", "l1b_y"]) as g:
            g.emit(lambda v, xs=xc[:, c*T:(c+1)*T], dcol=dpc[:, c:c+1]:
                   v.scalar_tensor_tensor(ybuf[:], xs, dcol, ybuf[:], ALU.mult, ALU.add))
            g.emit(lambda v, xs=xc[:, c*T:(c+1)*T]:
                   v.tensor_tensor(xs, ybuf[:], sres_st[:], ALU.mult))
    for hh in range(2):
        with P.group("gpsimd", writes=["l1b_brow", "y_fm"]) as g:
            g.dma_(brow[:], bo_d[0:1, hh*512:(hh+1)*512])
        pj = 4 + hh
        emit_matmul_acc(C, PS[pj][:, 0:512], [C.ones_row[0:1, 0:128]], [brow[0:1, :]],
                        reads=["l1b_brow", "ones"], writes=["ps%d" % pj])
        with P.group("vector", reads=["ps%d" % pj], writes=["l1b_bobc", "y_fm"]) as g:
            g.emit(lambda v, ps=PS[pj][:, 0:512], o=bobc[:, hh*512:(hh+1)*512]:
                   v.tensor_copy(out=o, in_=ps))

    for r in range(4):
        with P.group("gpsimd", writes=["wour%d" % r] + (["l1b_drt"] if r == 0 else [])) as g:
            g.dma_(wour[r][:], wout_d[:, r*D:(r+1)*D])
    for t in range(NT):
        with P.group("sync", reads=["xspill"],
                     writes=(["l1b_xst", "y_fm"] if t == 0 else ["l1b_xst"])) as g:
            g.dma_(xstage[:], xsp[:, t*D:(t+1)*D])
        for mc in range(MCI):
            with P.group("tensor", reads=["xc", "wour%d" % (mc % 4)],
                         writes=["ps0", "ps1"]) as g:
                for dh in range(2):
                    g.emit(lambda p, ps=PS[dh][:, 0:512],
                           lt=xc[:, mc*T + t*128: mc*T + (t+1)*128],
                           rh=wour[mc % 4][:, dh*512:(dh+1)*512],
                           st=(mc == 0), sp=(mc == MCI-1):
                           p.matmul(ps, lt, rh, start=st, stop=sp))
            nxt = mc + 4 if mc + 4 < MCI else (mc + 4 - MCI if t < NT-1 else -1)
            if nxt >= 0:
                with P.group("gpsimd", writes=["wour%d" % (nxt % 4)]) as g:
                    g.dma_(wour[nxt % 4][:], wout_d[:, nxt*D:(nxt+1)*D])
        for dh in range(2):
            with P.group("vector", reads=["ps%d" % dh, "l1b_xst", "l1b_bobc"],
                         writes=(["l1b_ost", "y_fm"] if t == 0 and dh == 0
                                 else ["l1b_ost"])) as g:
                g.emit(lambda v, ps=PS[dh][:, 0:512], o=ostage[:, dh*512:(dh+1)*512],
                       b=bobc[:, dh*512:(dh+1)*512]:
                       v.tensor_add(o, ps, b))
                g.emit(lambda v, o=ostage[:, dh*512:(dh+1)*512],
                       xs=xstage[:, dh*512:(dh+1)*512]:
                       v.tensor_add(o, o, xs))
        with P.group("sync", reads=["l1b_ost"], writes=["xspill"]) as g:
            g.dma_(xsp[:, t*D:(t+1)*D], ostage[:])
    emit_reload_x(C, claims=["xc"])


def prep_l1_attn_b(inputs, h):
    DI = 2048
    w_in = np.asarray(inputs["L1_attn_b_W_in"], np.float32)
    a = w_in.reshape(KC, 128, 32, 128).transpose(1, 2, 0, 3)
    h["l1ab_win"] = np.ascontiguousarray(a.reshape(128, 32*KC*128))
    h["l1ab_bin"] = np.ascontiguousarray(
        np.asarray(inputs["L1_attn_b_b_in"], np.float32).reshape(32, 128).T)
    h["l1ab_cw"] = np.ascontiguousarray(
        np.asarray(inputs["L1_attn_b_conv_w"], np.float32)[:, 0, :]
        .reshape(16, 128, 4).transpose(1, 0, 2).reshape(128, 64))
    h["l1ab_cbc"] = np.ascontiguousarray(
        np.asarray(inputs["L1_attn_b_conv_b"], np.float32).reshape(16, 128).T)
    h["l1ab_wx"] = sb_image_kc(np.ascontiguousarray(
        np.asarray(inputs["L1_attn_b_W_x"], np.float32)), kc=16)
    h["l1ab_wdt"] = np.ascontiguousarray(np.asarray(inputs["L1_attn_b_W_dt"], np.float32))
    h["l1ab_bdt"] = np.ascontiguousarray(
        np.asarray(inputs["L1_attn_b_b_dt"], np.float32).reshape(16, 128).T)
    h["l1ab_alog"] = np.ascontiguousarray(
        np.asarray(inputs["L1_attn_b_A_log"], np.float32).reshape(16, 128, 16)
        .transpose(1, 0, 2).reshape(128, 256))
    h["l1ab_dp"] = np.ascontiguousarray(
        np.asarray(inputs["L1_attn_b_D_par"], np.float32).reshape(16, 128).T)
    h["l1ab_wout"] = sb_image_kc(np.ascontiguousarray(
        np.asarray(inputs["L1_attn_b_W_out"], np.float32)), kc=16)
    h["l1ab_bo"] = np.asarray(inputs["L1_attn_b_b_out"], np.float32).reshape(1, D)


L0F_DEPTH = 6
L0F_GRP = [512, 512, 512, 512, 512, 256]
L0F_GOFF = [0, 512, 1024, 1536, 2048, 2560]
L0F_W1C = 16 * 2816
L0F_SKIP_ATTN = False
L0F_SKIP_FFN = False
L0F_WB_LIM = 8
L0F_SKIP_EPI = False
GP = 2816
NGC = GP // 128


def build_l0_ffn(C, tag):
    P = C.P
    i = C.blk_i
    A = OFF_ARENA
    ZROW = T + 1
    z = C.sb_at("l0f_z%d" % i, (128, KC*ZROW), F32, OFF_X)
    YR = OFF_X + 65568
    tabs = [C.sb_at("l0f_tab%d_%d" % (i, j), (128, NT*64), F32, YR + j*4096)
            for j in range(4)]
    qsbc = C.sb_at("l0f_qsb%d" % i, (128, 64), F32, YR + 16384)
    ksbc = C.sb_at("l0f_ksb%d" % i, (128, 64), F32, YR + 16640)
    madd = C.sb_at("l0f_mad%d" % i, (128, 4*512), F32, YR + 16896)
    rsc = C.sb_at("l0f_rsc%d" % i, (128, 1024), F32, YR + 25088)
    zstA = C.sb_at("l0f_zstA%d" % i, (128, 4), F32, YR + 29184)
    zstF = C.sb_at("l0f_zstF%d" % i, (128, 4), F32, YR + 29216)
    brow = C.sb_at("l0f_br%d" % i, (1, 512), F32, YR)
    rrowA = C.sb_at("l0f_rrA%d" % i, (1, 512), F32, YR + 2048)
    rrowB = C.sb_at("l0f_rrB%d" % i, (1, 512), F32, YR + 4096)
    rbc = C.sb_at("l0f_rbc%d" % i, (128, 512), F32, YR + 6144)
    nbbc = C.sb_at("l0f_nbc%d" % i, (128, 512), F32, YR + 8192)
    w1slB = C.sb_at("l0f_w1B%d" % i, (128, 16*512), BF16, YR + 10240)
    gsc = C.sb_at("l0f_gsc%d" % i, (128, 512), F32, YR + 26624)
    wqkv = C.sb_at("l0f_wqkv%d" % i, (128, KC*3072), BF16, A)
    wo = C.sb_at("l0f_wo%d" % i, (128, KC*1024), BF16, A + 49152)
    zb = C.sb_at("l0f_zb%d" % i, (128, KC*256), BF16, A + 65536)
    qkf = C.sb_at("l0f_qk%d" % i, (128, 1024), F32, A + 69632)
    qhat = C.sb_at("l0f_qh%d" % i, (128, 1024), F32, A + 73728)
    rstd = C.sb_at("l0f_rs%d" % i, (128, 16), F32, A + 77824)
    mcol = C.sb_at("l0f_mc%d" % i, (128, 16), F32, A + 77888)
    qrot = C.sb_at("l0f_qr%d" % i, (128, 2048), BF16, A + 77952)
    qpf = C.sb_at("l0f_qpf%d" % i, (128, 8*256), BF16, A + 82048)
    kpf2 = C.sb_at("l0f_kpf%d" % i, (128, 2*8*256), BF16, A + 86144)
    vtm = [C.sb_at("l0f_vtm%d_%d" % (i, j), (128, 2*1024), BF16, A + 94336 + j*4096)
           for j in range(2)]
    s1 = [C.sb_at("l0f_s1%d_%d" % (i, p), (128, 512), F32, A + 65536 + p*2048)
          for p in range(4)]
    attn_sb = [C.sb_at("l0f_at%d_%d" % (i, p), (128, 512), BF16, A + 77952 + p*1024)
               for p in range(2)]
    attnT_sb = [C.sb_at("l0f_aT%d_%d" % (i, p), (128, 512), BF16, A + 80000 + p*1024)
                for p in range(2)]
    aof = C.sb_at("l0f_aof%d" % i, (128, 8*256), BF16, A + 73728)
    w1slA = C.sb_at("l0f_w1A%d" % i, (128, 16*512), BF16, A)
    w1sl = [w1slA, w1slB]
    w2r = C.sb_at("l0f_w2%d" % i, (128, NGC*1024), BF16, A + 16384)
    hprod = C.sb_at("l0f_hp%d" % i, (128, NGC*512), BF16, A + 61440)
    xnb2 = [C.sb_at("l0f_xn%d_%d" % (i, p), (128, KC*512), BF16,
                    A + (83968 if p == 0 else 94336)) for p in range(2)]
    fs1 = C.sb_at("l0f_fs1%d" % i, (128, 512), F32, A + 92160)
    stat = C.sb_at("l0f_stc%d" % i, (128, 16), F32, A + 94208)
    xst = C.sb_at("l0f_xst%d" % i, (128, 1024), F32, A + 94336)
    ost = C.sb_at("l0f_ost%d" % i, (128, 1024), F32, A + 98432)
    b1ac = C.sb_at("l0f_b1a%d" % i, (128, NGC), F32, A + 102528)
    b1gc = C.sb_at("l0f_b1g%d" % i, (128, NGC), F32, A + 102656)
    b2c = C.sb_at("l0f_b2c%d" % i, (128, KC), F32, A + 102784)
    lngc = C.sb_at("l0f_lng%d" % i, (128, KC), F32, A + 102848)
    lnbc = C.sb_at("l0f_lnb%d" % i, (128, KC), F32, A + 102912)
    wqkv_d = C.dram_in("l0f_wqkv", (128, L0F_DEPTH*KC*3072), F32)
    wo_d = C.dram_in("l0f_wo", (128, L0F_DEPTH*KC*1024), F32)
    w1_d = C.dram_in("l0f_w1", (128, L0F_DEPTH*L0F_W1C), F32)
    w2_d = C.dram_in("l0f_w2", (128, L0F_DEPTH*NGC*1024), F32)
    b1a_d = C.dram_in("l0f_b1a", (128, L0F_DEPTH*NGC), F32)
    b1g_d = C.dram_in("l0f_b1g", (128, L0F_DEPTH*NGC), F32)
    b2_d = C.dram_in("l0f_b2", (128, L0F_DEPTH*KC), F32)
    lng_d = C.dram_in("l0f_lng", (128, L0F_DEPTH*KC), F32)
    lnb_d = C.dram_in("l0f_lnb", (128, L0F_DEPTH*KC), F32)
    qs_d = C.dram_in("l0f_qs", (L0F_DEPTH, 64), F32)
    ks_d = C.dram_in("l0f_ks", (L0F_DEPTH, 64), F32)
    tab_d = C.dram_in("l0f_tabs", (128, 4*NT*64), F32)
    madd_d = C.dram_in("l0f_madd", (128, 4*512), F32)
    xsp = C.dram_tmp("xspill", (128, NT*D), F32)
    PS = C.psum_f32

    emit_spill_x(C)
    P.barrier()
    with P.group("vector", reads=["y_fm"], writes=["z", "x"]) as g:
        for cc in range(KC):
            g.emit(lambda v, cc=cc: v.memset(z[:, cc*ZROW:cc*ZROW+1], 0.0))
            g.emit(lambda v, cc=cc: v.tensor_copy(
                out=z[:, cc*ZROW+1:(cc+1)*ZROW], in_=C.y_fm[:, cc*T:(cc+1)*T]))
    P.barrier()

    def attn_sublayer(l):
        with P.group("gpsimd", writes=["l0f_wqkv"]) as g:
            for cc in range(KC):
                g.dma_(wqkv[:, cc*3072:(cc+1)*3072],
                       wqkv_d[:, l*KC*3072 + cc*3072: l*KC*3072 + (cc+1)*3072])
        with P.group("gpsimd", writes=["l0f_wo"]) as g:
            g.dma_(wo[:], wo_d[:, l*KC*1024:(l+1)*KC*1024])
        with P.group("sync", writes=["l0f_tabs"]) as g:
            for j in range(4):
                g.dma_(tabs[j][:], tab_d[:, j*1024:(j+1)*1024])
        with P.group("gpsimd", writes=["l0f_qsrow"]) as g:
            g.dma_(qsbc[0:1, 0:64], qs_d[l:l+1, :])
            g.dma_(ksbc[0:1, 0:64], ks_d[l:l+1, :])
        for j in range(2):
            src = qsbc if j == 0 else ksbc
            emit_matmul_acc(C, PS[4+j][:, 0:64], [C.ones_row[0:1, 0:128]],
                            [src[0:1, 0:64]],
                            reads=["l0f_qsrow", "ones"], writes=["ps%d" % (4+j)])
            with P.group("vector", reads=["ps%d" % (4+j)],
                         writes=["l0f_qsbc", "l0f_qsrow"]) as g:
                g.emit(lambda v, ps=PS[4+j][:, 0:64], o=src:
                       v.tensor_copy(out=o[:], in_=ps))
        qsrep = bass.AP(qsbc, 0, [[64, 128], [0, NT], [1, 64]])
        qsrot = bass.AP(qsbc, 32, [[64, 128], [0, NT], [-32, 2], [1, 32]])
        ksrep = bass.AP(ksbc, 0, [[64, 128], [0, NT], [1, 64]])
        ksrot = bass.AP(ksbc, 32, [[64, 128], [0, NT], [-32, 2], [1, 32]])
        with P.group("vector", reads=["l0f_tabs", "l0f_qsbc"], writes=["l0f_tabs"], chain=False) as g:
            g.emit(lambda v: v.tensor_tensor(tabs[0][:], tabs[0][:], qsrep, ALU.mult))
            g.emit(lambda v: v.tensor_tensor(tabs[1][:], tabs[1][:], qsrot, ALU.mult))
            g.emit(lambda v: v.tensor_tensor(tabs[2][:], tabs[2][:], ksrep, ALU.mult))
            g.emit(lambda v: v.tensor_tensor(tabs[3][:], tabs[3][:], ksrot, ALU.mult))
        with P.group("gpsimd", writes=["l0f_kpf1", "l0f_vtm1"]) as g:
            g.emit(lambda g_: g_.memset(kpf2[:, 2048:4096], 0.0))
            g.emit(lambda g_: g_.memset(vtm[1][:], 0.0))
        with P.group("scalar", writes=["l0f_madd"]) as g:
            g.dma_(madd[:], madd_d[:])
        P.barrier()
        for wb in range(L0F_WB_LIM):
            cur = wb % 2
            prv = 1 - cur
            with P.group("vector", reads=["z", "l0f_zst"], writes=["l0f_zb"], chain=False) as g:
                for cc in range(KC):
                    off = (1 if cc < 4 else 0)
                    if cc < 4 or wb == 0:
                        g.emit(lambda v, o=zb[:, cc*256:(cc+1)*256],
                               s=z[:, cc*ZROW + off + wb*256: cc*ZROW + off + wb*256 + 256]:
                               v.tensor_copy(out=o, in_=s))
                    else:
                        g.emit(lambda v, o=zb[:, cc*256:cc*256+1],
                               s=zstA[:, cc-4:cc-3]: v.tensor_copy(out=o, in_=s))
                        g.emit(lambda v, o=zb[:, cc*256+1:(cc+1)*256],
                               s=z[:, cc*ZROW + 1 + wb*256: cc*ZROW + wb*256 + 256]:
                               v.tensor_copy(out=o, in_=s))
            with P.group("vector", reads=["z"], writes=["l0f_zst"], chain=False) as g:
                for cc in range(4, KC):
                    g.emit(lambda v, o=zstA[:, cc-4:cc-3],
                           s=z[:, cc*ZROW + 1 + wb*256 + 255: cc*ZROW + 1 + wb*256 + 256]:
                           v.tensor_copy(out=o, in_=s))
            for tt in range(2):
                for mg in range(6):
                    ps = PS[mg][:, 0:512]
                    lh = [zb[:, cc*256 + tt*128: cc*256 + (tt+1)*128] for cc in range(KC)]
                    rh = [wqkv[:, cc*3072 + mg*512: cc*3072 + (mg+1)*512] for cc in range(KC)]
                    emit_matmul_acc(C, ps, lh, rh, reads=["l0f_zb", "l0f_wqkv"],
                                    writes=["ps%d" % mg])
                with P.group("vector", reads=["ps4", "ps5"],
                             writes=["l0f_vtm%d" % cur], chain=False) as g:
                    g.emit(lambda v, o=vtm[cur][:, tt*1024:tt*1024+512]:
                           v.tensor_copy(out=o, in_=PS[4][:, 0:512]))
                    g.emit(lambda v, o=vtm[cur][:, tt*1024+512:(tt+1)*1024]:
                           v.tensor_copy(out=o, in_=PS[5][:, 0:512]))
                for qk in range(2):
                    pq = 0 if qk == 0 else 2
                    tA, tB = (tabs[0], tabs[1]) if qk == 0 else (tabs[2], tabs[3])
                    with P.group("vector", reads=["ps%d" % pq, "ps%d" % (pq+1)],
                                 writes=["l0f_qkf"], chain=False) as g:
                        g.emit(lambda v, pq=pq: v.tensor_copy(out=qkf[:, 0:512],
                                                              in_=PS[pq][:, 0:512]))
                        g.emit(lambda v, pq=pq: v.tensor_copy(out=qkf[:, 512:1024],
                                                              in_=PS[pq+1][:, 0:512]))
                    with P.group("scalar", reads=["l0f_qkf"], writes=["l0f_qh"]) as g:
                        g.emit(lambda s: s.activation(qhat[:], qkf[:], AF.Square))
                    with P.group("vector", reads=["l0f_qh"], writes=["l0f_rstd"]) as g:
                        g.emit(lambda v: v.tensor_reduce(
                            rstd[:], bass.AP(qhat, 0, [[1024, 128], [64, 16], [1, 64]]),
                            AX.X, ALU.add))
                        g.emit(lambda v: v.tensor_scalar_max(rstd[:], rstd[:], 1e-24))
                    with P.group("scalar", reads=["l0f_rstd"], writes=["l0f_rstd2"]) as g:
                        g.emit(lambda s: s.activation(rstd[:], rstd[:], AF.Ln))
                        g.emit(lambda s: s.activation(rstd[:], rstd[:], AF.Exp, scale=-0.5))
                    rrep = bass.AP(rstd, 0, [[16, 128], [1, 16], [0, 64]])
                    ttile = wb*2 + tt
                    tArep = bass.AP(tA, ttile*64, [[NT*64, 128], [0, 16], [1, 64]])
                    tBrep = bass.AP(tB, ttile*64, [[NT*64, 128], [0, 16], [1, 64]])
                    qhrot = bass.AP(qhat, 32, [[1024, 128], [64, 16], [-32, 2], [1, 32]])
                    with P.group("vector", reads=["l0f_rstd2", "l0f_qkf", "l0f_tabs"],
                                 writes=["l0f_qrot", "l0f_qkf", "l0f_qh"]) as g:
                        g.emit(lambda v, rr=rrep: v.tensor_tensor(qhat[:], qkf[:], rr, ALU.mult))
                        g.emit(lambda v, qr=qhrot, tr=tBrep:
                               v.tensor_tensor(rsc[:], qr, tr, ALU.mult))
                        g.emit(lambda v, tr=tArep: v.tensor_tensor(qkf[:], qhat[:], tr, ALU.mult))
                        g.emit(lambda v, o=qrot[:, qk*1024:(qk+1)*1024]:
                               v.tensor_add(o, qkf[:], rsc[:]))
                with P.group("tensor", reads=["l0f_qrot", "ident"], writes=["ps_tr"],
                             chain=False) as g:
                    for cc in range(KC):
                        g.emit(lambda p, cc=cc: p.transpose(
                            C.ps_tr[:, cc*128:(cc+1)*128],
                            qrot[:, cc*128:(cc+1)*128], C.ident[:]))
                with P.group("vector", reads=["ps_tr"], writes=["l0f_qpf"], chain=False) as g:
                    for cc in range(KC):
                        g.emit(lambda v, cc=cc, tt=tt: v.tensor_copy(
                            out=qpf[:, cc*256 + tt*128: cc*256 + (tt+1)*128],
                            in_=C.ps_tr[:, cc*128:(cc+1)*128]))
                with P.group("tensor", reads=["l0f_qrot", "ident"], writes=["ps_tr"],
                             chain=False) as g:
                    for cc in range(KC):
                        g.emit(lambda p, cc=cc: p.transpose(
                            C.ps_tr[:, cc*128:(cc+1)*128],
                            qrot[:, 1024 + cc*128: 1024 + (cc+1)*128], C.ident[:]))
                with P.group("vector", reads=["ps_tr"], writes=["l0f_kpf%d" % cur], chain=False) as g:
                    for cc in range(KC):
                        g.emit(lambda v, o=kpf2[:, cur*2048 + cc*256 + tt*128: cur*2048 + cc*256 + (tt+1)*128],
                               s=C.ps_tr[:, cc*128:(cc+1)*128]:
                               v.tensor_copy(out=o, in_=s))
            P.barrier()
            def st_qk(idx):
                h, it = idx // 2, idx % 2
                p0 = (h % 2) * 64
                cc = h // 2
                ps = PS[idx % 4][:, 0:512]
                rh = bass.AP(kpf2, p0*4096 + prv*2048 + cc*256,
                             [[4096, 64], [(cur - prv)*2048, 2], [1, 256]])
                with P.group("tensor", reads=["l0f_qpf", "l0f_kpf0", "l0f_kpf1"],
                             writes=["ps%d" % (idx % 4)], chain=False) as g:
                    g.emit(lambda p, ps=ps, rh=rh,
                           lt=qpf[p0:p0+64, cc*256 + it*128: cc*256 + (it+1)*128]:
                           p.matmul(ps, lt, rh))

            def st_s1(idx):
                j = idx % 4
                s1p = s1[j]
                mi = (idx % 2)*2 + (1 if wb == 0 else 0)
                ps = PS[j][:, 0:512]
                with P.group("vector", reads=["ps%d" % j, "l0f_madd"],
                             writes=["l0f_s1%d" % j, "l0f_s1e%d" % j]) as g:
                    g.emit(lambda v, ps=ps, s1p=s1p, m=madd[:, mi*512:(mi+1)*512]:
                           v.scalar_tensor_tensor(s1p[:], ps, -8.0, m, ALU.mult, ALU.add))

            def st_mx(idx):
                j = idx % 4
                with P.group("vector", reads=["l0f_s1%d" % j],
                             writes=["l0f_mc%d" % j]) as g:
                    g.emit(lambda v, s1p=s1[j], mc=mcol[:, j*4:j*4+4]:
                           v.tensor_reduce(mc[:, 0:1], s1p[:], AX.X, ALU.min))

            def st_exp(idx):
                j = idx % 4
                with P.group("scalar", reads=["l0f_s1%d" % j, "l0f_mc%d" % j],
                             writes=["l0f_s1e%d" % j, "l0f_mca%d" % j]) as g:
                    g.emit(lambda s, s1p=s1[j], mc=mcol[:, j*4:j*4+4]:
                           s.activation(s1p[:], s1p[:], AF.Exp, scale=-1.0,
                                        bias=mc[:, 0:1], accum_out=mc[:, 1:2]))

            def st_n1(idx):
                j = idx % 4
                with P.group("vector", reads=["l0f_mca%d" % j],
                             writes=["l0f_rcp%d" % j]) as g:
                    g.emit(lambda v, mc=mcol[:, j*4:j*4+4]:
                           v.reciprocal(mc[:, 2:3], mc[:, 1:2]))

            def st_n2(idx):
                j = idx % 4
                par = idx % 2
                with P.group("scalar", reads=["l0f_s1e%d" % j, "l0f_rcp%d" % j],
                             writes=["l0f_attn%d" % par]) as g:
                    g.emit(lambda s, s1p=s1[j], atp=attn_sb[par],
                           mc=mcol[:, j*4:j*4+4]:
                           s.activation(atp[:], s1p[:], AF.Identity,
                                        scale=mc[:, 2:3]))

            def st_tr(idx):
                par = idx % 2
                atp = attn_sb[par]
                pstr = C.ps_tr[:, par*512:(par+1)*512]
                with P.group("tensor", reads=["l0f_attn%d" % par, "ident"],
                             writes=["ps_tr"], chain=False) as g:
                    for jj in range(4):
                        g.emit(lambda p, jj=jj, atp=atp, pstr=pstr: p.transpose(
                            pstr[:, jj*128:(jj+1)*128],
                            atp[:, jj*128:(jj+1)*128], C.ident[:]))

            def st_tT(idx):
                par = idx % 2
                pstr = C.ps_tr[:, par*512:(par+1)*512]
                with P.group("vector", reads=["ps_tr"],
                             writes=["l0f_attnT%d" % par]) as g:
                    g.emit(lambda v, aTp=attnT_sb[par], pstr=pstr:
                           v.tensor_copy(out=aTp[:], in_=pstr))

            def st_pv(idx):
                h, it = idx // 2, idx % 2
                p0 = (h % 2) * 64
                par = idx % 2
                aTp = attnT_sb[par]
                po = PS[4 + par][p0:p0+64, 0:128]
                with P.group("tensor", reads=["l0f_attnT%d" % par,
                                              "l0f_vtm0", "l0f_vtm1"],
                             writes=["psv%d" % par], chain=False) as g:
                    for jj in range(4):
                        vsrc = vtm[prv] if jj < 2 else vtm[cur]
                        g.emit(lambda p, po=po, tp=(0, p0), aTp=aTp,
                               lt=vsrc[:, (jj % 2)*1024 + h*64: (jj % 2)*1024 + (h+1)*64],
                               rh=aTp[:, jj*128:(jj+1)*128],
                               st=(jj == 0), sp=(jj == 3):
                               p.matmul(po, lt, rh, start=st, stop=sp, tile_position=tp))

            def st_ao(idx):
                h, it = idx // 2, idx % 2
                p0 = (h % 2) * 64
                cc = h // 2
                par = idx % 2
                po = PS[4 + par][p0:p0+64, 0:128]
                with P.group("scalar", reads=["psv%d" % par],
                             writes=["l0f_aof"]) as g:
                    g.emit(lambda s, po=po,
                           o=aof[p0:p0+64, cc*256 + it*128: cc*256 + (it+1)*128]:
                           s.activation(o, po, AF.Identity))

            def st_pvao(idx):
                st_pv(idx); st_ao(idx)
            stages = [(8, st_pvao), (7, st_tT), (6, st_tr),
                      (5, st_n2), (4, st_n1), (3, st_exp), (2, st_mx),
                      (1, st_s1), (0, st_qk)]
            for step in range(32 + 8):
                for off, fn in stages:
                    if off <= step < 32 + off:
                        fn(step - off)
            for dp in range(4):
                pj = 2 + (dp % 2)
                ps = PS[pj]
                with P.group("tensor", reads=["l0f_aof", "l0f_wo"],
                             writes=["ps%d" % pj], chain=False) as g:
                    for half in range(2):
                        dc = dp*2 + half
                        for ic in range(KC):
                            g.emit(lambda p, ps=ps[:, half*256:(half+1)*256],
                                   lt=wo[:, ic*1024 + dc*128: ic*1024 + (dc+1)*128],
                                   rh=aof[:, ic*256:(ic+1)*256],
                                   st=(ic == 0), sp=(ic == KC-1):
                                   p.matmul(ps, lt, rh, start=st, stop=sp))
                with P.group("vector", reads=["ps%d" % pj, "z"], writes=["z"], chain=False) as g:
                    for half in range(2):
                        dc = dp*2 + half
                        zsl = z[:, dc*ZROW + 1 + wb*256: dc*ZROW + 1 + (wb+1)*256]
                        g.emit(lambda v, ps=ps[:, half*256:(half+1)*256], o=zsl:
                               v.tensor_add(o, o, ps))
            P.barrier()

    def ffn_sublayer(l):
        with P.group("gpsimd", writes=["l0f_fparm"]) as g:
            g.dma_(b1ac[:], b1a_d[:, l*NGC:(l+1)*NGC])
            g.dma_(b1gc[:], b1g_d[:, l*NGC:(l+1)*NGC])
            g.dma_(b2c[:], b2_d[:, l*KC:(l+1)*KC])
            g.dma_(lngc[:], lng_d[:, l*KC:(l+1)*KC])
            g.dma_(lnbc[:], lnb_d[:, l*KC:(l+1)*KC])
        with P.group("gpsimd", writes=["l0f_w2r"]) as g:
            for q4 in range(0, NGC, 6):
                qn = min(6, NGC - q4)
                g.dma_(w2r[:, q4*1024:(q4+qn)*1024],
                       w2_d[:, l*NGC*1024 + q4*1024: l*NGC*1024 + (q4+qn)*1024])
        P.barrier()

        def ffn_pre(tc):
            par = tc % 2
            xnb = xnb2[par]
            with P.group("vector", reads=["z", "l0f_zst"],
                         writes=["l0f_xnb%d" % par], chain=False) as g:
                for cc in range(KC):
                    off = (1 if cc < 4 else 0)
                    if cc < 4 or tc == 0:
                        g.emit(lambda v, o=xnb[:, cc*512:(cc+1)*512],
                               s=z[:, cc*ZROW + off + tc*512: cc*ZROW + off + tc*512 + 512]:
                               v.tensor_copy(out=o, in_=s))
                    else:
                        g.emit(lambda v, o=xnb[:, cc*512:cc*512+1],
                               s=zstF[:, cc-4:cc-3]: v.tensor_copy(out=o, in_=s))
                        g.emit(lambda v, o=xnb[:, cc*512+1:(cc+1)*512],
                               s=z[:, cc*ZROW + 1 + tc*512: cc*ZROW + tc*512 + 512]:
                               v.tensor_copy(out=o, in_=s))
            for tt in range(4):
                emit_matmul_acc(C, C.ps_aux[:, 128:129],
                                [xnb[:, cc*512 + tt*128: cc*512 + (tt+1)*128]
                                 for cc in range(KC)],
                                [C.ones_cb[:] for cc in range(KC)],
                                reads=["l0f_xnb%d" % par, "ones"], writes=["ps_aux"])
                emit_matmul_acc(C, C.ps_aux[:, 0:128],
                                [xnb[:, cc*512 + tt*128: cc*512 + (tt+1)*128]
                                 for cc in range(KC)],
                                [xnb[:, cc*512 + tt*128: cc*512 + (tt+1)*128]
                                 for cc in range(KC)],
                                reads=["l0f_xnb%d" % par], writes=["ps_aux"])
                with P.group("vector", reads=["ps_aux", "ident"],
                             writes=["l0f_stat"]) as g:
                    g.emit(lambda v, tt=tt: v.tensor_copy(out=stat[:, tt*4:tt*4+1],
                                                          in_=C.ps_aux[:, 128:129]))
                    g.emit(lambda v: v.tensor_tensor(
                        fs1[:, 0:128], C.ps_aux[:, 0:128], C.identf[:], ALU.mult))
                    g.emit(lambda v, o=stat[:, tt*4+1:tt*4+2]:
                           v.reduce_sum(o, fs1[:, 0:128], AX.X))
            sv = lambda c0: bass.AP(stat, c0, [[16, 128], [4, 4], [1, 1]])
            with P.group("vector", reads=["l0f_stat"], writes=["l0f_stat2"]) as g:
                g.emit(lambda v: v.tensor_scalar_mul(sv(0), sv(0), 1.0/D))
                g.emit(lambda v: v.tensor_scalar_mul(sv(1), sv(1), 1.0/D))
                g.emit(lambda v: v.tensor_tensor(sv(2), sv(0), sv(0), ALU.mult))
                g.emit(lambda v: v.tensor_tensor(sv(2), sv(1), sv(2), ALU.subtract))
                g.emit(lambda v: v.tensor_scalar_add(sv(2), sv(2), 1e-5))
            with P.group("scalar", reads=["l0f_stat2"], writes=["l0f_stat3"]) as g:
                g.emit(lambda s: s.activation(sv(3), sv(2), AF.Ln))
                g.emit(lambda s: s.activation(sv(3), sv(3), AF.Exp, scale=-0.5))
            with P.group("vector", reads=["l0f_stat3"], writes=["l0f_stat4"]) as g:
                g.emit(lambda v: v.tensor_tensor(sv(0), sv(0), sv(3), ALU.mult))
            for j in range(2):
                c0 = 3 if j == 0 else 0
                rrow = rrowA if j == 0 else rrowB
                with P.group("tensor", reads=["l0f_stat4", "ident"],
                             writes=["ps_aux"], chain=False) as g:
                    for tt in range(4):
                        g.emit(lambda p, tt=tt, c0=c0: p.transpose(
                            C.ps_aux[0:1, tt*128:(tt+1)*128],
                            stat[:, tt*4+c0:tt*4+c0+1], C.identf[:]))
                with P.group("vector", reads=["ps_aux"], writes=["l0f_rrow"]) as g:
                    g.emit(lambda v, rrow=rrow: v.tensor_copy(
                        out=rrow[0:1, :], in_=C.ps_aux[0:1, 0:512]))
                emit_matmul_acc(C, C.ps_aux[:, 0:512], [C.ones_row[0:1, 0:128]],
                                [rrow[0:1, :]],
                                reads=["l0f_rrow", "ones"], writes=["ps_aux"])
                with P.group("vector", reads=["ps_aux"], writes=["l0f_rbc"]) as g:
                    g.emit(lambda v, o=(rbc if j == 0 else nbbc):
                           v.tensor_copy(out=o[:], in_=C.ps_aux[:, 0:512]))
            for cc in range(KC):
                off = (1 if cc < 4 else 0)
                zsl = z[:, cc*ZROW + off + tc*512: cc*ZROW + off + tc*512 + 512]
                with P.group("vector", reads=["z", "l0f_rbc", "l0f_fparm", "l0f_zst"],
                             writes=["l0f_xnb%d" % par, "l0f_fs1"]) as g:
                    g.emit(lambda v, zsl=zsl: v.tensor_tensor(fs1[:], zsl, rbc[:], ALU.mult))
                    g.emit(lambda v: v.tensor_tensor(fs1[:], fs1[:], nbbc[:], ALU.subtract))
                    g.emit(lambda v, cc=cc, xnb=xnb: v.tensor_scalar(
                        xnb[:, cc*512:(cc+1)*512], fs1[:], lngc[:, cc:cc+1],
                        lnbc[:, cc:cc+1], ALU.mult, ALU.add))
                    if cc >= 4 and tc > 0:
                        g.emit(lambda v, s=zstF[:, cc-4:cc-3]: v.tensor_tensor(
                            fs1[:, 0:1], s, rbc[:, 0:1], ALU.mult))
                        g.emit(lambda v: v.tensor_tensor(
                            fs1[:, 0:1], fs1[:, 0:1], nbbc[:, 0:1], ALU.subtract))
                        g.emit(lambda v, cc=cc, xnb=xnb: v.tensor_scalar(
                            xnb[:, cc*512:cc*512+1], fs1[:, 0:1], lngc[:, cc:cc+1],
                            lnbc[:, cc:cc+1], ALU.mult, ALU.add))
            with P.group("vector", reads=["z"], writes=["l0f_zst"], chain=False) as g:
                for cc in range(4, KC):
                    g.emit(lambda v, o=zstF[:, cc-4:cc-3],
                           s=z[:, cc*ZROW + 1 + tc*512 + 511: cc*ZROW + 1 + tc*512 + 512]:
                           v.tensor_copy(out=o, in_=s))

        def ffn_w1(tc):
            par = tc % 2
            xnb = xnb2[par]
            for j in range(len(L0F_GRP)):
                G = L0F_GRP[j]
                w1s = w1sl[j % 2]
                with P.group("gpsimd", writes=["l0f_w1s%d" % (j % 2)]) as g:
                    base = l*L0F_W1C + 16*L0F_GOFF[j]
                    g.dma_(w1s[:, 0:16*G], w1_d[:, base: base + 16*G])
                for ms in range(G // 128):
                    gc = L0F_GOFF[j] // 128 + ms
                    bk = 4 if gc % 2 == 0 else 2
                    for ag in range(2):
                        pj = bk + ag
                        ps = PS[pj][:, 0:512]
                        lh = [w1s[:, cc*2*G + ag*G + ms*128: cc*2*G + ag*G + (ms+1)*128]
                              for cc in range(KC)]
                        rh = [xnb[:, cc*512:(cc+1)*512] for cc in range(KC)]
                        emit_matmul_acc(C, ps, lh, rh,
                                        reads=["l0f_w1s%d" % (j % 2), "l0f_xnb%d" % par],
                                        writes=["ps%d" % pj])
                    with P.group("scalar", reads=["ps%d" % (bk+1), "l0f_fparm", "l0f_gg2"],
                                 writes=["l0f_gg"]) as g:
                        g.emit(lambda s, b=b1gc[:, gc:gc+1], bk=bk:
                               s.activation(gsc[:], PS[bk+1][:, 0:512], AF.Gelu, bias=b))
                    with P.group("vector", reads=["ps%d" % bk, "l0f_gg", "l0f_fparm"],
                                 writes=["l0f_hp", "l0f_gg2"]) as g:
                        g.emit(lambda v, b=b1ac[:, gc:gc+1], bk=bk,
                               o=hprod[:, gc*512:(gc+1)*512]:
                               v.scalar_tensor_tensor(o, PS[bk][:, 0:512], b, gsc[:],
                                                      ALU.add, ALU.mult))

        def ffn_w2(tc):
            for th in range(2):
                for dp in range(4):
                    pj = dp % 4
                    ps = PS[pj]
                    with P.group("tensor", reads=["l0f_hp", "l0f_w2r"],
                                 writes=["ps%d" % pj], chain=False) as g:
                        for half in range(2):
                            dc = dp*2 + half
                            for gc in range(NGC):
                                g.emit(lambda p, ps=ps[:, half*256:(half+1)*256],
                                       lt=w2r[:, gc*1024 + dc*128: gc*1024 + (dc+1)*128],
                                       rh=hprod[:, gc*512 + th*256: gc*512 + (th+1)*256],
                                       st=(gc == 0), sp=(gc == NGC-1):
                                       p.matmul(ps, lt, rh, start=st, stop=sp))
                    with P.group("vector", reads=["ps%d" % pj, "z", "l0f_fparm"],
                                 writes=["z"], chain=False) as g:
                        for half in range(2):
                            dc = dp*2 + half
                            zsl = z[:, dc*ZROW + 1 + tc*512 + th*256:
                                    dc*ZROW + 1 + tc*512 + (th+1)*256]
                            g.emit(lambda v, ps=ps[:, half*256:(half+1)*256], o=zsl,
                                   b=b2c[:, dc:dc+1]:
                                   v.scalar_tensor_tensor(o, ps, b, o, ALU.add, ALU.add))

        ffn_pre(0)
        for tc in range(4):
            ffn_w1(tc)
            if tc < 3:
                ffn_pre(tc + 1)
            ffn_w2(tc)
        P.barrier()

    for l in range(L0F_DEPTH):
        if not L0F_SKIP_ATTN:
            attn_sublayer(l)
        P.barrier()
        if not L0F_SKIP_FFN:
            ffn_sublayer(l)
    P.barrier()
    for t in range(NT if not L0F_SKIP_EPI else 0):
        with P.group("sync", reads=["xspill"], writes=["l0f_xst"]) as g:
            g.dma_(xst[:], xsp[:, t*D:(t+1)*D])
        for hh in range(2):
            with P.group("tensor", reads=["z", "ident"], writes=["ps_aux"],
                         chain=False) as g:
                for q in range(4):
                    cc = hh*4 + q
                    g.emit(lambda p, cc=cc, q=q, t=t: p.transpose(
                        C.ps_aux[:, q*128:(q+1)*128],
                        z[:, cc*ZROW + 1 + t*128: cc*ZROW + 1 + (t+1)*128], C.identf[:]))
            with P.group("vector", reads=["ps_aux", "l0f_xst"], writes=["l0f_ost"]) as g:
                g.emit(lambda v, hh=hh: v.tensor_add(
                    ost[:, hh*512:(hh+1)*512], xst[:, hh*512:(hh+1)*512],
                    C.ps_aux[:, 0:512]))
        with P.group("sync", reads=["l0f_ost"], writes=["xspill"]) as g:
            g.dma_(xsp[:, t*D:(t+1)*D], ost[:])
    emit_reload_x(C, claims=["z"])


def prep_l0_ffn(inputs, h):
    if "l0f_wqkv" in h:
        return
    DEPTH = L0F_DEPTH
    q = np.arange(512, dtype=np.float64)[None, :] - np.arange(128, dtype=np.float64)[:, None]
    r = (np.arange(512)[None, :] >= 256)
    madd = np.zeros((128, 4*512), np.float32)
    for a in range(2):
        for w0 in range(2):
            m = (q >= a*128) & (q <= a*128 + 256)
            if w0:
                m = m & r
            madd[:, (a*2+w0)*512:(a*2+w0+1)*512] = \
                (m.astype(np.float32) - 1.0) * -1e30
    h["l0f_madd"] = madd
    wqkv = np.asarray(inputs["L0_ffn_Wqkv"], np.float32)
    h["l0f_wqkv"] = np.ascontiguousarray(np.concatenate(
        [sb_image_kc(np.ascontiguousarray(wqkv[l])) for l in range(DEPTH)], axis=1))
    wo = np.asarray(inputs["L0_ffn_Wo"], np.float32)
    h["l0f_wo"] = np.ascontiguousarray(np.concatenate(
        [sb_image_kc(np.ascontiguousarray(wo[l])) for l in range(DEPTH)], axis=1))
    W1 = np.asarray(inputs["L0_ffn_W1"], np.float32)
    b1 = np.asarray(inputs["L0_ffn_b1"], np.float32)
    W2 = np.asarray(inputs["L0_ffn_W2"], np.float32)
    w1_imgs, b1a_imgs, b1g_imgs, w2_imgs = [], [], [], []
    for l in range(DEPTH):
        W1a = np.zeros((D, GP), np.float32); W1a[:, :2730] = W1[l][:, :2730]
        W1g = np.zeros((D, GP), np.float32); W1g[:, :2730] = W1[l][:, 2730:]
        b1a = np.zeros(GP, np.float32); b1a[:2730] = b1[l][:2730]
        b1g = np.zeros(GP, np.float32); b1g[:2730] = b1[l][2730:]
        cols = []
        for j in range(len(L0F_GRP)):
            G, j0 = L0F_GRP[j], L0F_GOFF[j]
            grp = np.concatenate([W1a[:, j0:j0+G], W1g[:, j0:j0+G]], axis=1)
            cols.append(grp.reshape(KC, 128, 2*G).transpose(1, 0, 2).reshape(128, KC*2*G))
        w1_imgs.append(np.concatenate(cols, axis=1))
        b1a_imgs.append(b1a.reshape(NGC, 128).T)
        b1g_imgs.append(b1g.reshape(NGC, 128).T)
        W2p = np.zeros((GP, D), np.float32); W2p[:2730] = W2[l]
        w2_imgs.append(sb_image_kc(W2p, kc=NGC))
    h["l0f_w1"] = np.ascontiguousarray(np.concatenate(w1_imgs, axis=1))
    h["l0f_b1a"] = np.ascontiguousarray(np.concatenate(b1a_imgs, axis=1))
    h["l0f_b1g"] = np.ascontiguousarray(np.concatenate(b1g_imgs, axis=1))
    h["l0f_w2"] = np.ascontiguousarray(np.concatenate(w2_imgs, axis=1))
    h["l0f_b2"] = np.ascontiguousarray(
        np.asarray(inputs["L0_ffn_b2"], np.float32)[:DEPTH].reshape(DEPTH, KC, 128)
        .transpose(2, 0, 1).reshape(128, DEPTH*KC))
    h["l0f_lng"] = np.ascontiguousarray(
        np.asarray(inputs["L0_ffn_ln_g"], np.float32)[:DEPTH].reshape(DEPTH, KC, 128)
        .transpose(2, 0, 1).reshape(128, DEPTH*KC))
    h["l0f_lnb"] = np.ascontiguousarray(
        np.asarray(inputs["L0_ffn_ln_b"], np.float32)[:DEPTH].reshape(DEPTH, KC, 128)
        .transpose(2, 0, 1).reshape(128, DEPTH*KC))
    h["l0f_qs"] = np.ascontiguousarray(
        np.asarray(inputs["L0_ffn_q_scale"], np.float32)[:DEPTH])
    h["l0f_ks"] = np.ascontiguousarray(
        np.asarray(inputs["L0_ffn_k_scale"], np.float32)[:DEPTH])
    DIM_HEAD, WINDOW = 64, 256
    half = np.arange(0, DIM_HEAD, 2, dtype=np.float64)
    inv_freq = 1.0 / 10000.0 ** (half / DIM_HEAD)
    tt = np.arange(T, dtype=np.float64)
    freqs = np.concatenate([tt[:, None] * inv_freq[None, :]] * 2, axis=-1)
    base_scale = (half + 0.4 * DIM_HEAD) / (1.4 * DIM_HEAD)
    power = (tt - T // 2) / (WINDOW // 2)
    scale = np.concatenate([base_scale[None, :] ** power[:, None]] * 2, axis=-1)
    cos, sin = np.cos(freqs), np.sin(freqs)
    sgn = np.ones(64); sgn[:32] = -1.0
    def timg(tb):
        return tb.reshape(NT, 128, 64).transpose(1, 0, 2).reshape(128, NT*64)
    h["l0f_tabs"] = np.ascontiguousarray(np.concatenate(
        [timg((cos * scale).astype(np.float32)),
         timg((sin * scale * sgn[None, :]).astype(np.float32)),
         timg((cos / scale).astype(np.float32)),
         timg((sin / scale * sgn[None, :]).astype(np.float32))], axis=1))


BLOCK_BUILDERS = {"l1_ffn": build_l1_ffn, "l1_attn_a": build_l1_attn_a, "l0_attn_b": build_l0_attn_b, "l0_attn_a": build_l0_attn_a, "l1_attn_b": build_l1_attn_b, "l0_ffn": build_l0_ffn}


def sb_image_kc(wT, kc=8):
    F = wT.shape[1]
    return np.ascontiguousarray(wT.reshape(kc, 128, F).transpose(1, 0, 2).reshape(128, kc*F))


def sb_image_slabs(wT, mslab=128):
    K, M = wT.shape
    kc = K // 128
    nm = M // mslab
    a = wT.reshape(kc, 128, nm, mslab).transpose(1, 2, 0, 3)
    return np.ascontiguousarray(a.reshape(128, nm*kc*mslab))


def prep_l1_ffn(inputs, h):
    wgT = np.asarray(inputs["L1_ffn_w_gate"]).T
    wuT = np.asarray(inputs["L1_ffn_w_up"]).T
    wdT = np.asarray(inputs["L1_ffn_w_down"]).T
    h["l1ffn_wg_sb"] = sb_image_kc(np.ascontiguousarray(wgT))
    h["l1ffn_wu_sb"] = sb_image_slabs(np.ascontiguousarray(wuT), 128)
    h["l1ffn_wd_sb"] = sb_image_kc(np.ascontiguousarray(wdT), kc=24)


BLOCK_PREP = {"l1_ffn": prep_l1_ffn, "l1_attn_a": prep_l1_attn_a, "l0_attn_b": prep_l0_attn_b, "l0_attn_a": prep_l0_attn_a, "l1_attn_b": prep_l1_attn_b, "l0_ffn": prep_l0_ffn}

ALL_BLOCKS = ["l0_attn_a", "l0_ffn", "l0_attn_b", "l0_ffn", "l1_attn_a", "l1_ffn",
              "l1_attn_b", "l1_ffn"]


def build_program(blocks):
    C = Ctx()
    C.P = Prog(C.nc)
    _build_body(C, blocks)
    C.nc.all_engine_barrier()
    C.nc.reset()
    return C


def _build_body(C, blocks):
    P = C.P
    alloc_common(C)
    x_in = C.dram_in("x_in", (128, NT*D), F32)
    x_out = C.dram_out("x_out", (128, NT*D), F32)
    with P.group("sync", writes=["x"]) as g:
        for t in range(0, NT, 8):
            g.dma_(C.x_tm[:, t*D:(t+8)*D], x_in[:, t*D:(t+8)*D])
    for i, b in enumerate(blocks):
        C.blk_i = i
        P.barrier()
        emit_prologue(C)
        BLOCK_BUILDERS[b](C, "b%d" % i)
    P.barrier()
    for dn, (dap, dkeys) in getattr(C, "debug_tiles", {}).items():
        dd = C.dram_out("dbg_" + dn, list(dap.shape), dap.dtype)
        with P.group("sync", reads=list(dkeys)) as g:
            g.dma_(dd, dap)
    with P.group("sync", reads=["x"], writes=["xout"]) as g:
        for t in range(0, NT, 8):
            g.dma_(x_out[:, t*D:(t+8)*D], C.x_tm[:, t*D:(t+8)*D])
    sem, cnt = P.sem["sync.d"], P.count["sync.d"]
    P.ops["sync"].append(lambda en, s=sem, v=cnt: en.wait_ge(s, v))
    P.barrier()
    P.finalize()


def run_blocks(x, inputs, blocks):
    C = build_program(blocks)
    h = {}
    for b in set(blocks):
        BLOCK_PREP[b](inputs, h)
    in_maps = []
    for c in range(N_CORES):
        xc = np.ascontiguousarray(x[c].reshape(NT, 128, D).transpose(1, 0, 2).reshape(128, NT*D))
        m = {"x_in": xc}
        m.update(h)
        in_maps.append(m)
    res = run_bass_kernel_spmd(C.nc, in_maps, core_ids=list(range(N_CORES)))
    outs = []
    for r in res.results:
        xo = np.asarray(r["x_out"]).reshape(128, NT, D).transpose(1, 0, 2).reshape(T, D)
        outs.append(xo)
    return np.stack(outs)


def kernel(**inputs):
    x = np.asarray(inputs["L0_attn_a_x"])
    x = run_blocks(x, inputs, ALL_BLOCKS)
    return x


# version 14 (final)
# speedup vs baseline: 1.3768x; 1.0188x over previous
import numpy as np
import concourse.bass as bass
import concourse.mybir as mybir
from concourse.bass_utils import run_bass_kernel_spmd

F32 = mybir.dt.float32
BF16 = mybir.dt.bfloat16
I32 = mybir.dt.int32
AF = mybir.ActivationFunctionType
ALU = mybir.AluOpType
AX = mybir.AxisListType

T = 2048
D = 1024
NT = T // 128
KC = D // 128
N_CORES = 8

SB_BASE = 16512
OFF_X = SB_BASE
OFF_YFM = SB_BASE + 65536
OFF_COM = SB_BASE + 98304
OFF_ARENA = SB_BASE + 107520
ARENA_SZ = 229376 - OFF_ARENA

ENGINES = ("tensor", "vector", "scalar", "gpsimd", "sync")


class Prog:

    def __init__(self, nc):
        self.nc = nc
        self.ops = {e: [] for e in ENGINES}
        self.sem = {e: nc.alloc_semaphore(f"sem_{e}") for e in ENGINES}
        self.sem.update({e + ".d": nc.alloc_semaphore(f"semd_{e}")
                         for e in ("sync", "gpsimd", "scalar", "vector", "tensor")})
        self.count = {s: 0 for s in self.sem}
        self.last_wait = {}
        self.writer = {}
        self.readers = {}

    def group(self, eng, reads=(), writes=(), chain=None):
        return _Group(self, eng, list(reads), list(writes), chain)

    def barrier(self):
        snap = dict(self.count)
        for e in ENGINES:
            for tgt, v in snap.items():
                if v == 0:
                    continue
                k = (e, tgt)
                if self.last_wait.get(k, 0) >= v:
                    continue
                self.last_wait[k] = v
                sem = self.sem[tgt]
                self.ops[e].append(lambda en, s=sem, v=v: en.wait_ge(s, v))

    def finalize(self):
        with self.nc.Block() as blk:
            for e in ENGINES:
                ops = self.ops[e]
                getattr(blk, e)(lambda en, ops=ops: [f(en) for f in ops])


class _Group:

    def __init__(self, P, eng, reads, writes, chain=None):
        self.P, self.eng, self.reads, self.writes = P, eng, reads, writes
        self.chain = chain if chain is not None else (eng != "tensor")
        self.items = []

    def emit(self, f):
        self.items.append(f)

    def dma_(self, out, in_, **kw):
        self.items.append(("dma", lambda en, o=out, i=in_, kw=kw: en.dma_start(out=o, in_=i, **kw)))

    def __enter__(self):
        return self

    def __exit__(self, et, ev, tb):
        if et is not None:
            return False
        P, eng = self.P, self.eng
        n_dma = sum(1 for it in self.items if isinstance(it, tuple))
        assert n_dma == 0 or n_dma == len(self.items), "don't mix dma and compute"
        stream = eng + ".d" if n_dma else eng
        deps = []
        for key in self.reads:
            w = P.writer.get(key)
            if w:
                deps.append(w)
        for key in self.writes:
            w = P.writer.get(key)
            if w:
                deps.append(w)
            for s2, cnt in P.readers.get(key, {}).items():
                deps.append((s2, cnt))
        waits = {}
        for tgt, cnt in deps:
            if eng == "tensor" and tgt == eng:
                continue
            if P.last_wait.get((eng, tgt), 0) >= cnt:
                continue
            waits[tgt] = max(waits.get(tgt, 0), cnt)
        for tgt, cnt in waits.items():
            P.last_wait[(eng, tgt)] = cnt
            sem = P.sem[tgt]
            P.ops[eng].append(lambda en, s=sem, v=cnt: en.wait_ge(s, v))
        sem = P.sem[stream]
        if n_dma:
            for _, f in self.items:
                P.ops[eng].append(lambda en, f=f, s=sem: f(en).then_inc(s, 16))
            P.count[stream] += 16 * n_dma
            tot = P.count[stream]
            P.ops[eng].append(lambda en, s=sem, v=tot: en.wait_ge(s, v))
        else:
            assert self.items, "empty group"
            if eng == "tensor":
                for f in self.items[:-1]:
                    P.ops[eng].append(f)
                last = self.items[-1]
                P.ops[eng].append(lambda en, f=last, s=sem: f(en).then_inc(s, 1))
            else:
                for j, f in enumerate(self.items[:-1]):
                    if j and self.chain:
                        P.ops[eng].append(lambda en: en.drain(fusable=True))
                    P.ops[eng].append(f)
                if len(self.items) > 1 and self.chain:
                    P.ops[eng].append(lambda en: en.drain(fusable=True))
                last = self.items[-1]
                P.ops[eng].append(lambda en, f=last, s=sem: f(en).then_inc(s, 1))
            P.count[stream] += 1
        cnt = P.count[stream]
        for key in self.writes:
            P.writer[key] = (stream, cnt)
            P.readers[key] = {}
        for key in self.reads:
            P.readers.setdefault(key, {})[stream] = cnt
        if PARANOID:
            P.barrier()
        return False


PARANOID = False
L0B_HLIM = 16
L0B_TLIM = 16


class Ctx:
    def __init__(self):
        self.nc = bass.Bass("TRN2", target_bir_lowering=False)
        self.dram = {}

    def dram_in(self, name, shape, dtype=F32):
        if name not in self.dram:
            self.dram[name] = self.nc.dram_tensor(name, list(shape), dtype, kind="ExternalInput").ap()
        return self.dram[name]

    def dram_out(self, name, shape, dtype=F32):
        self.dram[name] = self.nc.dram_tensor(name, list(shape), dtype, kind="ExternalOutput").ap()
        return self.dram[name]

    def dram_tmp(self, name, shape, dtype=F32):
        if name not in self.dram:
            self.dram[name] = self.nc.dram_tensor(name, list(shape), dtype, kind="Internal").ap()
        return self.dram[name]

    def sb_at(self, name, shape, dtype, off):
        nbytes = int(np.prod(shape[1:])) * mybir.dt.size(dtype)
        assert off + nbytes <= 229376, (name, off, nbytes)
        return self.nc.alloc_sbuf_tensor_at(name, list(shape), dtype, offset=off)

    def ps(self, name, shape, dtype=F32):
        return self.nc.alloc_psum_tensor(name, list(shape), dtype)


def dram2d(ap, row0, nrows, ncols, col0=0, row_len=None, p=128):
    if row_len is None:
        row_len = ap.tensor.shape[-1]
    ng = nrows // p
    return bass.AP(ap.tensor, row0 * row_len + col0,
                   [[row_len, p], [p * row_len, ng], [1, ncols]])


def alloc_common(C):
    C.x_tm = C.sb_at("x_tm", (128, NT*D), F32, OFF_X)
    C.y_fm = C.sb_at("y_fm", (128, KC*T), BF16, OFF_YFM)
    o = OFF_COM
    C.ln_stats = C.sb_at("ln_stats", (128, 5*NT), F32, o); o += 5*NT*4
    C.sb_sq = C.sb_at("sb_sq", (128, D), F32, o); o += D*4
    C.ident = C.sb_at("ident", (128, 128), BF16, o); o += 256
    C.identf = C.sb_at("identf", (128, 128), F32, o); o += 512
    C.ones_row = C.sb_at("ones_row", (1, 512), F32, o); o += 2048
    C.ones_cb = C.sb_at("ones_cb", (128, 1), BF16, o); o += 32
    C.com_off = o
    assert o <= OFF_ARENA
    C.consts = C.sb_at("consts", (128, 8), F32, o); o += 32
    C.com_off = o
    CONST_VALS = [1e-5, 1.0, 1e-8]
    for j, v in enumerate(CONST_VALS):
        C.nc.const_aps.register(F32, v, C.consts[:, j:j+1])
    C.ps_tr = C.ps("ps_tr", (128, 1024), BF16)
    C.psum_f32 = [C.ps("psf%d" % i, (128, 512), F32) for i in range(6)]
    C.ps_aux = C.ps("ps_aux", (128, 512), F32)
    P = C.P
    with P.group("gpsimd", writes=["consts"]) as g:
        for j, v in enumerate(CONST_VALS):
            g.emit(lambda g_, j=j, v=v: g_.memset(C.consts[:, j:j+1], v))
    with P.group("gpsimd", writes=["ident", "ones"]) as g:
        g.emit(lambda g_: g_.iota(C.ident[:], [[1, 128]], base=0, channel_multiplier=-1,
                                  allow_small_or_imprecise_dtypes=True))
        g.emit(lambda g_: g_.tensor_single_scalar(C.ident[:], C.ident[:], 0.0, ALU.is_equal))
        g.emit(lambda g_: g_.iota(C.identf[:], [[1, 128]], base=0, channel_multiplier=-1,
                                  allow_small_or_imprecise_dtypes=True))
        g.emit(lambda g_: g_.tensor_single_scalar(C.identf[:], C.identf[:], 0.0, ALU.is_equal))
        g.emit(lambda g_: g_.memset(C.ones_row[:], 1.0))
        g.emit(lambda g_: g_.memset(C.ones_cb[:], 1.0))


def emit_layernorm(C, x_tm, y_tm, src_key="x", dst_key="y_tm", eps=1e-5, nt=NT,
                   x_row=D, y_row=D):
    P = C.P
    st = C.ln_stats
    sq = C.sb_sq
    s_sum = st[:, 0*NT:0*NT+nt]
    s_ss = st[:, 1*NT:1*NT+nt]
    s_var = st[:, 2*NT:2*NT+nt]
    s_rstd = st[:, 3*NT:3*NT+nt]
    s_nb = st[:, 4*NT:4*NT+nt]
    for t in range(nt):
        xt = x_tm[:, t*x_row:t*x_row+D]
        with P.group("vector", reads=[src_key], writes=["ln_sum"]) as g:
            g.emit(lambda v, xt=xt, o=st[:, 0*NT+t:0*NT+t+1]: v.reduce_sum(o, xt, AX.X))
        with P.group("scalar", reads=[src_key], writes=["ln_ss"]) as g:
            g.emit(lambda s, xt=xt, o=st[:, 1*NT+t:1*NT+t+1]:
                   s.activation(sq[:], xt, AF.Square, accum_out=o))
    with P.group("vector", reads=["ln_sum", "ln_ss"], writes=["ln_var", "ln_sum"]) as g:
        g.emit(lambda v: v.tensor_scalar_mul(s_sum, s_sum, 1.0/D))
        g.emit(lambda v: v.tensor_scalar_mul(s_ss, s_ss, 1.0/D))
        g.emit(lambda v: v.tensor_tensor(s_var, s_sum, s_sum, ALU.mult))
        g.emit(lambda v: v.tensor_tensor(s_var, s_ss, s_var, ALU.subtract))
    with P.group("scalar", reads=["ln_var", "consts"], writes=["ln_rstd"]) as g:
        g.emit(lambda s: s.activation(s_rstd, s_var, AF.Ln, bias=eps))
        g.emit(lambda s: s.activation(s_rstd, s_rstd, AF.Exp, scale=-0.5))
    with P.group("vector", reads=["ln_rstd", "ln_sum"], writes=["ln_nb"]) as g:
        g.emit(lambda v: v.scalar_tensor_tensor(s_nb, s_sum, -1.0, s_rstd, ALU.mult, ALU.mult))
    for t in range(nt):
        xt = x_tm[:, t*x_row:t*x_row+D]
        yt = y_tm[:, t*y_row:t*y_row+D]
        with P.group("scalar", reads=[src_key, "ln_rstd", "ln_nb"], writes=[dst_key]) as g:
            g.emit(lambda s, xt=xt, yt=yt, sc=st[:, 3*NT+t:3*NT+t+1], b=st[:, 4*NT+t:4*NT+t+1]:
                   s.activation(yt, xt, AF.Identity, bias=b, scale=sc))


def emit_tm_to_fm(C, src_tm, dst_fm, src_key, dst_key, nt=NT, kc=KC,
                  src_row=None, dst_row=None, dst_col0=0):
    P = C.P
    pst = C.ps_tr
    ident = C.ident
    if src_row is None:
        src_row = kc * 128
    if dst_row is None:
        dst_row = nt * 128
    for c in range(kc):
        for k0 in range(0, nt, 8):
            kn = min(8, nt - k0)
            with P.group("tensor", reads=[src_key], writes=["ps_tr"]) as g:
                for j in range(kn):
                    t = k0 + j
                    src = src_tm[:, t*src_row + c*128: t*src_row + (c+1)*128]
                    g.emit(lambda p, s=src, o=pst[:, j*128:(j+1)*128]:
                           p.transpose(o, s, ident[:]))
            dst = dst_fm[:, c*dst_row + dst_col0 + k0*128:
                         c*dst_row + dst_col0 + (k0+kn)*128]
            with P.group("vector", reads=["ps_tr"], writes=[dst_key]) as g:
                g.emit(lambda v, d=dst, s=pst[:, :kn*128]: v.tensor_copy(out=d, in_=s))


def emit_matmul_acc(C, psum, lhsT_slices, rhs_slices, reads, writes):
    P = C.P
    with P.group("tensor", reads=list(reads), writes=list(writes)) as g:
        n = len(lhsT_slices)
        for i, (lt, rh) in enumerate(zip(lhsT_slices, rhs_slices)):
            g.emit(lambda p, lt=lt, rh=rh, st=(i == 0), sp=(i == n-1):
                   p.matmul(psum, lt, rh, start=st, stop=sp))


def emit_spill_x(C):
    P = C.P
    xs = C.dram_tmp("xspill", (128, NT*D), F32)
    with P.group("sync", reads=["x"], writes=["xspill"]) as g:
        for t in range(0, NT, 8):
            g.dma_(xs[:, t*D:(t+8)*D], C.x_tm[:, t*D:(t+8)*D])


def emit_reload_x(C, claims=()):
    P = C.P
    xs = C.dram_tmp("xspill", (128, NT*D), F32)
    with P.group("sync", reads=["xspill"], writes=["x"] + list(claims)) as g:
        for t in range(0, NT, 8):
            g.dma_(C.x_tm[:, t*D:(t+8)*D], xs[:, t*D:(t+8)*D])


def emit_prologue(C):
    y_tm = C.sb_at("y_tm_%d" % C.blk_i, (128, NT*D), BF16, OFF_ARENA)
    emit_layernorm(C, C.x_tm, y_tm, "x", "y_tm")
    emit_tm_to_fm(C, y_tm, C.y_fm, "y_tm", "y_fm")


def build_l1_ffn(C, tag):
    P = C.P
    S = 1.0 / 127.0
    H = 3072
    MC = H // 128
    wg_d = C.dram_in("l1ffn_wg_sb", (128, KC*H), I32)
    wu_d = C.dram_in("l1ffn_wu_sb", (128, MC*1024), I32)
    wd_d = C.dram_in("l1ffn_wd_sb", (128, MC*1024), I32)
    i = C.blk_i
    wu = C.sb_at("l1f_wu%d" % i, (128, MC*1024), BF16, OFF_X)
    hbuf = C.sb_at("l1f_h%d" % i, (128, MC*256), BF16, OFF_X + 49152)
    sgbuf = C.sb_at("l1f_sg%d" % i, (128, 256), F32, OFF_X + 61440)
    wg = C.sb_at("l1f_wg%d" % i, (128, KC*H), BF16, OFF_ARENA)
    wd = C.sb_at("l1f_wd%d" % i, (128, MC*1024), BF16, OFF_ARENA + 49152)
    xstage = C.sb_at("l1f_xs%d" % i, (128, 512), F32, OFF_ARENA + 98304)
    xsp = C.dram_tmp("xspill", (128, NT*D), F32)
    emit_spill_x(C)
    NLC = 4
    for q in range(NLC):
        with P.group("gpsimd", writes=["wg%d" % q, "y_tm"]) as g:
            g.dma_(wg[:, q*2*H:(q+1)*2*H], wg_d[:, q*2*H:(q+1)*2*H])
        with P.group("gpsimd", writes=["wu%d" % q] + (["x"] if q == 0 else [])) as g:
            g.dma_(wu[:, q*6144:(q+1)*6144], wu_d[:, q*6144:(q+1)*6144])
    for q in range(NLC):
        with P.group("gpsimd", writes=["wd%d" % q]) as g:
            g.dma_(wd[:, q*6144:(q+1)*6144], wd_d[:, q*6144:(q+1)*6144])
    PS = C.psum_f32

    for tc in range(0, NT, 2):
        for m in range(MC):
            ps_g, ps_u = PS[4][:, 0:256], PS[5][:, 0:256]
            lh = [wg[:, c*H + m*128: c*H + (m+1)*128] for c in range(KC)]
            rh = [C.y_fm[:, c*T + tc*128: (c*T) + (tc+2)*128] for c in range(KC)]
            emit_matmul_acc(C, ps_g, lh, rh,
                            reads=["wg%d" % (c*H//6144) for c in range(KC)] + ["y_fm"],
                            writes=["ps4"])
            lh = [wu[:, m*1024 + c*128: m*1024 + (c+1)*128] for c in range(KC)]
            emit_matmul_acc(C, ps_u, lh, rh, reads=["wu%d" % (m*1024//6144), "y_fm"],
                            writes=["ps5"])
            first = (tc == 0 and m == 0)
            with P.group("scalar", reads=["ps4"],
                         writes=["sg"] + (["x"] if first else [])) as g:
                g.emit(lambda s, ip=ps_g: s.activation(sgbuf[:, :], ip, AF.Silu, scale=S))
            with P.group("vector", reads=["ps5", "sg"],
                         writes=["h"] + (["x"] if first else [])) as g:
                g.emit(lambda v, ip=ps_u, o=hbuf[:, m*256:(m+1)*256]:
                       v.scalar_tensor_tensor(o, ip, S, sgbuf[:, :], ALU.mult, ALU.mult))
        for m in range(MC):
            with P.group("tensor", reads=["h", "wd%d" % (m*1024//6144)],
                         writes=["ps0", "ps1", "ps2", "ps3"]) as g:
                for tt in range(2):
                    for dh in range(2):
                        g.emit(lambda p, ps=PS[tt*2+dh][:, 0:512],
                               lt=hbuf[:, m*256 + tt*128: m*256 + (tt+1)*128],
                               rh=wd[:, m*1024 + dh*512: m*1024 + (dh+1)*512],
                               st=(m == 0), sp=(m == MC-1):
                               p.matmul(ps, lt, rh, start=st, stop=sp))
        for tt in range(2):
            for dh in range(2):
                ps = PS[tt*2 + dh][:, 0:512]
                with P.group("sync", reads=["xspill", "l1f_xst"], writes=["l1f_xst"]) as g:
                    g.dma_(xstage[:], xsp[:, (tc+tt)*D + dh*512: (tc+tt)*D + (dh+1)*512])
                with P.group("vector", reads=["ps%d" % (tt*2+dh), "l1f_xst"],
                             writes=["l1f_xst"]) as g:
                    g.emit(lambda v, ip=ps:
                           v.scalar_tensor_tensor(xstage[:], ip, S, xstage[:],
                                                  ALU.mult, ALU.add))
                with P.group("sync", reads=["l1f_xst"], writes=["xspill"]) as g:
                    g.dma_(xsp[:, (tc+tt)*D + dh*512: (tc+tt)*D + (dh+1)*512], xstage[:])
    emit_reload_x(C, claims=["wu0", "wu1", "wu2", "wu3", "h", "sg"])


def build_l1_attn_a(C, tag):
    P = C.P
    H = 16
    i = C.blk_i
    A = OFF_ARENA
    k_tm = C.sb_at("l1a_k%d" % i, (128, NT*D), BF16, A)
    v_tm = C.sb_at("l1a_v%d" % i, (128, NT*D), BF16, A + 32768)
    wsl = C.sb_at("l1a_w%d" % i, (128, KC*D), BF16, A + 65536)
    bkbc = C.sb_at("l1a_bk%d" % i, (128, D), F32, A + 81920)
    bvbc = C.sb_at("l1a_bv%d" % i, (128, D), F32, A + 86016)
    bobc = C.sb_at("l1a_bo%d" % i, (128, D), F32, A + 90112)
    bqc = C.sb_at("l1a_bq%d" % i, (128, KC), F32, A + 94208)
    kv_sb = C.sb_at("l1a_kv%d" % i, (128, 8*65), BF16, A + 94272)
    onesc = C.sb_at("l1a_one%d" % i, (128, 1), BF16, A + 95360)
    s1 = C.sb_at("l1a_s1%d" % i, (128, 512), F32, A + 95424)
    s2 = C.sb_at("l1a_s2%d" % i, (128, 512), F32, A + 97472)
    rzc = C.sb_at("l1a_rz%d" % i, (128, 1), F32, A + 99520)
    brow = C.sb_at("l1a_br%d" % i, (1, 512), F32, A + 99584)
    wq_d = C.dram_in("l1aa_wq", (128, KC*D), F32)
    wk_d = C.dram_in("l1aa_wk", (128, KC*D), F32)
    wv_d = C.dram_in("l1aa_wv", (128, KC*D), F32)
    wo_d = C.dram_in("l1aa_wo", (128, KC*D), F32)
    bq_d = C.dram_in("l1aa_bq", (128, KC), F32)
    bk_d = C.dram_in("l1aa_bk", (1, D), F32)
    bv_d = C.dram_in("l1aa_bv", (1, D), F32)
    bo_d = C.dram_in("l1aa_bo", (1, D), F32)
    PS = C.psum_f32
    kvps = C.ps_aux

    with P.group("gpsimd", writes=["bqc"]) as g:
        g.dma_(bqc[:], bq_d)
    with P.group("gpsimd", writes=["onesc"]) as g:
        g.emit(lambda g_: g_.memset(onesc[:], 1.0))
    for j, (bc, bd) in enumerate([(bkbc, bk_d), (bvbc, bv_d), (bobc, bo_d)]):
        for hh in range(2):
            pj = 4 + (2*j + hh) % 2
            with P.group("gpsimd", writes=["brow"]) as g:
                g.dma_(brow[:], bd[0:1, hh*512:(hh+1)*512])
            emit_matmul_acc(C, PS[pj][:, 0:512],
                            [C.ones_row[0:1, 0:128]], [brow[0:1, :]],
                            reads=["brow", "ones"], writes=["ps%d" % pj])
            with P.group("vector", reads=["ps%d" % pj], writes=["biasbc"]) as g:
                g.emit(lambda v, ps=PS[pj][:, 0:512], o=bc[:, hh*512:(hh+1)*512]:
                       v.tensor_copy(out=o, in_=ps))

    def load_w(dram, extra_claims=()):
        with P.group("gpsimd", writes=["wsl"] + list(extra_claims)) as g:
            for c in range(0, KC, 4):
                g.dma_(wsl[:, c*D:(c+4)*D], dram[:, c*D:(c+4)*D])

    def fmap_evac(ps, pskey, dsl, bias_args, dst_keys):
        with P.group("vector", reads=[pskey, "biasbc", "bqc"], writes=["s1"]) as g:
            if bias_args[0] == "col":
                g.emit(lambda v, ps=ps, b=bias_args[1]:
                       v.tensor_scalar(s1[:, 0:ps.shape[1]], ps, b, None, ALU.add))
            else:
                g.emit(lambda v, ps=ps, b=bias_args[1]:
                       v.tensor_add(s1[:, 0:ps.shape[1]], ps, b))
            g.emit(lambda v, n=ps.shape[1]: v.tensor_scalar_min(s2[:, 0:n], s1[:, 0:n], 0.0))
        with P.group("scalar", reads=["s1"], writes=["s2e"]) as g:
            g.emit(lambda s, n=ps.shape[1]: s.activation(s2[:, 0:n], s2[:, 0:n], AF.Exp))
        with P.group("vector", reads=["s2e", "s1"], writes=list(dst_keys)) as g:
            g.emit(lambda v, o=dsl, n=ps.shape[1]: v.scalar_tensor_tensor(
                o, s1[:, 0:n], 0.0, s2[:, 0:n], ALU.max, ALU.add))

    def proj_tm(dst, dst_key, bias_bc, fmap, extra_claims=()):
        for t in range(NT):
            for mh in range(2):
                pj = (2*t + mh) % 2
                ps = PS[pj][:, 0:512]
                lh = [C.y_fm[:, c*T + t*128:(c*T) + (t+1)*128] for c in range(KC)]
                rh = [wsl[:, c*D + mh*512: c*D + (mh+1)*512] for c in range(KC)]
                emit_matmul_acc(C, ps, lh, rh, reads=["wsl", "y_fm"],
                                writes=["ps%d" % pj])
                dsl = dst[:, t*D + mh*512: t*D + (mh+1)*512]
                bsl = bias_bc[:, mh*512:(mh+1)*512]
                if not fmap:
                    with P.group("vector", reads=["ps%d" % pj, "biasbc"],
                                 writes=[dst_key] + list(extra_claims)) as g:
                        g.emit(lambda v, ps=ps, o=dsl, b=bsl: v.tensor_add(o, ps, b))
                else:
                    fmap_evac(ps, "ps%d" % pj, dsl, ("bc", bsl),
                              [dst_key] + list(extra_claims))

    load_w(wk_d, ["y_tm"])
    proj_tm(k_tm, "k_tm", bkbc, True)
    load_w(wv_d)
    proj_tm(v_tm, "v_tm", bvbc, False)
    for h in range(H):
        p0 = (h % 2) * 64
        tp = (0, p0)
        out_ap = kvps[p0:p0+64, 0:65]
        with P.group("tensor", reads=["k_tm", "v_tm", "onesc"], writes=["ps_aux"]) as g:
            for t in range(NT):
                g.emit(lambda p, lt=k_tm[:, t*D + h*64: t*D + (h+1)*64],
                       rh=v_tm[:, t*D + h*64: t*D + (h+1)*64], st=(t == 0),
                       sp=(t == NT-1), o=out_ap[:, 0:64], tp=tp:
                       p.matmul(o, lt, rh, start=st, stop=sp, tile_position=tp))
            for t in range(NT):
                g.emit(lambda p, lt=k_tm[:, t*D + h*64: t*D + (h+1)*64], st=(t == 0),
                       sp=(t == NT-1), o=out_ap[:, 64:65], tp=tp:
                       p.matmul(o, lt, onesc[:], start=st, stop=sp, tile_position=tp))
        with P.group("vector", reads=["ps_aux"], writes=["kv_sb"]) as g:
            g.emit(lambda v, o=kv_sb[p0:p0+64, (h//2)*65:(h//2+1)*65], s=out_ap:
                   v.tensor_copy(out=o, in_=s))
    q_fm = C.sb_at("l1a_q%d" % i, (128, KC*T), BF16, A)
    load_w(wq_d)
    for cc in range(KC):
        for tq in range(4):
            pj = (cc + tq) % 2
            ps = PS[pj][:, 0:512]
            lh = [wsl[:, c*D + cc*128: c*D + (cc+1)*128] for c in range(KC)]
            rh = [C.y_fm[:, c*T + tq*512:(c*T) + (tq+1)*512] for c in range(KC)]
            emit_matmul_acc(C, ps, lh, rh, reads=["wsl", "y_fm"],
                            writes=["ps%d" % pj])
            dsl = q_fm[:, cc*T + tq*512: cc*T + (tq+1)*512]
            fmap_evac(ps, "ps%d" % pj, dsl, ("col", bqc[:, cc:cc+1]),
                      ["q_fm", "k_tm"])
    out_tm = C.sb_at("l1a_o%d" % i, (128, NT*D), BF16, A + 32768)
    for t in range(NT):
        for h in range(H):
            p0 = (h % 2) * 64
            pj = h % 4
            ps = PS[pj][:, 0:65]
            with P.group("tensor", reads=["q_fm", "kv_sb"], writes=["ps%d" % pj]) as g:
                g.emit(lambda p, ps=ps,
                       lt=q_fm[p0:p0+64, (h//2)*T + t*128:(h//2)*T + (t+1)*128],
                       rh=kv_sb[p0:p0+64, (h//2)*65:(h//2+1)*65]:
                       p.matmul(ps, lt, rh))
            with P.group("vector", reads=["ps%d" % pj], writes=["out_tm", "v_tm"]) as g:
                g.emit(lambda v, ps=ps: v.tensor_scalar_max(rzc[:], ps[:, 64:65], 1e-6))
                g.emit(lambda v: v.reciprocal(rzc[:], rzc[:]))
                g.emit(lambda v, ps=ps, o=out_tm[:, t*D + h*64: t*D + (h+1)*64]:
                       v.tensor_scalar(o, ps[:, 0:64], rzc[:], None, ALU.mult))
    out_fm = C.sb_at("l1a_of%d" % i, (128, KC*T), BF16, A)
    emit_tm_to_fm(C, out_tm, out_fm, "out_tm", "out_fm")
    load_w(wo_d)
    for t in range(NT):
        for mh in range(2):
            pj = (2*t + mh) % 2
            ps = PS[pj][:, 0:512]
            lh = [out_fm[:, c*T + t*128:(c*T) + (t+1)*128] for c in range(KC)]
            rh = [wsl[:, c*D + mh*512: c*D + (mh+1)*512] for c in range(KC)]
            emit_matmul_acc(C, ps, lh, rh, reads=["wsl", "out_fm"],
                            writes=["ps%d" % pj])
            xs = C.x_tm[:, t*D + mh*512: t*D + (mh+1)*512]
            with P.group("vector", reads=["ps%d" % pj, "biasbc", "x"],
                         writes=["x"]) as g:
                g.emit(lambda v, ps=ps, b=bobc[:, mh*512:(mh+1)*512]:
                       v.tensor_add(s1[:], ps, b))
                g.emit(lambda v, o=xs: v.tensor_add(o, o, s1[:]))


def prep_l1_attn_a(inputs, h):
    for nm, key in [("Wq", "l1aa_wq"), ("Wk", "l1aa_wk"), ("Wv", "l1aa_wv"),
                    ("Wo", "l1aa_wo")]:
        w = np.asarray(inputs["L1_attn_a_" + nm], np.float32)
        h[key] = sb_image_kc(np.ascontiguousarray(w))
    h["l1aa_bq"] = np.ascontiguousarray(
        np.asarray(inputs["L1_attn_a_bq"], np.float32).reshape(KC, 128).T)
    h["l1aa_bk"] = np.asarray(inputs["L1_attn_a_bk"], np.float32).reshape(1, D)
    h["l1aa_bv"] = np.asarray(inputs["L1_attn_a_bv"], np.float32).reshape(1, D)
    h["l1aa_bo"] = np.asarray(inputs["L1_attn_a_bo"], np.float32).reshape(1, D)


def build_l0_attn_b(C, tag):
    P = C.P
    H = 16
    i = C.blk_i
    A = OFF_ARENA
    KROW = T + 256
    q_fm = C.sb_at("l0b_q%d" % i, (128, KC*T), BF16, OFF_X)
    out_fm = C.sb_at("l0b_of%d" % i, (128, KC*T), BF16, OFF_YFM)
    k_fm = C.sb_at("l0b_k%d" % i, (128, KC*KROW), BF16, A)
    v_tm = C.sb_at("l0b_v%d" % i, (128, 18*D), BF16, A + 36864)
    wsl = C.sb_at("l0b_w%d" % i, (128, KC*D), BF16, A + 73728)
    madd = C.sb_at("l0b_m%d" % i, (128, 384), F32, A + 90112)
    s1 = C.sb_at("l0b_s1%d" % i, (128, 384), F32, A + 91648)
    attn_sb = C.sb_at("l0b_a%d" % i, (128, 384), BF16, A + 93184)
    attnT_sb = C.sb_at("l0b_at%d" % i, (128, 384), BF16, A + 93952)
    bobc = C.sb_at("l0b_bo%d" % i, (128, D), F32, A + 94720)
    mcol = C.sb_at("l0b_mc%d" % i, (128, 4), F32, A + 98816)
    brow = C.sb_at("l0b_br%d" % i, (1, 512), F32, A + 98848)
    wq_d = C.dram_in("l0ab_wq", (128, KC*D), F32)
    wk_d = C.dram_in("l0ab_wk", (128, KC*D), F32)
    wv_d = C.dram_in("l0ab_wv", (128, KC*D), F32)
    wo_d = C.dram_in("l0ab_wo", (128, KC*D), F32)
    bo_d = C.dram_in("l0ab_bo", (1, D), F32)
    PS = C.psum_f32

    emit_spill_x(C)

    def load_w(dram, extra_claims=()):
        with P.group("gpsimd", writes=["wsl"] + list(extra_claims)) as g:
            for c in range(0, KC, 4):
                g.dma_(wsl[:, c*D:(c+4)*D], dram[:, c*D:(c+4)*D])

    for hh in range(2):
        with P.group("gpsimd", writes=["brow"]) as g:
            g.dma_(brow[:], bo_d[0:1, hh*512:(hh+1)*512])
        pj = 4 + hh
        emit_matmul_acc(C, PS[pj][:, 0:512], [C.ones_row[0:1, 0:128]], [brow[0:1, :]],
                        reads=["brow", "ones"], writes=["ps%d" % pj])
        with P.group("vector", reads=["ps%d" % pj], writes=["biasbc"]) as g:
            g.emit(lambda v, ps=PS[pj][:, 0:512], o=bobc[:, hh*512:(hh+1)*512]:
                   v.tensor_copy(out=o, in_=ps))
    with P.group("gpsimd", writes=["madd", "s1g"]) as g:
        g.emit(lambda g_: g_.iota(madd[:], [[1, 384]], base=0, channel_multiplier=-1,
                                  allow_small_or_imprecise_dtypes=True))
        g.emit(lambda g_: g_.tensor_single_scalar(s1[:], madd[:], 96.0, ALU.is_ge))
        g.emit(lambda g_: g_.tensor_single_scalar(madd[:], madd[:], 159.0, ALU.is_le))
        g.emit(lambda g_: g_.tensor_mul(madd[:], madd[:], s1[:]))
        g.emit(lambda g_: g_.tensor_scalar_add(madd[:], madd[:], -1.0))
        g.emit(lambda g_: g_.tensor_scalar_mul(madd[:], madd[:], 1e30))

    load_w(wq_d)
    for cc in range(KC):
        for tq in range(4):
            pj = (cc + tq) % 2
            ps = PS[pj][:, 0:512]
            lh = [wsl[:, c*D + cc*128: c*D + (cc+1)*128] for c in range(KC)]
            rh = [C.y_fm[:, c*T + tq*512:(c*T) + (tq+1)*512] for c in range(KC)]
            emit_matmul_acc(C, ps, lh, rh, reads=["wsl", "y_fm"], writes=["ps%d" % pj])
            with P.group("vector", reads=["ps%d" % pj], writes=["q_fm", "x"]) as g:
                g.emit(lambda v, ps=ps, o=q_fm[:, cc*T + tq*512: cc*T + (tq+1)*512]:
                       v.tensor_copy(out=o, in_=ps))
    load_w(wk_d, ["y_tm"])
    with P.group("gpsimd", writes=["k_fm"]) as g:
        for c in range(KC):
            g.emit(lambda g_, c=c: g_.memset(k_fm[:, c*KROW: c*KROW + 128], 0.0))
            g.emit(lambda g_, c=c: g_.memset(k_fm[:, c*KROW + 128 + T: (c+1)*KROW], 0.0))
    for cc in range(KC):
        for tq in range(4):
            pj = (cc + tq) % 2
            ps = PS[pj][:, 0:512]
            lh = [wsl[:, c*D + cc*128: c*D + (cc+1)*128] for c in range(KC)]
            rh = [C.y_fm[:, c*T + tq*512:(c*T) + (tq+1)*512] for c in range(KC)]
            emit_matmul_acc(C, ps, lh, rh, reads=["wsl", "y_fm"], writes=["ps%d" % pj])
            with P.group("vector", reads=["ps%d" % pj], writes=["k_fm"]) as g:
                g.emit(lambda v, ps=ps,
                       o=k_fm[:, cc*KROW + 128 + tq*512: cc*KROW + 128 + (tq+1)*512]:
                       v.tensor_copy(out=o, in_=ps))
    load_w(wv_d)
    with P.group("gpsimd", writes=["v_tm"]) as g:
        g.emit(lambda g_: g_.memset(v_tm[:, 0:D], 0.0))
        g.emit(lambda g_: g_.memset(v_tm[:, 17*D:18*D], 0.0))
    for t in range(NT):
        for mh in range(2):
            pj = (2*t + mh) % 2
            ps = PS[pj][:, 0:512]
            lh = [C.y_fm[:, c*T + t*128:(c*T) + (t+1)*128] for c in range(KC)]
            rh = [wsl[:, c*D + mh*512: c*D + (mh+1)*512] for c in range(KC)]
            emit_matmul_acc(C, ps, lh, rh, reads=["wsl", "y_fm"], writes=["ps%d" % pj])
            with P.group("vector", reads=["ps%d" % pj], writes=["v_tm"]) as g:
                g.emit(lambda v, ps=ps,
                       o=v_tm[:, (t+1)*D + mh*512: (t+1)*D + (mh+1)*512]:
                       v.tensor_copy(out=o, in_=ps))
    for h in range(L0B_HLIM):
        p0 = (h % 2) * 64
        cc = h // 2
        for t in range(L0B_TLIM):
            pj = (h + t) % 2
            ps = PS[pj][:, 0:384]
            with P.group("tensor", reads=["q_fm", "k_fm"], writes=["ps%d" % pj]) as g:
                g.emit(lambda p, ps=ps,
                       lt=q_fm[p0:p0+64, cc*T + t*128: cc*T + (t+1)*128],
                       rh=k_fm[p0:p0+64, cc*KROW + t*128: cc*KROW + t*128 + 384]:
                       p.matmul(ps, lt, rh))
            with P.group("vector", reads=["ps%d" % pj, "madd"], writes=["s1g"]) as g:
                g.emit(lambda v, ps=ps: v.scalar_tensor_tensor(
                    s1[:], ps, 0.125, madd[:], ALU.mult, ALU.add))
                g.emit(lambda v: v.reduce_max(mcol[:, 0:1], s1[:], AX.X))
                g.emit(lambda v: v.tensor_scalar_mul(mcol[:, 1:2], mcol[:, 0:1], -1.0))
            with P.group("scalar", reads=["s1g"], writes=["s1e"]) as g:
                g.emit(lambda s: s.activation(s1[:], s1[:], AF.Exp,
                                              bias=mcol[:, 1:2], accum_out=mcol[:, 2:3]))
            with P.group("vector", reads=["s1e"], writes=["attn_sb"]) as g:
                g.emit(lambda v: v.reciprocal(mcol[:, 3:4], mcol[:, 2:3]))
                g.emit(lambda v: v.tensor_scalar(attn_sb[:], s1[:], mcol[:, 3:4],
                                                 None, ALU.mult))
            with P.group("tensor", reads=["attn_sb"], writes=["ps_tr"]) as g:
                for jj in range(3):
                    g.emit(lambda p, jj=jj: p.transpose(
                        C.ps_tr[:, jj*128:(jj+1)*128],
                        attn_sb[:, jj*128:(jj+1)*128], C.ident[:]))
            with P.group("vector", reads=["ps_tr"], writes=["attnT"]) as g:
                g.emit(lambda v: v.tensor_copy(out=attnT_sb[:], in_=C.ps_tr[:, 0:384]))
            po = C.ps_aux[p0:p0+64, 0:128]
            with P.group("tensor", reads=["attnT", "v_tm"], writes=["ps_aux"]) as g:
                for jj in range(3):
                    g.emit(lambda p, po=po, tp=(0, p0),
                           lt=v_tm[:, (t+jj)*D + h*64: (t+jj)*D + (h+1)*64],
                           rh=attnT_sb[:, jj*128:(jj+1)*128],
                           st=(jj == 0), sp=(jj == 2):
                           p.matmul(po, lt, rh, start=st, stop=sp, tile_position=tp))
            with P.group("vector", reads=["ps_aux"], writes=["out_fm", "y_fm"]) as g:
                g.emit(lambda v, po=po,
                       o=out_fm[p0:p0+64, cc*T + t*128: cc*T + (t+1)*128]:
                       v.tensor_copy(out=o, in_=po))
            if L0B_HLIM == 1:
                with P.group("vector", reads=["ps_aux"], writes=["sb_sq"]) as g:
                    g.emit(lambda v, po=po: v.tensor_copy(out=C.sb_sq[p0:p0+64, 0:128], in_=po))
    emit_reload_x(C, claims=["q_fm"])
    load_w(wo_d)
    for t in range(NT):
        for mh in range(2):
            pj = (2*t + mh) % 2
            ps = PS[pj][:, 0:512]
            lh = [out_fm[:, c*T + t*128:(c*T) + (t+1)*128] for c in range(KC)]
            rh = [wsl[:, c*D + mh*512: c*D + (mh+1)*512] for c in range(KC)]
            emit_matmul_acc(C, ps, lh, rh, reads=["wsl", "out_fm"], writes=["ps%d" % pj])
            xs = C.x_tm[:, t*D + mh*512: t*D + (mh+1)*512]
            with P.group("vector", reads=["ps%d" % pj, "biasbc", "x"], writes=["x"]) as g:
                g.emit(lambda v, ps=ps, b=bobc[:, mh*512:(mh+1)*512]:
                       v.tensor_add(s1[:, 0:256] if False else C.sb_sq[:, 0:512], ps, b))
                g.emit(lambda v, o=xs: v.tensor_add(o, o, C.sb_sq[:, 0:512]))


def prep_l0_attn_b(inputs, h):
    for nm, key in [("Wq", "l0ab_wq"), ("Wk", "l0ab_wk"), ("Wv", "l0ab_wv"),
                    ("Wo", "l0ab_wo")]:
        w = np.asarray(inputs["L0_attn_b_" + nm], np.float32).T
        h[key] = sb_image_kc(np.ascontiguousarray(w))
    h["l0ab_bo"] = np.asarray(inputs["L0_attn_b_bo"], np.float32).reshape(1, D)


def build_l0_attn_a(C, tag):
    P = C.P
    M = 256
    i = C.blk_i
    A = OFF_ARENA
    CSC = 1.0 / (16.0 * float(np.sqrt(T)))
    k_tm = C.sb_at("l0a_k%d" % i, (128, NT*M), BF16, A)
    q_fm = C.sb_at("l0a_q%d" % i, (128, 2*T), BF16, A + 8192)
    v_fm = C.sb_at("l0a_v%d" % i, (128, 2*T), BF16, A + 16384)
    r_fm = C.sb_at("l0a_r%d" % i, (128, 2*T), BF16, A + 24576)
    cumbc = C.sb_at("l0a_cb%d" % i, (128, T), F32, A + 32768)
    cumT = C.sb_at("l0a_ct%d" % i, (128, NT), F32, A + 40960)
    masks = C.sb_at("l0a_mk%d" % i, (128, 4*512), F32, A + 41024)
    s1 = C.sb_at("l0a_s1%d" % i, (128, 512), F32, A + 49216)
    s2 = C.sb_at("l0a_s2%d" % i, (128, 512), F32, A + 51264)
    w16 = C.sb_at("l0a_w16%d" % i, (128, 512), BF16, A + 53312)
    wk = C.sb_at("l0a_wk%d" % i, (128, KC*M), BF16, A + 54336)
    wq = C.sb_at("l0a_wq%d" % i, (128, KC*M), BF16, A + 58432)
    wv = C.sb_at("l0a_wv%d" % i, (128, KC*M), BF16, A + 62528)
    wo = C.sb_at("l0a_wo%d" % i, (128, 2*D), BF16, A + 66624)
    wg = C.sb_at("l0a_wg%d" % i, (128, KC), BF16, A + 70720)
    grow = C.sb_at("l0a_g%d" % i, (1, T), F32, A + 70784)
    onerow = C.sb_at("l0a_or%d" % i, (1, T), F32, A + 78976)
    bgc = C.sb_at("l0a_bg%d" % i, (1, 1), F32, A + 87168)
    wk_d = C.dram_in("l0aa_wkT", (128, KC*M), F32)
    wq_d = C.dram_in("l0aa_wqT", (128, KC*M), F32)
    wv_d = C.dram_in("l0aa_wvT", (128, KC*M), F32)
    wo_d = C.dram_in("l0aa_woT", (128, 2*D), F32)
    wg_d = C.dram_in("l0aa_wgT", (128, KC), F32)
    bg_d = C.dram_in("l0aa_bg", (1, 1), F32)
    PS = C.psum_f32

    with P.group("gpsimd", writes=["l0a_w", "y_tm"]) as g:
        g.dma_(wk[:], wk_d)
        g.dma_(wq[:], wq_d)
        g.dma_(wv[:], wv_d)
        g.dma_(wo[:], wo_d)
        g.dma_(wg[:], wg_d)
        g.dma_(bgc[:], bg_d)
    with P.group("gpsimd", writes=["l0a_ones", "l0a_masks", "l0a_s1"]) as g:
        g.emit(lambda g_: g_.memset(onerow[:], 1.0))
        g.emit(lambda g_: g_.iota(s1[:], [[1, 512]], base=0, channel_multiplier=-1,
                                  allow_small_or_imprecise_dtypes=True))
        for r in range(4):
            g.emit(lambda g_, r=r: g_.tensor_single_scalar(
                masks[:, r*512:(r+1)*512], s1[:], float(r*128), ALU.is_ge))
        g.emit(lambda g_: g_.tensor_scalar_mul(masks[:], masks[:], CSC))
    for mc in range(2):
        for tq in range(4):
            for dst, w, key in [(q_fm, wq, "q_fm"), (v_fm, wv, "v_fm")]:
                pj = (mc*4 + tq) % 2
                ps = PS[pj][:, 0:512]
                lh = [w[:, c*M + mc*128: c*M + (mc+1)*128] for c in range(KC)]
                rh = [C.y_fm[:, c*T + tq*512:(c*T) + (tq+1)*512] for c in range(KC)]
                emit_matmul_acc(C, ps, lh, rh, reads=["l0a_w", "y_fm"],
                                writes=["ps%d" % pj])
                with P.group("vector", reads=["ps%d" % pj], writes=[key]) as g:
                    g.emit(lambda v, ps=ps, o=dst[:, mc*T + tq*512: mc*T + (tq+1)*512]:
                           v.tensor_copy(out=o, in_=ps))
    for t in range(NT):
        pj = t % 2
        ps = PS[pj][:, 0:M]
        lh = [C.y_fm[:, c*T + t*128:(c*T) + (t+1)*128] for c in range(KC)]
        rh = [wk[:, c*M:(c+1)*M] for c in range(KC)]
        emit_matmul_acc(C, ps, lh, rh, reads=["l0a_w", "y_fm"], writes=["ps%d" % pj])
        with P.group("vector", reads=["ps%d" % pj], writes=["k_tm"]) as g:
            g.emit(lambda v, ps=ps, o=k_tm[:, t*M:(t+1)*M]: v.tensor_copy(out=o, in_=ps))
    for tq in range(4):
        ps = PS[4][:, 0:512]
        lh = [wg[:, c:c+1] for c in range(KC)]
        rh = [C.y_fm[:, c*T + tq*512:(c*T) + (tq+1)*512] for c in range(KC)]
        emit_matmul_acc(C, ps[0:1, :], lh, rh, reads=["l0a_w", "y_fm"], writes=["ps4"])
        with P.group("scalar", reads=["ps4", "l0a_w", "consts"], writes=["grow"]) as g:
            g.emit(lambda s, ps=ps, o=grow[0:1, tq*512:(tq+1)*512]:
                   s.activation(o, ps[0:1, :], AF.Sigmoid, bias=bgc[0:1, 0:1]))
            g.emit(lambda s, o=grow[0:1, tq*512:(tq+1)*512]:
                   s.activation(o, o, AF.Ln, bias=1e-8))
    with P.group("vector", reads=["grow", "l0a_ones"], writes=["grow"]) as g:
        g.emit(lambda v: v.tensor_tensor_scan(grow[0:1, :], onerow[0:1, :],
                                              grow[0:1, :], 0.0, ALU.mult, ALU.add))
    for tq in range(4):
        pj = 4 + tq % 2
        emit_matmul_acc(C, PS[pj][:, 0:512], [C.ones_row[0:1, 0:128]],
                        [grow[0:1, tq*512:(tq+1)*512]],
                        reads=["grow", "ones"], writes=["ps%d" % pj])
        with P.group("vector", reads=["ps%d" % pj], writes=["cumbc"]) as g:
            g.emit(lambda v, ps=PS[pj][:, 0:512], o=cumbc[:, tq*512:(tq+1)*512]:
                   v.tensor_copy(out=o, in_=ps))
    with P.group("tensor", reads=["grow", "ident"], writes=["ps_aux"]) as g:
        for t in range(NT):
            g.emit(lambda p, t=t: p.transpose(C.ps_aux[:, t:t+1],
                                              grow[0:1, t*128:(t+1)*128],
                                              C.identf[0:1, 0:1]))
    with P.group("vector", reads=["ps_aux"], writes=["cumT"]) as g:
        g.emit(lambda v: v.tensor_copy(out=cumT[:], in_=C.ps_aux[:, 0:NT]))
    for jc in range(4):
        first = True
        for it in range(0, (jc+1)*4):
            pj = it % 2
            ps = PS[pj][:, 0:512]
            lh = [v_fm[:, mc2*T + it*128: mc2*T + (it+1)*128] for mc2 in range(2)]
            rh = [q_fm[:, mc2*T + jc*512: mc2*T + (jc+1)*512] for mc2 in range(2)]
            emit_matmul_acc(C, ps, lh, rh, reads=["q_fm", "v_fm"], writes=["ps%d" % pj])
            with P.group("vector", reads=["cumbc", "cumT"], writes=["l0a_s1"]) as g:
                g.emit(lambda v, j0=jc*512, it=it: v.tensor_scalar(
                    s1[:], cumbc[:, j0:j0+512], cumT[:, it:it+1], 0.0,
                    ALU.subtract, ALU.min))
            with P.group("scalar", reads=["l0a_s1"], writes=["l0a_s2"]) as g:
                g.emit(lambda s: s.activation(s2[:], s1[:], AF.Exp))
            diag = (it // 4 == jc)
            with P.group("vector", reads=["l0a_s2", "ps%d" % pj, "l0a_masks"],
                         writes=["l0a_w16"] + (["l0a_s1"] if diag else [])) as g:
                if diag:
                    g.emit(lambda v, ps=ps: v.tensor_mul(s1[:], ps, s2[:]))
                    g.emit(lambda v, r=it % 4: v.tensor_tensor(
                        w16[:], s1[:], masks[:, r*512:(r+1)*512], ALU.mult))
                else:
                    g.emit(lambda v, ps=ps: v.scalar_tensor_tensor(
                        w16[:], ps, CSC, s2[:], ALU.mult, ALU.mult))
            for mc in range(2):
                with P.group("tensor", reads=["l0a_w16", "k_tm"],
                             writes=["ps%d" % (2+mc)]) as g:
                    g.emit(lambda p, lt=k_tm[:, it*M + mc*128: it*M + (mc+1)*128],
                           ps=PS[2+mc][:, 0:512],
                           st=first, sp=(it == (jc+1)*4 - 1):
                           p.matmul(ps, lt, w16[:], start=st, stop=sp))
            first = False
        for mc in range(2):
            with P.group("vector", reads=["ps%d" % (2+mc)], writes=["r_fm"]) as g:
                g.emit(lambda v, ps=PS[2+mc][:, 0:512],
                       o=r_fm[:, mc*T + jc*512: mc*T + (jc+1)*512]:
                       v.tensor_copy(out=o, in_=ps))
    for t in range(NT):
        for dh in range(2):
            pj = (2*t + dh) % 2
            ps = PS[pj][:, 0:512]
            lh = [r_fm[:, mc*T + t*128: mc*T + (t+1)*128] for mc in range(2)]
            rh = [wo[:, mc*D + dh*512: mc*D + (dh+1)*512] for mc in range(2)]
            emit_matmul_acc(C, ps, lh, rh, reads=["r_fm", "l0a_w"], writes=["ps%d" % pj])
            xs = C.x_tm[:, t*D + dh*512: t*D + (dh+1)*512]
            with P.group("vector", reads=["ps%d" % pj, "x"], writes=["x"]) as g:
                g.emit(lambda v, ps=ps, o=xs: v.tensor_add(o, o, ps))


def prep_l0_attn_a(inputs, h):
    h["l0aa_wkT"] = sb_image_kc(np.ascontiguousarray(
        np.asarray(inputs["L0_attn_a_Wk"], np.float32).T))
    h["l0aa_wqT"] = sb_image_kc(np.ascontiguousarray(
        np.asarray(inputs["L0_attn_a_Wq"], np.float32).T))
    h["l0aa_wvT"] = sb_image_kc(np.ascontiguousarray(
        np.asarray(inputs["L0_attn_a_Wv"], np.float32).T))
    h["l0aa_woT"] = sb_image_kc(np.ascontiguousarray(
        np.asarray(inputs["L0_attn_a_Wo"], np.float32).T), kc=2)
    h["l0aa_wgT"] = sb_image_kc(np.ascontiguousarray(
        np.asarray(inputs["L0_attn_a_Wg"], np.float32).T))
    h["l0aa_bg"] = np.asarray(inputs["L0_attn_a_bg"], np.float32).reshape(1, 1)


def build_l1_attn_b(C, tag):
    P = C.P
    DI = 2048
    MCI = DI // 128
    DS = 16
    DR = 64
    i = C.blk_i
    A = OFF_ARENA
    xc = C.sb_at("l1b_xc%d" % i, (128, MCI*T), BF16, OFF_X)
    winr = [C.sb_at("l1b_wi%d_%d" % (i, r), (128, KC*128), BF16, A + r*2048)
            for r in range(4)]
    stage = C.sb_at("l1b_st%d" % i, (128, 516), F32, A + 8192)
    sres_st = C.sb_at("l1b_sr%d" % i, (128, T), BF16, A + 10304)
    binc = C.sb_at("l1b_bi%d" % i, (128, 32), F32, A + 14400)
    cwc = C.sb_at("l1b_cw%d" % i, (128, 64), F32, A + 14528)
    cbc = C.sb_at("l1b_cb%d" % i, (128, 16), F32, A + 14784)
    alog = C.sb_at("l1b_al%d" % i, (128, 256), F32, A + 14848)
    asb = C.sb_at("l1b_as%d" % i, (128, 256), F32, A + 15872)
    dpc = C.sb_at("l1b_dp%d" % i, (128, 16), F32, A + 16896)
    bdtc = C.sb_at("l1b_bd%d" % i, (128, 16), F32, A + 16960)
    wxc = C.sb_at("l1b_wx%d" % i, (128, MCI*96), BF16, A + 17024)
    wdt = C.sb_at("l1b_wd%d" % i, (64, DI), BF16, A + 20096)
    xdbl = C.sb_at("l1b_xd%d" % i, (128, NT*96), F32, A + 24192)
    dbct = C.sb_at("l1b_db%d" % i, (96, NT*128), F32, A + 30336)
    delta = C.sb_at("l1b_dl%d" % i, (128, T), F32, A + 38528)
    dtx = C.sb_at("l1b_dx%d" % i, (128, T), F32, A + 46720)
    dA = C.sb_at("l1b_da%d" % i, (128, T), F32, A + 54912)
    ubuf = C.sb_at("l1b_u%d" % i, (128, T), F32, A + 63104)
    hbuf = C.sb_at("l1b_h%d" % i, (128, T), F32, A + 71296)
    hcbuf = C.sb_at("l1b_hc%d" % i, (128, T), F32, A + 79488)
    ybuf = C.sb_at("l1b_y%d" % i, (128, T), F32, A + 87680)
    wour = [C.sb_at("l1b_wo%d_%d" % (i, r), (128, D), BF16, A + 95872 + r*2048)
            for r in range(4)]
    drt_sb = C.sb_at("l1b_drt%d" % i, (64, T), BF16, A + 95872)
    bbc = C.sb_at("l1b_bb%d" % i, (128, T), F32, OFF_YFM)
    cbc2 = C.sb_at("l1b_cc%d" % i, (128, T), F32, OFF_YFM + 8192)
    bobc = C.sb_at("l1b_bo%d" % i, (128, D), F32, OFF_YFM + 16384)
    xstage = C.sb_at("l1b_xs%d" % i, (128, D), F32, OFF_YFM + 20480)
    ostage = C.sb_at("l1b_os%d" % i, (128, D), F32, OFF_YFM + 24576)
    brow = C.sb_at("l1b_br%d" % i, (1, 512), F32, OFF_YFM + 28672)
    win_d = C.dram_in("l1ab_win", (128, 32*KC*128), F32)
    binc_d = C.dram_in("l1ab_bin", (128, 32), F32)
    cw_d = C.dram_in("l1ab_cw", (128, 64), F32)
    cb_d = C.dram_in("l1ab_cbc", (128, 16), F32)
    wx_d = C.dram_in("l1ab_wx", (128, MCI*96), F32)
    wdt_d = C.dram_in("l1ab_wdt", (64, DI), F32)
    bdt_d = C.dram_in("l1ab_bdt", (128, 16), F32)
    alog_d = C.dram_in("l1ab_alog", (128, 256), F32)
    dp_d = C.dram_in("l1ab_dp", (128, 16), F32)
    wout_d = C.dram_in("l1ab_wout", (128, MCI*D), F32)
    bo_d = C.dram_in("l1ab_bo", (1, D), F32)
    bct_dr = C.dram_tmp("l1b_bct", (32, T), F32)
    sres_dr = C.dram_tmp("l1b_sres", (128, MCI*T), BF16)
    xsp = C.dram_tmp("xspill", (128, NT*D), F32)
    PS = C.psum_f32

    emit_spill_x(C)
    with P.group("gpsimd", writes=["l1b_small", "y_tm"]) as g:
        g.dma_(binc[:], binc_d)
        g.dma_(cwc[:], cw_d)
        g.dma_(cbc[:], cb_d)
        g.dma_(alog[:], alog_d)
        g.dma_(dpc[:], dp_d)
        g.dma_(bdtc[:], bdt_d)
        g.dma_(wxc[:], wx_d)
        g.dma_(wdt[:], wdt_d)
    with P.group("scalar", reads=["l1b_small"], writes=["l1b_asb"]) as g:
        g.emit(lambda s: s.activation(asb[:], alog[:], AF.Exp))
    with P.group("vector", reads=["l1b_asb"], writes=["l1b_asb"]) as g:
        g.emit(lambda v: v.tensor_scalar_mul(asb[:], asb[:], -1.0))
    def load_win(mc):
        with P.group("gpsimd", writes=["winr%d" % (mc % 4)] + (["y_tm"] if mc < 4 else [])) as g:
            g.dma_(winr[mc % 4][:], win_d[:, mc*1024:(mc+1)*1024])

    for mc in range(4):
        load_win(mc)
    for mc in range(32):
        isx = mc < MCI
        c = mc if isx else mc - MCI
        for tq in range(4):
            pj = (mc + tq) % 2
            ps = PS[pj][:, 0:512]
            wsl = winr[mc % 4]
            lh = [wsl[:, cc*128:(cc+1)*128] for cc in range(KC)]
            rh = [C.y_fm[:, cc*T + tq*512:(cc*T) + (tq+1)*512] for cc in range(KC)]
            emit_matmul_acc(C, ps, lh, rh, reads=["winr%d" % (mc % 4), "y_fm"],
                            writes=["ps%d" % pj])
            if isx:
                with P.group("vector", reads=["ps%d" % pj, "l1b_small", "l1b_stage"],
                             writes=(["l1b_stage", "y_tm"] if mc == 0 and tq == 0
                                     else ["l1b_stage"])) as g:
                    if tq == 0:
                        g.emit(lambda v: v.memset(stage[:, 0:3], 0.0))
                    else:
                        g.emit(lambda v: v.tensor_copy(out=stage[:, 0:3],
                                                       in_=stage[:, 512:515]))
                    g.emit(lambda v, ps=ps, b=binc[:, c:c+1]:
                           v.tensor_scalar(stage[:, 3:515], ps, b, None, ALU.add))
                    g.emit(lambda v, w=cwc[:, c*4+3:c*4+4]:
                           v.tensor_scalar(stage[:, 0:0] if False else ubuf[:, 0:512],
                                           stage[:, 3:515], w, None, ALU.mult))
                    for k in range(3):
                        g.emit(lambda v, k=k, w=cwc[:, c*4+2-k:c*4+3-k]:
                               v.scalar_tensor_tensor(ubuf[:, 0:512], stage[:, 2-k:514-k],
                                                      w, ubuf[:, 0:512], ALU.mult, ALU.add))
                with P.group("scalar", reads=["l1b_stage", "l1b_small"],
                             writes=(["xc", "x"] if mc == 0 and tq == 0 else ["xc"])) as g:
                    g.emit(lambda s, o=xc[:, c*T + tq*512: c*T + (tq+1)*512],
                           b=cbc[:, c:c+1]:
                           s.activation(o, ubuf[:, 0:512], AF.Silu, bias=b))
            else:
                with P.group("scalar", reads=["ps%d" % pj, "l1b_small"],
                             writes=(["l1b_srst", "y_tm"] if mc == 16 and tq == 0
                                     else ["l1b_srst"])) as g:
                    g.emit(lambda s, ps=ps, o=sres_st[:, tq*512:(tq+1)*512],
                           b=binc[:, 16+c:17+c]:
                           s.activation(o, ps, AF.Silu, bias=b))
        if not isx:
            with P.group("sync", reads=["l1b_srst"], writes=["l1b_sres_dr"]) as g:
                g.dma_(sres_dr[:, c*T:(c+1)*T], sres_st[:])
        if mc + 4 < 32:
            load_win(mc + 4)
    for t in range(NT):
        pj = t % 2
        ps = PS[pj][:, 0:96]
        lh = [xc[:, c*T + t*128: c*T + (t+1)*128] for c in range(MCI)]
        rh = [wxc[:, c*96:(c+1)*96] for c in range(MCI)]
        emit_matmul_acc(C, ps, lh, rh, reads=["xc", "l1b_small"], writes=["ps%d" % pj])
        with P.group("vector", reads=["ps%d" % pj],
                     writes=(["l1b_xdbl", "y_tm"] if t == 0 else ["l1b_xdbl"])) as g:
            g.emit(lambda v, ps=ps, o=xdbl[:, t*96:(t+1)*96]: v.tensor_copy(out=o, in_=ps))
    for t in range(NT):
        with P.group("tensor", reads=["l1b_xdbl", "ident"], writes=["ps_aux"]) as g:
            g.emit(lambda p, t=t: p.transpose(C.ps_aux[0:96, 0:128],
                                              xdbl[:, t*96:(t+1)*96], C.identf[:]))
        with P.group("vector", reads=["ps_aux"],
                     writes=["l1b_dbct", "l1b_drt"] + (["y_tm"] if t == 0 else [])) as g:
            g.emit(lambda v, t=t: v.tensor_copy(out=dbct[:, t*128:(t+1)*128],
                                                in_=C.ps_aux[0:96, 0:128]))
            g.emit(lambda v, t=t: v.tensor_copy(out=drt_sb[:, t*128:(t+1)*128],
                                                in_=C.ps_aux[0:64, 0:128]))
    with P.group("sync", reads=["l1b_dbct"], writes=["l1b_bct_dr"]) as g:
        g.dma_(bct_dr[:], dbct[64:96, :])
    browB = C.sb_at("l1b_brb%d" % i, (1, T), F32, A + 30336)
    browC = C.sb_at("l1b_brc%d" % i, (1, T), F32, A)
    h2 = [hbuf, C.sb_at("l1b_h2b%d" % i, (128, T), F32, OFF_YFM + 16384)]
    for c in range(MCI):
        for tq in range(4):
            pj = (c + tq) % 2
            ps = PS[pj][:, 0:512]
            emit_matmul_acc(C, ps, [wdt[:, c*128:(c+1)*128]],
                            [drt_sb[:, tq*512:(tq+1)*512]],
                            reads=["l1b_drt", "l1b_small"], writes=["ps%d" % pj])
            with P.group("vector", reads=["ps%d" % pj, "l1b_small"], writes=["l1b_s1"]) as g:
                g.emit(lambda v, ps=ps, b=bdtc[:, c:c+1], o=delta[:, tq*512:(tq+1)*512]:
                       v.tensor_scalar(o, ps, b, None, ALU.add))
        with P.group("scalar", reads=["l1b_s1"], writes=["l1b_sp1"]) as g:
            g.emit(lambda s: s.activation(dA[:], delta[:], AF.Abs))
            g.emit(lambda s: s.activation(dA[:], dA[:], AF.Exp, scale=-1.0))
            g.emit(lambda s: s.activation(dA[:], dA[:], AF.Ln, bias=1.0))
        with P.group("vector", reads=["l1b_sp1", "xc"], writes=["l1b_delta"]) as g:
            g.emit(lambda v: v.scalar_tensor_tensor(delta[:], delta[:], 0.0, dA[:],
                                                    ALU.max, ALU.add))
            g.emit(lambda v, xs=xc[:, c*T:(c+1)*T]:
                   v.tensor_tensor(dtx[:], delta[:], xs, ALU.mult))
        def load_b(n):
            w = ["l1b_brb"]
            if c == 0 and n == 0:
                w += ["l1b_dbct"]
            with P.group("sync", reads=["l1b_bct_dr"], writes=w) as g:
                g.dma_(browB[0:1, :], bass.AP(bct_dr.tensor, n*T, [[0, 1], [1, T]]))

        def load_c(n):
            w = ["l1b_brc"]
            if c == 0 and n == 0:
                w += ["winr0", "winr1", "winr2", "winr3"]
            with P.group("scalar", reads=["l1b_bct_dr"], writes=w) as g:
                g.dma_(browC[0:1, :], bass.AP(bct_dr.tensor, (16+n)*T, [[0, 1], [1, T]]))

        def yadd(n):
            with P.group("vector", reads=["l1b_hc"], writes=["l1b_y"]) as g:
                if n == 0:
                    g.emit(lambda v: v.tensor_copy(out=ybuf[:], in_=hcbuf[:]))
                else:
                    g.emit(lambda v: v.tensor_add(ybuf[:], ybuf[:], hcbuf[:]))

        if c == 0:
            load_b(0)
            load_c(0)
        for n in range(DS):
            par = n % 2
            with P.group("scalar", reads=["l1b_delta", "l1b_asb"],
                         writes=["l1b_dA", "l1b_sp1"]) as g:
                g.emit(lambda s, a=asb[:, c*16+n:c*16+n+1]:
                       s.activation(dA[:], delta[:], AF.Exp, scale=a))
            for tq in range(4):
                pj = 2 + tq % 2
                emit_matmul_acc(C, PS[pj][:, 0:512], [C.ones_row[0:1, 0:128]],
                                [browB[0:1, tq*512:(tq+1)*512]],
                                reads=["l1b_brb", "ones"], writes=["ps%d" % pj])
                with P.group("vector", reads=["ps%d" % pj, "l1b_delta"],
                             writes=["l1b_u", "l1b_stage"]) as g:
                    g.emit(lambda v, ps=PS[pj][:, 0:512],
                           o=ubuf[:, tq*512:(tq+1)*512],
                           dx=dtx[:, tq*512:(tq+1)*512]:
                           v.tensor_tensor(o, dx, ps, ALU.mult))
            for tq in range(4):
                pj = 4 + tq % 2
                emit_matmul_acc(C, PS[pj][:, 0:512], [C.ones_row[0:1, 0:128]],
                                [browC[0:1, tq*512:(tq+1)*512]],
                                reads=["l1b_brc", "ones"], writes=["ps%d" % pj])
                ew = ["l1b_cc"]
                if c == 0 and n == 0 and tq == 0:
                    ew += ["y_fm"]
                with P.group("scalar", reads=["ps%d" % pj], writes=ew) as g:
                    g.emit(lambda s, ps=PS[pj][:, 0:512],
                           o=cbc2[:, tq*512:(tq+1)*512]:
                           s.activation(o, ps, AF.Identity))
            if n + 1 < DS:
                load_b(n + 1)
                load_c(n + 1)
            elif c + 1 < MCI:
                load_b(0)
                load_c(0)
            hw = ["l1b_h%d" % par]
            if c == 0 and n == 1:
                hw += ["y_fm"]
            with P.group("vector", reads=["l1b_u", "l1b_dA"], writes=hw) as g:
                g.emit(lambda v, h=h2[par]:
                       v.tensor_tensor_scan(h[:], dA[:], ubuf[:], 0.0,
                                            ALU.mult, ALU.add))
            if n > 0:
                yadd(n - 1)
            with P.group("gpsimd", reads=["l1b_h%d" % par, "l1b_cc"],
                         writes=["l1b_hc"]) as g:
                g.emit(lambda g_, h=h2[par]:
                       g_.tensor_tensor(hcbuf[:], h[:], cbc2[:], ALU.mult))
        yadd(DS - 1)
        with P.group("gpsimd", reads=["l1b_sres_dr"], writes=["l1b_srst"]) as g:
            g.dma_(sres_st[:], sres_dr[:, c*T:(c+1)*T])
        with P.group("vector", reads=["l1b_y", "xc", "l1b_small", "l1b_srst"],
                     writes=["xc", "l1b_y"]) as g:
            g.emit(lambda v, xs=xc[:, c*T:(c+1)*T], dcol=dpc[:, c:c+1]:
                   v.scalar_tensor_tensor(ybuf[:], xs, dcol, ybuf[:], ALU.mult, ALU.add))
            g.emit(lambda v, xs=xc[:, c*T:(c+1)*T]:
                   v.tensor_tensor(xs, ybuf[:], sres_st[:], ALU.mult))
    for hh in range(2):
        with P.group("gpsimd", writes=["l1b_brow", "y_fm"]) as g:
            g.dma_(brow[:], bo_d[0:1, hh*512:(hh+1)*512])
        pj = 4 + hh
        emit_matmul_acc(C, PS[pj][:, 0:512], [C.ones_row[0:1, 0:128]], [brow[0:1, :]],
                        reads=["l1b_brow", "ones"], writes=["ps%d" % pj])
        with P.group("vector", reads=["ps%d" % pj],
                     writes=["l1b_bobc", "y_fm", "l1b_h1"]) as g:
            g.emit(lambda v, ps=PS[pj][:, 0:512], o=bobc[:, hh*512:(hh+1)*512]:
                   v.tensor_copy(out=o, in_=ps))

    for r in range(4):
        with P.group("gpsimd", writes=["wour%d" % r] + (["l1b_drt"] if r == 0 else [])) as g:
            g.dma_(wour[r][:], wout_d[:, r*D:(r+1)*D])
    for t in range(NT):
        with P.group("sync", reads=["xspill"],
                     writes=(["l1b_xst", "y_fm", "l1b_h1"] if t == 0
                             else ["l1b_xst"])) as g:
            g.dma_(xstage[:], xsp[:, t*D:(t+1)*D])
        for mc in range(MCI):
            with P.group("tensor", reads=["xc", "wour%d" % (mc % 4)],
                         writes=["ps0", "ps1"]) as g:
                for dh in range(2):
                    g.emit(lambda p, ps=PS[dh][:, 0:512],
                           lt=xc[:, mc*T + t*128: mc*T + (t+1)*128],
                           rh=wour[mc % 4][:, dh*512:(dh+1)*512],
                           st=(mc == 0), sp=(mc == MCI-1):
                           p.matmul(ps, lt, rh, start=st, stop=sp))
            nxt = mc + 4 if mc + 4 < MCI else (mc + 4 - MCI if t < NT-1 else -1)
            if nxt >= 0:
                with P.group("gpsimd", writes=["wour%d" % (nxt % 4)]) as g:
                    g.dma_(wour[nxt % 4][:], wout_d[:, nxt*D:(nxt+1)*D])
        for dh in range(2):
            with P.group("vector", reads=["ps%d" % dh, "l1b_xst", "l1b_bobc"],
                         writes=(["l1b_ost", "y_fm"] if t == 0 and dh == 0
                                 else ["l1b_ost"])) as g:
                g.emit(lambda v, ps=PS[dh][:, 0:512], o=ostage[:, dh*512:(dh+1)*512],
                       b=bobc[:, dh*512:(dh+1)*512]:
                       v.tensor_add(o, ps, b))
                g.emit(lambda v, o=ostage[:, dh*512:(dh+1)*512],
                       xs=xstage[:, dh*512:(dh+1)*512]:
                       v.tensor_add(o, o, xs))
        with P.group("sync", reads=["l1b_ost"], writes=["xspill"]) as g:
            g.dma_(xsp[:, t*D:(t+1)*D], ostage[:])
    emit_reload_x(C, claims=["xc"])


def prep_l1_attn_b(inputs, h):
    DI = 2048
    w_in = np.asarray(inputs["L1_attn_b_W_in"], np.float32)
    a = w_in.reshape(KC, 128, 32, 128).transpose(1, 2, 0, 3)
    h["l1ab_win"] = np.ascontiguousarray(a.reshape(128, 32*KC*128))
    h["l1ab_bin"] = np.ascontiguousarray(
        np.asarray(inputs["L1_attn_b_b_in"], np.float32).reshape(32, 128).T)
    h["l1ab_cw"] = np.ascontiguousarray(
        np.asarray(inputs["L1_attn_b_conv_w"], np.float32)[:, 0, :]
        .reshape(16, 128, 4).transpose(1, 0, 2).reshape(128, 64))
    h["l1ab_cbc"] = np.ascontiguousarray(
        np.asarray(inputs["L1_attn_b_conv_b"], np.float32).reshape(16, 128).T)
    h["l1ab_wx"] = sb_image_kc(np.ascontiguousarray(
        np.asarray(inputs["L1_attn_b_W_x"], np.float32)), kc=16)
    h["l1ab_wdt"] = np.ascontiguousarray(np.asarray(inputs["L1_attn_b_W_dt"], np.float32))
    h["l1ab_bdt"] = np.ascontiguousarray(
        np.asarray(inputs["L1_attn_b_b_dt"], np.float32).reshape(16, 128).T)
    h["l1ab_alog"] = np.ascontiguousarray(
        np.asarray(inputs["L1_attn_b_A_log"], np.float32).reshape(16, 128, 16)
        .transpose(1, 0, 2).reshape(128, 256))
    h["l1ab_dp"] = np.ascontiguousarray(
        np.asarray(inputs["L1_attn_b_D_par"], np.float32).reshape(16, 128).T)
    h["l1ab_wout"] = sb_image_kc(np.ascontiguousarray(
        np.asarray(inputs["L1_attn_b_W_out"], np.float32)), kc=16)
    h["l1ab_bo"] = np.asarray(inputs["L1_attn_b_b_out"], np.float32).reshape(1, D)


L0F_DEPTH = 6
L0F_GRP = [512, 512, 512, 512, 512, 256]
L0F_GOFF = [0, 512, 1024, 1536, 2048, 2560]
L0F_W1C = 16 * 2816
L0F_SKIP_ATTN = False
L0F_SKIP_FFN = False
L0F_WB_LIM = 8
L0F_SKIP_EPI = False
GP = 2816
NGC = GP // 128


def build_l0_ffn(C, tag):
    P = C.P
    i = C.blk_i
    A = OFF_ARENA
    ZROW = T + 1
    z = C.sb_at("l0f_z%d" % i, (128, KC*ZROW), F32, OFF_X)
    YR = OFF_X + 65568
    tabs = [C.sb_at("l0f_tab%d_%d" % (i, j), (128, NT*64), F32, YR + j*4096)
            for j in range(4)]
    qsbc = C.sb_at("l0f_qsb%d" % i, (128, 64), F32, YR + 16384)
    ksbc = C.sb_at("l0f_ksb%d" % i, (128, 64), F32, YR + 16640)
    madd = C.sb_at("l0f_mad%d" % i, (128, 4*512), F32, YR + 16896)
    rsc = C.sb_at("l0f_rsc%d" % i, (128, 1024), F32, YR + 25088)
    zstA = C.sb_at("l0f_zstA%d" % i, (128, 4), F32, YR + 29184)
    zstF = C.sb_at("l0f_zstF%d" % i, (128, 4), F32, YR + 29216)
    brow = C.sb_at("l0f_br%d" % i, (1, 512), F32, YR)
    rrowA = C.sb_at("l0f_rrA%d" % i, (1, 512), F32, YR + 2048)
    rrowB = C.sb_at("l0f_rrB%d" % i, (1, 512), F32, YR + 4096)
    rbc = C.sb_at("l0f_rbc%d" % i, (128, 512), F32, YR + 6144)
    nbbc = C.sb_at("l0f_nbc%d" % i, (128, 512), F32, YR + 8192)
    w1slB = C.sb_at("l0f_w1B%d" % i, (128, 16*512), BF16, YR + 10240)
    gsc = C.sb_at("l0f_gsc%d" % i, (128, 512), F32, YR + 26624)
    wqkv = C.sb_at("l0f_wqkv%d" % i, (128, KC*3072), BF16, A)
    wo = C.sb_at("l0f_wo%d" % i, (128, KC*1024), BF16, A + 49152)
    zb = C.sb_at("l0f_zb%d" % i, (128, KC*256), BF16, A + 65536)
    qkf = C.sb_at("l0f_qk%d" % i, (128, 1024), F32, A + 69632)
    qhat = C.sb_at("l0f_qh%d" % i, (128, 1024), F32, A + 73728)
    rstd = C.sb_at("l0f_rs%d" % i, (128, 16), F32, A + 77824)
    mcol = C.sb_at("l0f_mc%d" % i, (128, 16), F32, A + 77888)
    qrot = C.sb_at("l0f_qr%d" % i, (128, 2048), BF16, A + 77952)
    qpf = C.sb_at("l0f_qpf%d" % i, (128, 8*256), BF16, A + 82048)
    kpf2 = C.sb_at("l0f_kpf%d" % i, (128, 2*8*256), BF16, A + 86144)
    vtm = [C.sb_at("l0f_vtm%d_%d" % (i, j), (128, 2*1024), BF16, A + 94336 + j*4096)
           for j in range(2)]
    s1 = [C.sb_at("l0f_s1%d_%d" % (i, p), (128, 512), F32, A + 65536 + p*2048)
          for p in range(4)]
    attn_sb = [C.sb_at("l0f_at%d_%d" % (i, p), (128, 512), BF16, A + 77952 + p*1024)
               for p in range(2)]
    attnT_sb = [C.sb_at("l0f_aT%d_%d" % (i, p), (128, 512), BF16, A + 80000 + p*1024)
                for p in range(2)]
    aof = C.sb_at("l0f_aof%d" % i, (128, 8*256), BF16, A + 73728)
    w1slA = C.sb_at("l0f_w1A%d" % i, (128, 16*512), BF16, A)
    w1sl = [w1slA, w1slB]
    w2r = C.sb_at("l0f_w2%d" % i, (128, NGC*1024), BF16, A + 16384)
    hprod = C.sb_at("l0f_hp%d" % i, (128, NGC*512), BF16, A + 61440)
    xnb2 = [C.sb_at("l0f_xn%d_%d" % (i, p), (128, KC*512), BF16,
                    A + (83968 if p == 0 else 94336)) for p in range(2)]
    fs1 = C.sb_at("l0f_fs1%d" % i, (128, 512), F32, A + 92160)
    stat = C.sb_at("l0f_stc%d" % i, (128, 16), F32, A + 94208)
    xst = C.sb_at("l0f_xst%d" % i, (128, 1024), F32, A + 94336)
    ost = C.sb_at("l0f_ost%d" % i, (128, 1024), F32, A + 98432)
    b1ac = C.sb_at("l0f_b1a%d" % i, (128, NGC), F32, A + 102528)
    b1gc = C.sb_at("l0f_b1g%d" % i, (128, NGC), F32, A + 102656)
    b2c = C.sb_at("l0f_b2c%d" % i, (128, KC), F32, A + 102784)
    lngc = C.sb_at("l0f_lng%d" % i, (128, KC), F32, A + 102848)
    lnbc = C.sb_at("l0f_lnb%d" % i, (128, KC), F32, A + 102912)
    wqkv_d = C.dram_in("l0f_wqkv", (128, L0F_DEPTH*KC*3072), F32)
    wo_d = C.dram_in("l0f_wo", (128, L0F_DEPTH*KC*1024), F32)
    w1_d = C.dram_in("l0f_w1", (128, L0F_DEPTH*L0F_W1C), F32)
    w2_d = C.dram_in("l0f_w2", (128, L0F_DEPTH*NGC*1024), F32)
    b1a_d = C.dram_in("l0f_b1a", (128, L0F_DEPTH*NGC), F32)
    b1g_d = C.dram_in("l0f_b1g", (128, L0F_DEPTH*NGC), F32)
    b2_d = C.dram_in("l0f_b2", (128, L0F_DEPTH*KC), F32)
    lng_d = C.dram_in("l0f_lng", (128, L0F_DEPTH*KC), F32)
    lnb_d = C.dram_in("l0f_lnb", (128, L0F_DEPTH*KC), F32)
    qs_d = C.dram_in("l0f_qs", (L0F_DEPTH, 64), F32)
    ks_d = C.dram_in("l0f_ks", (L0F_DEPTH, 64), F32)
    tab_d = C.dram_in("l0f_tabs", (128, 4*NT*64), F32)
    madd_d = C.dram_in("l0f_madd", (128, 4*512), F32)
    xsp = C.dram_tmp("xspill", (128, NT*D), F32)
    PS = C.psum_f32

    emit_spill_x(C)
    P.barrier()
    with P.group("vector", reads=["y_fm"], writes=["z", "x"]) as g:
        for cc in range(KC):
            g.emit(lambda v, cc=cc: v.memset(z[:, cc*ZROW:cc*ZROW+1], 0.0))
            g.emit(lambda v, cc=cc: v.tensor_copy(
                out=z[:, cc*ZROW+1:(cc+1)*ZROW], in_=C.y_fm[:, cc*T:(cc+1)*T]))
    P.barrier()

    def attn_sublayer(l):
        with P.group("gpsimd", writes=["l0f_wqkv"]) as g:
            for cc in range(KC):
                g.dma_(wqkv[:, cc*3072:(cc+1)*3072],
                       wqkv_d[:, l*KC*3072 + cc*3072: l*KC*3072 + (cc+1)*3072])
        with P.group("gpsimd", writes=["l0f_wo"]) as g:
            g.dma_(wo[:], wo_d[:, l*KC*1024:(l+1)*KC*1024])
        with P.group("sync", writes=["l0f_tabs"]) as g:
            for j in range(4):
                g.dma_(tabs[j][:], tab_d[:, j*1024:(j+1)*1024])
        with P.group("gpsimd", writes=["l0f_qsrow"]) as g:
            g.dma_(qsbc[0:1, 0:64], qs_d[l:l+1, :])
            g.dma_(ksbc[0:1, 0:64], ks_d[l:l+1, :])
        for j in range(2):
            src = qsbc if j == 0 else ksbc
            emit_matmul_acc(C, PS[4+j][:, 0:64], [C.ones_row[0:1, 0:128]],
                            [src[0:1, 0:64]],
                            reads=["l0f_qsrow", "ones"], writes=["ps%d" % (4+j)])
            with P.group("vector", reads=["ps%d" % (4+j)],
                         writes=["l0f_qsbc", "l0f_qsrow"]) as g:
                g.emit(lambda v, ps=PS[4+j][:, 0:64], o=src:
                       v.tensor_copy(out=o[:], in_=ps))
        qsrep = bass.AP(qsbc, 0, [[64, 128], [0, NT], [1, 64]])
        qsrot = bass.AP(qsbc, 32, [[64, 128], [0, NT], [-32, 2], [1, 32]])
        ksrep = bass.AP(ksbc, 0, [[64, 128], [0, NT], [1, 64]])
        ksrot = bass.AP(ksbc, 32, [[64, 128], [0, NT], [-32, 2], [1, 32]])
        with P.group("vector", reads=["l0f_tabs", "l0f_qsbc"], writes=["l0f_tabs"], chain=False) as g:
            g.emit(lambda v: v.tensor_tensor(tabs[0][:], tabs[0][:], qsrep, ALU.mult))
            g.emit(lambda v: v.tensor_tensor(tabs[1][:], tabs[1][:], qsrot, ALU.mult))
            g.emit(lambda v: v.tensor_tensor(tabs[2][:], tabs[2][:], ksrep, ALU.mult))
            g.emit(lambda v: v.tensor_tensor(tabs[3][:], tabs[3][:], ksrot, ALU.mult))
        with P.group("gpsimd", writes=["l0f_kpf1", "l0f_vtm1"]) as g:
            g.emit(lambda g_: g_.memset(kpf2[:, 2048:4096], 0.0))
            g.emit(lambda g_: g_.memset(vtm[1][:], 0.0))
        with P.group("scalar", writes=["l0f_madd"]) as g:
            g.dma_(madd[:], madd_d[:])
        P.barrier()
        for wb in range(L0F_WB_LIM):
            cur = wb % 2
            prv = 1 - cur
            with P.group("vector", reads=["z", "l0f_zst"], writes=["l0f_zb"], chain=False) as g:
                for cc in range(KC):
                    off = (1 if cc < 4 else 0)
                    if cc < 4 or wb == 0:
                        g.emit(lambda v, o=zb[:, cc*256:(cc+1)*256],
                               s=z[:, cc*ZROW + off + wb*256: cc*ZROW + off + wb*256 + 256]:
                               v.tensor_copy(out=o, in_=s))
                    else:
                        g.emit(lambda v, o=zb[:, cc*256:cc*256+1],
                               s=zstA[:, cc-4:cc-3]: v.tensor_copy(out=o, in_=s))
                        g.emit(lambda v, o=zb[:, cc*256+1:(cc+1)*256],
                               s=z[:, cc*ZROW + 1 + wb*256: cc*ZROW + wb*256 + 256]:
                               v.tensor_copy(out=o, in_=s))
            with P.group("vector", reads=["z"], writes=["l0f_zst"], chain=False) as g:
                for cc in range(4, KC):
                    g.emit(lambda v, o=zstA[:, cc-4:cc-3],
                           s=z[:, cc*ZROW + 1 + wb*256 + 255: cc*ZROW + 1 + wb*256 + 256]:
                           v.tensor_copy(out=o, in_=s))
            for tt in range(2):
                for mg in range(6):
                    ps = PS[mg][:, 0:512]
                    lh = [zb[:, cc*256 + tt*128: cc*256 + (tt+1)*128] for cc in range(KC)]
                    rh = [wqkv[:, cc*3072 + mg*512: cc*3072 + (mg+1)*512] for cc in range(KC)]
                    emit_matmul_acc(C, ps, lh, rh, reads=["l0f_zb", "l0f_wqkv"],
                                    writes=["ps%d" % mg])
                with P.group("vector", reads=["ps4", "ps5"],
                             writes=["l0f_vtm%d" % cur], chain=False) as g:
                    g.emit(lambda v, o=vtm[cur][:, tt*1024:tt*1024+512]:
                           v.tensor_copy(out=o, in_=PS[4][:, 0:512]))
                    g.emit(lambda v, o=vtm[cur][:, tt*1024+512:(tt+1)*1024]:
                           v.tensor_copy(out=o, in_=PS[5][:, 0:512]))
                for qk in range(2):
                    pq = 0 if qk == 0 else 2
                    tA, tB = (tabs[0], tabs[1]) if qk == 0 else (tabs[2], tabs[3])
                    with P.group("vector", reads=["ps%d" % pq, "ps%d" % (pq+1)],
                                 writes=["l0f_qkf"], chain=False) as g:
                        g.emit(lambda v, pq=pq: v.tensor_copy(out=qkf[:, 0:512],
                                                              in_=PS[pq][:, 0:512]))
                        g.emit(lambda v, pq=pq: v.tensor_copy(out=qkf[:, 512:1024],
                                                              in_=PS[pq+1][:, 0:512]))
                    with P.group("scalar", reads=["l0f_qkf"], writes=["l0f_qh"]) as g:
                        g.emit(lambda s: s.activation(qhat[:], qkf[:], AF.Square))
                    with P.group("vector", reads=["l0f_qh"], writes=["l0f_rstd"]) as g:
                        g.emit(lambda v: v.tensor_reduce(
                            rstd[:], bass.AP(qhat, 0, [[1024, 128], [64, 16], [1, 64]]),
                            AX.X, ALU.add))
                        g.emit(lambda v: v.tensor_scalar_max(rstd[:], rstd[:], 1e-24))
                    with P.group("scalar", reads=["l0f_rstd"], writes=["l0f_rstd2"]) as g:
                        g.emit(lambda s: s.activation(rstd[:], rstd[:], AF.Ln))
                        g.emit(lambda s: s.activation(rstd[:], rstd[:], AF.Exp, scale=-0.5))
                    rrep = bass.AP(rstd, 0, [[16, 128], [1, 16], [0, 64]])
                    ttile = wb*2 + tt
                    tArep = bass.AP(tA, ttile*64, [[NT*64, 128], [0, 16], [1, 64]])
                    tBrep = bass.AP(tB, ttile*64, [[NT*64, 128], [0, 16], [1, 64]])
                    qhrot = bass.AP(qhat, 32, [[1024, 128], [64, 16], [-32, 2], [1, 32]])
                    with P.group("vector", reads=["l0f_rstd2", "l0f_qkf", "l0f_tabs"],
                                 writes=["l0f_qrot", "l0f_qkf", "l0f_qh"]) as g:
                        g.emit(lambda v, rr=rrep: v.tensor_tensor(qhat[:], qkf[:], rr, ALU.mult))
                        g.emit(lambda v, qr=qhrot, tr=tBrep:
                               v.tensor_tensor(rsc[:], qr, tr, ALU.mult))
                        g.emit(lambda v, tr=tArep: v.tensor_tensor(qkf[:], qhat[:], tr, ALU.mult))
                        g.emit(lambda v, o=qrot[:, qk*1024:(qk+1)*1024]:
                               v.tensor_add(o, qkf[:], rsc[:]))
                with P.group("tensor", reads=["l0f_qrot", "ident"], writes=["ps_tr"],
                             chain=False) as g:
                    for cc in range(KC):
                        g.emit(lambda p, cc=cc: p.transpose(
                            C.ps_tr[:, cc*128:(cc+1)*128],
                            qrot[:, cc*128:(cc+1)*128], C.ident[:]))
                with P.group("vector", reads=["ps_tr"], writes=["l0f_qpf"], chain=False) as g:
                    for cc in range(KC):
                        g.emit(lambda v, cc=cc, tt=tt: v.tensor_copy(
                            out=qpf[:, cc*256 + tt*128: cc*256 + (tt+1)*128],
                            in_=C.ps_tr[:, cc*128:(cc+1)*128]))
                with P.group("tensor", reads=["l0f_qrot", "ident"], writes=["ps_tr"],
                             chain=False) as g:
                    for cc in range(KC):
                        g.emit(lambda p, cc=cc: p.transpose(
                            C.ps_tr[:, cc*128:(cc+1)*128],
                            qrot[:, 1024 + cc*128: 1024 + (cc+1)*128], C.ident[:]))
                with P.group("vector", reads=["ps_tr"], writes=["l0f_kpf%d" % cur], chain=False) as g:
                    for cc in range(KC):
                        g.emit(lambda v, o=kpf2[:, cur*2048 + cc*256 + tt*128: cur*2048 + cc*256 + (tt+1)*128],
                               s=C.ps_tr[:, cc*128:(cc+1)*128]:
                               v.tensor_copy(out=o, in_=s))
            P.barrier()
            def st_qk(idx):
                h, it = idx // 2, idx % 2
                p0 = (h % 2) * 64
                cc = h // 2
                ps = PS[idx % 4][:, 0:512]
                rh = bass.AP(kpf2, p0*4096 + prv*2048 + cc*256,
                             [[4096, 64], [(cur - prv)*2048, 2], [1, 256]])
                with P.group("tensor", reads=["l0f_qpf", "l0f_kpf0", "l0f_kpf1"],
                             writes=["ps%d" % (idx % 4)], chain=False) as g:
                    g.emit(lambda p, ps=ps, rh=rh,
                           lt=qpf[p0:p0+64, cc*256 + it*128: cc*256 + (it+1)*128]:
                           p.matmul(ps, lt, rh))

            def st_s1(idx):
                j = idx % 4
                s1p = s1[j]
                mi = (idx % 2)*2 + (1 if wb == 0 else 0)
                ps = PS[j][:, 0:512]
                with P.group("vector", reads=["ps%d" % j, "l0f_madd"],
                             writes=["l0f_s1%d" % j, "l0f_s1e%d" % j]) as g:
                    g.emit(lambda v, ps=ps, s1p=s1p, m=madd[:, mi*512:(mi+1)*512]:
                           v.scalar_tensor_tensor(s1p[:], ps, -8.0, m, ALU.mult, ALU.add))

            def st_mx(idx):
                j = idx % 4
                with P.group("vector", reads=["l0f_s1%d" % j],
                             writes=["l0f_mc%d" % j]) as g:
                    g.emit(lambda v, s1p=s1[j], mc=mcol[:, j*4:j*4+4]:
                           v.tensor_reduce(mc[:, 0:1], s1p[:], AX.X, ALU.min))

            def st_exp(idx):
                j = idx % 4
                with P.group("scalar", reads=["l0f_s1%d" % j, "l0f_mc%d" % j],
                             writes=["l0f_s1e%d" % j, "l0f_mca%d" % j]) as g:
                    g.emit(lambda s, s1p=s1[j], mc=mcol[:, j*4:j*4+4]:
                           s.activation(s1p[:], s1p[:], AF.Exp, scale=-1.0,
                                        bias=mc[:, 0:1], accum_out=mc[:, 1:2]))

            def st_n1(idx):
                j = idx % 4
                with P.group("vector", reads=["l0f_mca%d" % j],
                             writes=["l0f_rcp%d" % j]) as g:
                    g.emit(lambda v, mc=mcol[:, j*4:j*4+4]:
                           v.reciprocal(mc[:, 2:3], mc[:, 1:2]))

            def st_n2(idx):
                j = idx % 4
                par = idx % 2
                with P.group("scalar", reads=["l0f_s1e%d" % j, "l0f_rcp%d" % j],
                             writes=["l0f_attn%d" % par]) as g:
                    g.emit(lambda s, s1p=s1[j], atp=attn_sb[par],
                           mc=mcol[:, j*4:j*4+4]:
                           s.activation(atp[:], s1p[:], AF.Identity,
                                        scale=mc[:, 2:3]))

            def st_tr(idx):
                par = idx % 2
                atp = attn_sb[par]
                pstr = C.ps_tr[:, par*512:(par+1)*512]
                with P.group("tensor", reads=["l0f_attn%d" % par, "ident"],
                             writes=["ps_tr"], chain=False) as g:
                    for jj in range(4):
                        g.emit(lambda p, jj=jj, atp=atp, pstr=pstr: p.transpose(
                            pstr[:, jj*128:(jj+1)*128],
                            atp[:, jj*128:(jj+1)*128], C.ident[:]))

            def st_tT(idx):
                par = idx % 2
                pstr = C.ps_tr[:, par*512:(par+1)*512]
                with P.group("vector", reads=["ps_tr"],
                             writes=["l0f_attnT%d" % par]) as g:
                    g.emit(lambda v, aTp=attnT_sb[par], pstr=pstr:
                           v.tensor_copy(out=aTp[:], in_=pstr))

            def st_pv(idx):
                h, it = idx // 2, idx % 2
                p0 = (h % 2) * 64
                par = idx % 2
                aTp = attnT_sb[par]
                po = PS[4 + par][p0:p0+64, 0:128]
                with P.group("tensor", reads=["l0f_attnT%d" % par,
                                              "l0f_vtm0", "l0f_vtm1"],
                             writes=["psv%d" % par], chain=False) as g:
                    for jj in range(4):
                        vsrc = vtm[prv] if jj < 2 else vtm[cur]
                        g.emit(lambda p, po=po, tp=(0, p0), aTp=aTp,
                               lt=vsrc[:, (jj % 2)*1024 + h*64: (jj % 2)*1024 + (h+1)*64],
                               rh=aTp[:, jj*128:(jj+1)*128],
                               st=(jj == 0), sp=(jj == 3):
                               p.matmul(po, lt, rh, start=st, stop=sp, tile_position=tp))

            def st_ao(idx):
                h, it = idx // 2, idx % 2
                p0 = (h % 2) * 64
                cc = h // 2
                par = idx % 2
                po = PS[4 + par][p0:p0+64, 0:128]
                with P.group("scalar", reads=["psv%d" % par],
                             writes=["l0f_aof"]) as g:
                    g.emit(lambda s, po=po,
                           o=aof[p0:p0+64, cc*256 + it*128: cc*256 + (it+1)*128]:
                           s.activation(o, po, AF.Identity))

            def st_pvao(idx):
                st_pv(idx); st_ao(idx)
            stages = [(8, st_pvao), (7, st_tT), (6, st_tr),
                      (5, st_n2), (4, st_n1), (3, st_exp), (2, st_mx),
                      (1, st_s1), (0, st_qk)]
            for step in range(32 + 8):
                for off, fn in stages:
                    if off <= step < 32 + off:
                        fn(step - off)
            for dp in range(4):
                pj = 2 + (dp % 2)
                ps = PS[pj]
                with P.group("tensor", reads=["l0f_aof", "l0f_wo"],
                             writes=["ps%d" % pj], chain=False) as g:
                    for half in range(2):
                        dc = dp*2 + half
                        for ic in range(KC):
                            g.emit(lambda p, ps=ps[:, half*256:(half+1)*256],
                                   lt=wo[:, ic*1024 + dc*128: ic*1024 + (dc+1)*128],
                                   rh=aof[:, ic*256:(ic+1)*256],
                                   st=(ic == 0), sp=(ic == KC-1):
                                   p.matmul(ps, lt, rh, start=st, stop=sp))
                with P.group("vector", reads=["ps%d" % pj, "z"], writes=["z"], chain=False) as g:
                    for half in range(2):
                        dc = dp*2 + half
                        zsl = z[:, dc*ZROW + 1 + wb*256: dc*ZROW + 1 + (wb+1)*256]
                        g.emit(lambda v, ps=ps[:, half*256:(half+1)*256], o=zsl:
                               v.tensor_add(o, o, ps))
            P.barrier()

    def ffn_sublayer(l):
        with P.group("gpsimd", writes=["l0f_fparm"]) as g:
            g.dma_(b1ac[:], b1a_d[:, l*NGC:(l+1)*NGC])
            g.dma_(b1gc[:], b1g_d[:, l*NGC:(l+1)*NGC])
            g.dma_(b2c[:], b2_d[:, l*KC:(l+1)*KC])
            g.dma_(lngc[:], lng_d[:, l*KC:(l+1)*KC])
            g.dma_(lnbc[:], lnb_d[:, l*KC:(l+1)*KC])
        with P.group("gpsimd", writes=["l0f_w2r"]) as g:
            for q4 in range(0, NGC, 6):
                qn = min(6, NGC - q4)
                g.dma_(w2r[:, q4*1024:(q4+qn)*1024],
                       w2_d[:, l*NGC*1024 + q4*1024: l*NGC*1024 + (q4+qn)*1024])
        P.barrier()

        def ffn_pre(tc):
            par = tc % 2
            xnb = xnb2[par]
            with P.group("vector", reads=["z", "l0f_zst"],
                         writes=["l0f_xnb%d" % par], chain=False) as g:
                for cc in range(KC):
                    off = (1 if cc < 4 else 0)
                    if cc < 4 or tc == 0:
                        g.emit(lambda v, o=xnb[:, cc*512:(cc+1)*512],
                               s=z[:, cc*ZROW + off + tc*512: cc*ZROW + off + tc*512 + 512]:
                               v.tensor_copy(out=o, in_=s))
                    else:
                        g.emit(lambda v, o=xnb[:, cc*512:cc*512+1],
                               s=zstF[:, cc-4:cc-3]: v.tensor_copy(out=o, in_=s))
                        g.emit(lambda v, o=xnb[:, cc*512+1:(cc+1)*512],
                               s=z[:, cc*ZROW + 1 + tc*512: cc*ZROW + tc*512 + 512]:
                               v.tensor_copy(out=o, in_=s))
            for tt in range(4):
                emit_matmul_acc(C, C.ps_aux[:, 128:129],
                                [xnb[:, cc*512 + tt*128: cc*512 + (tt+1)*128]
                                 for cc in range(KC)],
                                [C.ones_cb[:] for cc in range(KC)],
                                reads=["l0f_xnb%d" % par, "ones"], writes=["ps_aux"])
                emit_matmul_acc(C, C.ps_aux[:, 0:128],
                                [xnb[:, cc*512 + tt*128: cc*512 + (tt+1)*128]
                                 for cc in range(KC)],
                                [xnb[:, cc*512 + tt*128: cc*512 + (tt+1)*128]
                                 for cc in range(KC)],
                                reads=["l0f_xnb%d" % par], writes=["ps_aux"])
                with P.group("vector", reads=["ps_aux", "ident"],
                             writes=["l0f_stat"]) as g:
                    g.emit(lambda v, tt=tt: v.tensor_copy(out=stat[:, tt*4:tt*4+1],
                                                          in_=C.ps_aux[:, 128:129]))
                    g.emit(lambda v: v.tensor_tensor(
                        fs1[:, 0:128], C.ps_aux[:, 0:128], C.identf[:], ALU.mult))
                    g.emit(lambda v, o=stat[:, tt*4+1:tt*4+2]:
                           v.reduce_sum(o, fs1[:, 0:128], AX.X))
            sv = lambda c0: bass.AP(stat, c0, [[16, 128], [4, 4], [1, 1]])
            with P.group("vector", reads=["l0f_stat"], writes=["l0f_stat2"]) as g:
                g.emit(lambda v: v.tensor_scalar_mul(sv(0), sv(0), 1.0/D))
                g.emit(lambda v: v.tensor_scalar_mul(sv(1), sv(1), 1.0/D))
                g.emit(lambda v: v.tensor_tensor(sv(2), sv(0), sv(0), ALU.mult))
                g.emit(lambda v: v.tensor_tensor(sv(2), sv(1), sv(2), ALU.subtract))
                g.emit(lambda v: v.tensor_scalar_add(sv(2), sv(2), 1e-5))
            with P.group("scalar", reads=["l0f_stat2"], writes=["l0f_stat3"]) as g:
                g.emit(lambda s: s.activation(sv(3), sv(2), AF.Ln))
                g.emit(lambda s: s.activation(sv(3), sv(3), AF.Exp, scale=-0.5))
            with P.group("vector", reads=["l0f_stat3"], writes=["l0f_stat4"]) as g:
                g.emit(lambda v: v.tensor_tensor(sv(0), sv(0), sv(3), ALU.mult))
            for j in range(2):
                c0 = 3 if j == 0 else 0
                rrow = rrowA if j == 0 else rrowB
                with P.group("tensor", reads=["l0f_stat4", "ident"],
                             writes=["ps_aux"], chain=False) as g:
                    for tt in range(4):
                        g.emit(lambda p, tt=tt, c0=c0: p.transpose(
                            C.ps_aux[0:1, tt*128:(tt+1)*128],
                            stat[:, tt*4+c0:tt*4+c0+1], C.identf[:]))
                with P.group("vector", reads=["ps_aux"], writes=["l0f_rrow"]) as g:
                    g.emit(lambda v, rrow=rrow: v.tensor_copy(
                        out=rrow[0:1, :], in_=C.ps_aux[0:1, 0:512]))
                emit_matmul_acc(C, C.ps_aux[:, 0:512], [C.ones_row[0:1, 0:128]],
                                [rrow[0:1, :]],
                                reads=["l0f_rrow", "ones"], writes=["ps_aux"])
                with P.group("vector", reads=["ps_aux"], writes=["l0f_rbc"]) as g:
                    g.emit(lambda v, o=(rbc if j == 0 else nbbc):
                           v.tensor_copy(out=o[:], in_=C.ps_aux[:, 0:512]))
            for cc in range(KC):
                off = (1 if cc < 4 else 0)
                zsl = z[:, cc*ZROW + off + tc*512: cc*ZROW + off + tc*512 + 512]
                with P.group("vector", reads=["z", "l0f_rbc", "l0f_fparm", "l0f_zst"],
                             writes=["l0f_xnb%d" % par, "l0f_fs1"]) as g:
                    g.emit(lambda v, zsl=zsl: v.tensor_tensor(fs1[:], zsl, rbc[:], ALU.mult))
                    g.emit(lambda v: v.tensor_tensor(fs1[:], fs1[:], nbbc[:], ALU.subtract))
                    g.emit(lambda v, cc=cc, xnb=xnb: v.tensor_scalar(
                        xnb[:, cc*512:(cc+1)*512], fs1[:], lngc[:, cc:cc+1],
                        lnbc[:, cc:cc+1], ALU.mult, ALU.add))
                    if cc >= 4 and tc > 0:
                        g.emit(lambda v, s=zstF[:, cc-4:cc-3]: v.tensor_tensor(
                            fs1[:, 0:1], s, rbc[:, 0:1], ALU.mult))
                        g.emit(lambda v: v.tensor_tensor(
                            fs1[:, 0:1], fs1[:, 0:1], nbbc[:, 0:1], ALU.subtract))
                        g.emit(lambda v, cc=cc, xnb=xnb: v.tensor_scalar(
                            xnb[:, cc*512:cc*512+1], fs1[:, 0:1], lngc[:, cc:cc+1],
                            lnbc[:, cc:cc+1], ALU.mult, ALU.add))
            with P.group("vector", reads=["z"], writes=["l0f_zst"], chain=False) as g:
                for cc in range(4, KC):
                    g.emit(lambda v, o=zstF[:, cc-4:cc-3],
                           s=z[:, cc*ZROW + 1 + tc*512 + 511: cc*ZROW + 1 + tc*512 + 512]:
                           v.tensor_copy(out=o, in_=s))

        def ffn_w1(tc):
            par = tc % 2
            xnb = xnb2[par]
            for j in range(len(L0F_GRP)):
                G = L0F_GRP[j]
                w1s = w1sl[j % 2]
                with P.group("gpsimd", writes=["l0f_w1s%d" % (j % 2)]) as g:
                    base = l*L0F_W1C + 16*L0F_GOFF[j]
                    g.dma_(w1s[:, 0:16*G], w1_d[:, base: base + 16*G])
                for ms in range(G // 128):
                    gc = L0F_GOFF[j] // 128 + ms
                    bk = 4 if gc % 2 == 0 else 2
                    for ag in range(2):
                        pj = bk + ag
                        ps = PS[pj][:, 0:512]
                        lh = [w1s[:, cc*2*G + ag*G + ms*128: cc*2*G + ag*G + (ms+1)*128]
                              for cc in range(KC)]
                        rh = [xnb[:, cc*512:(cc+1)*512] for cc in range(KC)]
                        emit_matmul_acc(C, ps, lh, rh,
                                        reads=["l0f_w1s%d" % (j % 2), "l0f_xnb%d" % par],
                                        writes=["ps%d" % pj])
                    with P.group("scalar", reads=["ps%d" % (bk+1), "l0f_fparm", "l0f_gg2"],
                                 writes=["l0f_gg"]) as g:
                        g.emit(lambda s, b=b1gc[:, gc:gc+1], bk=bk:
                               s.activation(gsc[:], PS[bk+1][:, 0:512], AF.Gelu, bias=b))
                    with P.group("vector", reads=["ps%d" % bk, "l0f_gg", "l0f_fparm"],
                                 writes=["l0f_hp", "l0f_gg2"]) as g:
                        g.emit(lambda v, b=b1ac[:, gc:gc+1], bk=bk,
                               o=hprod[:, gc*512:(gc+1)*512]:
                               v.scalar_tensor_tensor(o, PS[bk][:, 0:512], b, gsc[:],
                                                      ALU.add, ALU.mult))

        def ffn_w2(tc):
            for th in range(2):
                for dp in range(4):
                    pj = dp % 4
                    ps = PS[pj]
                    with P.group("tensor", reads=["l0f_hp", "l0f_w2r"],
                                 writes=["ps%d" % pj], chain=False) as g:
                        for half in range(2):
                            dc = dp*2 + half
                            for gc in range(NGC):
                                g.emit(lambda p, ps=ps[:, half*256:(half+1)*256],
                                       lt=w2r[:, gc*1024 + dc*128: gc*1024 + (dc+1)*128],
                                       rh=hprod[:, gc*512 + th*256: gc*512 + (th+1)*256],
                                       st=(gc == 0), sp=(gc == NGC-1):
                                       p.matmul(ps, lt, rh, start=st, stop=sp))
                    with P.group("vector", reads=["ps%d" % pj, "z", "l0f_fparm"],
                                 writes=["z"], chain=False) as g:
                        for half in range(2):
                            dc = dp*2 + half
                            zsl = z[:, dc*ZROW + 1 + tc*512 + th*256:
                                    dc*ZROW + 1 + tc*512 + (th+1)*256]
                            g.emit(lambda v, ps=ps[:, half*256:(half+1)*256], o=zsl,
                                   b=b2c[:, dc:dc+1]:
                                   v.scalar_tensor_tensor(o, ps, b, o, ALU.add, ALU.add))

        ffn_pre(0)
        for tc in range(4):
            ffn_w1(tc)
            if tc < 3:
                ffn_pre(tc + 1)
            ffn_w2(tc)
        P.barrier()

    for l in range(L0F_DEPTH):
        if not L0F_SKIP_ATTN:
            attn_sublayer(l)
        P.barrier()
        if not L0F_SKIP_FFN:
            ffn_sublayer(l)
    P.barrier()
    for t in range(NT if not L0F_SKIP_EPI else 0):
        with P.group("sync", reads=["xspill"], writes=["l0f_xst"]) as g:
            g.dma_(xst[:], xsp[:, t*D:(t+1)*D])
        for hh in range(2):
            with P.group("tensor", reads=["z", "ident"], writes=["ps_aux"],
                         chain=False) as g:
                for q in range(4):
                    cc = hh*4 + q
                    g.emit(lambda p, cc=cc, q=q, t=t: p.transpose(
                        C.ps_aux[:, q*128:(q+1)*128],
                        z[:, cc*ZROW + 1 + t*128: cc*ZROW + 1 + (t+1)*128], C.identf[:]))
            with P.group("vector", reads=["ps_aux", "l0f_xst"], writes=["l0f_ost"]) as g:
                g.emit(lambda v, hh=hh: v.tensor_add(
                    ost[:, hh*512:(hh+1)*512], xst[:, hh*512:(hh+1)*512],
                    C.ps_aux[:, 0:512]))
        with P.group("sync", reads=["l0f_ost"], writes=["xspill"]) as g:
            g.dma_(xsp[:, t*D:(t+1)*D], ost[:])
    emit_reload_x(C, claims=["z"])


def prep_l0_ffn(inputs, h):
    if "l0f_wqkv" in h:
        return
    DEPTH = L0F_DEPTH
    q = np.arange(512, dtype=np.float64)[None, :] - np.arange(128, dtype=np.float64)[:, None]
    r = (np.arange(512)[None, :] >= 256)
    madd = np.zeros((128, 4*512), np.float32)
    for a in range(2):
        for w0 in range(2):
            m = (q >= a*128) & (q <= a*128 + 256)
            if w0:
                m = m & r
            madd[:, (a*2+w0)*512:(a*2+w0+1)*512] = \
                (m.astype(np.float32) - 1.0) * -1e30
    h["l0f_madd"] = madd
    wqkv = np.asarray(inputs["L0_ffn_Wqkv"], np.float32)
    h["l0f_wqkv"] = np.ascontiguousarray(np.concatenate(
        [sb_image_kc(np.ascontiguousarray(wqkv[l])) for l in range(DEPTH)], axis=1))
    wo = np.asarray(inputs["L0_ffn_Wo"], np.float32)
    h["l0f_wo"] = np.ascontiguousarray(np.concatenate(
        [sb_image_kc(np.ascontiguousarray(wo[l])) for l in range(DEPTH)], axis=1))
    W1 = np.asarray(inputs["L0_ffn_W1"], np.float32)
    b1 = np.asarray(inputs["L0_ffn_b1"], np.float32)
    W2 = np.asarray(inputs["L0_ffn_W2"], np.float32)
    w1_imgs, b1a_imgs, b1g_imgs, w2_imgs = [], [], [], []
    for l in range(DEPTH):
        W1a = np.zeros((D, GP), np.float32); W1a[:, :2730] = W1[l][:, :2730]
        W1g = np.zeros((D, GP), np.float32); W1g[:, :2730] = W1[l][:, 2730:]
        b1a = np.zeros(GP, np.float32); b1a[:2730] = b1[l][:2730]
        b1g = np.zeros(GP, np.float32); b1g[:2730] = b1[l][2730:]
        cols = []
        for j in range(len(L0F_GRP)):
            G, j0 = L0F_GRP[j], L0F_GOFF[j]
            grp = np.concatenate([W1a[:, j0:j0+G], W1g[:, j0:j0+G]], axis=1)
            cols.append(grp.reshape(KC, 128, 2*G).transpose(1, 0, 2).reshape(128, KC*2*G))
        w1_imgs.append(np.concatenate(cols, axis=1))
        b1a_imgs.append(b1a.reshape(NGC, 128).T)
        b1g_imgs.append(b1g.reshape(NGC, 128).T)
        W2p = np.zeros((GP, D), np.float32); W2p[:2730] = W2[l]
        w2_imgs.append(sb_image_kc(W2p, kc=NGC))
    h["l0f_w1"] = np.ascontiguousarray(np.concatenate(w1_imgs, axis=1))
    h["l0f_b1a"] = np.ascontiguousarray(np.concatenate(b1a_imgs, axis=1))
    h["l0f_b1g"] = np.ascontiguousarray(np.concatenate(b1g_imgs, axis=1))
    h["l0f_w2"] = np.ascontiguousarray(np.concatenate(w2_imgs, axis=1))
    h["l0f_b2"] = np.ascontiguousarray(
        np.asarray(inputs["L0_ffn_b2"], np.float32)[:DEPTH].reshape(DEPTH, KC, 128)
        .transpose(2, 0, 1).reshape(128, DEPTH*KC))
    h["l0f_lng"] = np.ascontiguousarray(
        np.asarray(inputs["L0_ffn_ln_g"], np.float32)[:DEPTH].reshape(DEPTH, KC, 128)
        .transpose(2, 0, 1).reshape(128, DEPTH*KC))
    h["l0f_lnb"] = np.ascontiguousarray(
        np.asarray(inputs["L0_ffn_ln_b"], np.float32)[:DEPTH].reshape(DEPTH, KC, 128)
        .transpose(2, 0, 1).reshape(128, DEPTH*KC))
    h["l0f_qs"] = np.ascontiguousarray(
        np.asarray(inputs["L0_ffn_q_scale"], np.float32)[:DEPTH])
    h["l0f_ks"] = np.ascontiguousarray(
        np.asarray(inputs["L0_ffn_k_scale"], np.float32)[:DEPTH])
    DIM_HEAD, WINDOW = 64, 256
    half = np.arange(0, DIM_HEAD, 2, dtype=np.float64)
    inv_freq = 1.0 / 10000.0 ** (half / DIM_HEAD)
    tt = np.arange(T, dtype=np.float64)
    freqs = np.concatenate([tt[:, None] * inv_freq[None, :]] * 2, axis=-1)
    base_scale = (half + 0.4 * DIM_HEAD) / (1.4 * DIM_HEAD)
    power = (tt - T // 2) / (WINDOW // 2)
    scale = np.concatenate([base_scale[None, :] ** power[:, None]] * 2, axis=-1)
    cos, sin = np.cos(freqs), np.sin(freqs)
    sgn = np.ones(64); sgn[:32] = -1.0
    def timg(tb):
        return tb.reshape(NT, 128, 64).transpose(1, 0, 2).reshape(128, NT*64)
    h["l0f_tabs"] = np.ascontiguousarray(np.concatenate(
        [timg((cos * scale).astype(np.float32)),
         timg((sin * scale * sgn[None, :]).astype(np.float32)),
         timg((cos / scale).astype(np.float32)),
         timg((sin / scale * sgn[None, :]).astype(np.float32))], axis=1))


BLOCK_BUILDERS = {"l1_ffn": build_l1_ffn, "l1_attn_a": build_l1_attn_a, "l0_attn_b": build_l0_attn_b, "l0_attn_a": build_l0_attn_a, "l1_attn_b": build_l1_attn_b, "l0_ffn": build_l0_ffn}


def sb_image_kc(wT, kc=8):
    F = wT.shape[1]
    return np.ascontiguousarray(wT.reshape(kc, 128, F).transpose(1, 0, 2).reshape(128, kc*F))


def sb_image_slabs(wT, mslab=128):
    K, M = wT.shape
    kc = K // 128
    nm = M // mslab
    a = wT.reshape(kc, 128, nm, mslab).transpose(1, 2, 0, 3)
    return np.ascontiguousarray(a.reshape(128, nm*kc*mslab))


def prep_l1_ffn(inputs, h):
    wgT = np.asarray(inputs["L1_ffn_w_gate"]).T
    wuT = np.asarray(inputs["L1_ffn_w_up"]).T
    wdT = np.asarray(inputs["L1_ffn_w_down"]).T
    h["l1ffn_wg_sb"] = sb_image_kc(np.ascontiguousarray(wgT))
    h["l1ffn_wu_sb"] = sb_image_slabs(np.ascontiguousarray(wuT), 128)
    h["l1ffn_wd_sb"] = sb_image_kc(np.ascontiguousarray(wdT), kc=24)


BLOCK_PREP = {"l1_ffn": prep_l1_ffn, "l1_attn_a": prep_l1_attn_a, "l0_attn_b": prep_l0_attn_b, "l0_attn_a": prep_l0_attn_a, "l1_attn_b": prep_l1_attn_b, "l0_ffn": prep_l0_ffn}

ALL_BLOCKS = ["l0_attn_a", "l0_ffn", "l0_attn_b", "l0_ffn", "l1_attn_a", "l1_ffn",
              "l1_attn_b", "l1_ffn"]


def build_program(blocks):
    C = Ctx()
    C.P = Prog(C.nc)
    _build_body(C, blocks)
    C.nc.all_engine_barrier()
    C.nc.reset()
    return C


def _build_body(C, blocks):
    P = C.P
    alloc_common(C)
    x_in = C.dram_in("x_in", (128, NT*D), F32)
    x_out = C.dram_out("x_out", (128, NT*D), F32)
    with P.group("sync", writes=["x"]) as g:
        for t in range(0, NT, 8):
            g.dma_(C.x_tm[:, t*D:(t+8)*D], x_in[:, t*D:(t+8)*D])
    for i, b in enumerate(blocks):
        C.blk_i = i
        P.barrier()
        emit_prologue(C)
        BLOCK_BUILDERS[b](C, "b%d" % i)
    P.barrier()
    for dn, (dap, dkeys) in getattr(C, "debug_tiles", {}).items():
        dd = C.dram_out("dbg_" + dn, list(dap.shape), dap.dtype)
        with P.group("sync", reads=list(dkeys)) as g:
            g.dma_(dd, dap)
    with P.group("sync", reads=["x"], writes=["xout"]) as g:
        for t in range(0, NT, 8):
            g.dma_(x_out[:, t*D:(t+8)*D], C.x_tm[:, t*D:(t+8)*D])
    sem, cnt = P.sem["sync.d"], P.count["sync.d"]
    P.ops["sync"].append(lambda en, s=sem, v=cnt: en.wait_ge(s, v))
    P.barrier()
    P.finalize()


def run_blocks(x, inputs, blocks):
    C = build_program(blocks)
    h = {}
    for b in set(blocks):
        BLOCK_PREP[b](inputs, h)
    in_maps = []
    for c in range(N_CORES):
        xc = np.ascontiguousarray(x[c].reshape(NT, 128, D).transpose(1, 0, 2).reshape(128, NT*D))
        m = {"x_in": xc}
        m.update(h)
        in_maps.append(m)
    res = run_bass_kernel_spmd(C.nc, in_maps, core_ids=list(range(N_CORES)))
    outs = []
    for r in res.results:
        xo = np.asarray(r["x_out"]).reshape(128, NT, D).transpose(1, 0, 2).reshape(T, D)
        outs.append(xo)
    return np.stack(outs)


def kernel(**inputs):
    x = np.asarray(inputs["L0_attn_a_x"])
    x = run_blocks(x, inputs, ALL_BLOCKS)
    return x
